# version 06
# baseline (speedup 1.0000x reference)
import numpy as np
import ml_dtypes
import concourse.bass as bass
import concourse.mybir as mybir
from concourse.bass_utils import run_bass_kernel_spmd

F32 = mybir.dt.float32
BF16 = mybir.dt.bfloat16
AF = mybir.ActivationFunctionType
ALU = mybir.AluOpType
AX = mybir.AxisListType

T, D, H, DK = 4096, 2048, 16, 128
NT = T // 128
CT1 = 1024
NC1 = T // CT1
CC = 128
NCC = T // CC
INTER = 5632
NJB = (2 * INTER) // 512
LN_EPS = 1e-5
L2_EPS = 1e-6
SCALE = float(DK) ** -0.5

DEBUG = False
STOP_AFTER = None



class Sem:
    def __init__(self, handle):
        self.h = handle
        self.n = 0

    def inc(self, k=1):
        self.n += k
        return self.n


class Prog:
    ENGS = ("sync", "tensor", "vector", "scalar", "gpsimd")

    def __init__(self):
        self.ops = {e: [] for e in self.ENGS}

    def add(self, eng, fn):
        self.ops[eng].append(fn)


def _ap(x):
    return x.ap() if hasattr(x, "ap") else x


class _Chain:
    __slots__ = ("entry",)

    def __init__(self, entry):
        self.entry = entry

    def __getattr__(self, name):
        def call(*a, **k):
            self.entry.append((name, a, k))
            return self
        return call


class Rec:

    def __init__(self):
        self.calls = []

    def __getattr__(self, name):
        if name.startswith("_"):
            raise AttributeError(name)

        def call(*a, **k):
            entry = [(name, a, k)]
            self.calls.append(entry)
            return _Chain(entry)
        return call

    def replay(self, eng):
        for entry in self.calls:
            obj = eng
            for (name, a, k) in entry:
                obj = getattr(obj, name)(*a, **k)



def build_program(debug=False, stop_after=None):
    nc = bass.Bass("TRN2", target_bir_lowering=False,
                   detect_race_conditions=False)

    scratch_kind = "ExternalOutput" if debug else "Internal"

    x_in = nc.dram_tensor("x", [T, D], F32, kind="ExternalInput").ap()
    wq_in = nc.dram_tensor("Wq", [D, D], F32, kind="ExternalInput").ap()
    wk_in = nc.dram_tensor("Wk", [D, D], F32, kind="ExternalInput").ap()
    wv_in = nc.dram_tensor("Wv", [D, D], F32, kind="ExternalInput").ap()
    ww_in = nc.dram_tensor("Ww", [D, D], F32, kind="ExternalInput").ap()
    wo_in = nc.dram_tensor("Wo", [D, D], F32, kind="ExternalInput").ap()
    cq_in = nc.dram_tensor("cq", [D, 4], F32, kind="ExternalInput").ap()
    ck_in = nc.dram_tensor("ck", [D, 4], F32, kind="ExternalInput").ap()
    cv_in = nc.dram_tensor("cv", [D, 4], F32, kind="ExternalInput").ap()
    wf1_in = nc.dram_tensor("Wf1", [D, 128], F32, kind="ExternalInput").ap()
    wf2_in = nc.dram_tensor("Wf2", [128, D], F32, kind="ExternalInput").ap()
    wb_in = nc.dram_tensor("Wb", [D, 16], F32, kind="ExternalInput").ap()
    wg1_in = nc.dram_tensor("Wg1", [D, 128], F32, kind="ExternalInput").ap()
    wg2_in = nc.dram_tensor("Wg2", [128, D], F32, kind="ExternalInput").ap()
    bg2_in = nc.dram_tensor("bg2", [1, D], F32, kind="ExternalInput").ap()
    nw_in = nc.dram_tensor("norm_w", [1, 128], F32, kind="ExternalInput").ap()
    gw_in = nc.dram_tensor("gate_w", [2 * INTER, D], F32, kind="ExternalInput").ap()
    dw_in = nc.dram_tensor("down_w", [D, INTER], F32, kind="ExternalInput").ap()
    gnw_in = nc.dram_tensor("gate_norm_w", [1, D], F32, kind="ExternalInput").ap()
    dnw_in = nc.dram_tensor("down_norm_w", [1, INTER], F32, kind="ExternalInput").ap()
    cvec_in = nc.dram_tensor("cvec", [1, D], F32, kind="ExternalInput").ap()
    wmod_in = nc.dram_tensor("wmod", [D, 6 * D], F32, kind="ExternalInput").ap()

    out_hbm = nc.dram_tensor("out", [T, D], F32, kind="ExternalOutput").ap()

    wq_t = nc.dram_tensor("wq_t", [16, 128, 16, 128], F32, kind=scratch_kind).ap()
    wk_t = nc.dram_tensor("wk_t", [16, 128, 16, 128], F32, kind="Internal").ap()
    wv_t = nc.dram_tensor("wv_t", [16, 128, 16, 128], F32, kind="Internal").ap()
    ww_t = nc.dram_tensor("ww_t", [16, 128, 16, 128], F32, kind="Internal").ap()
    wo_t = nc.dram_tensor("wo_t", [16, 128, 16, 128], F32, kind="Internal").ap()
    gwqT_s = nc.dram_tensor("gwqT_s", [16, 88, 128, 128], BF16, kind=scratch_kind).ap()
    dwqT_s = nc.dram_tensor("dwqT_s", [44, 16, 128, 128], BF16, kind=scratch_kind).ap()
    mod_s = nc.dram_tensor("mod_s", [6, D], F32, kind=scratch_kind).ap()
    q_fm_s = nc.dram_tensor("q_fm_s", [H, DK, T], F32, kind=scratch_kind).ap()
    k_fm_s = nc.dram_tensor("k_fm_s", [H, DK, T], F32, kind=scratch_kind).ap()
    k_tm_s = nc.dram_tensor("k_tm_s", [NCC, H, CC, DK], F32, kind=scratch_kind).ap()
    v_tm_s = nc.dram_tensor("v_tm_s", [NCC, H, CC, DK], F32, kind=scratch_kind).ap()
    w_tm_s = nc.dram_tensor("w_tm_s", [NCC, H, CC, DK], F32, kind=scratch_kind).ap()
    z_tm_s = nc.dram_tensor("z_tm_s", [NCC, H, CC, DK], F32, kind=scratch_kind).ap()
    beta_s = nc.dram_tensor("beta_s", [NCC, CC, H], F32, kind=scratch_kind).ap()
    sg_s = nc.dram_tensor("sg_s", [T, D], F32, kind=scratch_kind).ap()
    yo_s = nc.dram_tensor("yo_s", [H, T, DK], F32, kind=scratch_kind).ap()
    x2_s = nc.dram_tensor("x2_s", [T, D], F32, kind=scratch_kind).ap()
    h3_s = nc.dram_tensor("h3_s", [16, NJB // 2, 128, 512], F32, kind=scratch_kind).ap()
    xq2_s = nc.dram_tensor("xq2_s", [16, 44, 128, 128], BF16, kind="Internal").ap()

    ident_f32 = nc.inline_tensor(np.eye(128, dtype=np.float32), "ident_f32")
    ident_bf16 = nc.inline_tensor(np.eye(128, dtype=ml_dtypes.bfloat16), "ident_bf16")
    ltri_np = np.triu(np.ones((128, 128), dtype=np.float32))
    ltri_f32 = nc.inline_tensor(ltri_np, "ltri_f32")
    ones_1x128 = nc.inline_tensor(np.ones((1, 128), dtype=np.float32), "ones_1x128")
    ones_128x1 = nc.inline_tensor(np.ones((128, 1), dtype=np.float32), "ones_128x1")

    p = Prog()

    import contextlib
    stack = contextlib.ExitStack()

    def sem(name):
        return Sem(stack.enter_context(nc.semaphore(name)))

    def sb(name, shape, dtype):
        return stack.enter_context(nc.sbuf_tensor(name, shape, dtype))

    def psum(name, shape, dtype=F32):
        return stack.enter_context(nc.psum_tensor(name, shape, dtype))

    with stack:
        sLD = sem("sLD")
        sGD = sem("sGD")
        sMM = sem("sMM")
        sVE = sem("sVE")
        sAC = sem("sAC")
        sST = sem("sST")

        identf = sb("identf", [128, 128], F32)
        identb = sb("identb", [128, 128], BF16)
        ltri = sb("ltri", [128, 128], F32)
        ones1r = sb("ones1r", [1, 128], F32)
        ones1c = sb("ones1c", [128, 1], F32)
        tiny = sb("tiny", [128, 64], F32)
        modcol = sb("modcol", [128, 96], F32)

        arA = sb("arA", [128, 16384], F32)
        arB = sb("arB", [128, 8192], F32)
        arC = sb("arC", [128, 6144], F32)
        arRep = sb("arRep", [128, 6144], F32)
        arW = sb("arW", [128, 4096], F32)
        arBF = sb("arBF", [128, 11264], BF16)
        stat = sb("stat", [128, 64], F32)

        pb = [psum(f"pb{i}", [128, 512]) for i in range(8)]

        def s0_consts(sync):
            sync.dma_start(identf[:], ident_f32.ap()).then_inc(sLD.h, 16)
            sync.dma_start(identb[:], ident_bf16.ap()).then_inc(sLD.h, 16)
            sync.dma_start(ltri[:], ltri_f32.ap()).then_inc(sLD.h, 16)
            sync.dma_start(ones1r[:], ones_1x128.ap()).then_inc(sLD.h, 16)
            sync.dma_start(ones1c[:], ones_128x1.ap()).then_inc(sLD.h, 16)
        p.add("sync", s0_consts)
        sLD.inc(16 * 5)
        consts_ld = sLD.n

        relayout = [(wq_in, wq_t), (wk_in, wk_t), (wv_in, wv_t),
                    (ww_in, ww_t), (wo_in, wo_t)]
        wrow = [arW[:, 0:2048], arW[:, 2048:4096]]
        for wi, (src, dst) in enumerate(relayout):
            for dc in range(16):
                buf = wrow[(wi * 16 + dc) % 2]
                ld_at = sLD.inc(16)

                def ld(sync, src=src, dc=dc, buf=buf, pr=sGD.n):
                    sync.wait_ge(sGD.h, pr)
                    sync.dma_start(buf, src[dc * 128:(dc + 1) * 128, :])\
                        .then_inc(sLD.h, 16)
                p.add("sync", ld)

                st_at = sGD.inc(16)

                def st(g, dst=dst, dc=dc, buf=buf, w=ld_at):
                    g.wait_ge(sLD.h, w)
                    g.dma_start(dst[:, :, dc, :].rearrange("jc p j -> p jc j"),
                                buf)\
                        .then_inc(sGD.h, 16)
                p.add("gpsimd", st)
        relayout_done_sG = sGD.n

        ascr = arC[:, 5888:5888 + 104]

        def s0qA_zero(v):
            v.memset(stat[:, 0:8], 0.0)
        p.add("vector", s0qA_zero)

        qbuf = [arA[:, 0:5632], arA[:, 5632:11264]]
        ACC_G, ACC_D = 0, 1
        for bi in range(88 + 16):
            is_gate = bi < 88
            buf = qbuf[bi % 2][:, 0:(2048 if is_gate else 5632)]
            src = (gw_in[bi * 128:(bi + 1) * 128, :] if is_gate
                   else dw_in[(bi - 88) * 128:(bi - 87) * 128, :])
            ld_at = sLD.inc(16)

            def ld(sync, src=src, buf=buf, pr=sVE.n):
                sync.wait_ge(sVE.h, pr)
                sync.dma_start(buf, src).then_inc(sLD.h, 16)
            p.add("sync", ld)

            red_at = sVE.inc(1)

            def red(v, buf=buf, w=ld_at, bi=bi):
                v.wait_ge(sLD.h, w)
                v.tensor_reduce(ascr[:, bi:bi + 1], buf, AX.X, ALU.add,
                                apply_absolute_value=True).then_inc(sVE.h, 1)
            p.add("vector", red)
        qA_ve = sVE.n

        ve_at = sVE.inc(2)

        def s0qA_comb(v, w=qA_ve):
            v.wait_ge(sVE.h, w)
            v.tensor_reduce(stat[:, 0:1], ascr[:, 0:88], AX.X, ALU.add)\
                .then_inc(sVE.h, 1)
            v.tensor_reduce(stat[:, 1:2], ascr[:, 88:104], AX.X, ALU.add)\
                .then_inc(sVE.h, 1)
        p.add("vector", s0qA_comb)
        qA_ve = sVE.n

        mm_at = sMM.inc(1)

        def s0qA_fin_mm(t, ve=qA_ve):
            t.wait_ge(sVE.h, ve)
            t.wait_ge(sLD.h, consts_ld)
            t.matmul(pb[0][:1, 0:2], ones1c[:, 0:1], stat[:, 0:2],
                     start=True, stop=True).then_inc(sMM.h, 1)
        p.add("tensor", s0qA_fin_mm)

        fin_base = sVE.n
        sVE.inc(3)

        def s0qA_fin_v(v, w=mm_at, base=fin_base):
            v.wait_ge(sMM.h, w)
            v.tensor_single_scalar(tiny[:1, 4:5], pb[0][0:1, 0:1],
                                   1.0 / (2 * INTER * D), ALU.mult)
            v.tensor_single_scalar(tiny[:1, 5:6], pb[0][0:1, 1:2],
                                   1.0 / (D * INTER), ALU.mult).then_inc(sVE.h, 1)
            v.wait_ge(sVE.h, base + 1)
            v.tensor_single_scalar(tiny[:1, 4:6], tiny[:1, 4:6], 1e-5, ALU.max)\
                .then_inc(sVE.h, 1)
            v.wait_ge(sVE.h, base + 2)
            v.reciprocal(tiny[:1, 6:8], tiny[:1, 4:6]).then_inc(sVE.h, 1)
        p.add("vector", s0qA_fin_v)

        mm_at2 = sMM.inc(1)

        def s0qA_rep_mm(t, ve=sVE.n):
            t.wait_ge(sVE.h, ve)
            t.matmul(pb[1][:, 0:2], ones1r[:1, :], tiny[:1, 6:8],
                     start=True, stop=True).then_inc(sMM.h, 1)
        p.add("tensor", s0qA_rep_mm)

        ve_at = sVE.inc(1)

        def s0qA_rep_v(v, w=mm_at2):
            v.wait_ge(sMM.h, w)
            v.tensor_copy(tiny[:, 0:1], pb[1][:, 0:1])
            v.tensor_copy(tiny[:, 2:3], pb[1][:, 1:2])
            v.tensor_single_scalar(tiny[:, 1:2], pb[1][:, 0:1], -1.0, ALU.mult)
            v.tensor_single_scalar(tiny[:, 3:4], pb[1][:, 1:2], -1.0, ALU.mult)\
                .then_inc(sVE.h, 1)
        p.add("vector", s0qA_rep_v)
        scales_ve = sVE.n

        tstage = arBF[:, 0:5632]
        for bi in range(88 + 16):
            is_gate = bi < 88
            ncols = 2048 if is_gate else 5632
            buf = qbuf[bi % 2][:, 0:ncols]
            src = (gw_in[bi * 128:(bi + 1) * 128, :] if is_gate
                   else dw_in[(bi - 88) * 128:(bi - 87) * 128, :])
            sc = 0 if is_gate else 2

            ld_at = sLD.inc(16)

            def ld(sync, src=src, buf=buf, pr=sVE.n, pm=sMM.n):
                sync.wait_ge(sVE.h, max(pr, scales_ve))
                sync.wait_ge(sMM.h, pm)
                sync.dma_start(buf, src).then_inc(sLD.h, 16)
            p.add("sync", ld)

            q_at = sVE.inc(1)

            def quant(v, buf=buf, w=ld_at, sc=sc, pr=sGD.n,
                      pm=sMM.n, n=ncols):
                v.wait_ge(sLD.h, w)
                v.wait_ge(sGD.h, pr)
                v.wait_ge(sMM.h, pm)
                v.tensor_scalar(arB[:, 0:n], buf,
                                tiny[:, sc:sc + 1], 0.5, ALU.mult, ALU.is_ge)
                v.tensor_scalar(buf, buf, tiny[:, sc + 1:sc + 2], 0.5,
                                ALU.mult, ALU.is_ge)
                v.tensor_tensor(buf, arB[:, 0:n], buf,
                                ALU.subtract).then_inc(sVE.h, 1)
            p.add("vector", quant)

            nch = ncols // 128
            for g4 in range((nch + 3) // 4):
                n4 = min(4, nch - g4 * 4)
                bk = 5 + (g4 % 2)
                mm_at = sMM.inc(1)

                def trq(t, buf=buf, g4=g4, n4=n4, bk=bk, w=q_at, pe=sVE.n):
                    t.wait_ge(sVE.h, max(w, pe))
                    for i in range(n4):
                        mi = t.matmul(pb[bk][:, i * 128:(i + 1) * 128],
                                      buf[:, (g4 * 4 + i) * 128:
                                          (g4 * 4 + i + 1) * 128],
                                      identf[:], is_transpose=True,
                                      start=(i == 0), stop=(i == n4 - 1))
                    mi.then_inc(sMM.h, 1)
                p.add("tensor", trq)

                ve_at2 = sVE.inc(1)

                def evq(v, g4=g4, n4=n4, bk=bk, w=mm_at):
                    v.wait_ge(sMM.h, w)
                    v.tensor_copy(tstage[:, g4 * 512:g4 * 512 + n4 * 128],
                                  pb[bk][:, 0:n4 * 128]).then_inc(sVE.h, 1)
                p.add("vector", evq)

            st_at = sGD.inc(16)

            def st(g, bi=bi, is_gate=is_gate, w=sVE.n):
                g.wait_ge(sVE.h, w)
                if is_gate:
                    g.dma_start(gwqT_s[:, bi].rearrange("dc p j -> p dc j"),
                                tstage[:, 0:2048]).then_inc(sGD.h, 16)
                else:
                    g.dma_start(dwqT_s[:, bi - 88].rearrange("jc p d -> p jc d"),
                                tstage[:, 0:5632]).then_inc(sGD.h, 16)
            p.add("gpsimd", st)
        quant_done_sG = sGD.n
        quant_done_ve = sVE.n

        crow = arC[:, 0:2048]
        scrow = arC[:, 2048:4096]
        scT = arC[:, 4096:4112]

        ld_at = sLD.inc(16)

        def s0m_ldc(sync):
            sync.dma_start(crow[:1, :], cvec_in).then_inc(sLD.h, 16)
        p.add("sync", s0m_ldc)

        ac_at = sAC.inc(1)

        def s0m_silu(a, w=ld_at):
            a.wait_ge(sLD.h, w)
            a.activation(scrow[:1, :], crow[:1, :], AF.Silu).then_inc(sAC.h, 1)
        p.add("scalar", s0m_silu)

        mm_at = sMM.inc(1)

        def s0m_sct(t, w=ac_at):
            t.wait_ge(sAC.h, w)
            for dc in range(16):
                mi = t.matmul(pb[2][:, dc:dc + 1],
                              scrow[:1, dc * 128:(dc + 1) * 128],
                              ones1c[:1, 0:1],
                              start=True, stop=True)
            mi.then_inc(sMM.h, 1)
        p.add("tensor", s0m_sct)

        ve_at = sVE.inc(1)

        def s0m_sct_ev(v, w=mm_at):
            v.wait_ge(sMM.h, w)
            v.tensor_copy(scT, pb[2][:, 0:16]).then_inc(sVE.h, 1)
        p.add("vector", s0m_sct_ev)
        sct_ve = sVE.n

        wmbuf = [arB[:, 0:4096], arB[:, 4096:8192]]
        def s0m_zero(v, pr=sct_ve):
            v.wait_ge(sVE.h, pr)
            v.memset(modcol[:], 0.0)
        p.add("vector", s0m_zero)

        for dc in range(16):
            for half in range(3):
                buf = wmbuf[(dc * 3 + half) % 2]
                ld_at = sLD.inc(16)

                def ld(sync, dc=dc, half=half, buf=buf, pr=sMM.n,
                       ve=quant_done_ve):
                    sync.wait_ge(sVE.h, ve)
                    sync.wait_ge(sMM.h, pr)
                    sync.dma_start(buf, wmod_in[dc * 128:(dc + 1) * 128,
                                                half * 4096:(half + 1) * 4096])\
                        .then_inc(sLD.h, 16)
                p.add("sync", ld)

                mm_at = sMM.inc(1)

                def mm(t, dc=dc, half=half, buf=buf, w=ld_at, ve=sVE.n):
                    t.wait_ge(sLD.h, w)
                    t.wait_ge(sVE.h, ve)
                    for j in range(32):
                        mi = t.matmul(pb[3][:, j:j + 1],
                                      buf[:, j * 128:(j + 1) * 128],
                                      scT[:, dc:dc + 1],
                                      start=True, stop=True)
                    mi.then_inc(sMM.h, 1)
                p.add("tensor", mm)

                ve_at = sVE.inc(1)

                def acc(v, half=half, w=mm_at):
                    v.wait_ge(sMM.h, w)
                    v.tensor_tensor(modcol[:, half * 32:(half + 1) * 32],
                                    modcol[:, half * 32:(half + 1) * 32],
                                    pb[3][:, 0:32], ALU.add).then_inc(sVE.h, 1)
                p.add("vector", acc)
        mod_acc_ve = sVE.n

        gfx_base = sVE.n
        sVE.inc(7)

        def s0m_gafix(v, base=gfx_base):
            sls = [modcol[:, 32:48], modcol[:, 80:96]]
            sgns = [arC[:, 4112:4128], arC[:, 4128:4144]]
            mags = [arC[:, 4144:4160], arC[:, 4160:4176]]
            for i in (0, 1):
                v.tensor_scalar(sgns[i], sls[i], 0.0, None, ALU.is_ge)
            v.engine_nop().then_inc(sVE.h, 1)
            v.wait_ge(sVE.h, base + 1)
            for i in (0, 1):
                v.tensor_scalar(sgns[i], sgns[i], 2.0, -1.0, ALU.mult, ALU.add)
            v.engine_nop().then_inc(sVE.h, 1)
            v.wait_ge(sVE.h, base + 2)
            for i in (0, 1):
                v.tensor_tensor(mags[i], sls[i], sgns[i], ALU.mult)
            v.engine_nop().then_inc(sVE.h, 1)
            v.wait_ge(sVE.h, base + 3)
            for i in (0, 1):
                v.tensor_single_scalar(mags[i], mags[i], 0.25, ALU.max)
            v.engine_nop().then_inc(sVE.h, 1)
            v.wait_ge(sVE.h, base + 4)
            for i in (0, 1):
                v.tensor_tensor(sls[i], sgns[i], mags[i], ALU.mult)\
                    .then_inc(sVE.h, 1)
            v.wait_ge(sVE.h, base + 6)
            v.engine_nop().then_inc(sVE.h, 1)
        p.add("vector", s0m_gafix)
        gafix_ve = sVE.n

        mm_at = sMM.inc(1)

        def s0m_tr(t, ve=gafix_ve):
            t.wait_ge(sVE.h, ve)
            t.transpose(pb[4][:96, 0:128], modcol[:, 0:96], identf[:])\
                .then_inc(sMM.h, 1)
        p.add("tensor", s0m_tr)

        ve_at = sVE.inc(1)

        def s0m_tr_ev(v, w=mm_at):
            v.wait_ge(sMM.h, w)
            v.tensor_copy(arC[:96, 4352:4480], pb[4][:96, 0:128]).then_inc(sVE.h, 1)
        p.add("vector", s0m_tr_ev)
        modT_ve = sVE.n

        st_at = sGD.inc(16)

        def s0m_st(g, w=modT_ve):
            g.wait_ge(sVE.h, w)
            g.dma_start(mod_s.rearrange("k (p f) -> (k p) f", p=16),
                        arC[:96, 4352:4480])\
                .then_inc(sGD.h, 16)
        p.add("gpsimd", s0m_st)
        mod_st_sG = sGD.n

        if stop_after != "s0":
            arS1 = sb("arS1", [128, 4608], F32)
            wf1res = arS1[:, 0:2048]
            wf2res = arS1[:, 2048:4096]
            wbres = arS1[:, 4096:4352]
            tails = arS1[:, 4352:4544]

            sy, te, ve, ac, gp = Rec(), Rec(), Rec(), Rec(), Rec()

            s0MM, s0VE, s0AC, s0GD = sMM.n, sVE.n, sAC.n, sGD.n
            sy.wait_ge(sMM.h, s0MM)
            sy.wait_ge(sVE.h, s0VE)
            sy.wait_ge(sAC.h, s0AC)
            sy.wait_ge(sGD.h, s0GD)

            for dc in range(16):
                sy.dma_start(wf1res[:, dc * 128:(dc + 1) * 128],
                             wf1_in[dc * 128:(dc + 1) * 128, :])\
                    .then_inc(sLD.h, 16); sLD.inc(16)
            sy.dma_start(wf2res[:, :], wf2_in[:, :]).then_inc(sLD.h, 16)
            sLD.inc(16)
            wg1res = arC[:, 3584:5632]
            for dc in range(16):
                sy.dma_start(wg1res[:, dc * 128:(dc + 1) * 128],
                             wg1_in[dc * 128:(dc + 1) * 128, :])\
                    .then_inc(sLD.h, 16); sLD.inc(16)
            for dc in range(16):
                sy.dma_start(wbres[:, dc * 16:(dc + 1) * 16],
                             wb_in[dc * 128:(dc + 1) * 128, :])\
                    .then_inc(sLD.h, 16); sLD.inc(16)
            sy.dma_start(arRep[:, 0:2048], bg2_in.broadcast_to([128, D]))\
                .then_inc(sLD.h, 16); sLD.inc(16)
            wg2res = arRep[:, 2048:4096]
            sy.dma_start(wg2res, wg2_in[:, :]).then_inc(sLD.h, 16); sLD.inc(16)
            ccA = arRep[:, 4096:4288]
            for si, cin in enumerate((cq_in, ck_in, cv_in)):
                for h in range(16):
                    sy.dma_start(ccA[:, (si * 16 + h) * 4:(si * 16 + h) * 4 + 4],
                                 cin[h * 128:(h + 1) * 128, :])\
                        .then_inc(sLD.h, 16); sLD.inc(16)
            s1p_ld = sLD.n

            ve.wait_ge(sVE.h, gafix_ve)
            ve.tensor_single_scalar(tiny[:, 8:24], modcol[:, 16:32], 1.0,
                                    ALU.add)
            ve.memset(tails[:, :], 0.0)
            ve.memset(stat[:, 16:64], 0.0).then_inc(sVE.h, 1)
            sVE.inc(1)
            s1p_ve = sVE.n

            xbuf = [arB[:, 0:2048], arB[:, 2048:4096]]
            xnbuf = [arB[:, 4096:6144], arB[:, 6144:8192]]
            qlin = arC[:, 0:1027]
            qconv = arC[:, 1027:2051]
            t12buf = arC[:, 2051:3075]
            bstage = arC[:, 3075:3203]
            qsil = [arB[:, 0:1024], arB[:, 1024:2048]]
            tokstage = arB[:, 2048:4096]
            sgstage = arB[:, 4096:6144]
            wpan = [arW[:, 0:2048], arW[:, 2048:4096]]

            bank_free = {i: 0 for i in range(8)}

            def bwait(i):
                if bank_free[i]:
                    te.wait_ge(sVE.h, bank_free[i])

            WSTREAMS = [("q", wq_t, 0, q_fm_s, None),
                        ("k", wk_t, 1, k_fm_s, k_tm_s),
                        ("v", wv_t, 2, None, v_tm_s),
                        ("w", ww_t, 2, None, w_tm_s)]

            pan_mm = {}
            qsil_free = {0: (0, 0), 1: (0, 0)}
            tok_free = 0
            sg_free = 0

            for cc1 in range(NC1):
                sy.wait_ge(sVE.h, sVE.n)
                sy.wait_ge(sAC.h, sAC.n)
                sy.wait_ge(sGD.h, sGD.n)
                sy.wait_ge(sMM.h, sMM.n)

                ld_t, ve_t, ac_t = [], [], []
                for tt in range(8):
                    if tt >= 2:
                        sy.wait_ge(sVE.h, ve_t[tt - 2])
                        sy.wait_ge(sAC.h, ac_t[tt - 2])
                    sy.dma_start(xbuf[tt % 2],
                                 x_in[cc1 * CT1 + tt * 128:
                                      cc1 * CT1 + (tt + 1) * 128, :])\
                        .then_inc(sLD.h, 16)
                    ld_t.append(sLD.inc(16))

                    ve.wait_ge(sLD.h, ld_t[tt])
                    ve.tensor_reduce(stat[:, 16 + tt:17 + tt], xbuf[tt % 2],
                                     AX.X, ALU.add).then_inc(sVE.h, 1)
                    ve_t.append(sVE.inc(1))

                    ac.wait_ge(sLD.h, ld_t[tt])
                    ac.activation(xnbuf[tt % 2], xbuf[tt % 2], AF.Square,
                                  accum_out=stat[:, 24 + tt:25 + tt])\
                        .then_inc(sAC.h, 1)
                    ac_t.append(sAC.inc(1))

                b = sVE.n
                ve.wait_ge(sVE.h, ve_t[-1])
                ve.wait_ge(sAC.h, ac_t[-1])
                ve.tensor_single_scalar(stat[:, 32:40], stat[:, 16:24],
                                        1.0 / D, ALU.mult)
                ve.tensor_single_scalar(stat[:, 40:48], stat[:, 24:32],
                                        1.0 / D, ALU.mult).then_inc(sVE.h, 1)
                ve.wait_ge(sVE.h, b + 1)
                ve.tensor_tensor(stat[:, 48:56], stat[:, 32:40],
                                 stat[:, 32:40], ALU.mult).then_inc(sVE.h, 1)
                ve.wait_ge(sVE.h, b + 2)
                ve.scalar_tensor_tensor(stat[:, 48:56], stat[:, 40:48],
                                        LN_EPS, stat[:, 48:56],
                                        ALU.add, ALU.subtract).then_inc(sVE.h, 1)
                sVE.inc(3)
                p2_ve = sVE.n

                ab = sAC.n
                ac.wait_ge(sVE.h, p2_ve)
                ac.activation(stat[:, 56:64], stat[:, 48:56], AF.Ln)\
                    .then_inc(sAC.h, 1)
                ac.wait_ge(sAC.h, ab + 1)
                ac.activation(stat[:, 48:56], stat[:, 56:64], AF.Exp,
                              scale=-0.5).then_inc(sAC.h, 1)
                sAC.inc(2)
                p2_ac = sAC.n

                b = sVE.n
                ve.wait_ge(sAC.h, p2_ac)
                ve.tensor_tensor(stat[:, 56:64], stat[:, 32:40],
                                 stat[:, 48:56], ALU.mult).then_inc(sVE.h, 1)
                ve.wait_ge(sVE.h, b + 1)
                ve.tensor_single_scalar(stat[:, 56:64], stat[:, 56:64],
                                        -1.0, ALU.mult).then_inc(sVE.h, 1)
                sVE.inc(2)
                p2_done = sVE.n

                xn_ac = []
                for tt in range(8):
                    sy.wait_ge(sVE.h, sVE.n)
                    sy.dma_start(xbuf[tt % 2],
                                 x_in[cc1 * CT1 + tt * 128:
                                      cc1 * CT1 + (tt + 1) * 128, :])\
                        .then_inc(sLD.h, 16)
                    w = sLD.inc(16)

                    ac.wait_ge(sLD.h, w)
                    ac.wait_ge(sVE.h, p2_done)
                    ac.activation(xnbuf[tt % 2], xbuf[tt % 2], AF.Identity,
                                  bias=stat[:, 56 + tt:57 + tt],
                                  scale=stat[:, 48 + tt:49 + tt])\
                        .then_inc(sAC.h, 1)
                    wa = sAC.inc(1)
                    xn_ac.append(wa)

                    for dc in range(16):
                        bk = dc % 4
                        te.wait_ge(sAC.h, wa)
                        bwait(bk)
                        te.transpose(pb[bk][:, 0:128],
                                     xnbuf[tt % 2][:, dc * 128:(dc + 1) * 128],
                                     identf[:]).then_inc(sMM.h, 1)
                        wm = sMM.inc(1)

                        ve.wait_ge(sMM.h, wm)
                        ve.tensor_scalar(
                            arA[:, dc * 1024 + tt * 128:
                                   dc * 1024 + (tt + 1) * 128],
                            pb[bk][:, 0:128],
                            tiny[:, 8 + dc:9 + dc],
                            modcol[:, dc:dc + 1],
                            ALU.mult, ALU.add).then_inc(sVE.h, 1)
                        bank_free[bk] = sVE.inc(1)
                h_ready = sVE.n

                def conv_silu_spill(snm, si, h, ci, fm_s, tm_s, ev_t):
                    tcol = (si * 16 + h) * 3
                    ve.wait_ge(sVE.h, ev_t)
                    ve.tensor_copy(qlin[:, 0:3], tails[:, tcol:tcol + 3])
                    cbase = (ci * 16 + h) * 4
                    ve.tensor_single_scalar(qconv[:, 0:1024], qlin[:, 0:1024],
                                            ccA[:, cbase:cbase + 1], ALU.mult)
                    for i in (1, 2, 3):
                        ve.scalar_tensor_tensor(qconv[:, 0:1024],
                                                qlin[:, i:i + 1024],
                                                ccA[:, cbase + i:cbase + i + 1],
                                                qconv[:, 0:1024],
                                                ALU.mult, ALU.add)
                    ve.tensor_copy(tails[:, tcol:tcol + 3], qlin[:, 1024:1027])\
                        .then_inc(sVE.h, 1)
                    conv_ve = sVE.inc(1)

                    slot = (si * 16 + h) % 2
                    gfree, mfree = qsil_free[slot]
                    ac.wait_ge(sVE.h, conv_ve)
                    if gfree:
                        ac.wait_ge(sGD.h, gfree)
                    if mfree:
                        ac.wait_ge(sMM.h, mfree)
                    ac.activation(qsil[slot], qconv[:, 0:1024], AF.Silu)\
                        .then_inc(sAC.h, 1)
                    sil_ac = sAC.inc(1)
                    return slot, sil_ac

                def spill_stream(snm, h, slot, dep, fm_s, tm_s, cc1=cc1):
                    nonlocal tok_free
                    dsem, dval = dep
                    last_g = 0
                    last_m = 0
                    if fm_s is not None:
                        gp.wait_ge(dsem.h, dval)
                        gp.dma_start(fm_s[h][:, cc1 * CT1:(cc1 + 1) * CT1],
                                     qsil[slot]).then_inc(sGD.h, 16)
                        last_g = sGD.inc(16)
                    if tm_s is not None:
                        for lc in range(8):
                            bk = lc % 4
                            te.wait_ge(dsem.h, dval)
                            bwait(bk)
                            te.transpose(pb[bk][:, 0:128],
                                         qsil[slot][:, lc * 128:(lc + 1) * 128],
                                         identf[:]).then_inc(sMM.h, 1)
                            wm = sMM.inc(1)
                            last_m = wm
                            ve.wait_ge(sMM.h, wm)
                            if lc == 0 and tok_free:
                                ve.wait_ge(sGD.h, tok_free)
                            ve.tensor_copy(tokstage[:, lc * 128:(lc + 1) * 128],
                                           pb[bk][:, 0:128]).then_inc(sVE.h, 1)
                            bank_free[bk] = sVE.inc(1)
                        gp.wait_ge(sVE.h, sVE.n)
                        gp.dma_start(
                            tm_s[cc1 * 8:(cc1 + 1) * 8, h]
                            .rearrange("cc t m -> t cc m"),
                            tokstage[:, 0:1024]).then_inc(sGD.h, 16)
                        tok_free = sGD.inc(16)
                        last_g = tok_free
                    qsil_free[slot] = (last_g, last_m)

                for si, (snm, wtil, ci, fm_s, tm_s) in enumerate(WSTREAMS):
                    for h in range(16):
                        slot = (si * 16 + h) % 2
                        if pan_mm.get(slot):
                            sy.wait_ge(sMM.h, pan_mm[slot])
                        sy.dma_start(wpan[slot],
                                     wtil[h].rearrange("p dc j -> p (dc j)"))\
                            .then_inc(sLD.h, 16)
                        wld = sLD.inc(16)

                        ev_t = 0
                        for tb in range(2):
                            bk = 4 + tb
                            te.wait_ge(sLD.h, wld)
                            if si == 0 and h == 0 and tb == 0:
                                te.wait_ge(sVE.h, h_ready)
                            bwait(bk)
                            for dc in range(16):
                                mi = te.matmul(
                                    pb[bk][:, 0:512],
                                    wpan[slot][:, dc * 128:(dc + 1) * 128],
                                    arA[:, dc * 1024 + tb * 512:
                                           dc * 1024 + (tb + 1) * 512],
                                    start=(dc == 0), stop=(dc == 15))
                            mi.then_inc(sMM.h, 1)
                            wm = sMM.inc(1)
                            pan_mm[slot] = wm

                            ve.wait_ge(sMM.h, wm)
                            ve.tensor_copy(qlin[:, 3 + tb * 512:3 + (tb + 1) * 512],
                                           pb[bk][:, 0:512]).then_inc(sVE.h, 1)
                            ev_t = sVE.inc(1)
                            bank_free[bk] = ev_t

                        slot_q, sil_ac = conv_silu_spill(snm, si, h, ci,
                                                         fm_s, tm_s, ev_t)
                        spill_stream(snm, h, slot_q, (sAC, sil_ac), fm_s, tm_s)

                def small_proj(res, outbuf):
                    ev = 0
                    for tb in range(2):
                        bk = 4 + tb
                        bwait(bk)
                        for dc in range(16):
                            mi = te.matmul(pb[bk][:, 0:512],
                                           res[:, dc * 128:(dc + 1) * 128],
                                           arA[:, dc * 1024 + tb * 512:
                                                  dc * 1024 + (tb + 1) * 512],
                                           start=(dc == 0), stop=(dc == 15))
                        mi.then_inc(sMM.h, 1)
                        wm = sMM.inc(1)
                        ve.wait_ge(sMM.h, wm)
                        ve.tensor_copy(outbuf[:, tb * 512:(tb + 1) * 512],
                                       pb[bk][:, 0:512]).then_inc(sVE.h, 1)
                        ev = sVE.inc(1)
                        bank_free[bk] = ev
                    return ev

                t1_ve = small_proj(wf1res, t12buf)
                for h2 in range(16):
                    slot = h2 % 2
                    gfree, mfree = qsil_free[slot]
                    ev = 0
                    for tb in range(2):
                        bk = 4 + tb
                        te.wait_ge(sVE.h, t1_ve)
                        bwait(bk)
                        mi = te.matmul(pb[bk][:, 0:512],
                                       wf2res[:, h2 * 128:(h2 + 1) * 128],
                                       t12buf[:, tb * 512:(tb + 1) * 512],
                                       start=True, stop=True)
                        mi.then_inc(sMM.h, 1)
                        wm = sMM.inc(1)
                        ve.wait_ge(sMM.h, wm)
                        if tb == 0 and gfree:
                            ve.wait_ge(sGD.h, gfree)
                        if tb == 0 and mfree:
                            ve.wait_ge(sMM.h, mfree)
                        ve.tensor_copy(qsil[slot][:, tb * 512:(tb + 1) * 512],
                                       pb[bk][:, 0:512]).then_inc(sVE.h, 1)
                        ev = sVE.inc(1)
                        bank_free[bk] = ev
                    spill_stream("z", h2, slot, (sVE, ev), None, z_tm_s)

                for tb in range(2):
                    bk = 6
                    bwait(bk)
                    for dc in range(16):
                        mi = te.matmul(pb[bk][:16, 0:512],
                                       wbres[:, dc * 16:(dc + 1) * 16],
                                       arA[:, dc * 1024 + tb * 512:
                                              dc * 1024 + (tb + 1) * 512],
                                       start=(dc == 0), stop=(dc == 15))
                    mi.then_inc(sMM.h, 1)
                    wm = sMM.inc(1)
                    ve.wait_ge(sMM.h, wm)
                    if tb == 0 and sg_free:
                        ve.wait_ge(sGD.h, sg_free)
                    ve.tensor_copy(sgstage[:16, tb * 512:(tb + 1) * 512],
                                   pb[bk][:16, 0:512]).then_inc(sVE.h, 1)
                    bank_free[bk] = sVE.inc(1)
                bv = sVE.n
                ac.wait_ge(sVE.h, bv)
                ac.activation(sgstage[:16, 0:1024], sgstage[:16, 0:1024],
                              AF.Tanh, scale=0.5).then_inc(sAC.h, 1)
                bt_ac = sAC.inc(1)
                ve.wait_ge(sAC.h, bt_ac)
                ve.tensor_scalar(sgstage[:16, 0:1024], sgstage[:16, 0:1024],
                                 0.5, 0.5, ALU.mult, ALU.add).then_inc(sVE.h, 1)
                bt_ve = sVE.inc(1)
                for lc in range(8):
                    bk = lc % 4
                    te.wait_ge(sVE.h, bt_ve)
                    bwait(bk)
                    te.transpose(pb[bk][:, 0:16],
                                 sgstage[:16, lc * 128:(lc + 1) * 128],
                                 identf[:16, :16]).then_inc(sMM.h, 1)
                    wm = sMM.inc(1)
                    ve.wait_ge(sMM.h, wm)
                    ve.tensor_copy(bstage[:, lc * 16:(lc + 1) * 16],
                                   pb[bk][:, 0:16]).then_inc(sVE.h, 1)
                    bank_free[bk] = sVE.inc(1)
                gp.wait_ge(sVE.h, sVE.n)
                for lc in range(8):
                    gp.dma_start(beta_s[cc1 * 8 + lc],
                                 bstage[:, lc * 16:(lc + 1) * 16])\
                        .then_inc(sGD.h, 16)
                    sGD.inc(16)
                beta_gd = sGD.n

                t2_ve = small_proj(wg1res, t12buf)
                for ttile in range(8):
                    evs = []
                    for jb in range(4):
                        bk = 6 + (jb % 2)
                        te.wait_ge(sVE.h, t2_ve)
                        bwait(bk)
                        mi = te.matmul(pb[bk][:, 0:512],
                                       t12buf[:, ttile * 128:(ttile + 1) * 128],
                                       wg2res[:, jb * 512:(jb + 1) * 512],
                                       start=True, stop=True)
                        mi.then_inc(sMM.h, 1)
                        wm = sMM.inc(1)
                        ve.wait_ge(sMM.h, wm)
                        if ttile == 0 and jb == 0 and beta_gd:
                            ve.wait_ge(sGD.h, beta_gd)
                        if sg_free:
                            ve.wait_ge(sGD.h, sg_free)
                        ve.tensor_tensor(sgstage[:, jb * 512:(jb + 1) * 512],
                                         pb[bk][:, 0:512],
                                         arRep[:, jb * 512:(jb + 1) * 512],
                                         ALU.add).then_inc(sVE.h, 1)
                        evs.append(sVE.inc(1))
                        bank_free[bk] = evs[-1]
                    ac.wait_ge(sVE.h, evs[-1])
                    ac.activation(sgstage[:, 0:2048], sgstage[:, 0:2048],
                                  AF.Tanh, scale=0.5).then_inc(sAC.h, 1)
                    sg_ac = sAC.inc(1)
                    ve.wait_ge(sAC.h, sg_ac)
                    ve.tensor_scalar(sgstage[:, 0:2048], sgstage[:, 0:2048],
                                     0.5, 0.5, ALU.mult, ALU.add)\
                        .then_inc(sVE.h, 1)
                    sg_ve = sVE.inc(1)
                    gp.wait_ge(sVE.h, sg_ve)
                    gp.dma_start(sg_s[cc1 * CT1 + ttile * 128:
                                      cc1 * CT1 + (ttile + 1) * 128, :],
                                 sgstage[:, 0:2048]).then_inc(sGD.h, 16)
                    sg_free = sGD.inc(16)

            s1_end = dict(MM=sMM.n, VE=sVE.n, AC=sAC.n, LD=sLD.n, GD=sGD.n)

            p.add("sync", lambda e, r=sy: r.replay(e))
            p.add("tensor", lambda e, r=te: r.replay(e))
            p.add("vector", lambda e, r=ve: r.replay(e))
            p.add("scalar", lambda e, r=ac: r.replay(e))
            p.add("gpsimd", lambda e, r=gp: r.replay(e))

        if stop_after not in ("s0", "s1"):
            sy2, te2, ve2, ac2, gp2 = Rec(), Rec(), Rec(), Rec(), Rec()
            C44 = 44.0

            SkA = arA[:, 0:2048]
            SvA = arA[:, 2048:4096]
            q_c = arA[:, 4096:6144]
            kfm_c = arA[:, 6144:8192]
            ktm_c = arA[:, 8192:10240]
            v_c = arA[:, 10240:12288]
            w_c = arA[:, 12288:14336]
            z_c = arA[:, 14336:16384]
            lam = arB[:, 0:2048]
            e1c = arB[:, 2048:4096]
            e2c = arB[:, 4096:6144]
            u2 = arB[:, 6144:8192]
            araw = arC[:, 0:2048]
            s_t = arC[:, 2048:4096]
            o_t = arC[:, 4096:6144]
            sgld = arS1[:, 0:2048]
            nsg = arS1[:, 2048:4096]
            bcol = arS1[:, 4096:4112]
            lfc = arS1[:, 4112:4128]
            lf44 = arS1[:, 4128:4144]
            nmx = arS1[:, 4144:4160]
            den = arS1[:, 4160:4176]
            rden = arS1[:, 4176:4192]
            ssq = arS1[:, 4192:4208]
            rst = arS1[:, 4208:4224]
            nwRep = arW[:, 0:2048]
            sktb = arW[:, 2048:2176]
            a1b = arW[:, 2176:2304]
            a2b = arW[:, 2304:2432]
            a1tb = arW[:, 2432:2560]
            a2tb = arW[:, 2560:2688]
            u2tb = arW[:, 2688:2816]
            atb = arW[:, 2816:2944]
            btb = arW[:, 2944:3072]
            skb = arW[:, 3072:3200]
            svb = arW[:, 3200:3328]

            sy2.wait_ge(sMM.h, sMM.n)
            sy2.wait_ge(sVE.h, sVE.n)
            sy2.wait_ge(sAC.h, sAC.n)
            sy2.wait_ge(sGD.h, sGD.n)

            sy2.dma_start(nwRep,
                          nw_in.rearrange("o (u m) -> o u m", u=1)
                          .broadcast_to([128, 16, 128])).then_inc(sLD.h, 16)
            sLD.inc(16)

            biasc = arW[:, 3328:3332]
            ve2.wait_ge(sVE.h, sVE.n)
            ve2.memset(SkA[:], 0.0)
            ve2.memset(SvA[:], 0.0)
            ve2.memset(biasc[:, 0:1], C44)
            ve2.memset(biasc[:, 1:2], -C44).then_inc(sVE.h, 1)
            sVE.inc(1)

            bank2_free = {i: None for i in range(8)}

            def bw2(i):
                if bank2_free[i]:
                    s_, n_ = bank2_free[i]
                    te2.wait_ge(s_.h, n_)

            yo_gd = 0

            for cc in range(NCC):
                sy2.wait_ge(sVE.h, sVE.n)
                sy2.wait_ge(sAC.h, sAC.n)
                sy2.wait_ge(sMM.h, sMM.n)

                def v3(buf, inner=128):
                    return buf.rearrange("p (a b) -> p a b", b=inner)
                sy2.dma_start(v3(q_c), q_fm_s[:, :, cc * 128:(cc + 1) * 128]
                              .rearrange("h p t -> p h t")).then_inc(sLD.h, 16)
                sy2.dma_start(v3(kfm_c), k_fm_s[:, :, cc * 128:(cc + 1) * 128]
                              .rearrange("h p t -> p h t")).then_inc(sLD.h, 16)
                sy2.dma_start(v3(ktm_c), k_tm_s[cc].rearrange("h t m -> t h m"))\
                    .then_inc(sLD.h, 16)
                sy2.dma_start(v3(v_c), v_tm_s[cc].rearrange("h t m -> t h m"))\
                    .then_inc(sLD.h, 16)
                sy2.dma_start(v3(w_c), w_tm_s[cc].rearrange("h t m -> t h m"))\
                    .then_inc(sLD.h, 16)
                sy2.dma_start(v3(z_c), z_tm_s[cc].rearrange("h t m -> t h m"))\
                    .then_inc(sLD.h, 16)
                sy2.dma_start(bcol, beta_s[cc]).then_inc(sLD.h, 16)
                sy2.dma_start(sgld, sg_s[cc * 128:(cc + 1) * 128, :])\
                    .then_inc(sLD.h, 16)
                sLD.inc(16 * 8)
                ld8 = sLD.n

                ac2.wait_ge(sLD.h, ld8)
                ac2.activation(araw, z_c, AF.Abs).then_inc(sAC.h, 1)
                aw = sAC.inc(1)
                ac2.wait_ge(sAC.h, aw)
                ac2.activation(araw, araw, AF.Exp, scale=-1.0).then_inc(sAC.h, 1)
                aw = sAC.inc(1)
                ac2.wait_ge(sAC.h, aw)
                ac2.activation(araw, araw, AF.Ln, bias=ones1c[:, 0:1])\
                    .then_inc(sAC.h, 1)
                aw = sAC.inc(1)
                ve2.wait_ge(sAC.h, aw)
                ve2.scalar_tensor_tensor(z_c, z_c, 0.0, araw,
                                         ALU.min, ALU.subtract).then_inc(sVE.h, 1)
                g_ve = sVE.inc(1)
                ve2.wait_ge(sLD.h, ld8)
                ve2.tensor_single_scalar(q_c, q_c, SCALE, ALU.mult)\
                    .then_inc(sVE.h, 1)
                q_ve = sVE.inc(1)
                ve2.tensor_tensor(nsg, sgld, nwRep, ALU.mult).then_inc(sVE.h, 1)
                nsg_ve = sVE.inc(1)

                exp_ac = []
                for jb in range(4):
                    bk = jb
                    te2.wait_ge(sVE.h, g_ve)
                    bw2(bk)
                    te2.matmul(pb[bk][:, 0:512], ltri[:],
                               z_c[:, jb * 512:(jb + 1) * 512],
                               start=True, stop=True).then_inc(sMM.h, 1)
                    wm = sMM.inc(1)
                    ac2.wait_ge(sMM.h, wm)
                    ac2.activation(lam[:, jb * 512:(jb + 1) * 512],
                                   pb[bk][:, 0:512], AF.Exp)
                    ac2.activation(e1c[:, jb * 512:(jb + 1) * 512],
                                   pb[bk][:, 0:512], AF.Exp, bias=biasc[:, 0:1])
                    ac2.activation(e2c[:, jb * 512:(jb + 1) * 512],
                                   pb[bk][:, 0:512], AF.Exp, scale=-1.0,
                                   bias=biasc[:, 1:2]).then_inc(sAC.h, 1)
                    exp_ac.append(sAC.inc(1))
                    bank2_free[bk] = (sAC, exp_ac[-1])


                bk = 4
                bw2(bk)
                te2.wait_ge(sVE.h, g_ve)
                for h in range(16):
                    mi = te2.matmul(pb[bk][:, h:h + 1],
                                    z_c[:, h * 128:(h + 1) * 128],
                                    ones1c[:, 0:1],
                                    start=True, stop=True)
                mi.then_inc(sMM.h, 1)
                wm = sMM.inc(1)
                ac2.wait_ge(sMM.h, wm)
                ac2.activation(lfc, pb[bk][:, 0:16], AF.Exp)
                ac2.activation(lf44, pb[bk][:, 0:16], AF.Exp,
                               bias=biasc[:, 0:1]).then_inc(sAC.h, 1)
                lf_ac = sAC.inc(1)
                bank2_free[bk] = (sAC, lf_ac)

                ve2.wait_ge(sLD.h, ld8)
                ve2.tensor_tensor(araw, w_c, w_c, ALU.mult).then_inc(sVE.h, 1)
                wsq_ve = sVE.inc(1)
                ve2.wait_ge(sVE.h, wsq_ve)
                ve2.tensor_reduce(ssq, araw.rearrange("p (h m) -> p h m", m=128),
                                  AX.X, ALU.add).then_inc(sVE.h, 1)
                wss_ve = sVE.inc(1)
                ve2.wait_ge(sVE.h, wss_ve)
                ve2.tensor_single_scalar(ssq, ssq, L2_EPS, ALU.add)\
                    .then_inc(sVE.h, 1)
                wse_ve = sVE.inc(1)
                wab = sAC.n
                ac2.wait_ge(sVE.h, wse_ve)
                ac2.activation(rst, ssq, AF.Ln).then_inc(sAC.h, 1)
                ac2.wait_ge(sAC.h, wab + 1)
                ac2.activation(rst, rst, AF.Exp, scale=-0.5).then_inc(sAC.h, 1)
                sAC.inc(2)
                wrst_ac = sAC.n
                ve2.wait_ge(sAC.h, wrst_ac)
                for h in range(16):
                    mi = ve2.tensor_single_scalar(
                        w_c[:, h * 128:(h + 1) * 128],
                        w_c[:, h * 128:(h + 1) * 128],
                        rst[:, h:h + 1], ALU.mult)
                mi.then_inc(sVE.h, 1)
                wnorm_ve = sVE.inc(1)

                ve2.wait_ge(sVE.h, wnorm_ve)
                ve2.wait_ge(sAC.h, exp_ac[-1])
                for h in range(16):
                    ve2.scalar_tensor_tensor(u2[:, h * 128:(h + 1) * 128],
                                             w_c[:, h * 128:(h + 1) * 128],
                                             bcol[:, h:h + 1],
                                             e2c[:, h * 128:(h + 1) * 128],
                                             ALU.mult, ALU.mult)
                ve2.engine_nop().then_inc(sVE.h, 1)
                u2_ve = sVE.inc(1)

                for h in range(16):
                    hs = slice(h * 128, (h + 1) * 128)
                    bk6 = 6
                    bw2(bk6)
                    if cc == 0 and h == 0:
                        te2.wait_ge(sVE.h, u2_ve)
                    te2.transpose(pb[bk6][:, 0:128], SkA[:, hs], identf[:])\
                        .then_inc(sMM.h, 1)
                    wm_skt = sMM.inc(1)
                    ac2.wait_ge(sMM.h, wm_skt)
                    ac2.activation(sktb, pb[bk6][:, 0:128], AF.Identity)\
                        .then_inc(sAC.h, 1)
                    skt_ac = sAC.inc(1)
                    bank2_free[bk6] = (sAC, skt_ac)

                    bk4 = 4
                    bw2(bk4)
                    te2.wait_ge(sVE.h, q_ve)
                    te2.matmul(pb[bk4][:, 0:128], kfm_c[:, hs], q_c[:, hs],
                               start=True, stop=True).then_inc(sMM.h, 1)
                    wm_at = sMM.inc(1)
                    ve2.wait_ge(sMM.h, wm_at)
                    ve2.tensor_tensor(atb, pb[bk4][:, 0:128], ltri[:],
                                      ALU.mult).then_inc(sVE.h, 1)
                    at_ve = sVE.inc(1)
                    bank2_free[bk4] = (sVE, at_ve)

                    bk7 = 7
                    bw2(bk7)
                    te2.wait_ge(sAC.h, skt_ac)
                    te2.matmul(pb[bk7][:, 0:128], q_c[:, hs], sktb,
                               start=True, stop=True).then_inc(sMM.h, 1)
                    wm_qs = sMM.inc(1)

                    bk5 = 5
                    bw2(bk5)
                    te2.wait_ge(sVE.h, max(at_ve, u2_ve))
                    te2.matmul(pb[bk5][:, 0:128], atb, u2[:, hs],
                               start=True, stop=True).then_inc(sMM.h, 1)
                    wm_in = sMM.inc(1)

                    ve2.wait_ge(sMM.h, wm_qs)
                    ve2.tensor_tensor(s_t[:, hs], lam[:, hs], pb[bk7][:, 0:128],
                                      ALU.mult).then_inc(sVE.h, 1)
                    sv1 = sVE.inc(1)
                    bank2_free[bk7] = (sVE, sv1)
                    ve2.wait_ge(sMM.h, wm_in)
                    ve2.tensor_tensor(araw[:, hs], e1c[:, hs], pb[bk5][:, 0:128],
                                      ALU.mult).then_inc(sVE.h, 1)
                    sv2 = sVE.inc(1)
                    bank2_free[bk5] = (sVE, sv2)
                    ve2.tensor_tensor(s_t[:, hs], s_t[:, hs], araw[:, hs],
                                      ALU.add).then_inc(sVE.h, 1)
                    sVE.inc(1)

                sm0 = sVE.n
                ve2.tensor_reduce(nmx, s_t.rearrange("p (h m) -> p h m", m=128),
                                  AX.X, ALU.max, negate=True).then_inc(sVE.h, 1)
                nmx_ve = sVE.inc(1)
                ac2.wait_ge(sVE.h, nmx_ve)
                for h in range(16):
                    hs = slice(h * 128, (h + 1) * 128)
                    mi = ac2.activation(araw[:, hs], s_t[:, hs], AF.Exp,
                                        bias=nmx[:, h:h + 1],
                                        accum_out=den[:, h:h + 1])
                mi.then_inc(sAC.h, 1)
                sm_ac = sAC.inc(1)
                ve2.wait_ge(sAC.h, sm_ac)
                ve2.reciprocal(rden, den).then_inc(sVE.h, 1)
                rden_ve = sVE.inc(1)

                for h in range(16):
                    hs = slice(h * 128, (h + 1) * 128)
                    ve2.wait_ge(sVE.h, rden_ve)
                    ve2.tensor_tensor(a1b, araw[:, hs], lam[:, hs], ALU.mult)
                    ve2.tensor_tensor(a2b, araw[:, hs], e1c[:, hs], ALU.mult)\
                        .then_inc(sVE.h, 1)
                    ab_ve = sVE.inc(1)

                    tps = []
                    for bsrc, bdst, bk in ((a1b, a1tb, 4), (a2b, a2tb, 5),
                                           (u2[:, hs], u2tb, 6)):
                        bw2(bk)
                        te2.wait_ge(sVE.h, ab_ve)
                        te2.transpose(pb[bk][:, 0:128], bsrc, identf[:])\
                            .then_inc(sMM.h, 1)
                        wm = sMM.inc(1)
                        ve2.wait_ge(sMM.h, wm)
                        ve2.tensor_copy(bdst, pb[bk][:, 0:128]).then_inc(sVE.h, 1)
                        bank2_free[bk] = (sVE, sVE.inc(1))
                        tps.append(bank2_free[bk][1])

                    bk4 = 4
                    bw2(bk4)
                    te2.wait_ge(sVE.h, tps[-1])
                    te2.matmul(pb[bk4][0:64, 0:64], u2tb[:, 0:64],
                               a2tb[:, 0:64], start=True, stop=True)
                    te2.matmul(pb[bk4][0:64, 64:128], u2tb[:, 0:64],
                               a2tb[:, 64:128], start=True, stop=True)
                    te2.matmul(pb[bk4][64:128, 64:128], u2tb[:, 64:128],
                               a2tb[:, 64:128], start=True, stop=True)\
                        .then_inc(sMM.h, 1)
                    wm_bt = sMM.inc(1)
                    ve2.wait_ge(sMM.h, wm_bt)
                    ve2.memset(btb[64:128, 0:64], 0.0)
                    ve2.tensor_tensor(btb[0:64, 0:64], pb[bk4][0:64, 0:64],
                                      ltri[0:64, 0:64], ALU.mult)
                    ve2.tensor_copy(btb[0:64, 64:128], pb[bk4][0:64, 64:128])
                    ve2.tensor_tensor(btb[64:128, 64:128],
                                      pb[bk4][64:128, 64:128],
                                      ltri[0:64, 0:64], ALU.mult)\
                        .then_inc(sVE.h, 1)
                    bt_ve = sVE.inc(1)
                    bank2_free[bk4] = (sVE, bt_ve)

                    bk7 = 7
                    bw2(bk7)
                    te2.matmul(pb[bk7][:, 0:128], a1tb, SvA[:, hs],
                               start=True, stop=False)
                    te2.wait_ge(sVE.h, bt_ve)
                    te2.matmul(pb[bk7][:, 0:128], btb, v_c[:, hs],
                               start=False, stop=True).then_inc(sMM.h, 1)
                    wm_o = sMM.inc(1)
                    ve2.wait_ge(sMM.h, wm_o)
                    ve2.tensor_scalar(o_t[:, hs], pb[bk7][:, 0:128],
                                      rden[:, h:h + 1], None, ALU.mult)\
                        .then_inc(sVE.h, 1)
                    bank2_free[bk7] = (sVE, sVE.inc(1))

                    bk5, bk6 = 5, 6
                    bw2(bk5)
                    te2.matmul(pb[bk5][:, 0:128], u2[:, hs], ktm_c[:, hs],
                               start=True, stop=True).then_inc(sMM.h, 1)
                    wm_uk = sMM.inc(1)
                    bw2(bk6)
                    te2.matmul(pb[bk6][:, 0:128], u2[:, hs], v_c[:, hs],
                               start=True, stop=True).then_inc(sMM.h, 1)
                    wm_uv = sMM.inc(1)
                    ac2.wait_ge(sAC.h, lf_ac)
                    ac2.activation(skb, SkA[:, hs], AF.Identity,
                                   scale=lfc[:, h:h + 1])
                    ac2.activation(svb, SvA[:, hs], AF.Identity,
                                   scale=lfc[:, h:h + 1]).then_inc(sAC.h, 1)
                    sc_ac = sAC.inc(1)
                    ve2.wait_ge(sMM.h, wm_uk)
                    ve2.wait_ge(sAC.h, sc_ac)
                    ve2.scalar_tensor_tensor(SkA[:, hs], pb[bk5][:, 0:128],
                                             lf44[:, h:h + 1], skb,
                                             ALU.mult, ALU.add).then_inc(sVE.h, 1)
                    bank2_free[bk5] = (sVE, sVE.inc(1))
                    ve2.wait_ge(sMM.h, wm_uv)
                    ve2.scalar_tensor_tensor(SvA[:, hs], pb[bk6][:, 0:128],
                                             lf44[:, h:h + 1], svb,
                                             ALU.mult, ALU.add).then_inc(sVE.h, 1)
                    bank2_free[bk6] = (sVE, sVE.inc(1))

                ve2.tensor_tensor(araw, o_t, o_t, ALU.mult).then_inc(sVE.h, 1)
                sq_ve = sVE.inc(1)
                ve2.wait_ge(sVE.h, sq_ve)
                ve2.tensor_reduce(ssq, araw.rearrange("p (h m) -> p h m", m=128),
                                  AX.X, ALU.add).then_inc(sVE.h, 1)
                ssq_ve = sVE.inc(1)
                ve2.wait_ge(sVE.h, ssq_ve)
                ve2.tensor_scalar(ssq, ssq, 1.0 / 128.0, LN_EPS, ALU.mult,
                                  ALU.add).then_inc(sVE.h, 1)
                ssq2_ve = sVE.inc(1)
                ab_ = sAC.n
                ac2.wait_ge(sVE.h, ssq2_ve)
                ac2.activation(rst, ssq, AF.Ln).then_inc(sAC.h, 1)
                ac2.wait_ge(sAC.h, ab_ + 1)
                ac2.activation(rst, rst, AF.Exp, scale=-0.5).then_inc(sAC.h, 1)
                sAC.inc(2)
                rst_ac = sAC.n
                for h in range(16):
                    hs = slice(h * 128, (h + 1) * 128)
                    if h == 0:
                        ve2.wait_ge(sAC.h, rst_ac)
                        ve2.wait_ge(sVE.h, nsg_ve)
                    ve2.scalar_tensor_tensor(o_t[:, hs], o_t[:, hs],
                                             rst[:, h:h + 1], nsg[:, hs],
                                             ALU.mult, ALU.mult)
                ve2.engine_nop().then_inc(sVE.h, 1)
                yo_ve = sVE.inc(1)

                gp2.wait_ge(sVE.h, yo_ve)
                gp2.dma_start(yo_s[:, cc * 128:(cc + 1) * 128, :]
                              .rearrange("h t m -> t h m"),
                              o_t.rearrange("p (h m) -> p h m", m=128))\
                    .then_inc(sGD.h, 16)
                yo_gd = sGD.inc(16)

            p.add("sync", lambda e, r=sy2: r.replay(e))
            p.add("tensor", lambda e, r=te2: r.replay(e))
            p.add("vector", lambda e, r=ve2: r.replay(e))
            p.add("scalar", lambda e, r=ac2: r.replay(e))
            p.add("gpsimd", lambda e, r=gp2: r.replay(e))

        if stop_after not in ("s0", "s1", "s2"):
            sy3, te3, ve3, ac3, gp3 = Rec(), Rec(), Rec(), Rec(), Rec()
            sy3.wait_ge(sMM.h, sMM.n)
            sy3.wait_ge(sVE.h, sVE.n)
            sy3.wait_ge(sAC.h, sAC.n)
            sy3.wait_ge(sGD.h, sGD.n)

            y_fm = arA[:, 0:8192]
            ybuf3 = arA[:, 8192:10240]
            x2acc = arB[:, 0:8192]
            afm = arC[:, 0:512]
            gaRep = arRep[:, 0:2048]
            wpan3 = [arW[:, 0:2048], arW[:, 2048:4096]]

            sy3.dma_start(gaRep, mod_s[2:3, :].broadcast_to([128, D]))\
                .then_inc(sLD.h, 16)
            sLD.inc(16)
            ga_ld = sLD.n

            bank3 = {i: None for i in range(8)}

            def bw3(i):
                if bank3[i]:
                    s_, n_ = bank3[i]
                    te3.wait_ge(s_.h, n_)

            pan3_mm = {}
            x2st_gd = 0
            for gg in range(8):
                sy3.wait_ge(sVE.h, sVE.n)
                sy3.wait_ge(sMM.h, sMM.n)
                if x2st_gd:
                    sy3.wait_ge(sGD.h, x2st_gd)
                for t4 in range(4):
                    sy3.dma_start(x2acc[:, t4 * 2048:(t4 + 1) * 2048],
                                  x_in[gg * 512 + t4 * 128:
                                       gg * 512 + (t4 + 1) * 128, :])\
                        .then_inc(sLD.h, 16)
                    sLD.inc(16)
                    sy3.wait_ge(sMM.h, sMM.n)
                    sy3.dma_start(
                        ybuf3.rearrange("p (h m) -> p h m", m=128),
                        yo_s[:, gg * 512 + t4 * 128:gg * 512 + (t4 + 1) * 128, :]
                        .rearrange("h t m -> t h m")).then_inc(sLD.h, 16)
                    yld = sLD.inc(16)
                    for dc in range(16):
                        bk = dc % 2
                        te3.wait_ge(sLD.h, yld)
                        bw3(bk)
                        te3.matmul(pb[bk][:, 0:128],
                                   ybuf3[:, dc * 128:(dc + 1) * 128],
                                   identf[:], is_transpose=True,
                                   start=True, stop=True).then_inc(sMM.h, 1)
                        wm = sMM.inc(1)
                        ve3.wait_ge(sMM.h, wm)
                        ve3.tensor_copy(y_fm[:, dc * 512 + t4 * 128:
                                             dc * 512 + (t4 + 1) * 128],
                                        pb[bk][:, 0:128]).then_inc(sVE.h, 1)
                        bank3[bk] = (sVE, sVE.inc(1))
                yfm_ve = sVE.n

                for jc in range(16):
                    slot = jc % 2
                    if pan3_mm.get(slot):
                        sy3.wait_ge(sMM.h, pan3_mm[slot])
                    sy3.dma_start(wpan3[slot],
                                  wo_t[jc].rearrange("p dc j -> p (dc j)"))\
                        .then_inc(sLD.h, 16)
                    wld = sLD.inc(16)

                    bk = 4
                    te3.wait_ge(sLD.h, wld)
                    if jc == 0:
                        te3.wait_ge(sVE.h, yfm_ve)
                        te3.wait_ge(sLD.h, ga_ld)
                    bw3(bk)
                    for dc in range(16):
                        mi = te3.matmul(pb[bk][:, 0:512],
                                        wpan3[slot][:, dc * 128:(dc + 1) * 128],
                                        y_fm[:, dc * 512:(dc + 1) * 512],
                                        start=(dc == 0), stop=(dc == 15))
                    mi.then_inc(sMM.h, 1)
                    wm = sMM.inc(1)
                    pan3_mm[slot] = wm
                    ve3.wait_ge(sMM.h, wm)
                    ve3.tensor_copy(afm, pb[bk][:, 0:512]).then_inc(sVE.h, 1)
                    afm_ve = sVE.inc(1)
                    bank3[bk] = (sVE, afm_ve)

                    for t4 in range(4):
                        bk2 = 5 + (t4 % 2)
                        te3.wait_ge(sVE.h, afm_ve)
                        bw3(bk2)
                        te3.matmul(pb[bk2][:, 0:128],
                                   afm[:, t4 * 128:(t4 + 1) * 128],
                                   identf[:], is_transpose=True,
                                   start=True, stop=True).then_inc(sMM.h, 1)
                        wm2 = sMM.inc(1)
                        ve3.wait_ge(sMM.h, wm2)
                        ve3.tensor_tensor(o_t[:, 0:128] if False else
                                          arC[:, 512:640],
                                          pb[bk2][:, 0:128],
                                          gaRep[:, jc * 128:(jc + 1) * 128],
                                          ALU.mult)
                        ve3.tensor_tensor(
                            x2acc[:, t4 * 2048 + jc * 128:
                                  t4 * 2048 + (jc + 1) * 128],
                            x2acc[:, t4 * 2048 + jc * 128:
                                  t4 * 2048 + (jc + 1) * 128],
                            arC[:, 512:640], ALU.add).then_inc(sVE.h, 1)
                        bank3[bk2] = (sVE, sVE.inc(1))

                gp3.wait_ge(sVE.h, sVE.n)
                for t4 in range(4):
                    gp3.dma_start(x2_s[gg * 512 + t4 * 128:
                                       gg * 512 + (t4 + 1) * 128, :],
                                  x2acc[:, t4 * 2048:(t4 + 1) * 2048])\
                        .then_inc(sGD.h, 16)
                    sGD.inc(16)
                x2st_gd = sGD.n

            p.add("sync", lambda e, r=sy3: r.replay(e))
            p.add("tensor", lambda e, r=te3: r.replay(e))
            p.add("vector", lambda e, r=ve3: r.replay(e))
            p.add("scalar", lambda e, r=ac3: r.replay(e))
            p.add("gpsimd", lambda e, r=gp3: r.replay(e))

        if stop_after is None:
            sy4, te4, ve4, ac4, gp4 = Rec(), Rec(), Rec(), Rec(), Rec()
            sy4.wait_ge(sMM.h, sMM.n)
            sy4.wait_ge(sVE.h, sVE.n)
            sy4.wait_ge(sAC.h, sAC.n)
            sy4.wait_ge(sGD.h, sGD.n)

            def _addr(tname):
                for a in nc.m.functions[0].allocations:
                    if not hasattr(a, "memorylocations"):
                        continue
                    ml = a.memorylocations[0]
                    if str(ml.name) == tname:
                        return ml.addr
                raise KeyError(tname)

            baseA = _addr("arA")
            def ov(name, shape, dtype, off):
                return nc.alloc_sbuf_tensor_at(name, shape, dtype,
                                               offset=baseA + off).ap()

            KB = 1024
            xq_fm = ov("s3_xq", [128, 16384], BF16, 0)
            ygbuf = ov("s3_yg", [128, 4096], F32, 32 * KB)
            rX0 = ov("s3_x2a", [128, 2048], F32, 48 * KB)
            rX1 = ov("s3_x2b", [128, 2048], F32, 56 * KB)
            rXn = ov("s3_xn", [128, 2048], F32, 64 * KB)
            tacc = ov("s3_t", [128, 16384], F32, 72 * KB)
            dnwRep = ov("s3_dnw", [128, 5632], F32, 72 * KB)
            wbuf4 = ov("s3_wb", [128, 8192], BF16, 94 * KB)
            h3buf = ov("s3_h3", [128, 5632], F32, 158 * KB)
            h3bufB = ov("s3_h3B", [128, 5632], F32, 94 * KB)
            scrT = ov("s3_scr", [128, 176], F32, 152 * KB)
            sscr = scrT[:, 0:88]
            ascr2 = scrT[:, 88:176]
            xq2buf = ov("s3_xq2", [128, 5632], BF16, 32 * KB)
            xq2I = ov("s3_xq2I", [128, 11264], BF16, 0)
            dwbuf = ov("s3_dw", [128, 11264], BF16, 22 * KB)
            silb = [ov("s3_sil0", [128, 512], F32, 48 * KB),
                    ov("s3_sil1", [128, 512], F32, 50 * KB)]
            h3blk = [ov("s3_h3b0", [128, 512], F32, 52 * KB),
                     ov("s3_h3b1", [128, 512], F32, 54 * KB)]
            maskb = ov("s3_mask", [128, 5632], F32, 48 * KB)
            x2j = ov("s3_x2j", [128, 2048], F32, 48 * KB)
            outj = ov("s3_outj", [128, 2048], F32, 56 * KB)

            cb1R = ov("s3_cb1", [128, 2048], F32, 136 * KB)
            sbR = ov("s3_sb", [128, 2048], F32, 144 * KB)
            gnwR = ov("s3_gnw", [128, 2048], F32, 183 * KB)
            gbR = ov("s3_gb", [128, 2048], F32, 191 * KB)

            sy4.dma_start(cb1R, mod_s[4:5, :].broadcast_to([128, D]))\
                .then_inc(sLD.h, 16)
            sy4.dma_start(sbR, mod_s[3:4, :].broadcast_to([128, D]))\
                .then_inc(sLD.h, 16)
            sy4.dma_start(gnwR, gnw_in.broadcast_to([128, D])).then_inc(sLD.h, 16)
            sy4.dma_start(gbR, mod_s[5:6, :].broadcast_to([128, D]))\
                .then_inc(sLD.h, 16)
            sLD.inc(64)
            reps_ld = sLD.n
            ve4.wait_ge(sVE.h, sVE.n)
            ve4.wait_ge(sLD.h, reps_ld)
            ve4.tensor_single_scalar(cb1R, cb1R, 1.0, ALU.add).then_inc(sVE.h, 1)
            b0 = sVE.inc(1)
            ve4.wait_ge(sVE.h, b0)
            ve4.reciprocal(tiny[:, 40:41], tiny[:, 0:1])
            ve4.reciprocal(tiny[:, 41:42], tiny[:, 2:3]).then_inc(sVE.h, 1)
            sVE.inc(1)
            mh_ve = sVE.n

            bank4f = {i: None for i in range(8)}

            def bw4(i):
                if bank4f[i]:
                    s_, n_ = bank4f[i]
                    te4.wait_ge(s_.h, n_)

            LAD = [128.0, 64.0, 32.0, 16.0, 8.0, 4.0, 2.0, 1.0]

            def ladder(buf, n, scol):
                ve4.tensor_scalar(maskb[:, 0:n], buf, scol, 128.5,
                                  ALU.mult, ALU.add)
                ve4.tensor_copy(rXn[:, 0:n] if n <= 2048 else h3buf[:, 0:0],
                                maskb[:, 0:n]) if False else None
                for k in LAD:
                    ve4.tensor_single_scalar(buf, maskb[:, 0:n], k, ALU.is_ge)
                    ve4.scalar_tensor_tensor(maskb[:, 0:n], buf, -k,
                                             maskb[:, 0:n], ALU.mult, ALU.add)
                raise RuntimeError("ladder needs orig")

            def ladder2(buf, n, scol, ve_sem):
                ve4.tensor_scalar(maskb[:, 0:n], buf, scol, 128.5,
                                  ALU.mult, ALU.add)
                ve4.tensor_copy(buf, maskb[:, 0:n]).then_inc(sVE.h, 1)
                ve_sem[0] = sVE.inc(1)
                for k in LAD:
                    ve4.tensor_single_scalar(maskb[:, 0:n], buf, k, ALU.is_ge)
                    ve4.scalar_tensor_tensor(buf, maskb[:, 0:n], -k, buf,
                                             ALU.mult, ALU.add)
                raise RuntimeError("unused")

            def ladder3(src, dst, n, scol):
                ve4.tensor_scalar(maskb[:, 0:n], src, scol, 128.5,
                                  ALU.mult, ALU.add)
                for k in LAD:
                    ve4.tensor_single_scalar(dst, maskb[:, 0:n], k, ALU.is_ge)
                    ve4.scalar_tensor_tensor(maskb[:, 0:n], dst, -k,
                                             maskb[:, 0:n], ALU.mult, ALU.add)
                ve4.tensor_scalar(dst, src, scol, 0.5, ALU.mult, ALU.add)
                mi = ve4.tensor_tensor(dst, dst, maskb[:, 0:n], ALU.subtract)
                return mi

            out_gd = 0
            for cc3 in range(4):
                sy4.wait_ge(sVE.h, sVE.n)
                sy4.wait_ge(sAC.h, sAC.n)
                sy4.wait_ge(sMM.h, sMM.n)
                sy4.wait_ge(sGD.h, sGD.n)

                ld_t, ve_t, ac_t = [], [], []
                for tt in range(8):
                    if tt >= 2:
                        sy4.wait_ge(sVE.h, ve_t[tt - 2])
                        sy4.wait_ge(sAC.h, ac_t[tt - 2])
                    buf = (rX0, rX1)[tt % 2]
                    sy4.dma_start(buf, x2_s[cc3 * 1024 + tt * 128:
                                            cc3 * 1024 + (tt + 1) * 128, :])\
                        .then_inc(sLD.h, 16)
                    ld_t.append(sLD.inc(16))
                    ve4.wait_ge(sLD.h, ld_t[tt])
                    ve4.tensor_reduce(stat[:, 16 + tt:17 + tt], buf,
                                      AX.X, ALU.add).then_inc(sVE.h, 1)
                    ve_t.append(sVE.inc(1))
                    ac4.wait_ge(sLD.h, ld_t[tt])
                    ac4.activation(rXn, buf, AF.Square,
                                   accum_out=stat[:, 24 + tt:25 + tt])\
                        .then_inc(sAC.h, 1)
                    ac_t.append(sAC.inc(1))

                b = sVE.n
                ve4.wait_ge(sVE.h, ve_t[-1])
                ve4.wait_ge(sAC.h, ac_t[-1])
                ve4.tensor_single_scalar(stat[:, 32:40], stat[:, 16:24],
                                         1.0 / D, ALU.mult)
                ve4.tensor_single_scalar(stat[:, 40:48], stat[:, 24:32],
                                         1.0 / D, ALU.mult).then_inc(sVE.h, 1)
                ve4.wait_ge(sVE.h, b + 1)
                ve4.tensor_tensor(stat[:, 48:56], stat[:, 32:40],
                                  stat[:, 32:40], ALU.mult).then_inc(sVE.h, 1)
                ve4.wait_ge(sVE.h, b + 2)
                ve4.scalar_tensor_tensor(stat[:, 48:56], stat[:, 40:48],
                                         LN_EPS, stat[:, 48:56],
                                         ALU.add, ALU.subtract).then_inc(sVE.h, 1)
                sVE.inc(3)
                p2v = sVE.n
                ab = sAC.n
                ac4.wait_ge(sVE.h, p2v)
                ac4.activation(stat[:, 56:64], stat[:, 48:56], AF.Ln)\
                    .then_inc(sAC.h, 1)
                ac4.wait_ge(sAC.h, ab + 1)
                ac4.activation(stat[:, 48:56], stat[:, 56:64], AF.Exp,
                               scale=-0.5).then_inc(sAC.h, 1)
                sAC.inc(2)
                p2a = sAC.n
                b = sVE.n
                ve4.wait_ge(sAC.h, p2a)
                ve4.tensor_tensor(stat[:, 56:64], stat[:, 32:40],
                                  stat[:, 48:56], ALU.mult).then_inc(sVE.h, 1)
                ve4.wait_ge(sVE.h, b + 1)
                ve4.tensor_single_scalar(stat[:, 56:64], stat[:, 56:64],
                                         -1.0, ALU.mult).then_inc(sVE.h, 1)
                sVE.inc(2)
                rstd1_ve = sVE.n

                for tt in range(8):
                    buf = (rX0, rX1)[tt % 2]
                    sy4.wait_ge(sVE.h, sVE.n)
                    sy4.dma_start(buf, x2_s[cc3 * 1024 + tt * 128:
                                            cc3 * 1024 + (tt + 1) * 128, :])\
                        .then_inc(sLD.h, 16)
                    w = sLD.inc(16)
                    ac4.wait_ge(sLD.h, w)
                    ac4.wait_ge(sVE.h, rstd1_ve)
                    ac4.activation(rXn, buf, AF.Identity,
                                   bias=stat[:, 56 + tt:57 + tt],
                                   scale=stat[:, 48 + tt:49 + tt])\
                        .then_inc(sAC.h, 1)
                    wa = sAC.inc(1)
                    ve4.wait_ge(sAC.h, wa)
                    ve4.tensor_tensor(buf, rXn, cb1R, ALU.mult)
                    ve4.tensor_tensor(buf, buf, sbR, ALU.add).then_inc(sVE.h, 1)
                    h2_ve = sVE.inc(1)
                    ac4.wait_ge(sVE.h, h2_ve)
                    ac4.activation(rXn, buf, AF.Square,
                                   accum_out=stat[:, 24 + tt:25 + tt])\
                        .then_inc(sAC.h, 1)
                    sAC.inc(1)
                    ve4.tensor_tensor(tacc[:, tt * 2048:(tt + 1) * 2048],
                                      buf, gnwR, ALU.mult).then_inc(sVE.h, 1)
                    t_ve = sVE.inc(1)
                    ve4.wait_ge(sVE.h, t_ve)
                    ve4.tensor_reduce(stat[:, 16 + tt:17 + tt],
                                      tacc[:, tt * 2048:(tt + 1) * 2048],
                                      AX.X, ALU.max, apply_absolute_value=True)\
                        .then_inc(sVE.h, 1)
                    sVE.inc(1)
                c_done_ve = sVE.n
                c_done_ac = sAC.n

                b = sVE.n
                ve4.wait_ge(sVE.h, c_done_ve)
                ve4.wait_ge(sAC.h, c_done_ac)
                ve4.tensor_single_scalar(stat[:, 40:48], stat[:, 24:32],
                                         1.0 / D, ALU.mult).then_inc(sVE.h, 1)
                ve4.wait_ge(sVE.h, b + 1)
                ve4.tensor_single_scalar(stat[:, 40:48], stat[:, 40:48],
                                         LN_EPS, ALU.add).then_inc(sVE.h, 1)
                sVE.inc(2)
                dv = sVE.n
                ab = sAC.n
                ac4.wait_ge(sVE.h, dv)
                ac4.activation(stat[:, 48:56], stat[:, 40:48], AF.Ln)\
                    .then_inc(sAC.h, 1)
                ac4.wait_ge(sAC.h, ab + 1)
                ac4.activation(stat[:, 40:48], stat[:, 48:56], AF.Exp,
                               scale=-0.5).then_inc(sAC.h, 1)
                sAC.inc(2)
                da = sAC.n
                b = sVE.n
                ve4.wait_ge(sAC.h, da)
                ve4.tensor_tensor(stat[:, 32:40], stat[:, 16:24],
                                  stat[:, 40:48], ALU.mult).then_inc(sVE.h, 1)
                ve4.wait_ge(sVE.h, b + 1)
                ve4.tensor_single_scalar(stat[:, 32:40], stat[:, 32:40],
                                         1e-5, ALU.max).then_inc(sVE.h, 1)
                ve4.wait_ge(sVE.h, b + 2)
                ve4.reciprocal(stat[:, 48:56], stat[:, 32:40]).then_inc(sVE.h, 1)
                ve4.wait_ge(sVE.h, b + 3)
                ve4.tensor_tensor(stat[:, 56:64], stat[:, 40:48],
                                  stat[:, 48:56], ALU.mult).then_inc(sVE.h, 1)
                ve4.wait_ge(sVE.h, b + 4)
                ve4.tensor_single_scalar(stat[:, 56:64], stat[:, 56:64],
                                         127.0, ALU.mult)
                ve4.wait_ge(sVE.h, mh_ve)
                ve4.tensor_single_scalar(stat[:, 32:40], stat[:, 32:40],
                                         1.0 / 127.0, ALU.mult).then_inc(sVE.h, 1)
                ve4.wait_ge(sVE.h, b + 5)
                for tt in range(8):
                    mi = ve4.tensor_single_scalar(
                        stat[:, 32 + tt:33 + tt], stat[:, 32 + tt:33 + tt],
                        tiny[:, 40:41], ALU.mult)
                mi.then_inc(sVE.h, 1)
                sVE.inc(6)
                d_done = sVE.n

                for tt in range(8):
                    ve4.wait_ge(sVE.h, d_done)
                    ve4.wait_ge(sMM.h, sMM.n)
                    mi = ladder3(tacc[:, tt * 2048:(tt + 1) * 2048],
                                 rXn, 2048, stat[:, 56 + tt:57 + tt])
                    mi.then_inc(sVE.h, 1)
                    lv = sVE.inc(1)
                    for dc in range(16):
                        bk = dc % 2
                        te4.wait_ge(sVE.h, lv)
                        bw4(bk)
                        te4.matmul(pb[bk][:, 0:128],
                                   rXn[:, dc * 128:(dc + 1) * 128],
                                   identf[:], is_transpose=True,
                                   start=True, stop=True).then_inc(sMM.h, 1)
                        wm = sMM.inc(1)
                        ac4.wait_ge(sMM.h, wm)
                        ac4.activation(
                            xq_fm[:, dc * 1024 + tt * 128:
                                  dc * 1024 + (tt + 1) * 128],
                            pb[bk][:, 0:128], AF.Identity).then_inc(sAC.h, 1)
                        bank4f[bk] = (sAC, sAC.inc(1))
                xq_ac = sAC.n

                sy4.wait_ge(sMM.h, sMM.n)
                sy4.dma_start(dnwRep, dnw_in.broadcast_to([128, INTER]))\
                    .then_inc(sLD.h, 16)
                dnw_ld = sLD.inc(16)
                ve4.wait_ge(sLD.h, dnw_ld)
                h3_spill_gd = {0: 0, 1: 0}
                wld_prev = 0
                for jp in range(11):
                    sy4.wait_ge(sMM.h, sMM.n)
                    for b_ in range(4):
                        sy4.dma_start(
                            wbuf4.rearrange("p (dc bj) -> p dc bj", dc=16)
                            [:, :, b_ * 128:(b_ + 1) * 128],
                            gwqT_s[:, jp * 4 + b_]
                            .rearrange("dc p j -> p dc j")).then_inc(sLD.h, 16)
                        sLD.inc(16)
                    wld = sLD.n
                    for tt in range(8):
                        bk = tt % 2
                        te4.wait_ge(sLD.h, wld)
                        if jp == 0 and tt == 0:
                            te4.wait_ge(sAC.h, xq_ac)
                        bw4(bk)
                        for dc in range(16):
                            mi = te4.matmul(
                                pb[bk][:, 0:512],
                                xq_fm[:, dc * 1024 + tt * 128:
                                      dc * 1024 + (tt + 1) * 128],
                                wbuf4[:, dc * 512:(dc + 1) * 512],
                                start=(dc == 0), stop=(dc == 15))
                        mi.then_inc(sMM.h, 1)
                        wm = sMM.inc(1)
                        ac4.wait_ge(sMM.h, wm)
                        ac4.activation(ygbuf[:, tt * 512:(tt + 1) * 512],
                                       pb[bk][:, 0:512], AF.Identity,
                                       scale=stat[:, 32 + tt:33 + tt])\
                            .then_inc(sAC.h, 1)
                        bank4f[bk] = (sAC, sAC.inc(1))
                    sy4.wait_ge(sMM.h, sMM.n)
                    for b_ in range(4):
                        sy4.dma_start(
                            wbuf4.rearrange("p (dc bj) -> p dc bj", dc=16)
                            [:, :, b_ * 128:(b_ + 1) * 128],
                            gwqT_s[:, 44 + jp * 4 + b_]
                            .rearrange("dc p j -> p dc j")).then_inc(sLD.h, 16)
                        sLD.inc(16)
                    wldu = sLD.n
                    for tt in range(8):
                        bk = 2 + tt % 2
                        te4.wait_ge(sLD.h, wldu)
                        bw4(bk)
                        for dc in range(16):
                            mi = te4.matmul(
                                pb[bk][:, 0:512],
                                xq_fm[:, dc * 1024 + tt * 128:
                                      dc * 1024 + (tt + 1) * 128],
                                wbuf4[:, dc * 512:(dc + 1) * 512],
                                start=(dc == 0), stop=(dc == 15))
                        mi.then_inc(sMM.h, 1)
                        wmu = sMM.inc(1)
                        slot = tt % 2
                        ac4.wait_ge(sGD.h, out_gd)
                        ac4.activation(silb[slot], ygbuf[:, tt * 512:(tt + 1) * 512],
                                       AF.Silu).then_inc(sAC.h, 1)
                        sil_ac = sAC.inc(1)
                        ve4.wait_ge(sMM.h, wmu)
                        ve4.wait_ge(sAC.h, sil_ac)
                        if h3_spill_gd[slot]:
                            ve4.wait_ge(sGD.h, h3_spill_gd[slot])
                        ve4.scalar_tensor_tensor(h3blk[slot], pb[bk][:, 0:512],
                                                 stat[:, 32 + tt:33 + tt],
                                                 silb[slot],
                                                 ALU.mult, ALU.mult)\
                            .then_inc(sVE.h, 1)
                        h3_ve = sVE.inc(1)
                        bank4f[bk] = (sVE, h3_ve)
                        ac4.wait_ge(sVE.h, h3_ve)
                        ac4.activation(silb[slot], h3blk[slot], AF.Square,
                                       accum_out=sscr[:, tt * 11 + jp:
                                                      tt * 11 + jp + 1])\
                            .then_inc(sAC.h, 1)
                        sq_ac = sAC.inc(1)
                        ve4.wait_ge(sAC.h, sq_ac)
                        ve4.tensor_tensor(h3blk[slot], h3blk[slot],
                                          dnwRep[:, jp * 512:(jp + 1) * 512],
                                          ALU.mult).then_inc(sVE.h, 1)
                        dnw_ve = sVE.inc(1)
                        ve4.wait_ge(sVE.h, dnw_ve)
                        ve4.tensor_reduce(ascr2[:, tt * 11 + jp:
                                                tt * 11 + jp + 1],
                                          h3blk[slot], AX.X, ALU.max,
                                          apply_absolute_value=True)\
                            .then_inc(sVE.h, 1)
                        sVE.inc(1)
                        gp4.wait_ge(sVE.h, dnw_ve)
                        gp4.dma_start(h3_s[(cc3 % 2) * 8 + tt, jp], h3blk[slot])\
                            .then_inc(sGD.h, 16)
                        out_gd = sGD.inc(16)
                        h3_spill_gd[slot] = out_gd
                f_ve, f_ac, f_gd = sVE.n, sAC.n, sGD.n

                b = sVE.n
                ve4.wait_ge(sVE.h, f_ve)
                ve4.wait_ge(sAC.h, f_ac)
                ve4.tensor_reduce(stat[:, 24:32],
                                  sscr.rearrange("p (t j) -> p t j", j=11),
                                  AX.X, ALU.add).then_inc(sVE.h, 1)
                ve4.wait_ge(sVE.h, b + 1)
                ve4.tensor_reduce(stat[:, 16:24],
                                  ascr2.rearrange("p (t j) -> p t j", j=11),
                                  AX.X, ALU.max).then_inc(sVE.h, 1)
                ve4.wait_ge(sVE.h, b + 2)
                ve4.tensor_scalar(stat[:, 40:48], stat[:, 24:32],
                                  1.0 / INTER, LN_EPS, ALU.mult, ALU.add)\
                    .then_inc(sVE.h, 1)
                sVE.inc(3)
                gv = sVE.n
                ab = sAC.n
                ac4.wait_ge(sVE.h, gv)
                ac4.activation(stat[:, 48:56], stat[:, 40:48], AF.Ln)\
                    .then_inc(sAC.h, 1)
                ac4.wait_ge(sAC.h, ab + 1)
                ac4.activation(stat[:, 40:48], stat[:, 48:56], AF.Exp,
                               scale=-0.5).then_inc(sAC.h, 1)
                sAC.inc(2)
                ga_ = sAC.n
                b = sVE.n
                ve4.wait_ge(sAC.h, ga_)
                ve4.tensor_tensor(stat[:, 32:40], stat[:, 16:24],
                                  stat[:, 40:48], ALU.mult).then_inc(sVE.h, 1)
                ve4.wait_ge(sVE.h, b + 1)
                ve4.tensor_single_scalar(stat[:, 32:40], stat[:, 32:40],
                                         1e-5, ALU.max).then_inc(sVE.h, 1)
                ve4.wait_ge(sVE.h, b + 2)
                ve4.reciprocal(stat[:, 48:56], stat[:, 32:40]).then_inc(sVE.h, 1)
                ve4.wait_ge(sVE.h, b + 3)
                ve4.tensor_tensor(stat[:, 56:64], stat[:, 40:48],
                                  stat[:, 48:56], ALU.mult).then_inc(sVE.h, 1)
                ve4.wait_ge(sVE.h, b + 4)
                ve4.tensor_single_scalar(stat[:, 56:64], stat[:, 56:64],
                                         127.0, ALU.mult)
                ve4.tensor_single_scalar(stat[:, 32:40], stat[:, 32:40],
                                         1.0 / 127.0, ALU.mult).then_inc(sVE.h, 1)
                ve4.wait_ge(sVE.h, b + 5)
                for tt in range(8):
                    mi = ve4.tensor_single_scalar(
                        stat[:, 32 + tt:33 + tt], stat[:, 32 + tt:33 + tt],
                        tiny[:, 41:42], ALU.mult)
                mi.then_inc(sVE.h, 1)
                sVE.inc(6)
                g_done = sVE.n

                xq2_gd = 0
                hslots = [h3buf, h3bufB]
                lad_done = [0, 0]
                sy4.wait_ge(sGD.h, f_gd)
                sy4.wait_ge(sVE.h, sVE.n)
                sy4.wait_ge(sMM.h, sMM.n)
                sy4.dma_start(hslots[0].rearrange("p (j f) -> p j f", f=512),
                              h3_s[(cc3 % 2) * 8]
                              .rearrange("j p f -> p j f")).then_inc(sLD.h, 16)
                h3_ld = [sLD.inc(16), 0]
                for tt in range(8):
                    if tt + 1 < 8:
                        nsl = (tt + 1) % 2
                        if lad_done[nsl]:
                            sy4.wait_ge(sVE.h, lad_done[nsl])
                        sy4.dma_start(
                            hslots[nsl].rearrange("p (j f) -> p j f", f=512),
                            h3_s[(cc3 % 2) * 8 + tt + 1]
                            .rearrange("j p f -> p j f")).then_inc(sLD.h, 16)
                        h3_ld[nsl] = sLD.inc(16)
                    ve4.wait_ge(sLD.h, h3_ld[tt % 2])
                    ve4.wait_ge(sMM.h, sMM.n)
                    mi = ladder3(hslots[tt % 2], tacc[:, 0:5632], 5632,
                                 stat[:, 56 + tt:57 + tt])
                    mi.then_inc(sVE.h, 1)
                    lv = sVE.inc(1)
                    lad_done[tt % 2] = lv
                    for jc in range(44):
                        bk = 4 + jc % 2
                        te4.wait_ge(sVE.h, lv)
                        bw4(bk)
                        te4.matmul(pb[bk][:, 0:128],
                                   tacc[:, jc * 128:(jc + 1) * 128],
                                   identf[:], is_transpose=True,
                                   start=True, stop=True).then_inc(sMM.h, 1)
                        wm = sMM.inc(1)
                        ac4.wait_ge(sMM.h, wm)
                        if jc == 0 and xq2_gd:
                            ac4.wait_ge(sGD.h, xq2_gd)
                        ac4.activation(xq2buf[:, jc * 128:(jc + 1) * 128],
                                       pb[bk][:, 0:128], AF.Identity)\
                            .then_inc(sAC.h, 1)
                        bank4f[bk] = (sAC, sAC.inc(1))
                    gp4.wait_ge(sAC.h, sAC.n)
                    gp4.dma_start(xq2_s[(cc3 % 2) * 8 + tt]
                                  .rearrange("jc p t -> p jc t"),
                                  xq2buf.rearrange("p (jc t) -> p jc t", t=128))\
                        .then_inc(sGD.h, 16)
                    xq2_gd = sGD.inc(16)

                for jcg in range(4):
                    sy4.wait_ge(sGD.h, xq2_gd)
                    if jcg > 0:
                        sy4.wait_ge(sMM.h, sMM.n)
                    for t_ in range(8):
                        sy4.dma_start(
                            xq2I.rearrange("p (t jcf) -> p t jcf", t=8)[:, t_],
                            xq2_s[(cc3 % 2) * 8 + t_, jcg * 11:(jcg + 1) * 11]
                            .rearrange("jc p f -> p jc f")).then_inc(sLD.h, 16)
                        sLD.inc(16)
                    xld = sLD.n
                    for dbh in range(2):
                        sy4.wait_ge(sMM.h, sMM.n)
                        for j_ in range(11):
                            sy4.dma_start(
                                dwbuf.rearrange("p (jc bf) -> p jc bf",
                                                jc=11)[:, j_],
                                dwqT_s[jcg * 11 + j_, dbh * 8:(dbh + 1) * 8]
                                .rearrange("b p f -> p b f")).then_inc(sLD.h, 16)
                            sLD.inc(16)
                        dld = sLD.n
                        for tt in range(8):
                            for db in range(2):
                                bk = (tt * 2 + db) % 8
                                te4.wait_ge(sLD.h, max(xld, dld))
                                bw4(bk)
                                for j_ in range(11):
                                    mi = te4.matmul(
                                        pb[bk][:, 0:512],
                                        xq2I[:, (tt * 11 + j_) * 128:
                                             (tt * 11 + j_ + 1) * 128],
                                        dwbuf[:, j_ * 1024 + db * 512:
                                              j_ * 1024 + (db + 1) * 512],
                                        start=(j_ == 0), stop=(j_ == 10))
                                mi.then_inc(sMM.h, 1)
                                wm = sMM.inc(1)
                                dsl = tacc[:, tt * 2048 + dbh * 1024 +
                                           db * 512:
                                           tt * 2048 + dbh * 1024 +
                                           (db + 1) * 512]
                                ve4.wait_ge(sMM.h, wm)
                                if jcg == 0:
                                    ve4.tensor_copy(dsl, pb[bk][:, 0:512])\
                                        .then_inc(sVE.h, 1)
                                else:
                                    ve4.tensor_tensor(dsl, dsl,
                                                      pb[bk][:, 0:512],
                                                      ALU.add).then_inc(sVE.h, 1)
                                bank4f[bk] = (sVE, sVE.inc(1))
                i_ve = sVE.n

                for tt in range(8):
                    sy4.wait_ge(sVE.h, sVE.n)
                    sy4.dma_start(x2j, x2_s[cc3 * 1024 + tt * 128:
                                            cc3 * 1024 + (tt + 1) * 128, :])\
                        .then_inc(sLD.h, 16)
                    w = sLD.inc(16)
                    ve4.wait_ge(sVE.h, i_ve)
                    ve4.wait_ge(sLD.h, w)
                    ve4.scalar_tensor_tensor(outj,
                                             tacc[:, tt * 2048:(tt + 1) * 2048],
                                             stat[:, 32 + tt:33 + tt],
                                             gbR, ALU.mult, ALU.mult)
                    ve4.tensor_tensor(outj, outj, x2j, ALU.add).then_inc(sVE.h, 1)
                    ov_ = sVE.inc(1)
                    sy4.wait_ge(sVE.h, ov_)
                    sy4.dma_start(out_hbm[cc3 * 1024 + tt * 128:
                                          cc3 * 1024 + (tt + 1) * 128, :],
                                  outj).then_inc(sST.h, 16)
                    sST.inc(16)

            def s3_fin(sync, st=sST.n):
                sync.wait_ge(sST.h, st)
            p.add("sync", lambda e, r=sy4: r.replay(e))
            p.add("tensor", lambda e, r=te4: r.replay(e))
            p.add("vector", lambda e, r=ve4: r.replay(e))
            p.add("scalar", lambda e, r=ac4: r.replay(e))
            p.add("gpsimd", lambda e, r=gp4: r.replay(e))
            p.add("sync", s3_fin)

        if debug:
            dbg_stat = nc.dram_tensor("dbg_stat", [128, 64], F32,
                                      kind="ExternalOutput").ap()
            dbg_tiny = nc.dram_tensor("dbg_tiny", [128, 64], F32,
                                      kind="ExternalOutput").ap()
            dbg_modc = nc.dram_tensor("dbg_modc", [128, 96], F32,
                                      kind="ExternalOutput").ap()
            dbg_modt = nc.dram_tensor("dbg_modt", [96, 128], F32,
                                      kind="ExternalOutput").ap()

            def dbg_dump(g, ve=sVE.n, mm=sMM.n):
                g.wait_ge(sVE.h, ve)
                g.wait_ge(sMM.h, mm)
                g.dma_start(dbg_stat, stat[:]).then_inc(sGD.h, 16)
                g.dma_start(dbg_tiny, tiny[:]).then_inc(sGD.h, 16)
                g.dma_start(dbg_modc, modcol[:]).then_inc(sGD.h, 16)
                g.dma_start(dbg_modt, arC[:96, 4352:4480]).then_inc(sGD.h, 16)
            p.add("gpsimd", dbg_dump)
            sGD.inc(64)

        if stop_after in ("s0", "s1", "s2", "s2b"):
            st_at = sST.inc(16 * NT)

            def fin_copy(sync, gd=sGD.n, ve=sVE.n, ac=sAC.n, mm=sMM.n):
                sync.wait_ge(sGD.h, gd)
                sync.wait_ge(sVE.h, ve)
                sync.wait_ge(sAC.h, ac)
                sync.wait_ge(sMM.h, mm)
                for tt in range(NT):
                    sync.dma_start(out_hbm[tt * 128:(tt + 1) * 128, :],
                                   x_in[tt * 128:(tt + 1) * 128, :])\
                        .then_inc(sST.h, 16)
                sync.wait_ge(sST.h, sST.n)
            p.add("sync", fin_copy)

        with nc.Block() as block:
            @block.sync
            def _(sync):
                for fn in p.ops["sync"]:
                    fn(sync)

            @block.tensor
            def _(t):
                for fn in p.ops["tensor"]:
                    fn(t)

            @block.vector
            def _(v):
                for fn in p.ops["vector"]:
                    fn(v)

            @block.scalar
            def _(a):
                for fn in p.ops["scalar"]:
                    fn(a)

            @block.gpsimd
            def _(g):
                for fn in p.ops["gpsimd"]:
                    fn(g)

    return nc



def kernel(**inputs):
    x = np.ascontiguousarray(inputs["L0_attn_hidden_states"], dtype=np.float32)
    B = x.shape[0]
    nc = build_program(debug=DEBUG, stop_after=STOP_AFTER)

    common = {
        "Wq": inputs["L0_attn_Wq"], "Wk": inputs["L0_attn_Wk"],
        "Wv": inputs["L0_attn_Wv"], "Ww": inputs["L0_attn_Ww"],
        "Wo": inputs["L0_attn_Wo"],
        "cq": inputs["L0_attn_cq"], "ck": inputs["L0_attn_ck"],
        "cv": inputs["L0_attn_cv"],
        "Wf1": inputs["L0_attn_Wf1"], "Wf2": inputs["L0_attn_Wf2"],
        "Wb": inputs["L0_attn_Wb"],
        "Wg1": inputs["L0_attn_Wg1"], "Wg2": inputs["L0_attn_Wg2"],
        "bg2": np.asarray(inputs["L0_attn_bg2"], dtype=np.float32).reshape(1, -1),
        "norm_w": np.asarray(inputs["L0_attn_norm_w"], dtype=np.float32).reshape(1, -1),
        "gate_w": inputs["L0_ffn_gate_w"], "down_w": inputs["L0_ffn_down_w"],
        "gate_norm_w": np.asarray(inputs["L0_ffn_gate_norm_w"],
                                  dtype=np.float32).reshape(1, -1),
        "down_norm_w": np.asarray(inputs["L0_ffn_down_norm_w"],
                                  dtype=np.float32).reshape(1, -1),
        "wmod": inputs["L0_wmod"],
    }
    common = {k: np.ascontiguousarray(v, dtype=np.float32) for k, v in common.items()}

    in_maps = []
    for b in range(B):
        m = dict(common)
        m["x"] = x[b]
        m["cvec"] = np.ascontiguousarray(
            np.asarray(inputs["L0_c"], dtype=np.float32)[b:b + 1])
        in_maps.append(m)

    res = run_bass_kernel_spmd(nc, in_maps, core_ids=list(range(B)))
    out = np.stack([np.asarray(res.results[b]["out"]) for b in range(B)], axis=0)
    return out


# version 07 (final)
# speedup vs baseline: 1.0504x; 1.0504x over previous
import numpy as np
import ml_dtypes
import concourse.bass as bass
import concourse.mybir as mybir
from concourse.bass_utils import run_bass_kernel_spmd

F32 = mybir.dt.float32
BF16 = mybir.dt.bfloat16
AF = mybir.ActivationFunctionType
ALU = mybir.AluOpType
AX = mybir.AxisListType

T, D, H, DK = 4096, 2048, 16, 128
NT = T // 128
CT1 = 1024
NC1 = T // CT1
CC = 128
NCC = T // CC
INTER = 5632
NJB = (2 * INTER) // 512
LN_EPS = 1e-5
L2_EPS = 1e-6
SCALE = float(DK) ** -0.5

DEBUG = False
STOP_AFTER = None



class Sem:
    def __init__(self, handle):
        self.h = handle
        self.n = 0

    def inc(self, k=1):
        self.n += k
        return self.n


class Prog:
    ENGS = ("sync", "tensor", "vector", "scalar", "gpsimd")

    def __init__(self):
        self.ops = {e: [] for e in self.ENGS}

    def add(self, eng, fn):
        self.ops[eng].append(fn)


def _ap(x):
    return x.ap() if hasattr(x, "ap") else x


class _Chain:
    __slots__ = ("entry",)

    def __init__(self, entry):
        self.entry = entry

    def __getattr__(self, name):
        def call(*a, **k):
            self.entry.append((name, a, k))
            return self
        return call


class Rec:

    def __init__(self):
        self.calls = []

    def __getattr__(self, name):
        if name.startswith("_"):
            raise AttributeError(name)

        def call(*a, **k):
            entry = [(name, a, k)]
            self.calls.append(entry)
            return _Chain(entry)
        return call

    def replay(self, eng):
        for entry in self.calls:
            obj = eng
            for (name, a, k) in entry:
                obj = getattr(obj, name)(*a, **k)



def build_program(debug=False, stop_after=None):
    nc = bass.Bass("TRN2", target_bir_lowering=False,
                   detect_race_conditions=False)

    scratch_kind = "ExternalOutput" if debug else "Internal"

    x_in = nc.dram_tensor("x", [T, D], F32, kind="ExternalInput").ap()
    wq_in = nc.dram_tensor("Wq", [D, D], F32, kind="ExternalInput").ap()
    wk_in = nc.dram_tensor("Wk", [D, D], F32, kind="ExternalInput").ap()
    wv_in = nc.dram_tensor("Wv", [D, D], F32, kind="ExternalInput").ap()
    ww_in = nc.dram_tensor("Ww", [D, D], F32, kind="ExternalInput").ap()
    wo_in = nc.dram_tensor("Wo", [D, D], F32, kind="ExternalInput").ap()
    cq_in = nc.dram_tensor("cq", [D, 4], F32, kind="ExternalInput").ap()
    ck_in = nc.dram_tensor("ck", [D, 4], F32, kind="ExternalInput").ap()
    cv_in = nc.dram_tensor("cv", [D, 4], F32, kind="ExternalInput").ap()
    wf1_in = nc.dram_tensor("Wf1", [D, 128], F32, kind="ExternalInput").ap()
    wf2_in = nc.dram_tensor("Wf2", [128, D], F32, kind="ExternalInput").ap()
    wb_in = nc.dram_tensor("Wb", [D, 16], F32, kind="ExternalInput").ap()
    wg1_in = nc.dram_tensor("Wg1", [D, 128], F32, kind="ExternalInput").ap()
    wg2_in = nc.dram_tensor("Wg2", [128, D], F32, kind="ExternalInput").ap()
    bg2_in = nc.dram_tensor("bg2", [1, D], F32, kind="ExternalInput").ap()
    nw_in = nc.dram_tensor("norm_w", [1, 128], F32, kind="ExternalInput").ap()
    gw_in = nc.dram_tensor("gate_w", [2 * INTER, D], F32, kind="ExternalInput").ap()
    dw_in = nc.dram_tensor("down_w", [D, INTER], F32, kind="ExternalInput").ap()
    gnw_in = nc.dram_tensor("gate_norm_w", [1, D], F32, kind="ExternalInput").ap()
    dnw_in = nc.dram_tensor("down_norm_w", [1, INTER], F32, kind="ExternalInput").ap()
    cvec_in = nc.dram_tensor("cvec", [1, D], F32, kind="ExternalInput").ap()
    wmod_in = nc.dram_tensor("wmod", [D, 6 * D], F32, kind="ExternalInput").ap()

    out_hbm = nc.dram_tensor("out", [T, D], F32, kind="ExternalOutput").ap()

    wq_t = nc.dram_tensor("wq_t", [16, 128, 16, 128], F32, kind=scratch_kind).ap()
    wk_t = nc.dram_tensor("wk_t", [16, 128, 16, 128], F32, kind="Internal").ap()
    wv_t = nc.dram_tensor("wv_t", [16, 128, 16, 128], F32, kind="Internal").ap()
    ww_t = nc.dram_tensor("ww_t", [16, 128, 16, 128], F32, kind="Internal").ap()
    wo_t = nc.dram_tensor("wo_t", [16, 128, 16, 128], F32, kind="Internal").ap()
    gwqT_s = nc.dram_tensor("gwqT_s", [16, 88, 128, 128], BF16, kind=scratch_kind).ap()
    dwqT_s = nc.dram_tensor("dwqT_s", [44, 16, 128, 128], BF16, kind=scratch_kind).ap()
    mod_s = nc.dram_tensor("mod_s", [6, D], F32, kind=scratch_kind).ap()
    q_fm_s = nc.dram_tensor("q_fm_s", [H, DK, T], F32, kind=scratch_kind).ap()
    k_fm_s = nc.dram_tensor("k_fm_s", [H, DK, T], F32, kind=scratch_kind).ap()
    k_tm_s = nc.dram_tensor("k_tm_s", [NCC, H, CC, DK], F32, kind=scratch_kind).ap()
    v_tm_s = nc.dram_tensor("v_tm_s", [NCC, H, CC, DK], F32, kind=scratch_kind).ap()
    w_tm_s = nc.dram_tensor("w_tm_s", [NCC, H, CC, DK], F32, kind=scratch_kind).ap()
    z_tm_s = nc.dram_tensor("z_tm_s", [NCC, H, CC, DK], F32, kind=scratch_kind).ap()
    beta_s = nc.dram_tensor("beta_s", [NCC, CC, H], F32, kind=scratch_kind).ap()
    sg_s = nc.dram_tensor("sg_s", [T, D], F32, kind=scratch_kind).ap()
    yo_s = nc.dram_tensor("yo_s", [H, T, DK], F32, kind=scratch_kind).ap()
    x2_s = nc.dram_tensor("x2_s", [T, D], F32, kind=scratch_kind).ap()
    h3_s = nc.dram_tensor("h3_s", [16, NJB // 2, 128, 512], F32, kind=scratch_kind).ap()
    xq2_s = nc.dram_tensor("xq2_s", [16, 44, 128, 128], BF16, kind="Internal").ap()

    ident_f32 = nc.inline_tensor(np.eye(128, dtype=np.float32), "ident_f32")
    ident_bf16 = nc.inline_tensor(np.eye(128, dtype=ml_dtypes.bfloat16), "ident_bf16")
    ltri_np = np.triu(np.ones((128, 128), dtype=np.float32))
    ltri_f32 = nc.inline_tensor(ltri_np, "ltri_f32")
    ones_1x128 = nc.inline_tensor(np.ones((1, 128), dtype=np.float32), "ones_1x128")
    ones_128x1 = nc.inline_tensor(np.ones((128, 1), dtype=np.float32), "ones_128x1")

    p = Prog()

    import contextlib
    stack = contextlib.ExitStack()

    def sem(name):
        return Sem(stack.enter_context(nc.semaphore(name)))

    def sb(name, shape, dtype):
        return stack.enter_context(nc.sbuf_tensor(name, shape, dtype))

    def psum(name, shape, dtype=F32):
        return stack.enter_context(nc.psum_tensor(name, shape, dtype))

    with stack:
        sLD = sem("sLD")
        sGD = sem("sGD")
        sMM = sem("sMM")
        sVE = sem("sVE")
        sAC = sem("sAC")
        sST = sem("sST")

        identf = sb("identf", [128, 128], F32)
        identb = sb("identb", [128, 128], BF16)
        ltri = sb("ltri", [128, 128], F32)
        ones1r = sb("ones1r", [1, 128], F32)
        ones1c = sb("ones1c", [128, 1], F32)
        tiny = sb("tiny", [128, 64], F32)
        modcol = sb("modcol", [128, 96], F32)

        arA = sb("arA", [128, 16384], F32)
        arB = sb("arB", [128, 8192], F32)
        arC = sb("arC", [128, 6144], F32)
        arRep = sb("arRep", [128, 6144], F32)
        arW = sb("arW", [128, 4096], F32)
        arBF = sb("arBF", [128, 11264], BF16)
        stat = sb("stat", [128, 64], F32)

        pb = [psum(f"pb{i}", [128, 512]) for i in range(8)]

        def s0_consts(sync):
            sync.dma_start(identf[:], ident_f32.ap()).then_inc(sLD.h, 16)
            sync.dma_start(identb[:], ident_bf16.ap()).then_inc(sLD.h, 16)
            sync.dma_start(ltri[:], ltri_f32.ap()).then_inc(sLD.h, 16)
            sync.dma_start(ones1r[:], ones_1x128.ap()).then_inc(sLD.h, 16)
            sync.dma_start(ones1c[:], ones_128x1.ap()).then_inc(sLD.h, 16)
        p.add("sync", s0_consts)
        sLD.inc(16 * 5)
        consts_ld = sLD.n

        relayout = [(wq_in, wq_t), (wk_in, wk_t), (wv_in, wv_t),
                    (ww_in, ww_t), (wo_in, wo_t)]
        wrow = [arW[:, 0:2048], arW[:, 2048:4096]]

        def emit_relayout_iter(wi, dc):
            src, dst = relayout[wi]
            buf = wrow[(wi * 16 + dc) % 2]
            ld_at = sLD.inc(16)

            def ld(sync, src=src, dc=dc, buf=buf, pr=sGD.n):
                sync.wait_ge(sGD.h, pr)
                sync.dma_start(buf, src[dc * 128:(dc + 1) * 128, :])\
                    .then_inc(sLD.h, 16)
            p.add("sync", ld)

            st_at = sGD.inc(16)

            def st(g, dst=dst, dc=dc, buf=buf, w=ld_at):
                g.wait_ge(sLD.h, w)
                g.dma_start(dst[:, :, dc, :].rearrange("jc p j -> p jc j"),
                            buf)\
                    .then_inc(sGD.h, 16)
            p.add("gpsimd", st)

        ascr = arC[:, 5888:5888 + 104]

        def s0qA_zero(v):
            v.memset(stat[:, 0:8], 0.0)
        p.add("vector", s0qA_zero)

        qbuf = [arA[:, 0:5632], arA[:, 5632:11264]]
        ACC_G, ACC_D = 0, 1

        def emit_qA_iter(bi):
            is_gate = bi < 88
            buf = qbuf[bi % 2][:, 0:(2048 if is_gate else 5632)]
            src = (gw_in[bi * 128:(bi + 1) * 128, :] if is_gate
                   else dw_in[(bi - 88) * 128:(bi - 87) * 128, :])
            ld_at = sLD.inc(16)

            def ld(sync, src=src, buf=buf, pr=sVE.n):
                sync.wait_ge(sVE.h, pr)
                sync.dma_start(buf, src).then_inc(sLD.h, 16)
            p.add("sync", ld)

            red_at = sVE.inc(1)

            def red(v, buf=buf, w=ld_at, bi=bi):
                v.wait_ge(sLD.h, w)
                v.tensor_reduce(ascr[:, bi:bi + 1], buf, AX.X, ALU.add,
                                apply_absolute_value=True).then_inc(sVE.h, 1)
            p.add("vector", red)

        for step in range(104):
            if step < 80:
                emit_relayout_iter(step // 16, step % 16)
            emit_qA_iter(step)
        relayout_done_sG = sGD.n
        qA_ve = sVE.n

        ve_at = sVE.inc(2)

        def s0qA_comb(v, w=qA_ve):
            v.wait_ge(sVE.h, w)
            v.tensor_reduce(stat[:, 0:1], ascr[:, 0:88], AX.X, ALU.add)\
                .then_inc(sVE.h, 1)
            v.tensor_reduce(stat[:, 1:2], ascr[:, 88:104], AX.X, ALU.add)\
                .then_inc(sVE.h, 1)
        p.add("vector", s0qA_comb)
        qA_ve = sVE.n

        mm_at = sMM.inc(1)

        def s0qA_fin_mm(t, ve=qA_ve):
            t.wait_ge(sVE.h, ve)
            t.wait_ge(sLD.h, consts_ld)
            t.matmul(pb[0][:1, 0:2], ones1c[:, 0:1], stat[:, 0:2],
                     start=True, stop=True).then_inc(sMM.h, 1)
        p.add("tensor", s0qA_fin_mm)

        fin_base = sVE.n
        sVE.inc(3)

        def s0qA_fin_v(v, w=mm_at, base=fin_base):
            v.wait_ge(sMM.h, w)
            v.tensor_single_scalar(tiny[:1, 4:5], pb[0][0:1, 0:1],
                                   1.0 / (2 * INTER * D), ALU.mult)
            v.tensor_single_scalar(tiny[:1, 5:6], pb[0][0:1, 1:2],
                                   1.0 / (D * INTER), ALU.mult).then_inc(sVE.h, 1)
            v.wait_ge(sVE.h, base + 1)
            v.tensor_single_scalar(tiny[:1, 4:6], tiny[:1, 4:6], 1e-5, ALU.max)\
                .then_inc(sVE.h, 1)
            v.wait_ge(sVE.h, base + 2)
            v.reciprocal(tiny[:1, 6:8], tiny[:1, 4:6]).then_inc(sVE.h, 1)
        p.add("vector", s0qA_fin_v)

        mm_at2 = sMM.inc(1)

        def s0qA_rep_mm(t, ve=sVE.n):
            t.wait_ge(sVE.h, ve)
            t.matmul(pb[1][:, 0:2], ones1r[:1, :], tiny[:1, 6:8],
                     start=True, stop=True).then_inc(sMM.h, 1)
        p.add("tensor", s0qA_rep_mm)

        ve_at = sVE.inc(1)

        def s0qA_rep_v(v, w=mm_at2):
            v.wait_ge(sMM.h, w)
            v.tensor_copy(tiny[:, 0:1], pb[1][:, 0:1])
            v.tensor_copy(tiny[:, 2:3], pb[1][:, 1:2])
            v.tensor_single_scalar(tiny[:, 1:2], pb[1][:, 0:1], -1.0, ALU.mult)
            v.tensor_single_scalar(tiny[:, 3:4], pb[1][:, 1:2], -1.0, ALU.mult)\
                .then_inc(sVE.h, 1)
        p.add("vector", s0qA_rep_v)
        scales_ve = sVE.n

        tstage = arBF[:, 0:5632]

        def emit_qB_iter(bi):
            is_gate = bi < 88
            ncols = 2048 if is_gate else 5632
            buf = qbuf[bi % 2][:, 0:ncols]
            scr = qbuf[(bi + 1) % 2][:, 0:ncols]
            src = (gw_in[bi * 128:(bi + 1) * 128, :] if is_gate
                   else dw_in[(bi - 88) * 128:(bi - 87) * 128, :])
            sc = 0 if is_gate else 2

            ld_at = sLD.inc(16)

            def ld(sync, src=src, buf=buf, pr=sVE.n, pm=sMM.n):
                sync.wait_ge(sVE.h, max(pr, scales_ve))
                sync.wait_ge(sMM.h, pm)
                sync.dma_start(buf, src).then_inc(sLD.h, 16)
            p.add("sync", ld)

            q_at = sVE.inc(1)

            def quant(v, buf=buf, scr=scr, w=ld_at, sc=sc, pr=sGD.n,
                      pm=sMM.n, n=ncols):
                v.wait_ge(sLD.h, w)
                v.wait_ge(sGD.h, pr)
                v.wait_ge(sMM.h, pm)
                v.tensor_scalar(scr, buf,
                                tiny[:, sc:sc + 1], 0.5, ALU.mult, ALU.is_ge)
                v.tensor_scalar(buf, buf, tiny[:, sc + 1:sc + 2], 0.5,
                                ALU.mult, ALU.is_ge)
                v.tensor_tensor(buf, scr, buf,
                                ALU.subtract).then_inc(sVE.h, 1)
            p.add("vector", quant)

            nch = ncols // 128
            for g4 in range((nch + 3) // 4):
                n4 = min(4, nch - g4 * 4)
                bk = 5 + (g4 % 2)
                mm_at = sMM.inc(1)

                def trq(t, buf=buf, g4=g4, n4=n4, bk=bk, w=q_at, pe=sVE.n):
                    t.wait_ge(sVE.h, max(w, pe))
                    for i in range(n4):
                        mi = t.matmul(pb[bk][:, i * 128:(i + 1) * 128],
                                      buf[:, (g4 * 4 + i) * 128:
                                          (g4 * 4 + i + 1) * 128],
                                      identf[:], is_transpose=True,
                                      start=(i == 0), stop=(i == n4 - 1))
                    mi.then_inc(sMM.h, 1)
                p.add("tensor", trq)

                ve_at2 = sVE.inc(1)

                def evq(v, g4=g4, n4=n4, bk=bk, w=mm_at):
                    v.wait_ge(sMM.h, w)
                    v.tensor_copy(tstage[:, g4 * 512:g4 * 512 + n4 * 128],
                                  pb[bk][:, 0:n4 * 128]).then_inc(sVE.h, 1)
                p.add("vector", evq)

            st_at = sGD.inc(16)

            def st(g, bi=bi, is_gate=is_gate, w=sVE.n):
                g.wait_ge(sVE.h, w)
                if is_gate:
                    g.dma_start(gwqT_s[:, bi].rearrange("dc p j -> p dc j"),
                                tstage[:, 0:2048]).then_inc(sGD.h, 16)
                else:
                    g.dma_start(dwqT_s[:, bi - 88].rearrange("jc p d -> p jc d"),
                                tstage[:, 0:5632]).then_inc(sGD.h, 16)
            p.add("gpsimd", st)

        crow = arC[:, 0:2048]
        scrow = arC[:, 2048:4096]
        scT = arC[:, 4096:4112]

        ld_at = sLD.inc(16)

        def s0m_ldc(sync):
            sync.dma_start(crow[:1, :], cvec_in).then_inc(sLD.h, 16)
        p.add("sync", s0m_ldc)

        ac_at = sAC.inc(1)

        def s0m_silu(a, w=ld_at):
            a.wait_ge(sLD.h, w)
            a.activation(scrow[:1, :], crow[:1, :], AF.Silu).then_inc(sAC.h, 1)
        p.add("scalar", s0m_silu)

        mm_at = sMM.inc(1)

        def s0m_sct(t, w=ac_at):
            t.wait_ge(sAC.h, w)
            for dc in range(16):
                mi = t.matmul(pb[2][:, dc:dc + 1],
                              scrow[:1, dc * 128:(dc + 1) * 128],
                              ones1c[:1, 0:1],
                              start=True, stop=True)
            mi.then_inc(sMM.h, 1)
        p.add("tensor", s0m_sct)

        ve_at = sVE.inc(1)

        def s0m_sct_ev(v, w=mm_at):
            v.wait_ge(sMM.h, w)
            v.tensor_copy(scT, pb[2][:, 0:16]).then_inc(sVE.h, 1)
        p.add("vector", s0m_sct_ev)
        sct_ve = sVE.n

        wmbuf = [arB[:, 0:4096], arB[:, 4096:8192]]
        def s0m_zero(v, pr=sct_ve):
            v.wait_ge(sVE.h, pr)
            v.memset(modcol[:], 0.0)
        p.add("vector", s0m_zero)

        mm_prev = {0: 0, 1: 0}

        def emit_mod_iter(dc, half):
            buf = wmbuf[(dc * 3 + half) % 2]
            ld_at = sLD.inc(16)

            def ld(sync, dc=dc, half=half, buf=buf,
                   pr=mm_prev[(dc * 3 + half) % 2]):
                sync.wait_ge(sMM.h, pr)
                sync.dma_start(buf, wmod_in[dc * 128:(dc + 1) * 128,
                                            half * 4096:(half + 1) * 4096])\
                    .then_inc(sLD.h, 16)
            p.add("sync", ld)

            mm_at = sMM.inc(1)
            mm_prev[(dc * 3 + half) % 2] = mm_at

            def mm(t, dc=dc, half=half, buf=buf, w=ld_at, ve=sVE.n):
                t.wait_ge(sLD.h, w)
                t.wait_ge(sVE.h, ve)
                for j in range(32):
                    mi = t.matmul(pb[3][:, j:j + 1],
                                  buf[:, j * 128:(j + 1) * 128],
                                  scT[:, dc:dc + 1],
                                  start=True, stop=True)
                mi.then_inc(sMM.h, 1)
            p.add("tensor", mm)

            ve_at = sVE.inc(1)

            def acc(v, half=half, w=mm_at):
                v.wait_ge(sMM.h, w)
                v.tensor_tensor(modcol[:, half * 32:(half + 1) * 32],
                                modcol[:, half * 32:(half + 1) * 32],
                                pb[3][:, 0:32], ALU.add).then_inc(sVE.h, 1)
            p.add("vector", acc)

        mod_iters = [(dc, half) for dc in range(16) for half in range(3)]
        for step in range(104):
            if step < 48:
                emit_mod_iter(*mod_iters[step])
            emit_qB_iter(step)
        quant_done_sG = sGD.n
        quant_done_ve = sVE.n
        mod_acc_ve = sVE.n

        gfx_base = sVE.n
        sVE.inc(7)

        def s0m_gafix(v, base=gfx_base):
            sls = [modcol[:, 32:48], modcol[:, 80:96]]
            sgns = [arC[:, 4112:4128], arC[:, 4128:4144]]
            mags = [arC[:, 4144:4160], arC[:, 4160:4176]]
            for i in (0, 1):
                v.tensor_scalar(sgns[i], sls[i], 0.0, None, ALU.is_ge)
            v.engine_nop().then_inc(sVE.h, 1)
            v.wait_ge(sVE.h, base + 1)
            for i in (0, 1):
                v.tensor_scalar(sgns[i], sgns[i], 2.0, -1.0, ALU.mult, ALU.add)
            v.engine_nop().then_inc(sVE.h, 1)
            v.wait_ge(sVE.h, base + 2)
            for i in (0, 1):
                v.tensor_tensor(mags[i], sls[i], sgns[i], ALU.mult)
            v.engine_nop().then_inc(sVE.h, 1)
            v.wait_ge(sVE.h, base + 3)
            for i in (0, 1):
                v.tensor_single_scalar(mags[i], mags[i], 0.25, ALU.max)
            v.engine_nop().then_inc(sVE.h, 1)
            v.wait_ge(sVE.h, base + 4)
            for i in (0, 1):
                v.tensor_tensor(sls[i], sgns[i], mags[i], ALU.mult)\
                    .then_inc(sVE.h, 1)
            v.wait_ge(sVE.h, base + 6)
            v.engine_nop().then_inc(sVE.h, 1)
        p.add("vector", s0m_gafix)
        gafix_ve = sVE.n

        mm_at = sMM.inc(1)

        def s0m_tr(t, ve=gafix_ve):
            t.wait_ge(sVE.h, ve)
            t.transpose(pb[4][:96, 0:128], modcol[:, 0:96], identf[:])\
                .then_inc(sMM.h, 1)
        p.add("tensor", s0m_tr)

        ve_at = sVE.inc(1)

        def s0m_tr_ev(v, w=mm_at):
            v.wait_ge(sMM.h, w)
            v.tensor_copy(arC[:96, 4352:4480], pb[4][:96, 0:128]).then_inc(sVE.h, 1)
        p.add("vector", s0m_tr_ev)
        modT_ve = sVE.n

        st_at = sGD.inc(16)

        def s0m_st(g, w=modT_ve):
            g.wait_ge(sVE.h, w)
            g.dma_start(mod_s.rearrange("k (p f) -> (k p) f", p=16),
                        arC[:96, 4352:4480])\
                .then_inc(sGD.h, 16)
        p.add("gpsimd", s0m_st)
        mod_st_sG = sGD.n

        if stop_after != "s0":
            arS1 = sb("arS1", [128, 4608], F32)
            wf1res = arS1[:, 0:2048]
            wf2res = arS1[:, 2048:4096]
            wbres = arS1[:, 4096:4352]
            tails = arS1[:, 4352:4544]

            sy, te, ve, ac, gp = Rec(), Rec(), Rec(), Rec(), Rec()

            s0MM, s0VE, s0AC, s0GD = sMM.n, sVE.n, sAC.n, sGD.n
            sy.wait_ge(sMM.h, s0MM)
            sy.wait_ge(sVE.h, s0VE)
            sy.wait_ge(sAC.h, s0AC)
            sy.wait_ge(sGD.h, s0GD)

            for dc in range(16):
                sy.dma_start(wf1res[:, dc * 128:(dc + 1) * 128],
                             wf1_in[dc * 128:(dc + 1) * 128, :])\
                    .then_inc(sLD.h, 16); sLD.inc(16)
            sy.dma_start(wf2res[:, :], wf2_in[:, :]).then_inc(sLD.h, 16)
            sLD.inc(16)
            wg1res = arC[:, 3584:5632]
            for dc in range(16):
                sy.dma_start(wg1res[:, dc * 128:(dc + 1) * 128],
                             wg1_in[dc * 128:(dc + 1) * 128, :])\
                    .then_inc(sLD.h, 16); sLD.inc(16)
            for dc in range(16):
                sy.dma_start(wbres[:, dc * 16:(dc + 1) * 16],
                             wb_in[dc * 128:(dc + 1) * 128, :])\
                    .then_inc(sLD.h, 16); sLD.inc(16)
            sy.dma_start(arRep[:, 0:2048], bg2_in.broadcast_to([128, D]))\
                .then_inc(sLD.h, 16); sLD.inc(16)
            wg2res = arRep[:, 2048:4096]
            sy.dma_start(wg2res, wg2_in[:, :]).then_inc(sLD.h, 16); sLD.inc(16)
            ccA = arRep[:, 4096:4288]
            for si, cin in enumerate((cq_in, ck_in, cv_in)):
                for h in range(16):
                    sy.dma_start(ccA[:, (si * 16 + h) * 4:(si * 16 + h) * 4 + 4],
                                 cin[h * 128:(h + 1) * 128, :])\
                        .then_inc(sLD.h, 16); sLD.inc(16)
            s1p_ld = sLD.n

            ve.wait_ge(sVE.h, gafix_ve)
            ve.tensor_single_scalar(tiny[:, 8:24], modcol[:, 16:32], 1.0,
                                    ALU.add)
            ve.memset(tails[:, :], 0.0)
            ve.memset(stat[:, 16:64], 0.0).then_inc(sVE.h, 1)
            sVE.inc(1)
            s1p_ve = sVE.n

            xbuf = [arB[:, 0:2048], arB[:, 2048:4096]]
            xnbuf = [arB[:, 4096:6144], arB[:, 6144:8192]]
            qlin = arC[:, 0:1027]
            qconv = arC[:, 1027:2051]
            t12buf = arC[:, 2051:3075]
            bstage = arC[:, 3075:3203]
            qsil = [arB[:, 0:1024], arB[:, 1024:2048]]
            tokstage = arB[:, 2048:4096]
            sgstage = arB[:, 4096:6144]
            wpan = [arW[:, 0:2048], arW[:, 2048:4096]]

            bank_free = {i: 0 for i in range(8)}

            def bwait(i):
                if bank_free[i]:
                    te.wait_ge(sVE.h, bank_free[i])

            WSTREAMS = [("q", wq_t, 0, q_fm_s, None),
                        ("k", wk_t, 1, k_fm_s, k_tm_s),
                        ("v", wv_t, 2, None, v_tm_s),
                        ("w", ww_t, 2, None, w_tm_s)]

            pan_mm = {}
            qsil_free = {0: (0, 0), 1: (0, 0)}
            tok_free = 0
            sg_free = 0

            for cc1 in range(NC1):
                sy.wait_ge(sVE.h, sVE.n)
                sy.wait_ge(sAC.h, sAC.n)
                sy.wait_ge(sGD.h, sGD.n)
                sy.wait_ge(sMM.h, sMM.n)

                ld_t, ve_t, ac_t = [], [], []
                for tt in range(8):
                    if tt >= 2:
                        sy.wait_ge(sVE.h, ve_t[tt - 2])
                        sy.wait_ge(sAC.h, ac_t[tt - 2])
                    sy.dma_start(xbuf[tt % 2],
                                 x_in[cc1 * CT1 + tt * 128:
                                      cc1 * CT1 + (tt + 1) * 128, :])\
                        .then_inc(sLD.h, 16)
                    ld_t.append(sLD.inc(16))

                    ve.wait_ge(sLD.h, ld_t[tt])
                    ve.tensor_reduce(stat[:, 16 + tt:17 + tt], xbuf[tt % 2],
                                     AX.X, ALU.add).then_inc(sVE.h, 1)
                    ve_t.append(sVE.inc(1))

                    ac.wait_ge(sLD.h, ld_t[tt])
                    ac.activation(xnbuf[tt % 2], xbuf[tt % 2], AF.Square,
                                  accum_out=stat[:, 24 + tt:25 + tt])\
                        .then_inc(sAC.h, 1)
                    ac_t.append(sAC.inc(1))

                b = sVE.n
                ve.wait_ge(sVE.h, ve_t[-1])
                ve.wait_ge(sAC.h, ac_t[-1])
                ve.tensor_single_scalar(stat[:, 32:40], stat[:, 16:24],
                                        1.0 / D, ALU.mult)
                ve.tensor_single_scalar(stat[:, 40:48], stat[:, 24:32],
                                        1.0 / D, ALU.mult).then_inc(sVE.h, 1)
                ve.wait_ge(sVE.h, b + 1)
                ve.tensor_tensor(stat[:, 48:56], stat[:, 32:40],
                                 stat[:, 32:40], ALU.mult).then_inc(sVE.h, 1)
                ve.wait_ge(sVE.h, b + 2)
                ve.scalar_tensor_tensor(stat[:, 48:56], stat[:, 40:48],
                                        LN_EPS, stat[:, 48:56],
                                        ALU.add, ALU.subtract).then_inc(sVE.h, 1)
                sVE.inc(3)
                p2_ve = sVE.n

                ab = sAC.n
                ac.wait_ge(sVE.h, p2_ve)
                ac.activation(stat[:, 56:64], stat[:, 48:56], AF.Ln)\
                    .then_inc(sAC.h, 1)
                ac.wait_ge(sAC.h, ab + 1)
                ac.activation(stat[:, 48:56], stat[:, 56:64], AF.Exp,
                              scale=-0.5).then_inc(sAC.h, 1)
                sAC.inc(2)
                p2_ac = sAC.n

                b = sVE.n
                ve.wait_ge(sAC.h, p2_ac)
                ve.tensor_tensor(stat[:, 56:64], stat[:, 32:40],
                                 stat[:, 48:56], ALU.mult).then_inc(sVE.h, 1)
                ve.wait_ge(sVE.h, b + 1)
                ve.tensor_single_scalar(stat[:, 56:64], stat[:, 56:64],
                                        -1.0, ALU.mult).then_inc(sVE.h, 1)
                sVE.inc(2)
                p2_done = sVE.n

                xn_ac = []
                for tt in range(8):
                    sy.wait_ge(sVE.h, sVE.n)
                    sy.dma_start(xbuf[tt % 2],
                                 x_in[cc1 * CT1 + tt * 128:
                                      cc1 * CT1 + (tt + 1) * 128, :])\
                        .then_inc(sLD.h, 16)
                    w = sLD.inc(16)

                    ac.wait_ge(sLD.h, w)
                    ac.wait_ge(sVE.h, p2_done)
                    ac.activation(xnbuf[tt % 2], xbuf[tt % 2], AF.Identity,
                                  bias=stat[:, 56 + tt:57 + tt],
                                  scale=stat[:, 48 + tt:49 + tt])\
                        .then_inc(sAC.h, 1)
                    wa = sAC.inc(1)
                    xn_ac.append(wa)

                    for dc in range(16):
                        bk = dc % 4
                        te.wait_ge(sAC.h, wa)
                        bwait(bk)
                        te.transpose(pb[bk][:, 0:128],
                                     xnbuf[tt % 2][:, dc * 128:(dc + 1) * 128],
                                     identf[:]).then_inc(sMM.h, 1)
                        wm = sMM.inc(1)

                        ve.wait_ge(sMM.h, wm)
                        ve.tensor_scalar(
                            arA[:, dc * 1024 + tt * 128:
                                   dc * 1024 + (tt + 1) * 128],
                            pb[bk][:, 0:128],
                            tiny[:, 8 + dc:9 + dc],
                            modcol[:, dc:dc + 1],
                            ALU.mult, ALU.add).then_inc(sVE.h, 1)
                        bank_free[bk] = sVE.inc(1)
                h_ready = sVE.n

                def conv_silu_spill(snm, si, h, ci, fm_s, tm_s, ev_t):
                    tcol = (si * 16 + h) * 3
                    ve.wait_ge(sVE.h, ev_t)
                    ve.tensor_copy(qlin[:, 0:3], tails[:, tcol:tcol + 3])
                    cbase = (ci * 16 + h) * 4
                    ve.tensor_single_scalar(qconv[:, 0:1024], qlin[:, 0:1024],
                                            ccA[:, cbase:cbase + 1], ALU.mult)
                    for i in (1, 2, 3):
                        ve.scalar_tensor_tensor(qconv[:, 0:1024],
                                                qlin[:, i:i + 1024],
                                                ccA[:, cbase + i:cbase + i + 1],
                                                qconv[:, 0:1024],
                                                ALU.mult, ALU.add)
                    ve.tensor_copy(tails[:, tcol:tcol + 3], qlin[:, 1024:1027])\
                        .then_inc(sVE.h, 1)
                    conv_ve = sVE.inc(1)

                    slot = (si * 16 + h) % 2
                    gfree, mfree = qsil_free[slot]
                    ac.wait_ge(sVE.h, conv_ve)
                    if gfree:
                        ac.wait_ge(sGD.h, gfree)
                    if mfree:
                        ac.wait_ge(sMM.h, mfree)
                    ac.activation(qsil[slot], qconv[:, 0:1024], AF.Silu)\
                        .then_inc(sAC.h, 1)
                    sil_ac = sAC.inc(1)
                    return slot, sil_ac

                def spill_stream(snm, h, slot, dep, fm_s, tm_s, cc1=cc1):
                    nonlocal tok_free
                    dsem, dval = dep
                    last_g = 0
                    last_m = 0
                    if fm_s is not None:
                        gp.wait_ge(dsem.h, dval)
                        gp.dma_start(fm_s[h][:, cc1 * CT1:(cc1 + 1) * CT1],
                                     qsil[slot]).then_inc(sGD.h, 16)
                        last_g = sGD.inc(16)
                    if tm_s is not None:
                        for lc in range(8):
                            bk = lc % 4
                            te.wait_ge(dsem.h, dval)
                            bwait(bk)
                            te.transpose(pb[bk][:, 0:128],
                                         qsil[slot][:, lc * 128:(lc + 1) * 128],
                                         identf[:]).then_inc(sMM.h, 1)
                            wm = sMM.inc(1)
                            last_m = wm
                            ve.wait_ge(sMM.h, wm)
                            if lc == 0 and tok_free:
                                ve.wait_ge(sGD.h, tok_free)
                            ve.tensor_copy(tokstage[:, lc * 128:(lc + 1) * 128],
                                           pb[bk][:, 0:128]).then_inc(sVE.h, 1)
                            bank_free[bk] = sVE.inc(1)
                        gp.wait_ge(sVE.h, sVE.n)
                        gp.dma_start(
                            tm_s[cc1 * 8:(cc1 + 1) * 8, h]
                            .rearrange("cc t m -> t cc m"),
                            tokstage[:, 0:1024]).then_inc(sGD.h, 16)
                        tok_free = sGD.inc(16)
                        last_g = tok_free
                    qsil_free[slot] = (last_g, last_m)

                for si, (snm, wtil, ci, fm_s, tm_s) in enumerate(WSTREAMS):
                    for h in range(16):
                        slot = (si * 16 + h) % 2
                        if pan_mm.get(slot):
                            sy.wait_ge(sMM.h, pan_mm[slot])
                        sy.dma_start(wpan[slot],
                                     wtil[h].rearrange("p dc j -> p (dc j)"))\
                            .then_inc(sLD.h, 16)
                        wld = sLD.inc(16)

                        ev_t = 0
                        for tb in range(2):
                            bk = 4 + tb
                            te.wait_ge(sLD.h, wld)
                            if si == 0 and h == 0 and tb == 0:
                                te.wait_ge(sVE.h, h_ready)
                            bwait(bk)
                            for dc in range(16):
                                mi = te.matmul(
                                    pb[bk][:, 0:512],
                                    wpan[slot][:, dc * 128:(dc + 1) * 128],
                                    arA[:, dc * 1024 + tb * 512:
                                           dc * 1024 + (tb + 1) * 512],
                                    start=(dc == 0), stop=(dc == 15))
                            mi.then_inc(sMM.h, 1)
                            wm = sMM.inc(1)
                            pan_mm[slot] = wm

                            ve.wait_ge(sMM.h, wm)
                            ve.tensor_copy(qlin[:, 3 + tb * 512:3 + (tb + 1) * 512],
                                           pb[bk][:, 0:512]).then_inc(sVE.h, 1)
                            ev_t = sVE.inc(1)
                            bank_free[bk] = ev_t

                        slot_q, sil_ac = conv_silu_spill(snm, si, h, ci,
                                                         fm_s, tm_s, ev_t)
                        spill_stream(snm, h, slot_q, (sAC, sil_ac), fm_s, tm_s)

                def small_proj(res, outbuf):
                    ev = 0
                    for tb in range(2):
                        bk = 4 + tb
                        bwait(bk)
                        for dc in range(16):
                            mi = te.matmul(pb[bk][:, 0:512],
                                           res[:, dc * 128:(dc + 1) * 128],
                                           arA[:, dc * 1024 + tb * 512:
                                                  dc * 1024 + (tb + 1) * 512],
                                           start=(dc == 0), stop=(dc == 15))
                        mi.then_inc(sMM.h, 1)
                        wm = sMM.inc(1)
                        ve.wait_ge(sMM.h, wm)
                        ve.tensor_copy(outbuf[:, tb * 512:(tb + 1) * 512],
                                       pb[bk][:, 0:512]).then_inc(sVE.h, 1)
                        ev = sVE.inc(1)
                        bank_free[bk] = ev
                    return ev

                t1_ve = small_proj(wf1res, t12buf)
                for h2 in range(16):
                    slot = h2 % 2
                    gfree, mfree = qsil_free[slot]
                    ev = 0
                    for tb in range(2):
                        bk = 4 + tb
                        te.wait_ge(sVE.h, t1_ve)
                        bwait(bk)
                        mi = te.matmul(pb[bk][:, 0:512],
                                       wf2res[:, h2 * 128:(h2 + 1) * 128],
                                       t12buf[:, tb * 512:(tb + 1) * 512],
                                       start=True, stop=True)
                        mi.then_inc(sMM.h, 1)
                        wm = sMM.inc(1)
                        ve.wait_ge(sMM.h, wm)
                        if tb == 0 and gfree:
                            ve.wait_ge(sGD.h, gfree)
                        if tb == 0 and mfree:
                            ve.wait_ge(sMM.h, mfree)
                        ve.tensor_copy(qsil[slot][:, tb * 512:(tb + 1) * 512],
                                       pb[bk][:, 0:512]).then_inc(sVE.h, 1)
                        ev = sVE.inc(1)
                        bank_free[bk] = ev
                    spill_stream("z", h2, slot, (sVE, ev), None, z_tm_s)

                for tb in range(2):
                    bk = 6
                    bwait(bk)
                    for dc in range(16):
                        mi = te.matmul(pb[bk][:16, 0:512],
                                       wbres[:, dc * 16:(dc + 1) * 16],
                                       arA[:, dc * 1024 + tb * 512:
                                              dc * 1024 + (tb + 1) * 512],
                                       start=(dc == 0), stop=(dc == 15))
                    mi.then_inc(sMM.h, 1)
                    wm = sMM.inc(1)
                    ve.wait_ge(sMM.h, wm)
                    if tb == 0 and sg_free:
                        ve.wait_ge(sGD.h, sg_free)
                    ve.tensor_copy(sgstage[:16, tb * 512:(tb + 1) * 512],
                                   pb[bk][:16, 0:512]).then_inc(sVE.h, 1)
                    bank_free[bk] = sVE.inc(1)
                bv = sVE.n
                ac.wait_ge(sVE.h, bv)
                ac.activation(sgstage[:16, 0:1024], sgstage[:16, 0:1024],
                              AF.Tanh, scale=0.5).then_inc(sAC.h, 1)
                bt_ac = sAC.inc(1)
                ve.wait_ge(sAC.h, bt_ac)
                ve.tensor_scalar(sgstage[:16, 0:1024], sgstage[:16, 0:1024],
                                 0.5, 0.5, ALU.mult, ALU.add).then_inc(sVE.h, 1)
                bt_ve = sVE.inc(1)
                for lc in range(8):
                    bk = lc % 4
                    te.wait_ge(sVE.h, bt_ve)
                    bwait(bk)
                    te.transpose(pb[bk][:, 0:16],
                                 sgstage[:16, lc * 128:(lc + 1) * 128],
                                 identf[:16, :16]).then_inc(sMM.h, 1)
                    wm = sMM.inc(1)
                    ve.wait_ge(sMM.h, wm)
                    ve.tensor_copy(bstage[:, lc * 16:(lc + 1) * 16],
                                   pb[bk][:, 0:16]).then_inc(sVE.h, 1)
                    bank_free[bk] = sVE.inc(1)
                gp.wait_ge(sVE.h, sVE.n)
                for lc in range(8):
                    gp.dma_start(beta_s[cc1 * 8 + lc],
                                 bstage[:, lc * 16:(lc + 1) * 16])\
                        .then_inc(sGD.h, 16)
                    sGD.inc(16)
                beta_gd = sGD.n

                t2_ve = small_proj(wg1res, t12buf)
                for ttile in range(8):
                    evs = []
                    for jb in range(4):
                        bk = 6 + (jb % 2)
                        te.wait_ge(sVE.h, t2_ve)
                        bwait(bk)
                        mi = te.matmul(pb[bk][:, 0:512],
                                       t12buf[:, ttile * 128:(ttile + 1) * 128],
                                       wg2res[:, jb * 512:(jb + 1) * 512],
                                       start=True, stop=True)
                        mi.then_inc(sMM.h, 1)
                        wm = sMM.inc(1)
                        ve.wait_ge(sMM.h, wm)
                        if ttile == 0 and jb == 0 and beta_gd:
                            ve.wait_ge(sGD.h, beta_gd)
                        if sg_free:
                            ve.wait_ge(sGD.h, sg_free)
                        ve.tensor_tensor(sgstage[:, jb * 512:(jb + 1) * 512],
                                         pb[bk][:, 0:512],
                                         arRep[:, jb * 512:(jb + 1) * 512],
                                         ALU.add).then_inc(sVE.h, 1)
                        evs.append(sVE.inc(1))
                        bank_free[bk] = evs[-1]
                    ac.wait_ge(sVE.h, evs[-1])
                    ac.activation(sgstage[:, 0:2048], sgstage[:, 0:2048],
                                  AF.Tanh, scale=0.5).then_inc(sAC.h, 1)
                    sg_ac = sAC.inc(1)
                    ve.wait_ge(sAC.h, sg_ac)
                    ve.tensor_scalar(sgstage[:, 0:2048], sgstage[:, 0:2048],
                                     0.5, 0.5, ALU.mult, ALU.add)\
                        .then_inc(sVE.h, 1)
                    sg_ve = sVE.inc(1)
                    gp.wait_ge(sVE.h, sg_ve)
                    gp.dma_start(sg_s[cc1 * CT1 + ttile * 128:
                                      cc1 * CT1 + (ttile + 1) * 128, :],
                                 sgstage[:, 0:2048]).then_inc(sGD.h, 16)
                    sg_free = sGD.inc(16)

            s1_end = dict(MM=sMM.n, VE=sVE.n, AC=sAC.n, LD=sLD.n, GD=sGD.n)

            p.add("sync", lambda e, r=sy: r.replay(e))
            p.add("tensor", lambda e, r=te: r.replay(e))
            p.add("vector", lambda e, r=ve: r.replay(e))
            p.add("scalar", lambda e, r=ac: r.replay(e))
            p.add("gpsimd", lambda e, r=gp: r.replay(e))

        if stop_after not in ("s0", "s1"):
            sy2, te2, ve2, ac2, gp2 = Rec(), Rec(), Rec(), Rec(), Rec()
            C44 = 44.0

            SkA = arA[:, 0:2048]
            SvA = arA[:, 2048:4096]
            q_c = arA[:, 4096:6144]
            kfm_c = arA[:, 6144:8192]
            ktm_c = arA[:, 8192:10240]
            v_c = arA[:, 10240:12288]
            w_c = arA[:, 12288:14336]
            z_c = arA[:, 14336:16384]
            lam = arB[:, 0:2048]
            e1c = arB[:, 2048:4096]
            e2c = arB[:, 4096:6144]
            u2 = arB[:, 6144:8192]
            araw = arC[:, 0:2048]
            s_t = arC[:, 2048:4096]
            o_t = arC[:, 4096:6144]
            sgld = arS1[:, 0:2048]
            nsg = arS1[:, 2048:4096]
            bcol = arS1[:, 4096:4112]
            lfc = arS1[:, 4112:4128]
            lf44 = arS1[:, 4128:4144]
            nmx = arS1[:, 4144:4160]
            den = arS1[:, 4160:4176]
            rden = arS1[:, 4176:4192]
            ssq = arS1[:, 4192:4208]
            rst = arS1[:, 4208:4224]
            nwRep = arW[:, 0:2048]
            sktb = arW[:, 2048:2176]
            a1b = arW[:, 2176:2304]
            a2b = arW[:, 2304:2432]
            a1tb = arW[:, 2432:2560]
            a2tb = arW[:, 2560:2688]
            u2tb = arW[:, 2688:2816]
            atb = arW[:, 2816:2944]
            btb = arW[:, 2944:3072]
            skb = arW[:, 3072:3200]
            svb = arW[:, 3200:3328]

            sy2.wait_ge(sMM.h, sMM.n)
            sy2.wait_ge(sVE.h, sVE.n)
            sy2.wait_ge(sAC.h, sAC.n)
            sy2.wait_ge(sGD.h, sGD.n)

            sy2.dma_start(nwRep,
                          nw_in.rearrange("o (u m) -> o u m", u=1)
                          .broadcast_to([128, 16, 128])).then_inc(sLD.h, 16)
            sLD.inc(16)

            biasc = arW[:, 3328:3332]
            ve2.wait_ge(sVE.h, sVE.n)
            ve2.memset(SkA[:], 0.0)
            ve2.memset(SvA[:], 0.0)
            ve2.memset(biasc[:, 0:1], C44)
            ve2.memset(biasc[:, 1:2], -C44).then_inc(sVE.h, 1)
            sVE.inc(1)

            bank2_free = {i: None for i in range(8)}

            def bw2(i):
                if bank2_free[i]:
                    s_, n_ = bank2_free[i]
                    te2.wait_ge(s_.h, n_)

            yo_gd = 0

            for cc in range(NCC):
                sy2.wait_ge(sVE.h, sVE.n)
                sy2.wait_ge(sAC.h, sAC.n)
                sy2.wait_ge(sMM.h, sMM.n)

                def v3(buf, inner=128):
                    return buf.rearrange("p (a b) -> p a b", b=inner)
                sy2.dma_start(v3(q_c), q_fm_s[:, :, cc * 128:(cc + 1) * 128]
                              .rearrange("h p t -> p h t")).then_inc(sLD.h, 16)
                sy2.dma_start(v3(kfm_c), k_fm_s[:, :, cc * 128:(cc + 1) * 128]
                              .rearrange("h p t -> p h t")).then_inc(sLD.h, 16)
                sy2.dma_start(v3(ktm_c), k_tm_s[cc].rearrange("h t m -> t h m"))\
                    .then_inc(sLD.h, 16)
                sy2.dma_start(v3(v_c), v_tm_s[cc].rearrange("h t m -> t h m"))\
                    .then_inc(sLD.h, 16)
                sy2.dma_start(v3(w_c), w_tm_s[cc].rearrange("h t m -> t h m"))\
                    .then_inc(sLD.h, 16)
                sy2.dma_start(v3(z_c), z_tm_s[cc].rearrange("h t m -> t h m"))\
                    .then_inc(sLD.h, 16)
                sy2.dma_start(bcol, beta_s[cc]).then_inc(sLD.h, 16)
                sy2.dma_start(sgld, sg_s[cc * 128:(cc + 1) * 128, :])\
                    .then_inc(sLD.h, 16)
                sLD.inc(16 * 8)
                ld8 = sLD.n

                ac2.wait_ge(sLD.h, ld8)
                ac2.activation(araw, z_c, AF.Abs).then_inc(sAC.h, 1)
                aw = sAC.inc(1)
                ac2.wait_ge(sAC.h, aw)
                ac2.activation(araw, araw, AF.Exp, scale=-1.0).then_inc(sAC.h, 1)
                aw = sAC.inc(1)
                ac2.wait_ge(sAC.h, aw)
                ac2.activation(araw, araw, AF.Ln, bias=ones1c[:, 0:1])\
                    .then_inc(sAC.h, 1)
                aw = sAC.inc(1)
                ve2.wait_ge(sAC.h, aw)
                ve2.scalar_tensor_tensor(z_c, z_c, 0.0, araw,
                                         ALU.min, ALU.subtract).then_inc(sVE.h, 1)
                g_ve = sVE.inc(1)
                ve2.wait_ge(sLD.h, ld8)
                ve2.tensor_single_scalar(q_c, q_c, SCALE, ALU.mult)\
                    .then_inc(sVE.h, 1)
                q_ve = sVE.inc(1)
                ve2.tensor_tensor(nsg, sgld, nwRep, ALU.mult).then_inc(sVE.h, 1)
                nsg_ve = sVE.inc(1)

                exp_ac = []
                for jb in range(4):
                    bk = jb
                    te2.wait_ge(sVE.h, g_ve)
                    bw2(bk)
                    te2.matmul(pb[bk][:, 0:512], ltri[:],
                               z_c[:, jb * 512:(jb + 1) * 512],
                               start=True, stop=True).then_inc(sMM.h, 1)
                    wm = sMM.inc(1)
                    ac2.wait_ge(sMM.h, wm)
                    ac2.activation(lam[:, jb * 512:(jb + 1) * 512],
                                   pb[bk][:, 0:512], AF.Exp)
                    ac2.activation(e1c[:, jb * 512:(jb + 1) * 512],
                                   pb[bk][:, 0:512], AF.Exp, bias=biasc[:, 0:1])
                    ac2.activation(e2c[:, jb * 512:(jb + 1) * 512],
                                   pb[bk][:, 0:512], AF.Exp, scale=-1.0,
                                   bias=biasc[:, 1:2]).then_inc(sAC.h, 1)
                    exp_ac.append(sAC.inc(1))
                    bank2_free[bk] = (sAC, exp_ac[-1])


                bk = 4
                bw2(bk)
                te2.wait_ge(sVE.h, g_ve)
                for h in range(16):
                    mi = te2.matmul(pb[bk][:, h:h + 1],
                                    z_c[:, h * 128:(h + 1) * 128],
                                    ones1c[:, 0:1],
                                    start=True, stop=True)
                mi.then_inc(sMM.h, 1)
                wm = sMM.inc(1)
                ac2.wait_ge(sMM.h, wm)
                ac2.activation(lfc, pb[bk][:, 0:16], AF.Exp)
                ac2.activation(lf44, pb[bk][:, 0:16], AF.Exp,
                               bias=biasc[:, 0:1]).then_inc(sAC.h, 1)
                lf_ac = sAC.inc(1)
                bank2_free[bk] = (sAC, lf_ac)

                ve2.wait_ge(sLD.h, ld8)
                ve2.tensor_tensor(araw, w_c, w_c, ALU.mult).then_inc(sVE.h, 1)
                wsq_ve = sVE.inc(1)
                ve2.wait_ge(sVE.h, wsq_ve)
                ve2.tensor_reduce(ssq, araw.rearrange("p (h m) -> p h m", m=128),
                                  AX.X, ALU.add).then_inc(sVE.h, 1)
                wss_ve = sVE.inc(1)
                ve2.wait_ge(sVE.h, wss_ve)
                ve2.tensor_single_scalar(ssq, ssq, L2_EPS, ALU.add)\
                    .then_inc(sVE.h, 1)
                wse_ve = sVE.inc(1)
                wab = sAC.n
                ac2.wait_ge(sVE.h, wse_ve)
                ac2.activation(rst, ssq, AF.Ln).then_inc(sAC.h, 1)
                ac2.wait_ge(sAC.h, wab + 1)
                ac2.activation(rst, rst, AF.Exp, scale=-0.5).then_inc(sAC.h, 1)
                sAC.inc(2)
                wrst_ac = sAC.n
                ve2.wait_ge(sAC.h, wrst_ac)
                for h in range(16):
                    mi = ve2.tensor_single_scalar(
                        w_c[:, h * 128:(h + 1) * 128],
                        w_c[:, h * 128:(h + 1) * 128],
                        rst[:, h:h + 1], ALU.mult)
                mi.then_inc(sVE.h, 1)
                wnorm_ve = sVE.inc(1)

                ve2.wait_ge(sVE.h, wnorm_ve)
                ve2.wait_ge(sAC.h, exp_ac[-1])
                for h in range(16):
                    ve2.scalar_tensor_tensor(u2[:, h * 128:(h + 1) * 128],
                                             w_c[:, h * 128:(h + 1) * 128],
                                             bcol[:, h:h + 1],
                                             e2c[:, h * 128:(h + 1) * 128],
                                             ALU.mult, ALU.mult)
                ve2.engine_nop().then_inc(sVE.h, 1)
                u2_ve = sVE.inc(1)

                for h in range(16):
                    hs = slice(h * 128, (h + 1) * 128)
                    bk6 = 6
                    bw2(bk6)
                    if cc == 0 and h == 0:
                        te2.wait_ge(sVE.h, u2_ve)
                    te2.transpose(pb[bk6][:, 0:128], SkA[:, hs], identf[:])\
                        .then_inc(sMM.h, 1)
                    wm_skt = sMM.inc(1)
                    ac2.wait_ge(sMM.h, wm_skt)
                    ac2.activation(sktb, pb[bk6][:, 0:128], AF.Identity)\
                        .then_inc(sAC.h, 1)
                    skt_ac = sAC.inc(1)
                    bank2_free[bk6] = (sAC, skt_ac)

                    bk4 = 4
                    bw2(bk4)
                    te2.wait_ge(sVE.h, q_ve)
                    te2.matmul(pb[bk4][:, 0:128], kfm_c[:, hs], q_c[:, hs],
                               start=True, stop=True).then_inc(sMM.h, 1)
                    wm_at = sMM.inc(1)
                    ve2.wait_ge(sMM.h, wm_at)
                    ve2.tensor_tensor(atb, pb[bk4][:, 0:128], ltri[:],
                                      ALU.mult).then_inc(sVE.h, 1)
                    at_ve = sVE.inc(1)
                    bank2_free[bk4] = (sVE, at_ve)

                    bk7 = 7
                    bw2(bk7)
                    te2.wait_ge(sAC.h, skt_ac)
                    te2.matmul(pb[bk7][:, 0:128], q_c[:, hs], sktb,
                               start=True, stop=True).then_inc(sMM.h, 1)
                    wm_qs = sMM.inc(1)

                    bk5 = 5
                    bw2(bk5)
                    te2.wait_ge(sVE.h, max(at_ve, u2_ve))
                    te2.matmul(pb[bk5][:, 0:128], atb, u2[:, hs],
                               start=True, stop=True).then_inc(sMM.h, 1)
                    wm_in = sMM.inc(1)

                    ve2.wait_ge(sMM.h, wm_qs)
                    ve2.tensor_tensor(s_t[:, hs], lam[:, hs], pb[bk7][:, 0:128],
                                      ALU.mult).then_inc(sVE.h, 1)
                    sv1 = sVE.inc(1)
                    bank2_free[bk7] = (sVE, sv1)
                    ve2.wait_ge(sMM.h, wm_in)
                    ve2.tensor_tensor(araw[:, hs], e1c[:, hs], pb[bk5][:, 0:128],
                                      ALU.mult).then_inc(sVE.h, 1)
                    sv2 = sVE.inc(1)
                    bank2_free[bk5] = (sVE, sv2)
                    ve2.tensor_tensor(s_t[:, hs], s_t[:, hs], araw[:, hs],
                                      ALU.add).then_inc(sVE.h, 1)
                    sVE.inc(1)

                sm0 = sVE.n
                ve2.tensor_reduce(nmx, s_t.rearrange("p (h m) -> p h m", m=128),
                                  AX.X, ALU.max, negate=True).then_inc(sVE.h, 1)
                nmx_ve = sVE.inc(1)
                ac2.wait_ge(sVE.h, nmx_ve)
                for h in range(16):
                    hs = slice(h * 128, (h + 1) * 128)
                    mi = ac2.activation(araw[:, hs], s_t[:, hs], AF.Exp,
                                        bias=nmx[:, h:h + 1],
                                        accum_out=den[:, h:h + 1])
                mi.then_inc(sAC.h, 1)
                sm_ac = sAC.inc(1)
                ve2.wait_ge(sAC.h, sm_ac)
                ve2.reciprocal(rden, den).then_inc(sVE.h, 1)
                rden_ve = sVE.inc(1)

                for h in range(16):
                    hs = slice(h * 128, (h + 1) * 128)
                    ve2.wait_ge(sVE.h, rden_ve)
                    ve2.tensor_tensor(a1b, araw[:, hs], lam[:, hs], ALU.mult)
                    ve2.tensor_tensor(a2b, araw[:, hs], e1c[:, hs], ALU.mult)\
                        .then_inc(sVE.h, 1)
                    ab_ve = sVE.inc(1)

                    tps = []
                    for bsrc, bdst, bk in ((a1b, a1tb, 4), (a2b, a2tb, 5),
                                           (u2[:, hs], u2tb, 6)):
                        bw2(bk)
                        te2.wait_ge(sVE.h, ab_ve)
                        te2.transpose(pb[bk][:, 0:128], bsrc, identf[:])\
                            .then_inc(sMM.h, 1)
                        wm = sMM.inc(1)
                        ve2.wait_ge(sMM.h, wm)
                        ve2.tensor_copy(bdst, pb[bk][:, 0:128]).then_inc(sVE.h, 1)
                        bank2_free[bk] = (sVE, sVE.inc(1))
                        tps.append(bank2_free[bk][1])

                    bk4 = 4
                    bw2(bk4)
                    te2.wait_ge(sVE.h, tps[-1])
                    te2.matmul(pb[bk4][0:64, 0:64], u2tb[:, 0:64],
                               a2tb[:, 0:64], start=True, stop=True)
                    te2.matmul(pb[bk4][0:64, 64:128], u2tb[:, 0:64],
                               a2tb[:, 64:128], start=True, stop=True)
                    te2.matmul(pb[bk4][64:128, 64:128], u2tb[:, 64:128],
                               a2tb[:, 64:128], start=True, stop=True)\
                        .then_inc(sMM.h, 1)
                    wm_bt = sMM.inc(1)
                    ve2.wait_ge(sMM.h, wm_bt)
                    ve2.memset(btb[64:128, 0:64], 0.0)
                    ve2.tensor_tensor(btb[0:64, 0:64], pb[bk4][0:64, 0:64],
                                      ltri[0:64, 0:64], ALU.mult)
                    ve2.tensor_copy(btb[0:64, 64:128], pb[bk4][0:64, 64:128])
                    ve2.tensor_tensor(btb[64:128, 64:128],
                                      pb[bk4][64:128, 64:128],
                                      ltri[0:64, 0:64], ALU.mult)\
                        .then_inc(sVE.h, 1)
                    bt_ve = sVE.inc(1)
                    bank2_free[bk4] = (sVE, bt_ve)

                    bk7 = 7
                    bw2(bk7)
                    te2.matmul(pb[bk7][:, 0:128], a1tb, SvA[:, hs],
                               start=True, stop=False)
                    te2.wait_ge(sVE.h, bt_ve)
                    te2.matmul(pb[bk7][:, 0:128], btb, v_c[:, hs],
                               start=False, stop=True).then_inc(sMM.h, 1)
                    wm_o = sMM.inc(1)
                    ve2.wait_ge(sMM.h, wm_o)
                    ve2.tensor_scalar(o_t[:, hs], pb[bk7][:, 0:128],
                                      rden[:, h:h + 1], None, ALU.mult)\
                        .then_inc(sVE.h, 1)
                    bank2_free[bk7] = (sVE, sVE.inc(1))

                    bk5, bk6 = 5, 6
                    bw2(bk5)
                    te2.matmul(pb[bk5][:, 0:128], u2[:, hs], ktm_c[:, hs],
                               start=True, stop=True).then_inc(sMM.h, 1)
                    wm_uk = sMM.inc(1)
                    bw2(bk6)
                    te2.matmul(pb[bk6][:, 0:128], u2[:, hs], v_c[:, hs],
                               start=True, stop=True).then_inc(sMM.h, 1)
                    wm_uv = sMM.inc(1)
                    ac2.wait_ge(sAC.h, lf_ac)
                    ac2.activation(skb, SkA[:, hs], AF.Identity,
                                   scale=lfc[:, h:h + 1])
                    ac2.activation(svb, SvA[:, hs], AF.Identity,
                                   scale=lfc[:, h:h + 1]).then_inc(sAC.h, 1)
                    sc_ac = sAC.inc(1)
                    ve2.wait_ge(sMM.h, wm_uk)
                    ve2.wait_ge(sAC.h, sc_ac)
                    ve2.scalar_tensor_tensor(SkA[:, hs], pb[bk5][:, 0:128],
                                             lf44[:, h:h + 1], skb,
                                             ALU.mult, ALU.add).then_inc(sVE.h, 1)
                    bank2_free[bk5] = (sVE, sVE.inc(1))
                    ve2.wait_ge(sMM.h, wm_uv)
                    ve2.scalar_tensor_tensor(SvA[:, hs], pb[bk6][:, 0:128],
                                             lf44[:, h:h + 1], svb,
                                             ALU.mult, ALU.add).then_inc(sVE.h, 1)
                    bank2_free[bk6] = (sVE, sVE.inc(1))

                ve2.tensor_tensor(araw, o_t, o_t, ALU.mult).then_inc(sVE.h, 1)
                sq_ve = sVE.inc(1)
                ve2.wait_ge(sVE.h, sq_ve)
                ve2.tensor_reduce(ssq, araw.rearrange("p (h m) -> p h m", m=128),
                                  AX.X, ALU.add).then_inc(sVE.h, 1)
                ssq_ve = sVE.inc(1)
                ve2.wait_ge(sVE.h, ssq_ve)
                ve2.tensor_scalar(ssq, ssq, 1.0 / 128.0, LN_EPS, ALU.mult,
                                  ALU.add).then_inc(sVE.h, 1)
                ssq2_ve = sVE.inc(1)
                ab_ = sAC.n
                ac2.wait_ge(sVE.h, ssq2_ve)
                ac2.activation(rst, ssq, AF.Ln).then_inc(sAC.h, 1)
                ac2.wait_ge(sAC.h, ab_ + 1)
                ac2.activation(rst, rst, AF.Exp, scale=-0.5).then_inc(sAC.h, 1)
                sAC.inc(2)
                rst_ac = sAC.n
                for h in range(16):
                    hs = slice(h * 128, (h + 1) * 128)
                    if h == 0:
                        ve2.wait_ge(sAC.h, rst_ac)
                        ve2.wait_ge(sVE.h, nsg_ve)
                    ve2.scalar_tensor_tensor(o_t[:, hs], o_t[:, hs],
                                             rst[:, h:h + 1], nsg[:, hs],
                                             ALU.mult, ALU.mult)
                ve2.engine_nop().then_inc(sVE.h, 1)
                yo_ve = sVE.inc(1)

                gp2.wait_ge(sVE.h, yo_ve)
                gp2.dma_start(yo_s[:, cc * 128:(cc + 1) * 128, :]
                              .rearrange("h t m -> t h m"),
                              o_t.rearrange("p (h m) -> p h m", m=128))\
                    .then_inc(sGD.h, 16)
                yo_gd = sGD.inc(16)

            p.add("sync", lambda e, r=sy2: r.replay(e))
            p.add("tensor", lambda e, r=te2: r.replay(e))
            p.add("vector", lambda e, r=ve2: r.replay(e))
            p.add("scalar", lambda e, r=ac2: r.replay(e))
            p.add("gpsimd", lambda e, r=gp2: r.replay(e))

        if stop_after not in ("s0", "s1", "s2"):
            sy3, te3, ve3, ac3, gp3 = Rec(), Rec(), Rec(), Rec(), Rec()
            sy3.wait_ge(sMM.h, sMM.n)
            sy3.wait_ge(sVE.h, sVE.n)
            sy3.wait_ge(sAC.h, sAC.n)
            sy3.wait_ge(sGD.h, sGD.n)

            y_fm = arA[:, 0:8192]
            ybuf3 = arA[:, 8192:10240]
            x2acc = arB[:, 0:8192]
            afm = arC[:, 0:512]
            gaRep = arRep[:, 0:2048]
            wpan3 = [arW[:, 0:2048], arW[:, 2048:4096]]

            sy3.dma_start(gaRep, mod_s[2:3, :].broadcast_to([128, D]))\
                .then_inc(sLD.h, 16)
            sLD.inc(16)
            ga_ld = sLD.n

            bank3 = {i: None for i in range(8)}

            def bw3(i):
                if bank3[i]:
                    s_, n_ = bank3[i]
                    te3.wait_ge(s_.h, n_)

            pan3_mm = {}
            x2st_gd = 0
            for gg in range(8):
                sy3.wait_ge(sVE.h, sVE.n)
                sy3.wait_ge(sMM.h, sMM.n)
                if x2st_gd:
                    sy3.wait_ge(sGD.h, x2st_gd)
                for t4 in range(4):
                    sy3.dma_start(x2acc[:, t4 * 2048:(t4 + 1) * 2048],
                                  x_in[gg * 512 + t4 * 128:
                                       gg * 512 + (t4 + 1) * 128, :])\
                        .then_inc(sLD.h, 16)
                    sLD.inc(16)
                    sy3.wait_ge(sMM.h, sMM.n)
                    sy3.dma_start(
                        ybuf3.rearrange("p (h m) -> p h m", m=128),
                        yo_s[:, gg * 512 + t4 * 128:gg * 512 + (t4 + 1) * 128, :]
                        .rearrange("h t m -> t h m")).then_inc(sLD.h, 16)
                    yld = sLD.inc(16)
                    for dc in range(16):
                        bk = dc % 2
                        te3.wait_ge(sLD.h, yld)
                        bw3(bk)
                        te3.matmul(pb[bk][:, 0:128],
                                   ybuf3[:, dc * 128:(dc + 1) * 128],
                                   identf[:], is_transpose=True,
                                   start=True, stop=True).then_inc(sMM.h, 1)
                        wm = sMM.inc(1)
                        ve3.wait_ge(sMM.h, wm)
                        ve3.tensor_copy(y_fm[:, dc * 512 + t4 * 128:
                                             dc * 512 + (t4 + 1) * 128],
                                        pb[bk][:, 0:128]).then_inc(sVE.h, 1)
                        bank3[bk] = (sVE, sVE.inc(1))
                yfm_ve = sVE.n

                for jc in range(16):
                    slot = jc % 2
                    if pan3_mm.get(slot):
                        sy3.wait_ge(sMM.h, pan3_mm[slot])
                    sy3.dma_start(wpan3[slot],
                                  wo_t[jc].rearrange("p dc j -> p (dc j)"))\
                        .then_inc(sLD.h, 16)
                    wld = sLD.inc(16)

                    bk = 4
                    te3.wait_ge(sLD.h, wld)
                    if jc == 0:
                        te3.wait_ge(sVE.h, yfm_ve)
                        te3.wait_ge(sLD.h, ga_ld)
                    bw3(bk)
                    for dc in range(16):
                        mi = te3.matmul(pb[bk][:, 0:512],
                                        wpan3[slot][:, dc * 128:(dc + 1) * 128],
                                        y_fm[:, dc * 512:(dc + 1) * 512],
                                        start=(dc == 0), stop=(dc == 15))
                    mi.then_inc(sMM.h, 1)
                    wm = sMM.inc(1)
                    pan3_mm[slot] = wm
                    ve3.wait_ge(sMM.h, wm)
                    ve3.tensor_copy(afm, pb[bk][:, 0:512]).then_inc(sVE.h, 1)
                    afm_ve = sVE.inc(1)
                    bank3[bk] = (sVE, afm_ve)

                    for t4 in range(4):
                        bk2 = 5 + (t4 % 2)
                        te3.wait_ge(sVE.h, afm_ve)
                        bw3(bk2)
                        te3.matmul(pb[bk2][:, 0:128],
                                   afm[:, t4 * 128:(t4 + 1) * 128],
                                   identf[:], is_transpose=True,
                                   start=True, stop=True).then_inc(sMM.h, 1)
                        wm2 = sMM.inc(1)
                        ve3.wait_ge(sMM.h, wm2)
                        ve3.tensor_tensor(o_t[:, 0:128] if False else
                                          arC[:, 512:640],
                                          pb[bk2][:, 0:128],
                                          gaRep[:, jc * 128:(jc + 1) * 128],
                                          ALU.mult)
                        ve3.tensor_tensor(
                            x2acc[:, t4 * 2048 + jc * 128:
                                  t4 * 2048 + (jc + 1) * 128],
                            x2acc[:, t4 * 2048 + jc * 128:
                                  t4 * 2048 + (jc + 1) * 128],
                            arC[:, 512:640], ALU.add).then_inc(sVE.h, 1)
                        bank3[bk2] = (sVE, sVE.inc(1))

                gp3.wait_ge(sVE.h, sVE.n)
                for t4 in range(4):
                    gp3.dma_start(x2_s[gg * 512 + t4 * 128:
                                       gg * 512 + (t4 + 1) * 128, :],
                                  x2acc[:, t4 * 2048:(t4 + 1) * 2048])\
                        .then_inc(sGD.h, 16)
                    sGD.inc(16)
                x2st_gd = sGD.n

            p.add("sync", lambda e, r=sy3: r.replay(e))
            p.add("tensor", lambda e, r=te3: r.replay(e))
            p.add("vector", lambda e, r=ve3: r.replay(e))
            p.add("scalar", lambda e, r=ac3: r.replay(e))
            p.add("gpsimd", lambda e, r=gp3: r.replay(e))

        if stop_after is None:
            sy4, te4, ve4, ac4, gp4 = Rec(), Rec(), Rec(), Rec(), Rec()
            sy4.wait_ge(sMM.h, sMM.n)
            sy4.wait_ge(sVE.h, sVE.n)
            sy4.wait_ge(sAC.h, sAC.n)
            sy4.wait_ge(sGD.h, sGD.n)

            def _addr(tname):
                for a in nc.m.functions[0].allocations:
                    if not hasattr(a, "memorylocations"):
                        continue
                    ml = a.memorylocations[0]
                    if str(ml.name) == tname:
                        return ml.addr
                raise KeyError(tname)

            baseA = _addr("arA")
            def ov(name, shape, dtype, off):
                return nc.alloc_sbuf_tensor_at(name, shape, dtype,
                                               offset=baseA + off).ap()

            KB = 1024
            xq_fm = ov("s3_xq", [128, 16384], BF16, 0)
            ygbuf = ov("s3_yg", [128, 4096], F32, 32 * KB)
            rX0 = ov("s3_x2a", [128, 2048], F32, 48 * KB)
            rX1 = ov("s3_x2b", [128, 2048], F32, 56 * KB)
            rXn = ov("s3_xn", [128, 2048], F32, 64 * KB)
            tacc = ov("s3_t", [128, 16384], F32, 72 * KB)
            dnwRep = ov("s3_dnw", [128, 5632], F32, 72 * KB)
            wbuf4 = ov("s3_wb", [128, 8192], BF16, 94 * KB)
            h3buf = ov("s3_h3", [128, 5632], F32, 158 * KB)
            h3bufB = ov("s3_h3B", [128, 5632], F32, 94 * KB)
            scrT = ov("s3_scr", [128, 176], F32, 152 * KB)
            sscr = scrT[:, 0:88]
            ascr2 = scrT[:, 88:176]
            xq2buf = ov("s3_xq2", [128, 5632], BF16, 32 * KB)
            xq2I = ov("s3_xq2I", [128, 11264], BF16, 0)
            dwbuf = ov("s3_dw", [128, 11264], BF16, 22 * KB)
            silb = [ov("s3_sil0", [128, 512], F32, 48 * KB),
                    ov("s3_sil1", [128, 512], F32, 50 * KB)]
            h3blk = [ov("s3_h3b0", [128, 512], F32, 52 * KB),
                     ov("s3_h3b1", [128, 512], F32, 54 * KB)]
            maskb = ov("s3_mask", [128, 5632], F32, 48 * KB)
            x2j = ov("s3_x2j", [128, 2048], F32, 48 * KB)
            outj = ov("s3_outj", [128, 2048], F32, 56 * KB)

            cb1R = ov("s3_cb1", [128, 2048], F32, 136 * KB)
            sbR = ov("s3_sb", [128, 2048], F32, 144 * KB)
            gnwR = ov("s3_gnw", [128, 2048], F32, 183 * KB)
            gbR = ov("s3_gb", [128, 2048], F32, 191 * KB)

            sy4.dma_start(cb1R, mod_s[4:5, :].broadcast_to([128, D]))\
                .then_inc(sLD.h, 16)
            sy4.dma_start(sbR, mod_s[3:4, :].broadcast_to([128, D]))\
                .then_inc(sLD.h, 16)
            sy4.dma_start(gnwR, gnw_in.broadcast_to([128, D])).then_inc(sLD.h, 16)
            sy4.dma_start(gbR, mod_s[5:6, :].broadcast_to([128, D]))\
                .then_inc(sLD.h, 16)
            sLD.inc(64)
            reps_ld = sLD.n
            ve4.wait_ge(sVE.h, sVE.n)
            ve4.wait_ge(sLD.h, reps_ld)
            ve4.tensor_single_scalar(cb1R, cb1R, 1.0, ALU.add).then_inc(sVE.h, 1)
            b0 = sVE.inc(1)
            ve4.wait_ge(sVE.h, b0)
            ve4.reciprocal(tiny[:, 40:41], tiny[:, 0:1])
            ve4.reciprocal(tiny[:, 41:42], tiny[:, 2:3]).then_inc(sVE.h, 1)
            sVE.inc(1)
            mh_ve = sVE.n

            bank4f = {i: None for i in range(8)}

            def bw4(i):
                if bank4f[i]:
                    s_, n_ = bank4f[i]
                    te4.wait_ge(s_.h, n_)

            LAD = [128.0, 64.0, 32.0, 16.0, 8.0, 4.0, 2.0, 1.0]

            def ladder(buf, n, scol):
                ve4.tensor_scalar(maskb[:, 0:n], buf, scol, 128.5,
                                  ALU.mult, ALU.add)
                ve4.tensor_copy(rXn[:, 0:n] if n <= 2048 else h3buf[:, 0:0],
                                maskb[:, 0:n]) if False else None
                for k in LAD:
                    ve4.tensor_single_scalar(buf, maskb[:, 0:n], k, ALU.is_ge)
                    ve4.scalar_tensor_tensor(maskb[:, 0:n], buf, -k,
                                             maskb[:, 0:n], ALU.mult, ALU.add)
                raise RuntimeError("ladder needs orig")

            def ladder2(buf, n, scol, ve_sem):
                ve4.tensor_scalar(maskb[:, 0:n], buf, scol, 128.5,
                                  ALU.mult, ALU.add)
                ve4.tensor_copy(buf, maskb[:, 0:n]).then_inc(sVE.h, 1)
                ve_sem[0] = sVE.inc(1)
                for k in LAD:
                    ve4.tensor_single_scalar(maskb[:, 0:n], buf, k, ALU.is_ge)
                    ve4.scalar_tensor_tensor(buf, maskb[:, 0:n], -k, buf,
                                             ALU.mult, ALU.add)
                raise RuntimeError("unused")

            def ladder3(src, dst, n, scol):
                ve4.tensor_scalar(maskb[:, 0:n], src, scol, 128.5,
                                  ALU.mult, ALU.add)
                for k in LAD:
                    ve4.tensor_single_scalar(dst, maskb[:, 0:n], k, ALU.is_ge)
                    ve4.scalar_tensor_tensor(maskb[:, 0:n], dst, -k,
                                             maskb[:, 0:n], ALU.mult, ALU.add)
                ve4.tensor_scalar(dst, src, scol, 0.5, ALU.mult, ALU.add)
                mi = ve4.tensor_tensor(dst, dst, maskb[:, 0:n], ALU.subtract)
                return mi

            out_gd = 0
            for cc3 in range(4):
                sy4.wait_ge(sVE.h, sVE.n)
                sy4.wait_ge(sAC.h, sAC.n)
                sy4.wait_ge(sMM.h, sMM.n)
                sy4.wait_ge(sGD.h, sGD.n)

                ld_t, ve_t, ac_t = [], [], []
                for tt in range(8):
                    if tt >= 2:
                        sy4.wait_ge(sVE.h, ve_t[tt - 2])
                        sy4.wait_ge(sAC.h, ac_t[tt - 2])
                    buf = (rX0, rX1)[tt % 2]
                    sy4.dma_start(buf, x2_s[cc3 * 1024 + tt * 128:
                                            cc3 * 1024 + (tt + 1) * 128, :])\
                        .then_inc(sLD.h, 16)
                    ld_t.append(sLD.inc(16))
                    ve4.wait_ge(sLD.h, ld_t[tt])
                    ve4.tensor_reduce(stat[:, 16 + tt:17 + tt], buf,
                                      AX.X, ALU.add).then_inc(sVE.h, 1)
                    ve_t.append(sVE.inc(1))
                    ac4.wait_ge(sLD.h, ld_t[tt])
                    ac4.activation(rXn, buf, AF.Square,
                                   accum_out=stat[:, 24 + tt:25 + tt])\
                        .then_inc(sAC.h, 1)
                    ac_t.append(sAC.inc(1))

                b = sVE.n
                ve4.wait_ge(sVE.h, ve_t[-1])
                ve4.wait_ge(sAC.h, ac_t[-1])
                ve4.tensor_single_scalar(stat[:, 32:40], stat[:, 16:24],
                                         1.0 / D, ALU.mult)
                ve4.tensor_single_scalar(stat[:, 40:48], stat[:, 24:32],
                                         1.0 / D, ALU.mult).then_inc(sVE.h, 1)
                ve4.wait_ge(sVE.h, b + 1)
                ve4.tensor_tensor(stat[:, 48:56], stat[:, 32:40],
                                  stat[:, 32:40], ALU.mult).then_inc(sVE.h, 1)
                ve4.wait_ge(sVE.h, b + 2)
                ve4.scalar_tensor_tensor(stat[:, 48:56], stat[:, 40:48],
                                         LN_EPS, stat[:, 48:56],
                                         ALU.add, ALU.subtract).then_inc(sVE.h, 1)
                sVE.inc(3)
                p2v = sVE.n
                ab = sAC.n
                ac4.wait_ge(sVE.h, p2v)
                ac4.activation(stat[:, 56:64], stat[:, 48:56], AF.Ln)\
                    .then_inc(sAC.h, 1)
                ac4.wait_ge(sAC.h, ab + 1)
                ac4.activation(stat[:, 48:56], stat[:, 56:64], AF.Exp,
                               scale=-0.5).then_inc(sAC.h, 1)
                sAC.inc(2)
                p2a = sAC.n
                b = sVE.n
                ve4.wait_ge(sAC.h, p2a)
                ve4.tensor_tensor(stat[:, 56:64], stat[:, 32:40],
                                  stat[:, 48:56], ALU.mult).then_inc(sVE.h, 1)
                ve4.wait_ge(sVE.h, b + 1)
                ve4.tensor_single_scalar(stat[:, 56:64], stat[:, 56:64],
                                         -1.0, ALU.mult).then_inc(sVE.h, 1)
                sVE.inc(2)
                rstd1_ve = sVE.n

                for tt in range(8):
                    buf = (rX0, rX1)[tt % 2]
                    sy4.wait_ge(sVE.h, sVE.n)
                    sy4.dma_start(buf, x2_s[cc3 * 1024 + tt * 128:
                                            cc3 * 1024 + (tt + 1) * 128, :])\
                        .then_inc(sLD.h, 16)
                    w = sLD.inc(16)
                    ac4.wait_ge(sLD.h, w)
                    ac4.wait_ge(sVE.h, rstd1_ve)
                    ac4.activation(rXn, buf, AF.Identity,
                                   bias=stat[:, 56 + tt:57 + tt],
                                   scale=stat[:, 48 + tt:49 + tt])\
                        .then_inc(sAC.h, 1)
                    wa = sAC.inc(1)
                    ve4.wait_ge(sAC.h, wa)
                    ve4.tensor_tensor(buf, rXn, cb1R, ALU.mult)
                    ve4.tensor_tensor(buf, buf, sbR, ALU.add).then_inc(sVE.h, 1)
                    h2_ve = sVE.inc(1)
                    ac4.wait_ge(sVE.h, h2_ve)
                    ac4.activation(rXn, buf, AF.Square,
                                   accum_out=stat[:, 24 + tt:25 + tt])\
                        .then_inc(sAC.h, 1)
                    sAC.inc(1)
                    ve4.tensor_tensor(tacc[:, tt * 2048:(tt + 1) * 2048],
                                      buf, gnwR, ALU.mult).then_inc(sVE.h, 1)
                    t_ve = sVE.inc(1)
                    ve4.wait_ge(sVE.h, t_ve)
                    ve4.tensor_reduce(stat[:, 16 + tt:17 + tt],
                                      tacc[:, tt * 2048:(tt + 1) * 2048],
                                      AX.X, ALU.max, apply_absolute_value=True)\
                        .then_inc(sVE.h, 1)
                    sVE.inc(1)
                c_done_ve = sVE.n
                c_done_ac = sAC.n

                b = sVE.n
                ve4.wait_ge(sVE.h, c_done_ve)
                ve4.wait_ge(sAC.h, c_done_ac)
                ve4.tensor_single_scalar(stat[:, 40:48], stat[:, 24:32],
                                         1.0 / D, ALU.mult).then_inc(sVE.h, 1)
                ve4.wait_ge(sVE.h, b + 1)
                ve4.tensor_single_scalar(stat[:, 40:48], stat[:, 40:48],
                                         LN_EPS, ALU.add).then_inc(sVE.h, 1)
                sVE.inc(2)
                dv = sVE.n
                ab = sAC.n
                ac4.wait_ge(sVE.h, dv)
                ac4.activation(stat[:, 48:56], stat[:, 40:48], AF.Ln)\
                    .then_inc(sAC.h, 1)
                ac4.wait_ge(sAC.h, ab + 1)
                ac4.activation(stat[:, 40:48], stat[:, 48:56], AF.Exp,
                               scale=-0.5).then_inc(sAC.h, 1)
                sAC.inc(2)
                da = sAC.n
                b = sVE.n
                ve4.wait_ge(sAC.h, da)
                ve4.tensor_tensor(stat[:, 32:40], stat[:, 16:24],
                                  stat[:, 40:48], ALU.mult).then_inc(sVE.h, 1)
                ve4.wait_ge(sVE.h, b + 1)
                ve4.tensor_single_scalar(stat[:, 32:40], stat[:, 32:40],
                                         1e-5, ALU.max).then_inc(sVE.h, 1)
                ve4.wait_ge(sVE.h, b + 2)
                ve4.reciprocal(stat[:, 48:56], stat[:, 32:40]).then_inc(sVE.h, 1)
                ve4.wait_ge(sVE.h, b + 3)
                ve4.tensor_tensor(stat[:, 56:64], stat[:, 40:48],
                                  stat[:, 48:56], ALU.mult).then_inc(sVE.h, 1)
                ve4.wait_ge(sVE.h, b + 4)
                ve4.tensor_single_scalar(stat[:, 56:64], stat[:, 56:64],
                                         127.0, ALU.mult)
                ve4.wait_ge(sVE.h, mh_ve)
                ve4.tensor_single_scalar(stat[:, 32:40], stat[:, 32:40],
                                         1.0 / 127.0, ALU.mult).then_inc(sVE.h, 1)
                ve4.wait_ge(sVE.h, b + 5)
                for tt in range(8):
                    mi = ve4.tensor_single_scalar(
                        stat[:, 32 + tt:33 + tt], stat[:, 32 + tt:33 + tt],
                        tiny[:, 40:41], ALU.mult)
                mi.then_inc(sVE.h, 1)
                sVE.inc(6)
                d_done = sVE.n

                for tt in range(8):
                    ve4.wait_ge(sVE.h, d_done)
                    ve4.wait_ge(sMM.h, sMM.n)
                    mi = ladder3(tacc[:, tt * 2048:(tt + 1) * 2048],
                                 rXn, 2048, stat[:, 56 + tt:57 + tt])
                    mi.then_inc(sVE.h, 1)
                    lv = sVE.inc(1)
                    for dc in range(16):
                        bk = dc % 2
                        te4.wait_ge(sVE.h, lv)
                        bw4(bk)
                        te4.matmul(pb[bk][:, 0:128],
                                   rXn[:, dc * 128:(dc + 1) * 128],
                                   identf[:], is_transpose=True,
                                   start=True, stop=True).then_inc(sMM.h, 1)
                        wm = sMM.inc(1)
                        ac4.wait_ge(sMM.h, wm)
                        ac4.activation(
                            xq_fm[:, dc * 1024 + tt * 128:
                                  dc * 1024 + (tt + 1) * 128],
                            pb[bk][:, 0:128], AF.Identity).then_inc(sAC.h, 1)
                        bank4f[bk] = (sAC, sAC.inc(1))
                xq_ac = sAC.n

                sy4.wait_ge(sMM.h, sMM.n)
                sy4.dma_start(dnwRep, dnw_in.broadcast_to([128, INTER]))\
                    .then_inc(sLD.h, 16)
                dnw_ld = sLD.inc(16)
                ve4.wait_ge(sLD.h, dnw_ld)
                h3_spill_gd = {0: 0, 1: 0}
                wld_prev = 0
                for jp in range(11):
                    sy4.wait_ge(sMM.h, sMM.n)
                    for b_ in range(4):
                        sy4.dma_start(
                            wbuf4.rearrange("p (dc bj) -> p dc bj", dc=16)
                            [:, :, b_ * 128:(b_ + 1) * 128],
                            gwqT_s[:, jp * 4 + b_]
                            .rearrange("dc p j -> p dc j")).then_inc(sLD.h, 16)
                        sLD.inc(16)
                    wld = sLD.n
                    for tt in range(8):
                        bk = tt % 2
                        te4.wait_ge(sLD.h, wld)
                        if jp == 0 and tt == 0:
                            te4.wait_ge(sAC.h, xq_ac)
                        bw4(bk)
                        for dc in range(16):
                            mi = te4.matmul(
                                pb[bk][:, 0:512],
                                xq_fm[:, dc * 1024 + tt * 128:
                                      dc * 1024 + (tt + 1) * 128],
                                wbuf4[:, dc * 512:(dc + 1) * 512],
                                start=(dc == 0), stop=(dc == 15))
                        mi.then_inc(sMM.h, 1)
                        wm = sMM.inc(1)
                        ac4.wait_ge(sMM.h, wm)
                        ac4.activation(ygbuf[:, tt * 512:(tt + 1) * 512],
                                       pb[bk][:, 0:512], AF.Identity,
                                       scale=stat[:, 32 + tt:33 + tt])\
                            .then_inc(sAC.h, 1)
                        bank4f[bk] = (sAC, sAC.inc(1))
                    sy4.wait_ge(sMM.h, sMM.n)
                    for b_ in range(4):
                        sy4.dma_start(
                            wbuf4.rearrange("p (dc bj) -> p dc bj", dc=16)
                            [:, :, b_ * 128:(b_ + 1) * 128],
                            gwqT_s[:, 44 + jp * 4 + b_]
                            .rearrange("dc p j -> p dc j")).then_inc(sLD.h, 16)
                        sLD.inc(16)
                    wldu = sLD.n
                    for tt in range(8):
                        bk = 2 + tt % 2
                        te4.wait_ge(sLD.h, wldu)
                        bw4(bk)
                        for dc in range(16):
                            mi = te4.matmul(
                                pb[bk][:, 0:512],
                                xq_fm[:, dc * 1024 + tt * 128:
                                      dc * 1024 + (tt + 1) * 128],
                                wbuf4[:, dc * 512:(dc + 1) * 512],
                                start=(dc == 0), stop=(dc == 15))
                        mi.then_inc(sMM.h, 1)
                        wmu = sMM.inc(1)
                        slot = tt % 2
                        ac4.wait_ge(sGD.h, out_gd)
                        ac4.activation(silb[slot], ygbuf[:, tt * 512:(tt + 1) * 512],
                                       AF.Silu).then_inc(sAC.h, 1)
                        sil_ac = sAC.inc(1)
                        ve4.wait_ge(sMM.h, wmu)
                        ve4.wait_ge(sAC.h, sil_ac)
                        if h3_spill_gd[slot]:
                            ve4.wait_ge(sGD.h, h3_spill_gd[slot])
                        ve4.scalar_tensor_tensor(h3blk[slot], pb[bk][:, 0:512],
                                                 stat[:, 32 + tt:33 + tt],
                                                 silb[slot],
                                                 ALU.mult, ALU.mult)\
                            .then_inc(sVE.h, 1)
                        h3_ve = sVE.inc(1)
                        bank4f[bk] = (sVE, h3_ve)
                        ac4.wait_ge(sVE.h, h3_ve)
                        ac4.activation(silb[slot], h3blk[slot], AF.Square,
                                       accum_out=sscr[:, tt * 11 + jp:
                                                      tt * 11 + jp + 1])\
                            .then_inc(sAC.h, 1)
                        sq_ac = sAC.inc(1)
                        ve4.wait_ge(sAC.h, sq_ac)
                        ve4.tensor_tensor(h3blk[slot], h3blk[slot],
                                          dnwRep[:, jp * 512:(jp + 1) * 512],
                                          ALU.mult).then_inc(sVE.h, 1)
                        dnw_ve = sVE.inc(1)
                        ve4.wait_ge(sVE.h, dnw_ve)
                        ve4.tensor_reduce(ascr2[:, tt * 11 + jp:
                                                tt * 11 + jp + 1],
                                          h3blk[slot], AX.X, ALU.max,
                                          apply_absolute_value=True)\
                            .then_inc(sVE.h, 1)
                        sVE.inc(1)
                        gp4.wait_ge(sVE.h, dnw_ve)
                        gp4.dma_start(h3_s[(cc3 % 2) * 8 + tt, jp], h3blk[slot])\
                            .then_inc(sGD.h, 16)
                        out_gd = sGD.inc(16)
                        h3_spill_gd[slot] = out_gd
                f_ve, f_ac, f_gd = sVE.n, sAC.n, sGD.n

                b = sVE.n
                ve4.wait_ge(sVE.h, f_ve)
                ve4.wait_ge(sAC.h, f_ac)
                ve4.tensor_reduce(stat[:, 24:32],
                                  sscr.rearrange("p (t j) -> p t j", j=11),
                                  AX.X, ALU.add).then_inc(sVE.h, 1)
                ve4.wait_ge(sVE.h, b + 1)
                ve4.tensor_reduce(stat[:, 16:24],
                                  ascr2.rearrange("p (t j) -> p t j", j=11),
                                  AX.X, ALU.max).then_inc(sVE.h, 1)
                ve4.wait_ge(sVE.h, b + 2)
                ve4.tensor_scalar(stat[:, 40:48], stat[:, 24:32],
                                  1.0 / INTER, LN_EPS, ALU.mult, ALU.add)\
                    .then_inc(sVE.h, 1)
                sVE.inc(3)
                gv = sVE.n
                ab = sAC.n
                ac4.wait_ge(sVE.h, gv)
                ac4.activation(stat[:, 48:56], stat[:, 40:48], AF.Ln)\
                    .then_inc(sAC.h, 1)
                ac4.wait_ge(sAC.h, ab + 1)
                ac4.activation(stat[:, 40:48], stat[:, 48:56], AF.Exp,
                               scale=-0.5).then_inc(sAC.h, 1)
                sAC.inc(2)
                ga_ = sAC.n
                b = sVE.n
                ve4.wait_ge(sAC.h, ga_)
                ve4.tensor_tensor(stat[:, 32:40], stat[:, 16:24],
                                  stat[:, 40:48], ALU.mult).then_inc(sVE.h, 1)
                ve4.wait_ge(sVE.h, b + 1)
                ve4.tensor_single_scalar(stat[:, 32:40], stat[:, 32:40],
                                         1e-5, ALU.max).then_inc(sVE.h, 1)
                ve4.wait_ge(sVE.h, b + 2)
                ve4.reciprocal(stat[:, 48:56], stat[:, 32:40]).then_inc(sVE.h, 1)
                ve4.wait_ge(sVE.h, b + 3)
                ve4.tensor_tensor(stat[:, 56:64], stat[:, 40:48],
                                  stat[:, 48:56], ALU.mult).then_inc(sVE.h, 1)
                ve4.wait_ge(sVE.h, b + 4)
                ve4.tensor_single_scalar(stat[:, 56:64], stat[:, 56:64],
                                         127.0, ALU.mult)
                ve4.tensor_single_scalar(stat[:, 32:40], stat[:, 32:40],
                                         1.0 / 127.0, ALU.mult).then_inc(sVE.h, 1)
                ve4.wait_ge(sVE.h, b + 5)
                for tt in range(8):
                    mi = ve4.tensor_single_scalar(
                        stat[:, 32 + tt:33 + tt], stat[:, 32 + tt:33 + tt],
                        tiny[:, 41:42], ALU.mult)
                mi.then_inc(sVE.h, 1)
                sVE.inc(6)
                g_done = sVE.n

                xq2_gd = 0
                hslots = [h3buf, h3bufB]
                lad_done = [0, 0]
                sy4.wait_ge(sGD.h, f_gd)
                sy4.wait_ge(sVE.h, sVE.n)
                sy4.wait_ge(sMM.h, sMM.n)
                sy4.dma_start(hslots[0].rearrange("p (j f) -> p j f", f=512),
                              h3_s[(cc3 % 2) * 8]
                              .rearrange("j p f -> p j f")).then_inc(sLD.h, 16)
                h3_ld = [sLD.inc(16), 0]
                for tt in range(8):
                    if tt + 1 < 8:
                        nsl = (tt + 1) % 2
                        if lad_done[nsl]:
                            sy4.wait_ge(sVE.h, lad_done[nsl])
                        sy4.dma_start(
                            hslots[nsl].rearrange("p (j f) -> p j f", f=512),
                            h3_s[(cc3 % 2) * 8 + tt + 1]
                            .rearrange("j p f -> p j f")).then_inc(sLD.h, 16)
                        h3_ld[nsl] = sLD.inc(16)
                    ve4.wait_ge(sLD.h, h3_ld[tt % 2])
                    ve4.wait_ge(sMM.h, sMM.n)
                    mi = ladder3(hslots[tt % 2], tacc[:, 0:5632], 5632,
                                 stat[:, 56 + tt:57 + tt])
                    mi.then_inc(sVE.h, 1)
                    lv = sVE.inc(1)
                    lad_done[tt % 2] = lv
                    for jc in range(44):
                        bk = 4 + jc % 2
                        te4.wait_ge(sVE.h, lv)
                        bw4(bk)
                        te4.matmul(pb[bk][:, 0:128],
                                   tacc[:, jc * 128:(jc + 1) * 128],
                                   identf[:], is_transpose=True,
                                   start=True, stop=True).then_inc(sMM.h, 1)
                        wm = sMM.inc(1)
                        ac4.wait_ge(sMM.h, wm)
                        if jc == 0 and xq2_gd:
                            ac4.wait_ge(sGD.h, xq2_gd)
                        ac4.activation(xq2buf[:, jc * 128:(jc + 1) * 128],
                                       pb[bk][:, 0:128], AF.Identity)\
                            .then_inc(sAC.h, 1)
                        bank4f[bk] = (sAC, sAC.inc(1))
                    gp4.wait_ge(sAC.h, sAC.n)
                    gp4.dma_start(xq2_s[(cc3 % 2) * 8 + tt]
                                  .rearrange("jc p t -> p jc t"),
                                  xq2buf.rearrange("p (jc t) -> p jc t", t=128))\
                        .then_inc(sGD.h, 16)
                    xq2_gd = sGD.inc(16)

                for jcg in range(4):
                    sy4.wait_ge(sGD.h, xq2_gd)
                    if jcg > 0:
                        sy4.wait_ge(sMM.h, sMM.n)
                    for t_ in range(8):
                        sy4.dma_start(
                            xq2I.rearrange("p (t jcf) -> p t jcf", t=8)[:, t_],
                            xq2_s[(cc3 % 2) * 8 + t_, jcg * 11:(jcg + 1) * 11]
                            .rearrange("jc p f -> p jc f")).then_inc(sLD.h, 16)
                        sLD.inc(16)
                    xld = sLD.n
                    for dbh in range(2):
                        sy4.wait_ge(sMM.h, sMM.n)
                        for j_ in range(11):
                            sy4.dma_start(
                                dwbuf.rearrange("p (jc bf) -> p jc bf",
                                                jc=11)[:, j_],
                                dwqT_s[jcg * 11 + j_, dbh * 8:(dbh + 1) * 8]
                                .rearrange("b p f -> p b f")).then_inc(sLD.h, 16)
                            sLD.inc(16)
                        dld = sLD.n
                        for tt in range(8):
                            for db in range(2):
                                bk = (tt * 2 + db) % 8
                                te4.wait_ge(sLD.h, max(xld, dld))
                                bw4(bk)
                                for j_ in range(11):
                                    mi = te4.matmul(
                                        pb[bk][:, 0:512],
                                        xq2I[:, (tt * 11 + j_) * 128:
                                             (tt * 11 + j_ + 1) * 128],
                                        dwbuf[:, j_ * 1024 + db * 512:
                                              j_ * 1024 + (db + 1) * 512],
                                        start=(j_ == 0), stop=(j_ == 10))
                                mi.then_inc(sMM.h, 1)
                                wm = sMM.inc(1)
                                dsl = tacc[:, tt * 2048 + dbh * 1024 +
                                           db * 512:
                                           tt * 2048 + dbh * 1024 +
                                           (db + 1) * 512]
                                ve4.wait_ge(sMM.h, wm)
                                if jcg == 0:
                                    ve4.tensor_copy(dsl, pb[bk][:, 0:512])\
                                        .then_inc(sVE.h, 1)
                                else:
                                    ve4.tensor_tensor(dsl, dsl,
                                                      pb[bk][:, 0:512],
                                                      ALU.add).then_inc(sVE.h, 1)
                                bank4f[bk] = (sVE, sVE.inc(1))
                i_ve = sVE.n

                for tt in range(8):
                    sy4.wait_ge(sVE.h, sVE.n)
                    sy4.dma_start(x2j, x2_s[cc3 * 1024 + tt * 128:
                                            cc3 * 1024 + (tt + 1) * 128, :])\
                        .then_inc(sLD.h, 16)
                    w = sLD.inc(16)
                    ve4.wait_ge(sVE.h, i_ve)
                    ve4.wait_ge(sLD.h, w)
                    ve4.scalar_tensor_tensor(outj,
                                             tacc[:, tt * 2048:(tt + 1) * 2048],
                                             stat[:, 32 + tt:33 + tt],
                                             gbR, ALU.mult, ALU.mult)
                    ve4.tensor_tensor(outj, outj, x2j, ALU.add).then_inc(sVE.h, 1)
                    ov_ = sVE.inc(1)
                    sy4.wait_ge(sVE.h, ov_)
                    sy4.dma_start(out_hbm[cc3 * 1024 + tt * 128:
                                          cc3 * 1024 + (tt + 1) * 128, :],
                                  outj).then_inc(sST.h, 16)
                    sST.inc(16)

            def s3_fin(sync, st=sST.n):
                sync.wait_ge(sST.h, st)
            p.add("sync", lambda e, r=sy4: r.replay(e))
            p.add("tensor", lambda e, r=te4: r.replay(e))
            p.add("vector", lambda e, r=ve4: r.replay(e))
            p.add("scalar", lambda e, r=ac4: r.replay(e))
            p.add("gpsimd", lambda e, r=gp4: r.replay(e))
            p.add("sync", s3_fin)

        if debug:
            dbg_stat = nc.dram_tensor("dbg_stat", [128, 64], F32,
                                      kind="ExternalOutput").ap()
            dbg_tiny = nc.dram_tensor("dbg_tiny", [128, 64], F32,
                                      kind="ExternalOutput").ap()
            dbg_modc = nc.dram_tensor("dbg_modc", [128, 96], F32,
                                      kind="ExternalOutput").ap()
            dbg_modt = nc.dram_tensor("dbg_modt", [96, 128], F32,
                                      kind="ExternalOutput").ap()

            def dbg_dump(g, ve=sVE.n, mm=sMM.n):
                g.wait_ge(sVE.h, ve)
                g.wait_ge(sMM.h, mm)
                g.dma_start(dbg_stat, stat[:]).then_inc(sGD.h, 16)
                g.dma_start(dbg_tiny, tiny[:]).then_inc(sGD.h, 16)
                g.dma_start(dbg_modc, modcol[:]).then_inc(sGD.h, 16)
                g.dma_start(dbg_modt, arC[:96, 4352:4480]).then_inc(sGD.h, 16)
            p.add("gpsimd", dbg_dump)
            sGD.inc(64)

        if stop_after in ("s0", "s1", "s2", "s2b"):
            st_at = sST.inc(16 * NT)

            def fin_copy(sync, gd=sGD.n, ve=sVE.n, ac=sAC.n, mm=sMM.n):
                sync.wait_ge(sGD.h, gd)
                sync.wait_ge(sVE.h, ve)
                sync.wait_ge(sAC.h, ac)
                sync.wait_ge(sMM.h, mm)
                for tt in range(NT):
                    sync.dma_start(out_hbm[tt * 128:(tt + 1) * 128, :],
                                   x_in[tt * 128:(tt + 1) * 128, :])\
                        .then_inc(sST.h, 16)
                sync.wait_ge(sST.h, sST.n)
            p.add("sync", fin_copy)

        with nc.Block() as block:
            @block.sync
            def _(sync):
                for fn in p.ops["sync"]:
                    fn(sync)

            @block.tensor
            def _(t):
                for fn in p.ops["tensor"]:
                    fn(t)

            @block.vector
            def _(v):
                for fn in p.ops["vector"]:
                    fn(v)

            @block.scalar
            def _(a):
                for fn in p.ops["scalar"]:
                    fn(a)

            @block.gpsimd
            def _(g):
                for fn in p.ops["gpsimd"]:
                    fn(g)

    return nc



def kernel(**inputs):
    x = np.ascontiguousarray(inputs["L0_attn_hidden_states"], dtype=np.float32)
    B = x.shape[0]
    nc = build_program(debug=DEBUG, stop_after=STOP_AFTER)

    common = {
        "Wq": inputs["L0_attn_Wq"], "Wk": inputs["L0_attn_Wk"],
        "Wv": inputs["L0_attn_Wv"], "Ww": inputs["L0_attn_Ww"],
        "Wo": inputs["L0_attn_Wo"],
        "cq": inputs["L0_attn_cq"], "ck": inputs["L0_attn_ck"],
        "cv": inputs["L0_attn_cv"],
        "Wf1": inputs["L0_attn_Wf1"], "Wf2": inputs["L0_attn_Wf2"],
        "Wb": inputs["L0_attn_Wb"],
        "Wg1": inputs["L0_attn_Wg1"], "Wg2": inputs["L0_attn_Wg2"],
        "bg2": np.asarray(inputs["L0_attn_bg2"], dtype=np.float32).reshape(1, -1),
        "norm_w": np.asarray(inputs["L0_attn_norm_w"], dtype=np.float32).reshape(1, -1),
        "gate_w": inputs["L0_ffn_gate_w"], "down_w": inputs["L0_ffn_down_w"],
        "gate_norm_w": np.asarray(inputs["L0_ffn_gate_norm_w"],
                                  dtype=np.float32).reshape(1, -1),
        "down_norm_w": np.asarray(inputs["L0_ffn_down_norm_w"],
                                  dtype=np.float32).reshape(1, -1),
        "wmod": inputs["L0_wmod"],
    }
    common = {k: np.ascontiguousarray(v, dtype=np.float32) for k, v in common.items()}

    in_maps = []
    for b in range(B):
        m = dict(common)
        m["x"] = x[b]
        m["cvec"] = np.ascontiguousarray(
            np.asarray(inputs["L0_c"], dtype=np.float32)[b:b + 1])
        in_maps.append(m)

    res = run_bass_kernel_spmd(nc, in_maps, core_ids=list(range(B)))
    out = np.stack([np.asarray(res.results[b]["out"]) for b in range(B)], axis=0)
    return out


# version 09
# speedup vs baseline: 1.0524x; 1.0019x over previous
import numpy as np
import ml_dtypes
import concourse.bass as bass
import concourse.mybir as mybir
from concourse.bass_utils import run_bass_kernel_spmd

F32 = mybir.dt.float32
BF16 = mybir.dt.bfloat16
AF = mybir.ActivationFunctionType
ALU = mybir.AluOpType
AX = mybir.AxisListType

T, D, H, DK = 4096, 2048, 16, 128
NT = T // 128
CT1 = 1024
NC1 = T // CT1
CC = 128
NCC = T // CC
INTER = 5632
NJB = (2 * INTER) // 512
LN_EPS = 1e-5
L2_EPS = 1e-6
SCALE = float(DK) ** -0.5

DEBUG = False
STOP_AFTER = None



class Sem:
    def __init__(self, handle):
        self.h = handle
        self.n = 0

    def inc(self, k=1):
        self.n += k
        return self.n


class Prog:
    ENGS = ("sync", "tensor", "vector", "scalar", "gpsimd")

    def __init__(self):
        self.ops = {e: [] for e in self.ENGS}

    def add(self, eng, fn):
        self.ops[eng].append(fn)


def _ap(x):
    return x.ap() if hasattr(x, "ap") else x


class _Chain:
    __slots__ = ("entry",)

    def __init__(self, entry):
        self.entry = entry

    def __getattr__(self, name):
        def call(*a, **k):
            self.entry.append((name, a, k))
            return self
        return call


class Rec:

    def __init__(self):
        self.calls = []

    def __getattr__(self, name):
        if name.startswith("_"):
            raise AttributeError(name)

        def call(*a, **k):
            entry = [(name, a, k)]
            self.calls.append(entry)
            return _Chain(entry)
        return call

    def replay(self, eng):
        for entry in self.calls:
            obj = eng
            for (name, a, k) in entry:
                obj = getattr(obj, name)(*a, **k)



def build_program(debug=False, stop_after=None):
    nc = bass.Bass("TRN2", target_bir_lowering=False,
                   detect_race_conditions=False)

    scratch_kind = "ExternalOutput" if debug else "Internal"

    x_in = nc.dram_tensor("x", [T, D], F32, kind="ExternalInput").ap()
    wq_in = nc.dram_tensor("Wq", [D, D], F32, kind="ExternalInput").ap()
    wk_in = nc.dram_tensor("Wk", [D, D], F32, kind="ExternalInput").ap()
    wv_in = nc.dram_tensor("Wv", [D, D], F32, kind="ExternalInput").ap()
    ww_in = nc.dram_tensor("Ww", [D, D], F32, kind="ExternalInput").ap()
    wo_in = nc.dram_tensor("Wo", [D, D], F32, kind="ExternalInput").ap()
    cq_in = nc.dram_tensor("cq", [D, 4], F32, kind="ExternalInput").ap()
    ck_in = nc.dram_tensor("ck", [D, 4], F32, kind="ExternalInput").ap()
    cv_in = nc.dram_tensor("cv", [D, 4], F32, kind="ExternalInput").ap()
    wf1_in = nc.dram_tensor("Wf1", [D, 128], F32, kind="ExternalInput").ap()
    wf2_in = nc.dram_tensor("Wf2", [128, D], F32, kind="ExternalInput").ap()
    wb_in = nc.dram_tensor("Wb", [D, 16], F32, kind="ExternalInput").ap()
    wg1_in = nc.dram_tensor("Wg1", [D, 128], F32, kind="ExternalInput").ap()
    wg2_in = nc.dram_tensor("Wg2", [128, D], F32, kind="ExternalInput").ap()
    bg2_in = nc.dram_tensor("bg2", [1, D], F32, kind="ExternalInput").ap()
    nw_in = nc.dram_tensor("norm_w", [1, 128], F32, kind="ExternalInput").ap()
    gw_in = nc.dram_tensor("gate_w", [2 * INTER, D], F32, kind="ExternalInput").ap()
    dw_in = nc.dram_tensor("down_w", [D, INTER], F32, kind="ExternalInput").ap()
    gnw_in = nc.dram_tensor("gate_norm_w", [1, D], F32, kind="ExternalInput").ap()
    dnw_in = nc.dram_tensor("down_norm_w", [1, INTER], F32, kind="ExternalInput").ap()
    cvec_in = nc.dram_tensor("cvec", [1, D], F32, kind="ExternalInput").ap()
    wmod_in = nc.dram_tensor("wmod", [D, 6 * D], F32, kind="ExternalInput").ap()

    out_hbm = nc.dram_tensor("out", [T, D], F32, kind="ExternalOutput").ap()

    wq_t = nc.dram_tensor("wq_t", [16, 128, 16, 128], F32, kind=scratch_kind).ap()
    wk_t = nc.dram_tensor("wk_t", [16, 128, 16, 128], F32, kind="Internal").ap()
    wv_t = nc.dram_tensor("wv_t", [16, 128, 16, 128], F32, kind="Internal").ap()
    ww_t = nc.dram_tensor("ww_t", [16, 128, 16, 128], F32, kind="Internal").ap()
    wo_t = nc.dram_tensor("wo_t", [16, 128, 16, 128], F32, kind="Internal").ap()
    gwqT_s = nc.dram_tensor("gwqT_s", [16, 88, 128, 128], BF16, kind=scratch_kind).ap()
    dwqT_s = nc.dram_tensor("dwqT_s", [44, 16, 128, 128], BF16, kind=scratch_kind).ap()
    mod_s = nc.dram_tensor("mod_s", [6, D], F32, kind=scratch_kind).ap()
    q_fm_s = nc.dram_tensor("q_fm_s", [H, DK, T], F32, kind=scratch_kind).ap()
    k_fm_s = nc.dram_tensor("k_fm_s", [H, DK, T], F32, kind=scratch_kind).ap()
    k_tm_s = nc.dram_tensor("k_tm_s", [NCC, H, CC, DK], F32, kind=scratch_kind).ap()
    v_tm_s = nc.dram_tensor("v_tm_s", [NCC, H, CC, DK], F32, kind=scratch_kind).ap()
    w_tm_s = nc.dram_tensor("w_tm_s", [NCC, H, CC, DK], F32, kind=scratch_kind).ap()
    z_tm_s = nc.dram_tensor("z_tm_s", [NCC, H, CC, DK], F32, kind=scratch_kind).ap()
    beta_s = nc.dram_tensor("beta_s", [NCC, CC, H], F32, kind=scratch_kind).ap()
    sg_s = nc.dram_tensor("sg_s", [T, D], F32, kind=scratch_kind).ap()
    yo_s = nc.dram_tensor("yo_s", [H, T, DK], F32, kind=scratch_kind).ap()
    x2_s = nc.dram_tensor("x2_s", [T, D], F32, kind=scratch_kind).ap()
    h3_s = nc.dram_tensor("h3_s", [16, NJB // 2, 128, 512], F32, kind=scratch_kind).ap()
    xq2_s = nc.dram_tensor("xq2_s", [16, 44, 128, 128], BF16, kind="Internal").ap()

    ident_f32 = nc.inline_tensor(np.eye(128, dtype=np.float32), "ident_f32")
    ident_bf16 = nc.inline_tensor(np.eye(128, dtype=ml_dtypes.bfloat16), "ident_bf16")
    ltri_np = np.triu(np.ones((128, 128), dtype=np.float32))
    ltri_f32 = nc.inline_tensor(ltri_np, "ltri_f32")
    ones_1x128 = nc.inline_tensor(np.ones((1, 128), dtype=np.float32), "ones_1x128")
    ones_128x1 = nc.inline_tensor(np.ones((128, 1), dtype=np.float32), "ones_128x1")

    p = Prog()

    import contextlib
    stack = contextlib.ExitStack()

    def sem(name):
        return Sem(stack.enter_context(nc.semaphore(name)))

    def sb(name, shape, dtype):
        return stack.enter_context(nc.sbuf_tensor(name, shape, dtype))

    def psum(name, shape, dtype=F32):
        return stack.enter_context(nc.psum_tensor(name, shape, dtype))

    with stack:
        sLD = sem("sLD")
        sGD = sem("sGD")
        sMM = sem("sMM")
        sVE = sem("sVE")
        sAC = sem("sAC")
        sST = sem("sST")

        identf = sb("identf", [128, 128], F32)
        identb = sb("identb", [128, 128], BF16)
        ltri = sb("ltri", [128, 128], F32)
        ones1r = sb("ones1r", [1, 128], F32)
        ones1c = sb("ones1c", [128, 1], F32)
        tiny = sb("tiny", [128, 64], F32)
        modcol = sb("modcol", [128, 96], F32)

        arA = sb("arA", [128, 16384], F32)
        arB = sb("arB", [128, 8192], F32)
        arC = sb("arC", [128, 6144], F32)
        arRep = sb("arRep", [128, 6144], F32)
        arW = sb("arW", [128, 4096], F32)
        arBF = sb("arBF", [128, 11264], BF16)
        stat = sb("stat", [128, 64], F32)

        pb = [psum(f"pb{i}", [128, 512]) for i in range(8)]

        def s0_consts(sync):
            sync.dma_start(identf[:], ident_f32.ap()).then_inc(sLD.h, 16)
            sync.dma_start(identb[:], ident_bf16.ap()).then_inc(sLD.h, 16)
            sync.dma_start(ltri[:], ltri_f32.ap()).then_inc(sLD.h, 16)
            sync.dma_start(ones1r[:], ones_1x128.ap()).then_inc(sLD.h, 16)
            sync.dma_start(ones1c[:], ones_128x1.ap()).then_inc(sLD.h, 16)
        p.add("sync", s0_consts)
        sLD.inc(16 * 5)
        consts_ld = sLD.n

        relayout = [(wq_in, wq_t), (wk_in, wk_t), (wv_in, wv_t),
                    (ww_in, ww_t), (wo_in, wo_t)]
        wrow = [arW[:, 0:2048], arW[:, 2048:4096]]

        def emit_relayout_iter(wi, dc):
            src, dst = relayout[wi]
            buf = wrow[(wi * 16 + dc) % 2]
            ld_at = sLD.inc(16)

            def ld(sync, src=src, dc=dc, buf=buf, pr=sGD.n):
                sync.wait_ge(sGD.h, pr)
                sync.dma_start(buf, src[dc * 128:(dc + 1) * 128, :])\
                    .then_inc(sLD.h, 16)
            p.add("sync", ld)

            st_at = sGD.inc(16)

            def st(g, dst=dst, dc=dc, buf=buf, w=ld_at):
                g.wait_ge(sLD.h, w)
                g.dma_start(dst[:, :, dc, :].rearrange("jc p j -> p jc j"),
                            buf)\
                    .then_inc(sGD.h, 16)
            p.add("gpsimd", st)

        ascr = arC[:, 5888:5888 + 104]

        def s0qA_zero(v):
            v.memset(stat[:, 0:8], 0.0)
        p.add("vector", s0qA_zero)

        qbuf = [arA[:, 0:5632], arA[:, 5632:11264]]
        ACC_G, ACC_D = 0, 1

        def emit_qA_iter(bi):
            is_gate = bi < 88
            buf = qbuf[bi % 2][:, 0:(2048 if is_gate else 5632)]
            src = (gw_in[bi * 128:(bi + 1) * 128, :] if is_gate
                   else dw_in[(bi - 88) * 128:(bi - 87) * 128, :])
            ld_at = sLD.inc(16)

            def ld(sync, src=src, buf=buf, pr=sVE.n):
                sync.wait_ge(sVE.h, pr)
                sync.dma_start(buf, src).then_inc(sLD.h, 16)
            p.add("sync", ld)

            red_at = sVE.inc(1)

            def red(v, buf=buf, w=ld_at, bi=bi):
                v.wait_ge(sLD.h, w)
                v.tensor_reduce(ascr[:, bi:bi + 1], buf, AX.X, ALU.add,
                                apply_absolute_value=True).then_inc(sVE.h, 1)
            p.add("vector", red)

        for step in range(104):
            if step < 80:
                emit_relayout_iter(step // 16, step % 16)
            emit_qA_iter(step)
        relayout_done_sG = sGD.n
        qA_ve = sVE.n

        ve_at = sVE.inc(2)

        def s0qA_comb(v, w=qA_ve):
            v.wait_ge(sVE.h, w)
            v.tensor_reduce(stat[:, 0:1], ascr[:, 0:88], AX.X, ALU.add)\
                .then_inc(sVE.h, 1)
            v.tensor_reduce(stat[:, 1:2], ascr[:, 88:104], AX.X, ALU.add)\
                .then_inc(sVE.h, 1)
        p.add("vector", s0qA_comb)
        qA_ve = sVE.n

        mm_at = sMM.inc(1)

        def s0qA_fin_mm(t, ve=qA_ve):
            t.wait_ge(sVE.h, ve)
            t.wait_ge(sLD.h, consts_ld)
            t.matmul(pb[0][:1, 0:2], ones1c[:, 0:1], stat[:, 0:2],
                     start=True, stop=True).then_inc(sMM.h, 1)
        p.add("tensor", s0qA_fin_mm)

        fin_base = sVE.n
        sVE.inc(3)

        def s0qA_fin_v(v, w=mm_at, base=fin_base):
            v.wait_ge(sMM.h, w)
            v.tensor_single_scalar(tiny[:1, 4:5], pb[0][0:1, 0:1],
                                   1.0 / (2 * INTER * D), ALU.mult)
            v.tensor_single_scalar(tiny[:1, 5:6], pb[0][0:1, 1:2],
                                   1.0 / (D * INTER), ALU.mult).then_inc(sVE.h, 1)
            v.wait_ge(sVE.h, base + 1)
            v.tensor_single_scalar(tiny[:1, 4:6], tiny[:1, 4:6], 1e-5, ALU.max)\
                .then_inc(sVE.h, 1)
            v.wait_ge(sVE.h, base + 2)
            v.reciprocal(tiny[:1, 6:8], tiny[:1, 4:6]).then_inc(sVE.h, 1)
        p.add("vector", s0qA_fin_v)

        mm_at2 = sMM.inc(1)

        def s0qA_rep_mm(t, ve=sVE.n):
            t.wait_ge(sVE.h, ve)
            t.matmul(pb[1][:, 0:2], ones1r[:1, :], tiny[:1, 6:8],
                     start=True, stop=True).then_inc(sMM.h, 1)
        p.add("tensor", s0qA_rep_mm)

        ve_at = sVE.inc(1)

        def s0qA_rep_v(v, w=mm_at2):
            v.wait_ge(sMM.h, w)
            v.tensor_copy(tiny[:, 0:1], pb[1][:, 0:1])
            v.tensor_copy(tiny[:, 2:3], pb[1][:, 1:2])
            v.tensor_single_scalar(tiny[:, 1:2], pb[1][:, 0:1], -1.0, ALU.mult)
            v.tensor_single_scalar(tiny[:, 3:4], pb[1][:, 1:2], -1.0, ALU.mult)\
                .then_inc(sVE.h, 1)
        p.add("vector", s0qA_rep_v)
        scales_ve = sVE.n

        tstage = arBF[:, 0:5632]

        def emit_qB_iter(bi):
            is_gate = bi < 88
            ncols = 2048 if is_gate else 5632
            buf = qbuf[bi % 2][:, 0:ncols]
            scr = qbuf[(bi + 1) % 2][:, 0:ncols]
            src = (gw_in[bi * 128:(bi + 1) * 128, :] if is_gate
                   else dw_in[(bi - 88) * 128:(bi - 87) * 128, :])
            sc = 0 if is_gate else 2

            ld_at = sLD.inc(16)

            def ld(sync, src=src, buf=buf, pr=sVE.n, pm=sMM.n):
                sync.wait_ge(sVE.h, max(pr, scales_ve))
                sync.wait_ge(sMM.h, pm)
                sync.dma_start(buf, src).then_inc(sLD.h, 16)
            p.add("sync", ld)

            q_at = sVE.inc(1)

            def quant(v, buf=buf, scr=scr, w=ld_at, sc=sc, pr=sGD.n,
                      pm=sMM.n, n=ncols):
                v.wait_ge(sLD.h, w)
                v.wait_ge(sGD.h, pr)
                v.wait_ge(sMM.h, pm)
                v.tensor_scalar(scr, buf,
                                tiny[:, sc:sc + 1], 0.5, ALU.mult, ALU.is_ge)
                v.tensor_scalar(buf, buf, tiny[:, sc + 1:sc + 2], 0.5,
                                ALU.mult, ALU.is_ge)
                v.tensor_tensor(buf, scr, buf,
                                ALU.subtract).then_inc(sVE.h, 1)
            p.add("vector", quant)

            nch = ncols // 128
            for g4 in range((nch + 3) // 4):
                n4 = min(4, nch - g4 * 4)
                bk = 5 + (g4 % 2)
                mm_at = sMM.inc(1)

                def trq(t, buf=buf, g4=g4, n4=n4, bk=bk, w=q_at, pe=sVE.n):
                    t.wait_ge(sVE.h, max(w, pe))
                    for i in range(n4):
                        mi = t.matmul(pb[bk][:, i * 128:(i + 1) * 128],
                                      buf[:, (g4 * 4 + i) * 128:
                                          (g4 * 4 + i + 1) * 128],
                                      identf[:], is_transpose=True,
                                      start=(i == 0), stop=(i == n4 - 1))
                    mi.then_inc(sMM.h, 1)
                p.add("tensor", trq)

                ve_at2 = sVE.inc(1)

                def evq(v, g4=g4, n4=n4, bk=bk, w=mm_at):
                    v.wait_ge(sMM.h, w)
                    v.tensor_copy(tstage[:, g4 * 512:g4 * 512 + n4 * 128],
                                  pb[bk][:, 0:n4 * 128]).then_inc(sVE.h, 1)
                p.add("vector", evq)

            st_at = sGD.inc(16)

            def st(g, bi=bi, is_gate=is_gate, w=sVE.n):
                g.wait_ge(sVE.h, w)
                if is_gate:
                    g.dma_start(gwqT_s[:, bi].rearrange("dc p j -> p dc j"),
                                tstage[:, 0:2048]).then_inc(sGD.h, 16)
                else:
                    g.dma_start(dwqT_s[:, bi - 88].rearrange("jc p d -> p jc d"),
                                tstage[:, 0:5632]).then_inc(sGD.h, 16)
            p.add("gpsimd", st)

        crow = arC[:, 0:2048]
        scrow = arC[:, 2048:4096]
        scT = arC[:, 4096:4112]

        ld_at = sLD.inc(16)

        def s0m_ldc(sync):
            sync.dma_start(crow[:1, :], cvec_in).then_inc(sLD.h, 16)
        p.add("sync", s0m_ldc)

        ac_at = sAC.inc(1)

        def s0m_silu(a, w=ld_at):
            a.wait_ge(sLD.h, w)
            a.activation(scrow[:1, :], crow[:1, :], AF.Silu).then_inc(sAC.h, 1)
        p.add("scalar", s0m_silu)

        mm_at = sMM.inc(1)

        def s0m_sct(t, w=ac_at):
            t.wait_ge(sAC.h, w)
            for dc in range(16):
                mi = t.matmul(pb[2][:, dc:dc + 1],
                              scrow[:1, dc * 128:(dc + 1) * 128],
                              ones1c[:1, 0:1],
                              start=True, stop=True)
            mi.then_inc(sMM.h, 1)
        p.add("tensor", s0m_sct)

        ve_at = sVE.inc(1)

        def s0m_sct_ev(v, w=mm_at):
            v.wait_ge(sMM.h, w)
            v.tensor_copy(scT, pb[2][:, 0:16]).then_inc(sVE.h, 1)
        p.add("vector", s0m_sct_ev)
        sct_ve = sVE.n

        wmbuf = [arB[:, 0:4096], arB[:, 4096:8192]]
        def s0m_zero(v, pr=sct_ve):
            v.wait_ge(sVE.h, pr)
            v.memset(modcol[:], 0.0)
        p.add("vector", s0m_zero)

        mm_prev = {0: 0, 1: 0}

        def emit_mod_iter(dc, half):
            buf = wmbuf[(dc * 3 + half) % 2]
            ld_at = sLD.inc(16)

            def ld(sync, dc=dc, half=half, buf=buf,
                   pr=mm_prev[(dc * 3 + half) % 2]):
                sync.wait_ge(sMM.h, pr)
                sync.dma_start(buf, wmod_in[dc * 128:(dc + 1) * 128,
                                            half * 4096:(half + 1) * 4096])\
                    .then_inc(sLD.h, 16)
            p.add("sync", ld)

            mm_at = sMM.inc(1)
            mm_prev[(dc * 3 + half) % 2] = mm_at

            def mm(t, dc=dc, half=half, buf=buf, w=ld_at, ve=sVE.n):
                t.wait_ge(sLD.h, w)
                t.wait_ge(sVE.h, ve)
                for j in range(32):
                    mi = t.matmul(pb[3][:, j:j + 1],
                                  buf[:, j * 128:(j + 1) * 128],
                                  scT[:, dc:dc + 1],
                                  start=True, stop=True)
                mi.then_inc(sMM.h, 1)
            p.add("tensor", mm)

            ve_at = sVE.inc(1)

            def acc(v, half=half, w=mm_at):
                v.wait_ge(sMM.h, w)
                v.tensor_tensor(modcol[:, half * 32:(half + 1) * 32],
                                modcol[:, half * 32:(half + 1) * 32],
                                pb[3][:, 0:32], ALU.add).then_inc(sVE.h, 1)
            p.add("vector", acc)

        mod_iters = [(dc, half) for dc in range(16) for half in range(3)]
        for step in range(104):
            if step < 48:
                emit_mod_iter(*mod_iters[step])
            emit_qB_iter(step)
        quant_done_sG = sGD.n
        quant_done_ve = sVE.n
        mod_acc_ve = sVE.n

        gfx_base = sVE.n
        sVE.inc(7)

        def s0m_gafix(v, base=gfx_base):
            sls = [modcol[:, 32:48], modcol[:, 80:96]]
            sgns = [arC[:, 4112:4128], arC[:, 4128:4144]]
            mags = [arC[:, 4144:4160], arC[:, 4160:4176]]
            for i in (0, 1):
                v.tensor_scalar(sgns[i], sls[i], 0.0, None, ALU.is_ge)
            v.engine_nop().then_inc(sVE.h, 1)
            v.wait_ge(sVE.h, base + 1)
            for i in (0, 1):
                v.tensor_scalar(sgns[i], sgns[i], 2.0, -1.0, ALU.mult, ALU.add)
            v.engine_nop().then_inc(sVE.h, 1)
            v.wait_ge(sVE.h, base + 2)
            for i in (0, 1):
                v.tensor_tensor(mags[i], sls[i], sgns[i], ALU.mult)
            v.engine_nop().then_inc(sVE.h, 1)
            v.wait_ge(sVE.h, base + 3)
            for i in (0, 1):
                v.tensor_single_scalar(mags[i], mags[i], 0.25, ALU.max)
            v.engine_nop().then_inc(sVE.h, 1)
            v.wait_ge(sVE.h, base + 4)
            for i in (0, 1):
                v.tensor_tensor(sls[i], sgns[i], mags[i], ALU.mult)\
                    .then_inc(sVE.h, 1)
            v.wait_ge(sVE.h, base + 6)
            v.engine_nop().then_inc(sVE.h, 1)
        p.add("vector", s0m_gafix)
        gafix_ve = sVE.n

        mm_at = sMM.inc(1)

        def s0m_tr(t, ve=gafix_ve):
            t.wait_ge(sVE.h, ve)
            t.transpose(pb[4][:96, 0:128], modcol[:, 0:96], identf[:])\
                .then_inc(sMM.h, 1)
        p.add("tensor", s0m_tr)

        ve_at = sVE.inc(1)

        def s0m_tr_ev(v, w=mm_at):
            v.wait_ge(sMM.h, w)
            v.tensor_copy(arC[:96, 4352:4480], pb[4][:96, 0:128]).then_inc(sVE.h, 1)
        p.add("vector", s0m_tr_ev)
        modT_ve = sVE.n

        st_at = sGD.inc(16)

        def s0m_st(g, w=modT_ve):
            g.wait_ge(sVE.h, w)
            g.dma_start(mod_s.rearrange("k (p f) -> (k p) f", p=16),
                        arC[:96, 4352:4480])\
                .then_inc(sGD.h, 16)
        p.add("gpsimd", s0m_st)
        mod_st_sG = sGD.n

        if stop_after != "s0":
            arS1 = sb("arS1", [128, 4608], F32)
            wf1res = arS1[:, 0:2048]
            wf2res = arS1[:, 2048:4096]
            wbres = arS1[:, 4096:4352]
            tails = arS1[:, 4352:4544]

            sy, te, ve, ac, gp = Rec(), Rec(), Rec(), Rec(), Rec()

            s0MM, s0VE, s0AC, s0GD = sMM.n, sVE.n, sAC.n, sGD.n
            sy.wait_ge(sMM.h, s0MM)
            sy.wait_ge(sVE.h, s0VE)
            sy.wait_ge(sAC.h, s0AC)
            sy.wait_ge(sGD.h, s0GD)

            for dc in range(16):
                sy.dma_start(wf1res[:, dc * 128:(dc + 1) * 128],
                             wf1_in[dc * 128:(dc + 1) * 128, :])\
                    .then_inc(sLD.h, 16); sLD.inc(16)
            sy.dma_start(wf2res[:, :], wf2_in[:, :]).then_inc(sLD.h, 16)
            sLD.inc(16)
            wg1res = arC[:, 3584:5632]
            for dc in range(16):
                sy.dma_start(wg1res[:, dc * 128:(dc + 1) * 128],
                             wg1_in[dc * 128:(dc + 1) * 128, :])\
                    .then_inc(sLD.h, 16); sLD.inc(16)
            for dc in range(16):
                sy.dma_start(wbres[:, dc * 16:(dc + 1) * 16],
                             wb_in[dc * 128:(dc + 1) * 128, :])\
                    .then_inc(sLD.h, 16); sLD.inc(16)
            sy.dma_start(arRep[:, 0:2048], bg2_in.broadcast_to([128, D]))\
                .then_inc(sLD.h, 16); sLD.inc(16)
            wg2res = arRep[:, 2048:4096]
            sy.dma_start(wg2res, wg2_in[:, :]).then_inc(sLD.h, 16); sLD.inc(16)
            ccA = arRep[:, 4096:4288]
            for si, cin in enumerate((cq_in, ck_in, cv_in)):
                for h in range(16):
                    sy.dma_start(ccA[:, (si * 16 + h) * 4:(si * 16 + h) * 4 + 4],
                                 cin[h * 128:(h + 1) * 128, :])\
                        .then_inc(sLD.h, 16); sLD.inc(16)
            s1p_ld = sLD.n

            ve.wait_ge(sVE.h, gafix_ve)
            ve.tensor_single_scalar(tiny[:, 8:24], modcol[:, 16:32], 1.0,
                                    ALU.add)
            ve.memset(tails[:, :], 0.0)
            ve.memset(stat[:, 16:64], 0.0).then_inc(sVE.h, 1)
            sVE.inc(1)
            s1p_ve = sVE.n

            xbuf = [arB[:, 0:2048], arB[:, 2048:4096]]
            xnbuf = [arB[:, 4096:6144], arB[:, 6144:8192]]
            qlin = arC[:, 0:1027]
            qconv = arC[:, 1027:2051]
            t12buf = arC[:, 2051:3075]
            bstage = arC[:, 3075:3203]
            qsil = [arB[:, 0:1024], arB[:, 1024:2048]]
            tokstage = arB[:, 2048:4096]
            sgstage = arB[:, 4096:6144]
            wpan = [arW[:, 0:2048], arW[:, 2048:4096]]

            bank_free = {i: None for i in range(8)}

            def bwait(i):
                if bank_free[i]:
                    s_, n_ = bank_free[i]
                    te.wait_ge(s_.h, n_)

            WSTREAMS = [("q", wq_t, 0, q_fm_s, None),
                        ("k", wk_t, 1, k_fm_s, k_tm_s),
                        ("v", wv_t, 2, None, v_tm_s),
                        ("w", ww_t, 2, None, w_tm_s)]

            pan_mm = {}
            qsil_free = {0: (0, 0), 1: (0, 0)}
            tok_free = 0
            last_conv_ve = 0
            sg_free = 0

            for cc1 in range(NC1):
                sy.wait_ge(sVE.h, sVE.n)
                sy.wait_ge(sAC.h, sAC.n)
                sy.wait_ge(sGD.h, sGD.n)
                sy.wait_ge(sMM.h, sMM.n)

                ld_t, ve_t, ac_t = [], [], []
                for tt in range(8):
                    if tt >= 2:
                        sy.wait_ge(sVE.h, ve_t[tt - 2])
                        sy.wait_ge(sAC.h, ac_t[tt - 2])
                    sy.dma_start(xbuf[tt % 2],
                                 x_in[cc1 * CT1 + tt * 128:
                                      cc1 * CT1 + (tt + 1) * 128, :])\
                        .then_inc(sLD.h, 16)
                    ld_t.append(sLD.inc(16))

                    ve.wait_ge(sLD.h, ld_t[tt])
                    ve.tensor_reduce(stat[:, 16 + tt:17 + tt], xbuf[tt % 2],
                                     AX.X, ALU.add).then_inc(sVE.h, 1)
                    ve_t.append(sVE.inc(1))

                    ac.wait_ge(sLD.h, ld_t[tt])
                    ac.activation(xnbuf[tt % 2], xbuf[tt % 2], AF.Square,
                                  accum_out=stat[:, 24 + tt:25 + tt])\
                        .then_inc(sAC.h, 1)
                    ac_t.append(sAC.inc(1))

                b = sVE.n
                ve.wait_ge(sVE.h, ve_t[-1])
                ve.wait_ge(sAC.h, ac_t[-1])
                ve.tensor_single_scalar(stat[:, 32:40], stat[:, 16:24],
                                        1.0 / D, ALU.mult)
                ve.tensor_single_scalar(stat[:, 40:48], stat[:, 24:32],
                                        1.0 / D, ALU.mult).then_inc(sVE.h, 1)
                ve.wait_ge(sVE.h, b + 1)
                ve.tensor_tensor(stat[:, 48:56], stat[:, 32:40],
                                 stat[:, 32:40], ALU.mult).then_inc(sVE.h, 1)
                ve.wait_ge(sVE.h, b + 2)
                ve.scalar_tensor_tensor(stat[:, 48:56], stat[:, 40:48],
                                        LN_EPS, stat[:, 48:56],
                                        ALU.add, ALU.subtract).then_inc(sVE.h, 1)
                sVE.inc(3)
                p2_ve = sVE.n

                ab = sAC.n
                ac.wait_ge(sVE.h, p2_ve)
                ac.activation(stat[:, 56:64], stat[:, 48:56], AF.Ln)\
                    .then_inc(sAC.h, 1)
                ac.wait_ge(sAC.h, ab + 1)
                ac.activation(stat[:, 48:56], stat[:, 56:64], AF.Exp,
                              scale=-0.5).then_inc(sAC.h, 1)
                sAC.inc(2)
                p2_ac = sAC.n

                b = sVE.n
                ve.wait_ge(sAC.h, p2_ac)
                ve.tensor_tensor(stat[:, 56:64], stat[:, 32:40],
                                 stat[:, 48:56], ALU.mult).then_inc(sVE.h, 1)
                ve.wait_ge(sVE.h, b + 1)
                ve.tensor_single_scalar(stat[:, 56:64], stat[:, 56:64],
                                        -1.0, ALU.mult).then_inc(sVE.h, 1)
                sVE.inc(2)
                p2_done = sVE.n

                xn_ac = []
                for tt in range(8):
                    sy.wait_ge(sAC.h, sAC.n)
                    sy.dma_start(xbuf[tt % 2],
                                 x_in[cc1 * CT1 + tt * 128:
                                      cc1 * CT1 + (tt + 1) * 128, :])\
                        .then_inc(sLD.h, 16)
                    w = sLD.inc(16)

                    ac.wait_ge(sLD.h, w)
                    ac.wait_ge(sVE.h, p2_done)
                    ac.activation(xnbuf[tt % 2], xbuf[tt % 2], AF.Identity,
                                  bias=stat[:, 56 + tt:57 + tt],
                                  scale=stat[:, 48 + tt:49 + tt])\
                        .then_inc(sAC.h, 1)
                    wa = sAC.inc(1)
                    xn_ac.append(wa)

                    for dc in range(16):
                        bk = dc % 4
                        te.wait_ge(sAC.h, wa)
                        bwait(bk)
                        te.transpose(pb[bk][:, 0:128],
                                     xnbuf[tt % 2][:, dc * 128:(dc + 1) * 128],
                                     identf[:]).then_inc(sMM.h, 1)
                        wm = sMM.inc(1)

                        ac.wait_ge(sMM.h, wm)
                        ac.activation(
                            arA[:, dc * 1024 + tt * 128:
                                   dc * 1024 + (tt + 1) * 128],
                            pb[bk][:, 0:128], AF.Identity,
                            bias=modcol[:, dc:dc + 1],
                            scale=tiny[:, 8 + dc:9 + dc]).then_inc(sAC.h, 1)
                        bank_free[bk] = (sAC, sAC.inc(1))
                h_ready = sAC.n

                def conv_silu_spill(snm, si, h, ci, fm_s, tm_s, ev_t):
                    tcol = (si * 16 + h) * 3
                    ve.wait_ge(sAC.h, ev_t)
                    ve.tensor_copy(qlin[:, 0:3], tails[:, tcol:tcol + 3])
                    cbase = (ci * 16 + h) * 4
                    ve.tensor_single_scalar(qconv[:, 0:1024], qlin[:, 0:1024],
                                            ccA[:, cbase:cbase + 1], ALU.mult)
                    for i in (1, 2, 3):
                        ve.scalar_tensor_tensor(qconv[:, 0:1024],
                                                qlin[:, i:i + 1024],
                                                ccA[:, cbase + i:cbase + i + 1],
                                                qconv[:, 0:1024],
                                                ALU.mult, ALU.add)
                    ve.tensor_copy(tails[:, tcol:tcol + 3], qlin[:, 1024:1027])\
                        .then_inc(sVE.h, 1)
                    conv_ve = sVE.inc(1)
                    nonlocal last_conv_ve
                    last_conv_ve = conv_ve

                    slot = (si * 16 + h) % 2
                    gfree, mfree = qsil_free[slot]
                    ac.wait_ge(sVE.h, conv_ve)
                    if gfree:
                        ac.wait_ge(sGD.h, gfree)
                    if mfree:
                        ac.wait_ge(sMM.h, mfree)
                    ac.activation(qsil[slot], qconv[:, 0:1024], AF.Silu)\
                        .then_inc(sAC.h, 1)
                    sil_ac = sAC.inc(1)
                    return slot, sil_ac

                def spill_stream(snm, h, slot, dep, fm_s, tm_s, cc1=cc1):
                    nonlocal tok_free
                    dsem, dval = dep
                    last_g = 0
                    last_m = 0
                    if fm_s is not None:
                        gp.wait_ge(dsem.h, dval)
                        gp.dma_start(fm_s[h][:, cc1 * CT1:(cc1 + 1) * CT1],
                                     qsil[slot]).then_inc(sGD.h, 16)
                        last_g = sGD.inc(16)
                    if tm_s is not None:
                        for lc in range(8):
                            bk = lc % 4
                            te.wait_ge(dsem.h, dval)
                            bwait(bk)
                            te.transpose(pb[bk][:, 0:128],
                                         qsil[slot][:, lc * 128:(lc + 1) * 128],
                                         identf[:]).then_inc(sMM.h, 1)
                            wm = sMM.inc(1)
                            last_m = wm
                            ve.wait_ge(sMM.h, wm)
                            if lc == 0 and tok_free:
                                ve.wait_ge(sGD.h, tok_free)
                            ve.tensor_copy(tokstage[:, lc * 128:(lc + 1) * 128],
                                           pb[bk][:, 0:128]).then_inc(sVE.h, 1)
                            bank_free[bk] = (sVE, sVE.inc(1))
                        gp.wait_ge(sVE.h, sVE.n)
                        gp.dma_start(
                            tm_s[cc1 * 8:(cc1 + 1) * 8, h]
                            .rearrange("cc t m -> t cc m"),
                            tokstage[:, 0:1024]).then_inc(sGD.h, 16)
                        tok_free = sGD.inc(16)
                        last_g = tok_free
                    qsil_free[slot] = (last_g, last_m)

                for si, (snm, wtil, ci, fm_s, tm_s) in enumerate(WSTREAMS):
                    for h in range(16):
                        slot = (si * 16 + h) % 2
                        if pan_mm.get(slot):
                            sy.wait_ge(sMM.h, pan_mm[slot])
                        sy.dma_start(wpan[slot],
                                     wtil[h].rearrange("p dc j -> p (dc j)"))\
                            .then_inc(sLD.h, 16)
                        wld = sLD.inc(16)

                        ev_t = 0
                        for tb in range(2):
                            bk = 4 + tb
                            te.wait_ge(sLD.h, wld)
                            if si == 0 and h == 0 and tb == 0:
                                te.wait_ge(sAC.h, h_ready)
                            bwait(bk)
                            for dc in range(16):
                                mi = te.matmul(
                                    pb[bk][:, 0:512],
                                    wpan[slot][:, dc * 128:(dc + 1) * 128],
                                    arA[:, dc * 1024 + tb * 512:
                                           dc * 1024 + (tb + 1) * 512],
                                    start=(dc == 0), stop=(dc == 15))
                            mi.then_inc(sMM.h, 1)
                            wm = sMM.inc(1)
                            pan_mm[slot] = wm

                            ac.wait_ge(sMM.h, wm)
                            if tb == 0 and last_conv_ve:
                                ac.wait_ge(sVE.h, last_conv_ve)
                            ac.activation(qlin[:, 3 + tb * 512:3 + (tb + 1) * 512],
                                          pb[bk][:, 0:512], AF.Identity)\
                                .then_inc(sAC.h, 1)
                            ev_t = sAC.inc(1)
                            bank_free[bk] = (sAC, ev_t)

                        slot_q, sil_ac = conv_silu_spill(snm, si, h, ci,
                                                         fm_s, tm_s, ev_t)
                        spill_stream(snm, h, slot_q, (sAC, sil_ac), fm_s, tm_s)

                def small_proj(res, outbuf):
                    ev = 0
                    for tb in range(2):
                        bk = 4 + tb
                        bwait(bk)
                        for dc in range(16):
                            mi = te.matmul(pb[bk][:, 0:512],
                                           res[:, dc * 128:(dc + 1) * 128],
                                           arA[:, dc * 1024 + tb * 512:
                                                  dc * 1024 + (tb + 1) * 512],
                                           start=(dc == 0), stop=(dc == 15))
                        mi.then_inc(sMM.h, 1)
                        wm = sMM.inc(1)
                        ac.wait_ge(sMM.h, wm)
                        ac.activation(outbuf[:, tb * 512:(tb + 1) * 512],
                                      pb[bk][:, 0:512], AF.Identity)\
                            .then_inc(sAC.h, 1)
                        ev = sAC.inc(1)
                        bank_free[bk] = (sAC, ev)
                    return ev

                t1_ac = small_proj(wf1res, t12buf)
                for h2 in range(16):
                    slot = h2 % 2
                    gfree, mfree = qsil_free[slot]
                    ev = 0
                    for tb in range(2):
                        bk = 4 + tb
                        te.wait_ge(sAC.h, t1_ac)
                        bwait(bk)
                        mi = te.matmul(pb[bk][:, 0:512],
                                       wf2res[:, h2 * 128:(h2 + 1) * 128],
                                       t12buf[:, tb * 512:(tb + 1) * 512],
                                       start=True, stop=True)
                        mi.then_inc(sMM.h, 1)
                        wm = sMM.inc(1)
                        ac.wait_ge(sMM.h, wm)
                        if tb == 0 and gfree:
                            ac.wait_ge(sGD.h, gfree)
                        if tb == 0 and mfree:
                            ac.wait_ge(sMM.h, mfree)
                        ac.activation(qsil[slot][:, tb * 512:(tb + 1) * 512],
                                      pb[bk][:, 0:512], AF.Identity)\
                            .then_inc(sAC.h, 1)
                        ev = sAC.inc(1)
                        bank_free[bk] = (sAC, ev)
                    spill_stream("z", h2, slot, (sAC, ev), None, z_tm_s)

                for tb in range(2):
                    bk = 6
                    bwait(bk)
                    for dc in range(16):
                        mi = te.matmul(pb[bk][:16, 0:512],
                                       wbres[:, dc * 16:(dc + 1) * 16],
                                       arA[:, dc * 1024 + tb * 512:
                                              dc * 1024 + (tb + 1) * 512],
                                       start=(dc == 0), stop=(dc == 15))
                    mi.then_inc(sMM.h, 1)
                    wm = sMM.inc(1)
                    ve.wait_ge(sMM.h, wm)
                    if tb == 0 and sg_free:
                        ve.wait_ge(sGD.h, sg_free)
                    ve.tensor_copy(sgstage[:16, tb * 512:(tb + 1) * 512],
                                   pb[bk][:16, 0:512]).then_inc(sVE.h, 1)
                    bank_free[bk] = (sVE, sVE.inc(1))
                bv = sVE.n
                ac.wait_ge(sVE.h, bv)
                ac.activation(sgstage[:16, 0:1024], sgstage[:16, 0:1024],
                              AF.Tanh, scale=0.5).then_inc(sAC.h, 1)
                bt_ac = sAC.inc(1)
                ve.wait_ge(sAC.h, bt_ac)
                ve.tensor_scalar(sgstage[:16, 0:1024], sgstage[:16, 0:1024],
                                 0.5, 0.5, ALU.mult, ALU.add).then_inc(sVE.h, 1)
                bt_ve = sVE.inc(1)
                for lc in range(8):
                    bk = lc % 4
                    te.wait_ge(sVE.h, bt_ve)
                    bwait(bk)
                    te.transpose(pb[bk][:, 0:16],
                                 sgstage[:16, lc * 128:(lc + 1) * 128],
                                 identf[:16, :16]).then_inc(sMM.h, 1)
                    wm = sMM.inc(1)
                    ve.wait_ge(sMM.h, wm)
                    ve.tensor_copy(bstage[:, lc * 16:(lc + 1) * 16],
                                   pb[bk][:, 0:16]).then_inc(sVE.h, 1)
                    bank_free[bk] = (sVE, sVE.inc(1))
                gp.wait_ge(sVE.h, sVE.n)
                for lc in range(8):
                    gp.dma_start(beta_s[cc1 * 8 + lc],
                                 bstage[:, lc * 16:(lc + 1) * 16])\
                        .then_inc(sGD.h, 16)
                    sGD.inc(16)
                beta_gd = sGD.n

                t2_ve = small_proj(wg1res, t12buf)
                for ttile in range(8):
                    evs = []
                    for jb in range(4):
                        bk = 6 + (jb % 2)
                        te.wait_ge(sAC.h, t2_ve)
                        bwait(bk)
                        mi = te.matmul(pb[bk][:, 0:512],
                                       t12buf[:, ttile * 128:(ttile + 1) * 128],
                                       wg2res[:, jb * 512:(jb + 1) * 512],
                                       start=True, stop=True)
                        mi.then_inc(sMM.h, 1)
                        wm = sMM.inc(1)
                        ve.wait_ge(sMM.h, wm)
                        if ttile == 0 and jb == 0 and beta_gd:
                            ve.wait_ge(sGD.h, beta_gd)
                        if sg_free:
                            ve.wait_ge(sGD.h, sg_free)
                        ve.tensor_tensor(sgstage[:, jb * 512:(jb + 1) * 512],
                                         pb[bk][:, 0:512],
                                         arRep[:, jb * 512:(jb + 1) * 512],
                                         ALU.add).then_inc(sVE.h, 1)
                        evs.append(sVE.inc(1))
                        bank_free[bk] = (sVE, evs[-1])
                    ac.wait_ge(sVE.h, evs[-1])
                    ac.activation(sgstage[:, 0:2048], sgstage[:, 0:2048],
                                  AF.Tanh, scale=0.5).then_inc(sAC.h, 1)
                    sg_ac = sAC.inc(1)
                    ve.wait_ge(sAC.h, sg_ac)
                    ve.tensor_scalar(sgstage[:, 0:2048], sgstage[:, 0:2048],
                                     0.5, 0.5, ALU.mult, ALU.add)\
                        .then_inc(sVE.h, 1)
                    sg_ve = sVE.inc(1)
                    gp.wait_ge(sVE.h, sg_ve)
                    gp.dma_start(sg_s[cc1 * CT1 + ttile * 128:
                                      cc1 * CT1 + (ttile + 1) * 128, :],
                                 sgstage[:, 0:2048]).then_inc(sGD.h, 16)
                    sg_free = sGD.inc(16)

            s1_end = dict(MM=sMM.n, VE=sVE.n, AC=sAC.n, LD=sLD.n, GD=sGD.n)

            p.add("sync", lambda e, r=sy: r.replay(e))
            p.add("tensor", lambda e, r=te: r.replay(e))
            p.add("vector", lambda e, r=ve: r.replay(e))
            p.add("scalar", lambda e, r=ac: r.replay(e))
            p.add("gpsimd", lambda e, r=gp: r.replay(e))

        if stop_after not in ("s0", "s1"):
            sy2, te2, ve2, ac2, gp2 = Rec(), Rec(), Rec(), Rec(), Rec()
            C44 = 44.0

            SkA = arA[:, 0:2048]
            SvA = arA[:, 2048:4096]
            q_c = arA[:, 4096:6144]
            kfm_c = arA[:, 6144:8192]
            ktm_c = arA[:, 8192:10240]
            v_c = arA[:, 10240:12288]
            w_c = arA[:, 12288:14336]
            z_c = arA[:, 14336:16384]
            lam = arB[:, 0:2048]
            e1c = arB[:, 2048:4096]
            e2c = arB[:, 4096:6144]
            u2 = arB[:, 6144:8192]
            araw = arC[:, 0:2048]
            s_t = arC[:, 2048:4096]
            o_t = arC[:, 4096:6144]
            sgld = arS1[:, 0:2048]
            nsg = arS1[:, 2048:4096]
            bcol = arS1[:, 4096:4112]
            lfc = arS1[:, 4112:4128]
            lf44 = arS1[:, 4128:4144]
            nmx = arS1[:, 4144:4160]
            den = arS1[:, 4160:4176]
            rden = arS1[:, 4176:4192]
            ssq = arS1[:, 4192:4208]
            rst = arS1[:, 4208:4224]
            nwRep = arW[:, 0:2048]
            sktb = arW[:, 2048:2176]
            a1b = arW[:, 2176:2304]
            a2b = arW[:, 2304:2432]
            a1tb = arW[:, 2432:2560]
            a2tb = arW[:, 2560:2688]
            u2tb = arW[:, 2688:2816]
            atb = arW[:, 2816:2944]
            btb = arW[:, 2944:3072]
            skb = arW[:, 3072:3200]
            svb = arW[:, 3200:3328]

            sy2.wait_ge(sMM.h, sMM.n)
            sy2.wait_ge(sVE.h, sVE.n)
            sy2.wait_ge(sAC.h, sAC.n)
            sy2.wait_ge(sGD.h, sGD.n)

            sy2.dma_start(nwRep,
                          nw_in.rearrange("o (u m) -> o u m", u=1)
                          .broadcast_to([128, 16, 128])).then_inc(sLD.h, 16)
            sLD.inc(16)

            biasc = arW[:, 3328:3332]
            ve2.wait_ge(sVE.h, sVE.n)
            ve2.memset(SkA[:], 0.0)
            ve2.memset(SvA[:], 0.0)
            ve2.memset(biasc[:, 0:1], C44)
            ve2.memset(biasc[:, 1:2], -C44).then_inc(sVE.h, 1)
            sVE.inc(1)

            bank2_free = {i: None for i in range(8)}

            def bw2(i):
                if bank2_free[i]:
                    s_, n_ = bank2_free[i]
                    te2.wait_ge(s_.h, n_)

            yo_gd = 0

            for cc in range(NCC):
                sy2.wait_ge(sVE.h, sVE.n)
                sy2.wait_ge(sAC.h, sAC.n)
                sy2.wait_ge(sMM.h, sMM.n)

                def v3(buf, inner=128):
                    return buf.rearrange("p (a b) -> p a b", b=inner)
                sy2.dma_start(v3(q_c), q_fm_s[:, :, cc * 128:(cc + 1) * 128]
                              .rearrange("h p t -> p h t")).then_inc(sLD.h, 16)
                sy2.dma_start(v3(kfm_c), k_fm_s[:, :, cc * 128:(cc + 1) * 128]
                              .rearrange("h p t -> p h t")).then_inc(sLD.h, 16)
                sy2.dma_start(v3(ktm_c), k_tm_s[cc].rearrange("h t m -> t h m"))\
                    .then_inc(sLD.h, 16)
                sy2.dma_start(v3(v_c), v_tm_s[cc].rearrange("h t m -> t h m"))\
                    .then_inc(sLD.h, 16)
                sy2.dma_start(v3(w_c), w_tm_s[cc].rearrange("h t m -> t h m"))\
                    .then_inc(sLD.h, 16)
                sy2.dma_start(v3(z_c), z_tm_s[cc].rearrange("h t m -> t h m"))\
                    .then_inc(sLD.h, 16)
                sy2.dma_start(bcol, beta_s[cc]).then_inc(sLD.h, 16)
                sy2.dma_start(sgld, sg_s[cc * 128:(cc + 1) * 128, :])\
                    .then_inc(sLD.h, 16)
                sLD.inc(16 * 8)
                ld8 = sLD.n

                ac2.wait_ge(sLD.h, ld8)
                ac2.activation(araw, z_c, AF.Abs).then_inc(sAC.h, 1)
                aw = sAC.inc(1)
                ac2.wait_ge(sAC.h, aw)
                ac2.activation(araw, araw, AF.Exp, scale=-1.0).then_inc(sAC.h, 1)
                aw = sAC.inc(1)
                ac2.wait_ge(sAC.h, aw)
                ac2.activation(araw, araw, AF.Ln, bias=ones1c[:, 0:1])\
                    .then_inc(sAC.h, 1)
                aw = sAC.inc(1)
                ve2.wait_ge(sAC.h, aw)
                ve2.scalar_tensor_tensor(z_c, z_c, 0.0, araw,
                                         ALU.min, ALU.subtract).then_inc(sVE.h, 1)
                g_ve = sVE.inc(1)
                ve2.wait_ge(sLD.h, ld8)
                ve2.tensor_single_scalar(q_c, q_c, SCALE, ALU.mult)\
                    .then_inc(sVE.h, 1)
                q_ve = sVE.inc(1)
                ve2.tensor_tensor(nsg, sgld, nwRep, ALU.mult).then_inc(sVE.h, 1)
                nsg_ve = sVE.inc(1)

                exp_ac = []
                for jb in range(4):
                    bk = jb
                    te2.wait_ge(sVE.h, g_ve)
                    bw2(bk)
                    te2.matmul(pb[bk][:, 0:512], ltri[:],
                               z_c[:, jb * 512:(jb + 1) * 512],
                               start=True, stop=True).then_inc(sMM.h, 1)
                    wm = sMM.inc(1)
                    ac2.wait_ge(sMM.h, wm)
                    ac2.activation(lam[:, jb * 512:(jb + 1) * 512],
                                   pb[bk][:, 0:512], AF.Exp)
                    ac2.activation(e1c[:, jb * 512:(jb + 1) * 512],
                                   pb[bk][:, 0:512], AF.Exp, bias=biasc[:, 0:1])
                    ac2.activation(e2c[:, jb * 512:(jb + 1) * 512],
                                   pb[bk][:, 0:512], AF.Exp, scale=-1.0,
                                   bias=biasc[:, 1:2]).then_inc(sAC.h, 1)
                    exp_ac.append(sAC.inc(1))
                    bank2_free[bk] = (sAC, exp_ac[-1])


                bk = 4
                bw2(bk)
                te2.wait_ge(sVE.h, g_ve)
                for h in range(16):
                    mi = te2.matmul(pb[bk][:, h:h + 1],
                                    z_c[:, h * 128:(h + 1) * 128],
                                    ones1c[:, 0:1],
                                    start=True, stop=True)
                mi.then_inc(sMM.h, 1)
                wm = sMM.inc(1)
                ac2.wait_ge(sMM.h, wm)
                ac2.activation(lfc, pb[bk][:, 0:16], AF.Exp)
                ac2.activation(lf44, pb[bk][:, 0:16], AF.Exp,
                               bias=biasc[:, 0:1]).then_inc(sAC.h, 1)
                lf_ac = sAC.inc(1)
                bank2_free[bk] = (sAC, lf_ac)

                ve2.wait_ge(sLD.h, ld8)
                ve2.tensor_tensor(araw, w_c, w_c, ALU.mult).then_inc(sVE.h, 1)
                wsq_ve = sVE.inc(1)
                ve2.wait_ge(sVE.h, wsq_ve)
                ve2.tensor_reduce(ssq, araw.rearrange("p (h m) -> p h m", m=128),
                                  AX.X, ALU.add).then_inc(sVE.h, 1)
                wss_ve = sVE.inc(1)
                ve2.wait_ge(sVE.h, wss_ve)
                ve2.tensor_single_scalar(ssq, ssq, L2_EPS, ALU.add)\
                    .then_inc(sVE.h, 1)
                wse_ve = sVE.inc(1)
                wab = sAC.n
                ac2.wait_ge(sVE.h, wse_ve)
                ac2.activation(rst, ssq, AF.Ln).then_inc(sAC.h, 1)
                ac2.wait_ge(sAC.h, wab + 1)
                ac2.activation(rst, rst, AF.Exp, scale=-0.5).then_inc(sAC.h, 1)
                sAC.inc(2)
                wrst_ac = sAC.n
                ve2.wait_ge(sAC.h, wrst_ac)
                for h in range(16):
                    mi = ve2.tensor_single_scalar(
                        w_c[:, h * 128:(h + 1) * 128],
                        w_c[:, h * 128:(h + 1) * 128],
                        rst[:, h:h + 1], ALU.mult)
                mi.then_inc(sVE.h, 1)
                wnorm_ve = sVE.inc(1)

                ve2.wait_ge(sVE.h, wnorm_ve)
                ve2.wait_ge(sAC.h, exp_ac[-1])
                for h in range(16):
                    ve2.scalar_tensor_tensor(u2[:, h * 128:(h + 1) * 128],
                                             w_c[:, h * 128:(h + 1) * 128],
                                             bcol[:, h:h + 1],
                                             e2c[:, h * 128:(h + 1) * 128],
                                             ALU.mult, ALU.mult)
                ve2.engine_nop().then_inc(sVE.h, 1)
                u2_ve = sVE.inc(1)

                for h in range(16):
                    hs = slice(h * 128, (h + 1) * 128)
                    bk6 = 6
                    bw2(bk6)
                    if cc == 0 and h == 0:
                        te2.wait_ge(sVE.h, u2_ve)
                    te2.transpose(pb[bk6][:, 0:128], SkA[:, hs], identf[:])\
                        .then_inc(sMM.h, 1)
                    wm_skt = sMM.inc(1)
                    ac2.wait_ge(sMM.h, wm_skt)
                    ac2.activation(sktb, pb[bk6][:, 0:128], AF.Identity)\
                        .then_inc(sAC.h, 1)
                    skt_ac = sAC.inc(1)
                    bank2_free[bk6] = (sAC, skt_ac)

                    bk4 = 4
                    bw2(bk4)
                    te2.wait_ge(sVE.h, q_ve)
                    te2.matmul(pb[bk4][:, 0:128], kfm_c[:, hs], q_c[:, hs],
                               start=True, stop=True).then_inc(sMM.h, 1)
                    wm_at = sMM.inc(1)
                    ve2.wait_ge(sMM.h, wm_at)
                    ve2.tensor_tensor(atb, pb[bk4][:, 0:128], ltri[:],
                                      ALU.mult).then_inc(sVE.h, 1)
                    at_ve = sVE.inc(1)
                    bank2_free[bk4] = (sVE, at_ve)

                    bk7 = 7
                    bw2(bk7)
                    te2.wait_ge(sAC.h, skt_ac)
                    te2.matmul(pb[bk7][:, 0:128], q_c[:, hs], sktb,
                               start=True, stop=True).then_inc(sMM.h, 1)
                    wm_qs = sMM.inc(1)

                    bk5 = 5
                    bw2(bk5)
                    te2.wait_ge(sVE.h, max(at_ve, u2_ve))
                    te2.matmul(pb[bk5][:, 0:128], atb, u2[:, hs],
                               start=True, stop=True).then_inc(sMM.h, 1)
                    wm_in = sMM.inc(1)

                    ve2.wait_ge(sMM.h, wm_qs)
                    ve2.tensor_tensor(s_t[:, hs], lam[:, hs], pb[bk7][:, 0:128],
                                      ALU.mult).then_inc(sVE.h, 1)
                    sv1 = sVE.inc(1)
                    bank2_free[bk7] = (sVE, sv1)
                    ve2.wait_ge(sMM.h, wm_in)
                    ve2.tensor_tensor(araw[:, hs], e1c[:, hs], pb[bk5][:, 0:128],
                                      ALU.mult).then_inc(sVE.h, 1)
                    sv2 = sVE.inc(1)
                    bank2_free[bk5] = (sVE, sv2)
                    ve2.tensor_tensor(s_t[:, hs], s_t[:, hs], araw[:, hs],
                                      ALU.add).then_inc(sVE.h, 1)
                    sVE.inc(1)

                sm0 = sVE.n
                ve2.tensor_reduce(nmx, s_t.rearrange("p (h m) -> p h m", m=128),
                                  AX.X, ALU.max, negate=True).then_inc(sVE.h, 1)
                nmx_ve = sVE.inc(1)
                ac2.wait_ge(sVE.h, nmx_ve)
                for h in range(16):
                    hs = slice(h * 128, (h + 1) * 128)
                    mi = ac2.activation(araw[:, hs], s_t[:, hs], AF.Exp,
                                        bias=nmx[:, h:h + 1],
                                        accum_out=den[:, h:h + 1])
                mi.then_inc(sAC.h, 1)
                sm_ac = sAC.inc(1)
                ve2.wait_ge(sAC.h, sm_ac)
                ve2.reciprocal(rden, den).then_inc(sVE.h, 1)
                rden_ve = sVE.inc(1)

                for h in range(16):
                    hs = slice(h * 128, (h + 1) * 128)
                    ve2.wait_ge(sVE.h, rden_ve)
                    ve2.tensor_tensor(a1b, araw[:, hs], lam[:, hs], ALU.mult)
                    ve2.tensor_tensor(a2b, araw[:, hs], e1c[:, hs], ALU.mult)\
                        .then_inc(sVE.h, 1)
                    ab_ve = sVE.inc(1)

                    tps = []
                    for bsrc, bdst, bk in ((a1b, a1tb, 4), (a2b, a2tb, 5),
                                           (u2[:, hs], u2tb, 6)):
                        bw2(bk)
                        te2.wait_ge(sVE.h, ab_ve)
                        te2.transpose(pb[bk][:, 0:128], bsrc, identf[:])\
                            .then_inc(sMM.h, 1)
                        wm = sMM.inc(1)
                        ve2.wait_ge(sMM.h, wm)
                        ve2.tensor_copy(bdst, pb[bk][:, 0:128]).then_inc(sVE.h, 1)
                        bank2_free[bk] = (sVE, sVE.inc(1))
                        tps.append(bank2_free[bk][1])

                    bk4 = 4
                    bw2(bk4)
                    te2.wait_ge(sVE.h, tps[-1])
                    te2.matmul(pb[bk4][0:64, 0:64], u2tb[:, 0:64],
                               a2tb[:, 0:64], start=True, stop=True)
                    te2.matmul(pb[bk4][0:64, 64:128], u2tb[:, 0:64],
                               a2tb[:, 64:128], start=True, stop=True)
                    te2.matmul(pb[bk4][64:128, 64:128], u2tb[:, 64:128],
                               a2tb[:, 64:128], start=True, stop=True)\
                        .then_inc(sMM.h, 1)
                    wm_bt = sMM.inc(1)
                    ve2.wait_ge(sMM.h, wm_bt)
                    ve2.memset(btb[64:128, 0:64], 0.0)
                    ve2.tensor_tensor(btb[0:64, 0:64], pb[bk4][0:64, 0:64],
                                      ltri[0:64, 0:64], ALU.mult)
                    ve2.tensor_copy(btb[0:64, 64:128], pb[bk4][0:64, 64:128])
                    ve2.tensor_tensor(btb[64:128, 64:128],
                                      pb[bk4][64:128, 64:128],
                                      ltri[0:64, 0:64], ALU.mult)\
                        .then_inc(sVE.h, 1)
                    bt_ve = sVE.inc(1)
                    bank2_free[bk4] = (sVE, bt_ve)

                    bk7 = 7
                    bw2(bk7)
                    te2.matmul(pb[bk7][:, 0:128], a1tb, SvA[:, hs],
                               start=True, stop=False)
                    te2.wait_ge(sVE.h, bt_ve)
                    te2.matmul(pb[bk7][:, 0:128], btb, v_c[:, hs],
                               start=False, stop=True).then_inc(sMM.h, 1)
                    wm_o = sMM.inc(1)
                    ve2.wait_ge(sMM.h, wm_o)
                    ve2.tensor_scalar(o_t[:, hs], pb[bk7][:, 0:128],
                                      rden[:, h:h + 1], None, ALU.mult)\
                        .then_inc(sVE.h, 1)
                    bank2_free[bk7] = (sVE, sVE.inc(1))

                    bk5, bk6 = 5, 6
                    bw2(bk5)
                    te2.matmul(pb[bk5][:, 0:128], u2[:, hs], ktm_c[:, hs],
                               start=True, stop=True).then_inc(sMM.h, 1)
                    wm_uk = sMM.inc(1)
                    bw2(bk6)
                    te2.matmul(pb[bk6][:, 0:128], u2[:, hs], v_c[:, hs],
                               start=True, stop=True).then_inc(sMM.h, 1)
                    wm_uv = sMM.inc(1)
                    ac2.wait_ge(sAC.h, lf_ac)
                    ac2.activation(skb, SkA[:, hs], AF.Identity,
                                   scale=lfc[:, h:h + 1])
                    ac2.activation(svb, SvA[:, hs], AF.Identity,
                                   scale=lfc[:, h:h + 1]).then_inc(sAC.h, 1)
                    sc_ac = sAC.inc(1)
                    ve2.wait_ge(sMM.h, wm_uk)
                    ve2.wait_ge(sAC.h, sc_ac)
                    ve2.scalar_tensor_tensor(SkA[:, hs], pb[bk5][:, 0:128],
                                             lf44[:, h:h + 1], skb,
                                             ALU.mult, ALU.add).then_inc(sVE.h, 1)
                    bank2_free[bk5] = (sVE, sVE.inc(1))
                    ve2.wait_ge(sMM.h, wm_uv)
                    ve2.scalar_tensor_tensor(SvA[:, hs], pb[bk6][:, 0:128],
                                             lf44[:, h:h + 1], svb,
                                             ALU.mult, ALU.add).then_inc(sVE.h, 1)
                    bank2_free[bk6] = (sVE, sVE.inc(1))

                ve2.tensor_tensor(araw, o_t, o_t, ALU.mult).then_inc(sVE.h, 1)
                sq_ve = sVE.inc(1)
                ve2.wait_ge(sVE.h, sq_ve)
                ve2.tensor_reduce(ssq, araw.rearrange("p (h m) -> p h m", m=128),
                                  AX.X, ALU.add).then_inc(sVE.h, 1)
                ssq_ve = sVE.inc(1)
                ve2.wait_ge(sVE.h, ssq_ve)
                ve2.tensor_scalar(ssq, ssq, 1.0 / 128.0, LN_EPS, ALU.mult,
                                  ALU.add).then_inc(sVE.h, 1)
                ssq2_ve = sVE.inc(1)
                ab_ = sAC.n
                ac2.wait_ge(sVE.h, ssq2_ve)
                ac2.activation(rst, ssq, AF.Ln).then_inc(sAC.h, 1)
                ac2.wait_ge(sAC.h, ab_ + 1)
                ac2.activation(rst, rst, AF.Exp, scale=-0.5).then_inc(sAC.h, 1)
                sAC.inc(2)
                rst_ac = sAC.n
                for h in range(16):
                    hs = slice(h * 128, (h + 1) * 128)
                    if h == 0:
                        ve2.wait_ge(sAC.h, rst_ac)
                        ve2.wait_ge(sVE.h, nsg_ve)
                    ve2.scalar_tensor_tensor(o_t[:, hs], o_t[:, hs],
                                             rst[:, h:h + 1], nsg[:, hs],
                                             ALU.mult, ALU.mult)
                ve2.engine_nop().then_inc(sVE.h, 1)
                yo_ve = sVE.inc(1)

                gp2.wait_ge(sVE.h, yo_ve)
                gp2.dma_start(yo_s[:, cc * 128:(cc + 1) * 128, :]
                              .rearrange("h t m -> t h m"),
                              o_t.rearrange("p (h m) -> p h m", m=128))\
                    .then_inc(sGD.h, 16)
                yo_gd = sGD.inc(16)

            p.add("sync", lambda e, r=sy2: r.replay(e))
            p.add("tensor", lambda e, r=te2: r.replay(e))
            p.add("vector", lambda e, r=ve2: r.replay(e))
            p.add("scalar", lambda e, r=ac2: r.replay(e))
            p.add("gpsimd", lambda e, r=gp2: r.replay(e))

        if stop_after not in ("s0", "s1", "s2"):
            sy3, te3, ve3, ac3, gp3 = Rec(), Rec(), Rec(), Rec(), Rec()
            sy3.wait_ge(sMM.h, sMM.n)
            sy3.wait_ge(sVE.h, sVE.n)
            sy3.wait_ge(sAC.h, sAC.n)
            sy3.wait_ge(sGD.h, sGD.n)

            y_fm = arA[:, 0:8192]
            ybuf3 = arA[:, 8192:10240]
            x2acc = arB[:, 0:8192]
            afm = arC[:, 0:512]
            gaRep = arRep[:, 0:2048]
            wpan3 = [arW[:, 0:2048], arW[:, 2048:4096]]

            sy3.dma_start(gaRep, mod_s[2:3, :].broadcast_to([128, D]))\
                .then_inc(sLD.h, 16)
            sLD.inc(16)
            ga_ld = sLD.n

            bank3 = {i: None for i in range(8)}

            def bw3(i):
                if bank3[i]:
                    s_, n_ = bank3[i]
                    te3.wait_ge(s_.h, n_)

            pan3_mm = {}
            x2st_gd = 0
            for gg in range(8):
                sy3.wait_ge(sVE.h, sVE.n)
                sy3.wait_ge(sMM.h, sMM.n)
                if x2st_gd:
                    sy3.wait_ge(sGD.h, x2st_gd)
                for t4 in range(4):
                    sy3.dma_start(x2acc[:, t4 * 2048:(t4 + 1) * 2048],
                                  x_in[gg * 512 + t4 * 128:
                                       gg * 512 + (t4 + 1) * 128, :])\
                        .then_inc(sLD.h, 16)
                    sLD.inc(16)
                    sy3.wait_ge(sMM.h, sMM.n)
                    sy3.dma_start(
                        ybuf3.rearrange("p (h m) -> p h m", m=128),
                        yo_s[:, gg * 512 + t4 * 128:gg * 512 + (t4 + 1) * 128, :]
                        .rearrange("h t m -> t h m")).then_inc(sLD.h, 16)
                    yld = sLD.inc(16)
                    for dc in range(16):
                        bk = dc % 2
                        te3.wait_ge(sLD.h, yld)
                        bw3(bk)
                        te3.matmul(pb[bk][:, 0:128],
                                   ybuf3[:, dc * 128:(dc + 1) * 128],
                                   identf[:], is_transpose=True,
                                   start=True, stop=True).then_inc(sMM.h, 1)
                        wm = sMM.inc(1)
                        ve3.wait_ge(sMM.h, wm)
                        ve3.tensor_copy(y_fm[:, dc * 512 + t4 * 128:
                                             dc * 512 + (t4 + 1) * 128],
                                        pb[bk][:, 0:128]).then_inc(sVE.h, 1)
                        bank3[bk] = (sVE, sVE.inc(1))
                yfm_ve = sVE.n

                for jc in range(16):
                    slot = jc % 2
                    if pan3_mm.get(slot):
                        sy3.wait_ge(sMM.h, pan3_mm[slot])
                    sy3.dma_start(wpan3[slot],
                                  wo_t[jc].rearrange("p dc j -> p (dc j)"))\
                        .then_inc(sLD.h, 16)
                    wld = sLD.inc(16)

                    bk = 4
                    te3.wait_ge(sLD.h, wld)
                    if jc == 0:
                        te3.wait_ge(sVE.h, yfm_ve)
                        te3.wait_ge(sLD.h, ga_ld)
                    bw3(bk)
                    for dc in range(16):
                        mi = te3.matmul(pb[bk][:, 0:512],
                                        wpan3[slot][:, dc * 128:(dc + 1) * 128],
                                        y_fm[:, dc * 512:(dc + 1) * 512],
                                        start=(dc == 0), stop=(dc == 15))
                    mi.then_inc(sMM.h, 1)
                    wm = sMM.inc(1)
                    pan3_mm[slot] = wm
                    ve3.wait_ge(sMM.h, wm)
                    ve3.tensor_copy(afm, pb[bk][:, 0:512]).then_inc(sVE.h, 1)
                    afm_ve = sVE.inc(1)
                    bank3[bk] = (sVE, afm_ve)

                    for t4 in range(4):
                        bk2 = 5 + (t4 % 2)
                        te3.wait_ge(sVE.h, afm_ve)
                        bw3(bk2)
                        te3.matmul(pb[bk2][:, 0:128],
                                   afm[:, t4 * 128:(t4 + 1) * 128],
                                   identf[:], is_transpose=True,
                                   start=True, stop=True).then_inc(sMM.h, 1)
                        wm2 = sMM.inc(1)
                        ve3.wait_ge(sMM.h, wm2)
                        ve3.tensor_tensor(o_t[:, 0:128] if False else
                                          arC[:, 512:640],
                                          pb[bk2][:, 0:128],
                                          gaRep[:, jc * 128:(jc + 1) * 128],
                                          ALU.mult)
                        ve3.tensor_tensor(
                            x2acc[:, t4 * 2048 + jc * 128:
                                  t4 * 2048 + (jc + 1) * 128],
                            x2acc[:, t4 * 2048 + jc * 128:
                                  t4 * 2048 + (jc + 1) * 128],
                            arC[:, 512:640], ALU.add).then_inc(sVE.h, 1)
                        bank3[bk2] = (sVE, sVE.inc(1))

                gp3.wait_ge(sVE.h, sVE.n)
                for t4 in range(4):
                    gp3.dma_start(x2_s[gg * 512 + t4 * 128:
                                       gg * 512 + (t4 + 1) * 128, :],
                                  x2acc[:, t4 * 2048:(t4 + 1) * 2048])\
                        .then_inc(sGD.h, 16)
                    sGD.inc(16)
                x2st_gd = sGD.n

            p.add("sync", lambda e, r=sy3: r.replay(e))
            p.add("tensor", lambda e, r=te3: r.replay(e))
            p.add("vector", lambda e, r=ve3: r.replay(e))
            p.add("scalar", lambda e, r=ac3: r.replay(e))
            p.add("gpsimd", lambda e, r=gp3: r.replay(e))

        if stop_after is None:
            sy4, te4, ve4, ac4, gp4 = Rec(), Rec(), Rec(), Rec(), Rec()
            sy4.wait_ge(sMM.h, sMM.n)
            sy4.wait_ge(sVE.h, sVE.n)
            sy4.wait_ge(sAC.h, sAC.n)
            sy4.wait_ge(sGD.h, sGD.n)

            def _addr(tname):
                for a in nc.m.functions[0].allocations:
                    if not hasattr(a, "memorylocations"):
                        continue
                    ml = a.memorylocations[0]
                    if str(ml.name) == tname:
                        return ml.addr
                raise KeyError(tname)

            baseA = _addr("arA")
            def ov(name, shape, dtype, off):
                return nc.alloc_sbuf_tensor_at(name, shape, dtype,
                                               offset=baseA + off).ap()

            KB = 1024
            xq_fm = ov("s3_xq", [128, 16384], BF16, 0)
            ygbuf = ov("s3_yg", [128, 4096], F32, 32 * KB)
            rX0 = ov("s3_x2a", [128, 2048], F32, 48 * KB)
            rX1 = ov("s3_x2b", [128, 2048], F32, 56 * KB)
            rXn = ov("s3_xn", [128, 2048], F32, 64 * KB)
            tacc = ov("s3_t", [128, 16384], F32, 72 * KB)
            dnwRep = ov("s3_dnw", [128, 5632], F32, 72 * KB)
            wbuf4 = ov("s3_wb", [128, 8192], BF16, 94 * KB)
            h3buf = ov("s3_h3", [128, 5632], F32, 158 * KB)
            h3bufB = ov("s3_h3B", [128, 5632], F32, 94 * KB)
            scrT = ov("s3_scr", [128, 176], F32, 152 * KB)
            sscr = scrT[:, 0:88]
            ascr2 = scrT[:, 88:176]
            xq2buf = ov("s3_xq2", [128, 5632], BF16, 32 * KB)
            xq2I = ov("s3_xq2I", [128, 11264], BF16, 0)
            dwbuf = ov("s3_dw", [128, 11264], BF16, 22 * KB)
            silb = [ov("s3_sil0", [128, 512], F32, 48 * KB),
                    ov("s3_sil1", [128, 512], F32, 50 * KB)]
            h3blk = [ov("s3_h3b0", [128, 512], F32, 52 * KB),
                     ov("s3_h3b1", [128, 512], F32, 54 * KB)]
            maskb = ov("s3_mask", [128, 5632], F32, 48 * KB)
            x2j = ov("s3_x2j", [128, 2048], F32, 48 * KB)
            outj = ov("s3_outj", [128, 2048], F32, 56 * KB)

            cb1R = ov("s3_cb1", [128, 2048], F32, 136 * KB)
            sbR = ov("s3_sb", [128, 2048], F32, 144 * KB)
            gnwR = ov("s3_gnw", [128, 2048], F32, 183 * KB)
            gbR = ov("s3_gb", [128, 2048], F32, 191 * KB)

            sy4.dma_start(cb1R, mod_s[4:5, :].broadcast_to([128, D]))\
                .then_inc(sLD.h, 16)
            sy4.dma_start(sbR, mod_s[3:4, :].broadcast_to([128, D]))\
                .then_inc(sLD.h, 16)
            sy4.dma_start(gnwR, gnw_in.broadcast_to([128, D])).then_inc(sLD.h, 16)
            sy4.dma_start(gbR, mod_s[5:6, :].broadcast_to([128, D]))\
                .then_inc(sLD.h, 16)
            sLD.inc(64)
            reps_ld = sLD.n
            ve4.wait_ge(sVE.h, sVE.n)
            ve4.wait_ge(sLD.h, reps_ld)
            ve4.tensor_single_scalar(cb1R, cb1R, 1.0, ALU.add).then_inc(sVE.h, 1)
            b0 = sVE.inc(1)
            ve4.wait_ge(sVE.h, b0)
            ve4.reciprocal(tiny[:, 40:41], tiny[:, 0:1])
            ve4.reciprocal(tiny[:, 41:42], tiny[:, 2:3]).then_inc(sVE.h, 1)
            sVE.inc(1)
            mh_ve = sVE.n

            bank4f = {i: None for i in range(8)}

            def bw4(i):
                if bank4f[i]:
                    s_, n_ = bank4f[i]
                    te4.wait_ge(s_.h, n_)

            LAD = [128.0, 64.0, 32.0, 16.0, 8.0, 4.0, 2.0, 1.0]

            def ladder(buf, n, scol):
                ve4.tensor_scalar(maskb[:, 0:n], buf, scol, 128.5,
                                  ALU.mult, ALU.add)
                ve4.tensor_copy(rXn[:, 0:n] if n <= 2048 else h3buf[:, 0:0],
                                maskb[:, 0:n]) if False else None
                for k in LAD:
                    ve4.tensor_single_scalar(buf, maskb[:, 0:n], k, ALU.is_ge)
                    ve4.scalar_tensor_tensor(maskb[:, 0:n], buf, -k,
                                             maskb[:, 0:n], ALU.mult, ALU.add)
                raise RuntimeError("ladder needs orig")

            def ladder2(buf, n, scol, ve_sem):
                ve4.tensor_scalar(maskb[:, 0:n], buf, scol, 128.5,
                                  ALU.mult, ALU.add)
                ve4.tensor_copy(buf, maskb[:, 0:n]).then_inc(sVE.h, 1)
                ve_sem[0] = sVE.inc(1)
                for k in LAD:
                    ve4.tensor_single_scalar(maskb[:, 0:n], buf, k, ALU.is_ge)
                    ve4.scalar_tensor_tensor(buf, maskb[:, 0:n], -k, buf,
                                             ALU.mult, ALU.add)
                raise RuntimeError("unused")

            def ladder3(src, dst, n, scol):
                ve4.tensor_scalar(maskb[:, 0:n], src, scol, 128.5,
                                  ALU.mult, ALU.add)
                for k in LAD:
                    ve4.tensor_single_scalar(dst, maskb[:, 0:n], k, ALU.is_ge)
                    ve4.scalar_tensor_tensor(maskb[:, 0:n], dst, -k,
                                             maskb[:, 0:n], ALU.mult, ALU.add)
                ve4.tensor_scalar(dst, src, scol, 0.5, ALU.mult, ALU.add)
                mi = ve4.tensor_tensor(dst, dst, maskb[:, 0:n], ALU.subtract)
                return mi

            out_gd = 0
            for cc3 in range(4):
                sy4.wait_ge(sVE.h, sVE.n)
                sy4.wait_ge(sAC.h, sAC.n)
                sy4.wait_ge(sMM.h, sMM.n)
                sy4.wait_ge(sGD.h, sGD.n)

                ld_t, ve_t, ac_t = [], [], []
                for tt in range(8):
                    if tt >= 2:
                        sy4.wait_ge(sVE.h, ve_t[tt - 2])
                        sy4.wait_ge(sAC.h, ac_t[tt - 2])
                    buf = (rX0, rX1)[tt % 2]
                    sy4.dma_start(buf, x2_s[cc3 * 1024 + tt * 128:
                                            cc3 * 1024 + (tt + 1) * 128, :])\
                        .then_inc(sLD.h, 16)
                    ld_t.append(sLD.inc(16))
                    ve4.wait_ge(sLD.h, ld_t[tt])
                    ve4.tensor_reduce(stat[:, 16 + tt:17 + tt], buf,
                                      AX.X, ALU.add).then_inc(sVE.h, 1)
                    ve_t.append(sVE.inc(1))
                    ac4.wait_ge(sLD.h, ld_t[tt])
                    ac4.activation(rXn, buf, AF.Square,
                                   accum_out=stat[:, 24 + tt:25 + tt])\
                        .then_inc(sAC.h, 1)
                    ac_t.append(sAC.inc(1))

                b = sVE.n
                ve4.wait_ge(sVE.h, ve_t[-1])
                ve4.wait_ge(sAC.h, ac_t[-1])
                ve4.tensor_single_scalar(stat[:, 32:40], stat[:, 16:24],
                                         1.0 / D, ALU.mult)
                ve4.tensor_single_scalar(stat[:, 40:48], stat[:, 24:32],
                                         1.0 / D, ALU.mult).then_inc(sVE.h, 1)
                ve4.wait_ge(sVE.h, b + 1)
                ve4.tensor_tensor(stat[:, 48:56], stat[:, 32:40],
                                  stat[:, 32:40], ALU.mult).then_inc(sVE.h, 1)
                ve4.wait_ge(sVE.h, b + 2)
                ve4.scalar_tensor_tensor(stat[:, 48:56], stat[:, 40:48],
                                         LN_EPS, stat[:, 48:56],
                                         ALU.add, ALU.subtract).then_inc(sVE.h, 1)
                sVE.inc(3)
                p2v = sVE.n
                ab = sAC.n
                ac4.wait_ge(sVE.h, p2v)
                ac4.activation(stat[:, 56:64], stat[:, 48:56], AF.Ln)\
                    .then_inc(sAC.h, 1)
                ac4.wait_ge(sAC.h, ab + 1)
                ac4.activation(stat[:, 48:56], stat[:, 56:64], AF.Exp,
                               scale=-0.5).then_inc(sAC.h, 1)
                sAC.inc(2)
                p2a = sAC.n
                b = sVE.n
                ve4.wait_ge(sAC.h, p2a)
                ve4.tensor_tensor(stat[:, 56:64], stat[:, 32:40],
                                  stat[:, 48:56], ALU.mult).then_inc(sVE.h, 1)
                ve4.wait_ge(sVE.h, b + 1)
                ve4.tensor_single_scalar(stat[:, 56:64], stat[:, 56:64],
                                         -1.0, ALU.mult).then_inc(sVE.h, 1)
                sVE.inc(2)
                rstd1_ve = sVE.n

                for tt in range(8):
                    buf = (rX0, rX1)[tt % 2]
                    sy4.wait_ge(sVE.h, sVE.n)
                    sy4.dma_start(buf, x2_s[cc3 * 1024 + tt * 128:
                                            cc3 * 1024 + (tt + 1) * 128, :])\
                        .then_inc(sLD.h, 16)
                    w = sLD.inc(16)
                    ac4.wait_ge(sLD.h, w)
                    ac4.wait_ge(sVE.h, rstd1_ve)
                    ac4.activation(rXn, buf, AF.Identity,
                                   bias=stat[:, 56 + tt:57 + tt],
                                   scale=stat[:, 48 + tt:49 + tt])\
                        .then_inc(sAC.h, 1)
                    wa = sAC.inc(1)
                    ve4.wait_ge(sAC.h, wa)
                    ve4.tensor_tensor(buf, rXn, cb1R, ALU.mult)
                    ve4.tensor_tensor(buf, buf, sbR, ALU.add).then_inc(sVE.h, 1)
                    h2_ve = sVE.inc(1)
                    ac4.wait_ge(sVE.h, h2_ve)
                    ac4.activation(rXn, buf, AF.Square,
                                   accum_out=stat[:, 24 + tt:25 + tt])\
                        .then_inc(sAC.h, 1)
                    sAC.inc(1)
                    ve4.tensor_tensor(tacc[:, tt * 2048:(tt + 1) * 2048],
                                      buf, gnwR, ALU.mult).then_inc(sVE.h, 1)
                    t_ve = sVE.inc(1)
                    ve4.wait_ge(sVE.h, t_ve)
                    ve4.tensor_reduce(stat[:, 16 + tt:17 + tt],
                                      tacc[:, tt * 2048:(tt + 1) * 2048],
                                      AX.X, ALU.max, apply_absolute_value=True)\
                        .then_inc(sVE.h, 1)
                    sVE.inc(1)
                c_done_ve = sVE.n
                c_done_ac = sAC.n

                b = sVE.n
                ve4.wait_ge(sVE.h, c_done_ve)
                ve4.wait_ge(sAC.h, c_done_ac)
                ve4.tensor_single_scalar(stat[:, 40:48], stat[:, 24:32],
                                         1.0 / D, ALU.mult).then_inc(sVE.h, 1)
                ve4.wait_ge(sVE.h, b + 1)
                ve4.tensor_single_scalar(stat[:, 40:48], stat[:, 40:48],
                                         LN_EPS, ALU.add).then_inc(sVE.h, 1)
                sVE.inc(2)
                dv = sVE.n
                ab = sAC.n
                ac4.wait_ge(sVE.h, dv)
                ac4.activation(stat[:, 48:56], stat[:, 40:48], AF.Ln)\
                    .then_inc(sAC.h, 1)
                ac4.wait_ge(sAC.h, ab + 1)
                ac4.activation(stat[:, 40:48], stat[:, 48:56], AF.Exp,
                               scale=-0.5).then_inc(sAC.h, 1)
                sAC.inc(2)
                da = sAC.n
                b = sVE.n
                ve4.wait_ge(sAC.h, da)
                ve4.tensor_tensor(stat[:, 32:40], stat[:, 16:24],
                                  stat[:, 40:48], ALU.mult).then_inc(sVE.h, 1)
                ve4.wait_ge(sVE.h, b + 1)
                ve4.tensor_single_scalar(stat[:, 32:40], stat[:, 32:40],
                                         1e-5, ALU.max).then_inc(sVE.h, 1)
                ve4.wait_ge(sVE.h, b + 2)
                ve4.reciprocal(stat[:, 48:56], stat[:, 32:40]).then_inc(sVE.h, 1)
                ve4.wait_ge(sVE.h, b + 3)
                ve4.tensor_tensor(stat[:, 56:64], stat[:, 40:48],
                                  stat[:, 48:56], ALU.mult).then_inc(sVE.h, 1)
                ve4.wait_ge(sVE.h, b + 4)
                ve4.tensor_single_scalar(stat[:, 56:64], stat[:, 56:64],
                                         127.0, ALU.mult)
                ve4.wait_ge(sVE.h, mh_ve)
                ve4.tensor_single_scalar(stat[:, 32:40], stat[:, 32:40],
                                         1.0 / 127.0, ALU.mult).then_inc(sVE.h, 1)
                ve4.wait_ge(sVE.h, b + 5)
                for tt in range(8):
                    mi = ve4.tensor_single_scalar(
                        stat[:, 32 + tt:33 + tt], stat[:, 32 + tt:33 + tt],
                        tiny[:, 40:41], ALU.mult)
                mi.then_inc(sVE.h, 1)
                sVE.inc(6)
                d_done = sVE.n

                for tt in range(8):
                    ve4.wait_ge(sVE.h, d_done)
                    ve4.wait_ge(sMM.h, sMM.n)
                    mi = ladder3(tacc[:, tt * 2048:(tt + 1) * 2048],
                                 rXn, 2048, stat[:, 56 + tt:57 + tt])
                    mi.then_inc(sVE.h, 1)
                    lv = sVE.inc(1)
                    for dc in range(16):
                        bk = dc % 2
                        te4.wait_ge(sVE.h, lv)
                        bw4(bk)
                        te4.matmul(pb[bk][:, 0:128],
                                   rXn[:, dc * 128:(dc + 1) * 128],
                                   identf[:], is_transpose=True,
                                   start=True, stop=True).then_inc(sMM.h, 1)
                        wm = sMM.inc(1)
                        ac4.wait_ge(sMM.h, wm)
                        ac4.activation(
                            xq_fm[:, dc * 1024 + tt * 128:
                                  dc * 1024 + (tt + 1) * 128],
                            pb[bk][:, 0:128], AF.Identity).then_inc(sAC.h, 1)
                        bank4f[bk] = (sAC, sAC.inc(1))
                xq_ac = sAC.n

                sy4.wait_ge(sMM.h, sMM.n)
                sy4.dma_start(dnwRep, dnw_in.broadcast_to([128, INTER]))\
                    .then_inc(sLD.h, 16)
                dnw_ld = sLD.inc(16)
                ve4.wait_ge(sLD.h, dnw_ld)
                h3_spill_gd = {0: 0, 1: 0}
                wld_prev = 0
                for jp in range(11):
                    sy4.wait_ge(sMM.h, sMM.n)
                    for b_ in range(4):
                        sy4.dma_start(
                            wbuf4.rearrange("p (dc bj) -> p dc bj", dc=16)
                            [:, :, b_ * 128:(b_ + 1) * 128],
                            gwqT_s[:, jp * 4 + b_]
                            .rearrange("dc p j -> p dc j")).then_inc(sLD.h, 16)
                        sLD.inc(16)
                    wld = sLD.n
                    for tt in range(8):
                        bk = tt % 2
                        te4.wait_ge(sLD.h, wld)
                        if jp == 0 and tt == 0:
                            te4.wait_ge(sAC.h, xq_ac)
                        bw4(bk)
                        for dc in range(16):
                            mi = te4.matmul(
                                pb[bk][:, 0:512],
                                xq_fm[:, dc * 1024 + tt * 128:
                                      dc * 1024 + (tt + 1) * 128],
                                wbuf4[:, dc * 512:(dc + 1) * 512],
                                start=(dc == 0), stop=(dc == 15))
                        mi.then_inc(sMM.h, 1)
                        wm = sMM.inc(1)
                        ac4.wait_ge(sMM.h, wm)
                        ac4.activation(ygbuf[:, tt * 512:(tt + 1) * 512],
                                       pb[bk][:, 0:512], AF.Identity,
                                       scale=stat[:, 32 + tt:33 + tt])\
                            .then_inc(sAC.h, 1)
                        bank4f[bk] = (sAC, sAC.inc(1))
                    sy4.wait_ge(sMM.h, sMM.n)
                    for b_ in range(4):
                        sy4.dma_start(
                            wbuf4.rearrange("p (dc bj) -> p dc bj", dc=16)
                            [:, :, b_ * 128:(b_ + 1) * 128],
                            gwqT_s[:, 44 + jp * 4 + b_]
                            .rearrange("dc p j -> p dc j")).then_inc(sLD.h, 16)
                        sLD.inc(16)
                    wldu = sLD.n
                    for tt in range(8):
                        bk = 2 + tt % 2
                        te4.wait_ge(sLD.h, wldu)
                        bw4(bk)
                        for dc in range(16):
                            mi = te4.matmul(
                                pb[bk][:, 0:512],
                                xq_fm[:, dc * 1024 + tt * 128:
                                      dc * 1024 + (tt + 1) * 128],
                                wbuf4[:, dc * 512:(dc + 1) * 512],
                                start=(dc == 0), stop=(dc == 15))
                        mi.then_inc(sMM.h, 1)
                        wmu = sMM.inc(1)
                        slot = tt % 2
                        ac4.wait_ge(sGD.h, out_gd)
                        ac4.activation(silb[slot], ygbuf[:, tt * 512:(tt + 1) * 512],
                                       AF.Silu).then_inc(sAC.h, 1)
                        sil_ac = sAC.inc(1)
                        ve4.wait_ge(sMM.h, wmu)
                        ve4.wait_ge(sAC.h, sil_ac)
                        if h3_spill_gd[slot]:
                            ve4.wait_ge(sGD.h, h3_spill_gd[slot])
                        ve4.scalar_tensor_tensor(h3blk[slot], pb[bk][:, 0:512],
                                                 stat[:, 32 + tt:33 + tt],
                                                 silb[slot],
                                                 ALU.mult, ALU.mult)\
                            .then_inc(sVE.h, 1)
                        h3_ve = sVE.inc(1)
                        bank4f[bk] = (sVE, h3_ve)
                        ac4.wait_ge(sVE.h, h3_ve)
                        ac4.activation(silb[slot], h3blk[slot], AF.Square,
                                       accum_out=sscr[:, tt * 11 + jp:
                                                      tt * 11 + jp + 1])\
                            .then_inc(sAC.h, 1)
                        sq_ac = sAC.inc(1)
                        ve4.wait_ge(sAC.h, sq_ac)
                        ve4.tensor_tensor(h3blk[slot], h3blk[slot],
                                          dnwRep[:, jp * 512:(jp + 1) * 512],
                                          ALU.mult).then_inc(sVE.h, 1)
                        dnw_ve = sVE.inc(1)
                        ve4.wait_ge(sVE.h, dnw_ve)
                        ve4.tensor_reduce(ascr2[:, tt * 11 + jp:
                                                tt * 11 + jp + 1],
                                          h3blk[slot], AX.X, ALU.max,
                                          apply_absolute_value=True)\
                            .then_inc(sVE.h, 1)
                        sVE.inc(1)
                        gp4.wait_ge(sVE.h, dnw_ve)
                        gp4.dma_start(h3_s[(cc3 % 2) * 8 + tt, jp], h3blk[slot])\
                            .then_inc(sGD.h, 16)
                        out_gd = sGD.inc(16)
                        h3_spill_gd[slot] = out_gd
                f_ve, f_ac, f_gd = sVE.n, sAC.n, sGD.n

                b = sVE.n
                ve4.wait_ge(sVE.h, f_ve)
                ve4.wait_ge(sAC.h, f_ac)
                ve4.tensor_reduce(stat[:, 24:32],
                                  sscr.rearrange("p (t j) -> p t j", j=11),
                                  AX.X, ALU.add).then_inc(sVE.h, 1)
                ve4.wait_ge(sVE.h, b + 1)
                ve4.tensor_reduce(stat[:, 16:24],
                                  ascr2.rearrange("p (t j) -> p t j", j=11),
                                  AX.X, ALU.max).then_inc(sVE.h, 1)
                ve4.wait_ge(sVE.h, b + 2)
                ve4.tensor_scalar(stat[:, 40:48], stat[:, 24:32],
                                  1.0 / INTER, LN_EPS, ALU.mult, ALU.add)\
                    .then_inc(sVE.h, 1)
                sVE.inc(3)
                gv = sVE.n
                ab = sAC.n
                ac4.wait_ge(sVE.h, gv)
                ac4.activation(stat[:, 48:56], stat[:, 40:48], AF.Ln)\
                    .then_inc(sAC.h, 1)
                ac4.wait_ge(sAC.h, ab + 1)
                ac4.activation(stat[:, 40:48], stat[:, 48:56], AF.Exp,
                               scale=-0.5).then_inc(sAC.h, 1)
                sAC.inc(2)
                ga_ = sAC.n
                b = sVE.n
                ve4.wait_ge(sAC.h, ga_)
                ve4.tensor_tensor(stat[:, 32:40], stat[:, 16:24],
                                  stat[:, 40:48], ALU.mult).then_inc(sVE.h, 1)
                ve4.wait_ge(sVE.h, b + 1)
                ve4.tensor_single_scalar(stat[:, 32:40], stat[:, 32:40],
                                         1e-5, ALU.max).then_inc(sVE.h, 1)
                ve4.wait_ge(sVE.h, b + 2)
                ve4.reciprocal(stat[:, 48:56], stat[:, 32:40]).then_inc(sVE.h, 1)
                ve4.wait_ge(sVE.h, b + 3)
                ve4.tensor_tensor(stat[:, 56:64], stat[:, 40:48],
                                  stat[:, 48:56], ALU.mult).then_inc(sVE.h, 1)
                ve4.wait_ge(sVE.h, b + 4)
                ve4.tensor_single_scalar(stat[:, 56:64], stat[:, 56:64],
                                         127.0, ALU.mult)
                ve4.tensor_single_scalar(stat[:, 32:40], stat[:, 32:40],
                                         1.0 / 127.0, ALU.mult).then_inc(sVE.h, 1)
                ve4.wait_ge(sVE.h, b + 5)
                for tt in range(8):
                    mi = ve4.tensor_single_scalar(
                        stat[:, 32 + tt:33 + tt], stat[:, 32 + tt:33 + tt],
                        tiny[:, 41:42], ALU.mult)
                mi.then_inc(sVE.h, 1)
                sVE.inc(6)
                g_done = sVE.n

                xq2_gd = 0
                hslots = [h3buf, h3bufB]
                lad_done = [0, 0]
                sy4.wait_ge(sGD.h, f_gd)
                sy4.wait_ge(sVE.h, sVE.n)
                sy4.wait_ge(sMM.h, sMM.n)
                sy4.dma_start(hslots[0].rearrange("p (j f) -> p j f", f=512),
                              h3_s[(cc3 % 2) * 8]
                              .rearrange("j p f -> p j f")).then_inc(sLD.h, 16)
                h3_ld = [sLD.inc(16), 0]
                for tt in range(8):
                    if tt + 1 < 8:
                        nsl = (tt + 1) % 2
                        if lad_done[nsl]:
                            sy4.wait_ge(sVE.h, lad_done[nsl])
                        sy4.dma_start(
                            hslots[nsl].rearrange("p (j f) -> p j f", f=512),
                            h3_s[(cc3 % 2) * 8 + tt + 1]
                            .rearrange("j p f -> p j f")).then_inc(sLD.h, 16)
                        h3_ld[nsl] = sLD.inc(16)
                    ve4.wait_ge(sLD.h, h3_ld[tt % 2])
                    ve4.wait_ge(sMM.h, sMM.n)
                    mi = ladder3(hslots[tt % 2], tacc[:, 0:5632], 5632,
                                 stat[:, 56 + tt:57 + tt])
                    mi.then_inc(sVE.h, 1)
                    lv = sVE.inc(1)
                    lad_done[tt % 2] = lv
                    for jc in range(44):
                        bk = 4 + jc % 2
                        te4.wait_ge(sVE.h, lv)
                        bw4(bk)
                        te4.matmul(pb[bk][:, 0:128],
                                   tacc[:, jc * 128:(jc + 1) * 128],
                                   identf[:], is_transpose=True,
                                   start=True, stop=True).then_inc(sMM.h, 1)
                        wm = sMM.inc(1)
                        ac4.wait_ge(sMM.h, wm)
                        if jc == 0 and xq2_gd:
                            ac4.wait_ge(sGD.h, xq2_gd)
                        ac4.activation(xq2buf[:, jc * 128:(jc + 1) * 128],
                                       pb[bk][:, 0:128], AF.Identity)\
                            .then_inc(sAC.h, 1)
                        bank4f[bk] = (sAC, sAC.inc(1))
                    gp4.wait_ge(sAC.h, sAC.n)
                    gp4.dma_start(xq2_s[(cc3 % 2) * 8 + tt]
                                  .rearrange("jc p t -> p jc t"),
                                  xq2buf.rearrange("p (jc t) -> p jc t", t=128))\
                        .then_inc(sGD.h, 16)
                    xq2_gd = sGD.inc(16)

                for jcg in range(4):
                    sy4.wait_ge(sGD.h, xq2_gd)
                    if jcg > 0:
                        sy4.wait_ge(sMM.h, sMM.n)
                    for t_ in range(8):
                        sy4.dma_start(
                            xq2I.rearrange("p (t jcf) -> p t jcf", t=8)[:, t_],
                            xq2_s[(cc3 % 2) * 8 + t_, jcg * 11:(jcg + 1) * 11]
                            .rearrange("jc p f -> p jc f")).then_inc(sLD.h, 16)
                        sLD.inc(16)
                    xld = sLD.n
                    for dbh in range(2):
                        sy4.wait_ge(sMM.h, sMM.n)
                        for j_ in range(11):
                            sy4.dma_start(
                                dwbuf.rearrange("p (jc bf) -> p jc bf",
                                                jc=11)[:, j_],
                                dwqT_s[jcg * 11 + j_, dbh * 8:(dbh + 1) * 8]
                                .rearrange("b p f -> p b f")).then_inc(sLD.h, 16)
                            sLD.inc(16)
                        dld = sLD.n
                        for tt in range(8):
                            for db in range(2):
                                bk = (tt * 2 + db) % 8
                                te4.wait_ge(sLD.h, max(xld, dld))
                                bw4(bk)
                                for j_ in range(11):
                                    mi = te4.matmul(
                                        pb[bk][:, 0:512],
                                        xq2I[:, (tt * 11 + j_) * 128:
                                             (tt * 11 + j_ + 1) * 128],
                                        dwbuf[:, j_ * 1024 + db * 512:
                                              j_ * 1024 + (db + 1) * 512],
                                        start=(j_ == 0), stop=(j_ == 10))
                                mi.then_inc(sMM.h, 1)
                                wm = sMM.inc(1)
                                dsl = tacc[:, tt * 2048 + dbh * 1024 +
                                           db * 512:
                                           tt * 2048 + dbh * 1024 +
                                           (db + 1) * 512]
                                ve4.wait_ge(sMM.h, wm)
                                if jcg == 0:
                                    ve4.tensor_copy(dsl, pb[bk][:, 0:512])\
                                        .then_inc(sVE.h, 1)
                                else:
                                    ve4.tensor_tensor(dsl, dsl,
                                                      pb[bk][:, 0:512],
                                                      ALU.add).then_inc(sVE.h, 1)
                                bank4f[bk] = (sVE, sVE.inc(1))
                i_ve = sVE.n

                for tt in range(8):
                    sy4.wait_ge(sVE.h, sVE.n)
                    sy4.dma_start(x2j, x2_s[cc3 * 1024 + tt * 128:
                                            cc3 * 1024 + (tt + 1) * 128, :])\
                        .then_inc(sLD.h, 16)
                    w = sLD.inc(16)
                    ve4.wait_ge(sVE.h, i_ve)
                    ve4.wait_ge(sLD.h, w)
                    ve4.scalar_tensor_tensor(outj,
                                             tacc[:, tt * 2048:(tt + 1) * 2048],
                                             stat[:, 32 + tt:33 + tt],
                                             gbR, ALU.mult, ALU.mult)
                    ve4.tensor_tensor(outj, outj, x2j, ALU.add).then_inc(sVE.h, 1)
                    ov_ = sVE.inc(1)
                    sy4.wait_ge(sVE.h, ov_)
                    sy4.dma_start(out_hbm[cc3 * 1024 + tt * 128:
                                          cc3 * 1024 + (tt + 1) * 128, :],
                                  outj).then_inc(sST.h, 16)
                    sST.inc(16)

            def s3_fin(sync, st=sST.n):
                sync.wait_ge(sST.h, st)
            p.add("sync", lambda e, r=sy4: r.replay(e))
            p.add("tensor", lambda e, r=te4: r.replay(e))
            p.add("vector", lambda e, r=ve4: r.replay(e))
            p.add("scalar", lambda e, r=ac4: r.replay(e))
            p.add("gpsimd", lambda e, r=gp4: r.replay(e))
            p.add("sync", s3_fin)

        if debug:
            dbg_stat = nc.dram_tensor("dbg_stat", [128, 64], F32,
                                      kind="ExternalOutput").ap()
            dbg_tiny = nc.dram_tensor("dbg_tiny", [128, 64], F32,
                                      kind="ExternalOutput").ap()
            dbg_modc = nc.dram_tensor("dbg_modc", [128, 96], F32,
                                      kind="ExternalOutput").ap()
            dbg_modt = nc.dram_tensor("dbg_modt", [96, 128], F32,
                                      kind="ExternalOutput").ap()

            def dbg_dump(g, ve=sVE.n, mm=sMM.n):
                g.wait_ge(sVE.h, ve)
                g.wait_ge(sMM.h, mm)
                g.dma_start(dbg_stat, stat[:]).then_inc(sGD.h, 16)
                g.dma_start(dbg_tiny, tiny[:]).then_inc(sGD.h, 16)
                g.dma_start(dbg_modc, modcol[:]).then_inc(sGD.h, 16)
                g.dma_start(dbg_modt, arC[:96, 4352:4480]).then_inc(sGD.h, 16)
            p.add("gpsimd", dbg_dump)
            sGD.inc(64)

        if stop_after in ("s0", "s1", "s2", "s2b"):
            st_at = sST.inc(16 * NT)

            def fin_copy(sync, gd=sGD.n, ve=sVE.n, ac=sAC.n, mm=sMM.n):
                sync.wait_ge(sGD.h, gd)
                sync.wait_ge(sVE.h, ve)
                sync.wait_ge(sAC.h, ac)
                sync.wait_ge(sMM.h, mm)
                for tt in range(NT):
                    sync.dma_start(out_hbm[tt * 128:(tt + 1) * 128, :],
                                   x_in[tt * 128:(tt + 1) * 128, :])\
                        .then_inc(sST.h, 16)
                sync.wait_ge(sST.h, sST.n)
            p.add("sync", fin_copy)

        with nc.Block() as block:
            @block.sync
            def _(sync):
                for fn in p.ops["sync"]:
                    fn(sync)

            @block.tensor
            def _(t):
                for fn in p.ops["tensor"]:
                    fn(t)

            @block.vector
            def _(v):
                for fn in p.ops["vector"]:
                    fn(v)

            @block.scalar
            def _(a):
                for fn in p.ops["scalar"]:
                    fn(a)

            @block.gpsimd
            def _(g):
                for fn in p.ops["gpsimd"]:
                    fn(g)

    return nc



def kernel(**inputs):
    x = np.ascontiguousarray(inputs["L0_attn_hidden_states"], dtype=np.float32)
    B = x.shape[0]
    nc = build_program(debug=DEBUG, stop_after=STOP_AFTER)

    common = {
        "Wq": inputs["L0_attn_Wq"], "Wk": inputs["L0_attn_Wk"],
        "Wv": inputs["L0_attn_Wv"], "Ww": inputs["L0_attn_Ww"],
        "Wo": inputs["L0_attn_Wo"],
        "cq": inputs["L0_attn_cq"], "ck": inputs["L0_attn_ck"],
        "cv": inputs["L0_attn_cv"],
        "Wf1": inputs["L0_attn_Wf1"], "Wf2": inputs["L0_attn_Wf2"],
        "Wb": inputs["L0_attn_Wb"],
        "Wg1": inputs["L0_attn_Wg1"], "Wg2": inputs["L0_attn_Wg2"],
        "bg2": np.asarray(inputs["L0_attn_bg2"], dtype=np.float32).reshape(1, -1),
        "norm_w": np.asarray(inputs["L0_attn_norm_w"], dtype=np.float32).reshape(1, -1),
        "gate_w": inputs["L0_ffn_gate_w"], "down_w": inputs["L0_ffn_down_w"],
        "gate_norm_w": np.asarray(inputs["L0_ffn_gate_norm_w"],
                                  dtype=np.float32).reshape(1, -1),
        "down_norm_w": np.asarray(inputs["L0_ffn_down_norm_w"],
                                  dtype=np.float32).reshape(1, -1),
        "wmod": inputs["L0_wmod"],
    }
    common = {k: np.ascontiguousarray(v, dtype=np.float32) for k, v in common.items()}

    in_maps = []
    for b in range(B):
        m = dict(common)
        m["x"] = x[b]
        m["cvec"] = np.ascontiguousarray(
            np.asarray(inputs["L0_c"], dtype=np.float32)[b:b + 1])
        in_maps.append(m)

    res = run_bass_kernel_spmd(nc, in_maps, core_ids=list(range(B)))
    out = np.stack([np.asarray(res.results[b]["out"]) for b in range(B)], axis=0)
    return out


# version 10
# speedup vs baseline: 1.0823x; 1.0284x over previous
import numpy as np
import ml_dtypes
import concourse.bass as bass
import concourse.mybir as mybir
from concourse.bass_utils import run_bass_kernel_spmd

F32 = mybir.dt.float32
BF16 = mybir.dt.bfloat16
AF = mybir.ActivationFunctionType
ALU = mybir.AluOpType
AX = mybir.AxisListType

T, D, H, DK = 4096, 2048, 16, 128
NT = T // 128
CT1 = 1024
NC1 = T // CT1
CC = 128
NCC = T // CC
INTER = 5632
NJB = (2 * INTER) // 512
LN_EPS = 1e-5
L2_EPS = 1e-6
SCALE = float(DK) ** -0.5

DEBUG = False
STOP_AFTER = None



class Sem:
    def __init__(self, handle):
        self.h = handle
        self.n = 0

    def inc(self, k=1):
        self.n += k
        return self.n


class Prog:
    ENGS = ("sync", "tensor", "vector", "scalar", "gpsimd")

    def __init__(self):
        self.ops = {e: [] for e in self.ENGS}

    def add(self, eng, fn):
        self.ops[eng].append(fn)


def _ap(x):
    return x.ap() if hasattr(x, "ap") else x


class _Chain:
    __slots__ = ("entry",)

    def __init__(self, entry):
        self.entry = entry

    def __getattr__(self, name):
        def call(*a, **k):
            self.entry.append((name, a, k))
            return self
        return call


class Rec:

    def __init__(self):
        self.calls = []

    def __getattr__(self, name):
        if name.startswith("_"):
            raise AttributeError(name)

        def call(*a, **k):
            entry = [(name, a, k)]
            self.calls.append(entry)
            return _Chain(entry)
        return call

    def replay(self, eng):
        for entry in self.calls:
            obj = eng
            for (name, a, k) in entry:
                obj = getattr(obj, name)(*a, **k)



def build_program(debug=False, stop_after=None):
    nc = bass.Bass("TRN2", target_bir_lowering=False,
                   detect_race_conditions=False)

    scratch_kind = "ExternalOutput" if debug else "Internal"

    x_in = nc.dram_tensor("x", [T, D], F32, kind="ExternalInput").ap()
    wq_in = nc.dram_tensor("Wq", [D, D], F32, kind="ExternalInput").ap()
    wk_in = nc.dram_tensor("Wk", [D, D], F32, kind="ExternalInput").ap()
    wv_in = nc.dram_tensor("Wv", [D, D], F32, kind="ExternalInput").ap()
    ww_in = nc.dram_tensor("Ww", [D, D], F32, kind="ExternalInput").ap()
    wo_in = nc.dram_tensor("Wo", [D, D], F32, kind="ExternalInput").ap()
    cq_in = nc.dram_tensor("cq", [D, 4], F32, kind="ExternalInput").ap()
    ck_in = nc.dram_tensor("ck", [D, 4], F32, kind="ExternalInput").ap()
    cv_in = nc.dram_tensor("cv", [D, 4], F32, kind="ExternalInput").ap()
    wf1_in = nc.dram_tensor("Wf1", [D, 128], F32, kind="ExternalInput").ap()
    wf2_in = nc.dram_tensor("Wf2", [128, D], F32, kind="ExternalInput").ap()
    wb_in = nc.dram_tensor("Wb", [D, 16], F32, kind="ExternalInput").ap()
    wg1_in = nc.dram_tensor("Wg1", [D, 128], F32, kind="ExternalInput").ap()
    wg2_in = nc.dram_tensor("Wg2", [128, D], F32, kind="ExternalInput").ap()
    bg2_in = nc.dram_tensor("bg2", [1, D], F32, kind="ExternalInput").ap()
    nw_in = nc.dram_tensor("norm_w", [1, 128], F32, kind="ExternalInput").ap()
    gw_in = nc.dram_tensor("gate_w", [2 * INTER, D], F32, kind="ExternalInput").ap()
    dw_in = nc.dram_tensor("down_w", [D, INTER], F32, kind="ExternalInput").ap()
    gnw_in = nc.dram_tensor("gate_norm_w", [1, D], F32, kind="ExternalInput").ap()
    dnw_in = nc.dram_tensor("down_norm_w", [1, INTER], F32, kind="ExternalInput").ap()
    cvec_in = nc.dram_tensor("cvec", [1, D], F32, kind="ExternalInput").ap()
    wmod_in = nc.dram_tensor("wmod", [D, 6 * D], F32, kind="ExternalInput").ap()

    out_hbm = nc.dram_tensor("out", [T, D], F32, kind="ExternalOutput").ap()

    wq_t = nc.dram_tensor("wq_t", [16, 128, 16, 128], F32, kind=scratch_kind).ap()
    wk_t = nc.dram_tensor("wk_t", [16, 128, 16, 128], F32, kind="Internal").ap()
    wv_t = nc.dram_tensor("wv_t", [16, 128, 16, 128], F32, kind="Internal").ap()
    ww_t = nc.dram_tensor("ww_t", [16, 128, 16, 128], F32, kind="Internal").ap()
    wo_t = nc.dram_tensor("wo_t", [16, 128, 16, 128], F32, kind="Internal").ap()
    gwqT_s = nc.dram_tensor("gwqT_s", [16, 88, 128, 128], BF16, kind=scratch_kind).ap()
    dwqT_s = nc.dram_tensor("dwqT_s", [44, 16, 128, 128], BF16, kind=scratch_kind).ap()
    mod_s = nc.dram_tensor("mod_s", [6, D], F32, kind=scratch_kind).ap()
    q_fm_s = nc.dram_tensor("q_fm_s", [H, DK, T], F32, kind=scratch_kind).ap()
    k_fm_s = nc.dram_tensor("k_fm_s", [H, DK, T], F32, kind=scratch_kind).ap()
    k_tm_s = nc.dram_tensor("k_tm_s", [NCC, H, CC, DK], F32, kind=scratch_kind).ap()
    v_tm_s = nc.dram_tensor("v_tm_s", [NCC, H, CC, DK], F32, kind=scratch_kind).ap()
    w_tm_s = nc.dram_tensor("w_tm_s", [NCC, H, CC, DK], F32, kind=scratch_kind).ap()
    z_tm_s = nc.dram_tensor("z_tm_s", [NCC, H, CC, DK], F32, kind=scratch_kind).ap()
    beta_s = nc.dram_tensor("beta_s", [NCC, CC, H], F32, kind=scratch_kind).ap()
    sg_s = nc.dram_tensor("sg_s", [T, D], F32, kind=scratch_kind).ap()
    yo_s = nc.dram_tensor("yo_s", [H, T, DK], F32, kind=scratch_kind).ap()
    x2_s = nc.dram_tensor("x2_s", [T, D], F32, kind=scratch_kind).ap()
    h3_s = nc.dram_tensor("h3_s", [16, NJB // 2, 128, 512], F32, kind=scratch_kind).ap()
    xq2_s = nc.dram_tensor("xq2_s", [16, 44, 128, 128], BF16, kind="Internal").ap()

    ident_f32 = nc.inline_tensor(np.eye(128, dtype=np.float32), "ident_f32")
    ident_bf16 = nc.inline_tensor(np.eye(128, dtype=ml_dtypes.bfloat16), "ident_bf16")
    ltri_np = np.triu(np.ones((128, 128), dtype=np.float32))
    ltri_f32 = nc.inline_tensor(ltri_np, "ltri_f32")
    ones_1x128 = nc.inline_tensor(np.ones((1, 128), dtype=np.float32), "ones_1x128")
    ones_128x1 = nc.inline_tensor(np.ones((128, 1), dtype=np.float32), "ones_128x1")

    p = Prog()

    import contextlib
    stack = contextlib.ExitStack()

    def sem(name):
        return Sem(stack.enter_context(nc.semaphore(name)))

    def sb(name, shape, dtype):
        return stack.enter_context(nc.sbuf_tensor(name, shape, dtype))

    def psum(name, shape, dtype=F32):
        return stack.enter_context(nc.psum_tensor(name, shape, dtype))

    with stack:
        sLD = sem("sLD")
        sGD = sem("sGD")
        sMM = sem("sMM")
        sVE = sem("sVE")
        sAC = sem("sAC")
        sST = sem("sST")

        identf = sb("identf", [128, 128], F32)
        identb = sb("identb", [128, 128], BF16)
        ltri = sb("ltri", [128, 128], F32)
        ones1r = sb("ones1r", [1, 128], F32)
        ones1c = sb("ones1c", [128, 1], F32)
        tiny = sb("tiny", [128, 64], F32)
        modcol = sb("modcol", [128, 96], F32)

        arA = sb("arA", [128, 16384], F32)
        arB = sb("arB", [128, 8192], F32)
        arC = sb("arC", [128, 6144], F32)
        arRep = sb("arRep", [128, 6144], F32)
        arW = sb("arW", [128, 4096], F32)
        arBF = sb("arBF", [128, 11264], BF16)
        stat = sb("stat", [128, 64], F32)

        pb = [psum(f"pb{i}", [128, 512]) for i in range(8)]

        def s0_consts(sync):
            sync.dma_start(identf[:], ident_f32.ap()).then_inc(sLD.h, 16)
            sync.dma_start(identb[:], ident_bf16.ap()).then_inc(sLD.h, 16)
            sync.dma_start(ltri[:], ltri_f32.ap()).then_inc(sLD.h, 16)
            sync.dma_start(ones1r[:], ones_1x128.ap()).then_inc(sLD.h, 16)
            sync.dma_start(ones1c[:], ones_128x1.ap()).then_inc(sLD.h, 16)
        p.add("sync", s0_consts)
        sLD.inc(16 * 5)
        consts_ld = sLD.n

        relayout = [(wq_in, wq_t), (wk_in, wk_t), (wv_in, wv_t),
                    (ww_in, ww_t), (wo_in, wo_t)]
        wrow = [arW[:, 0:2048], arW[:, 2048:4096]]

        def emit_relayout_iter(wi, dc):
            src, dst = relayout[wi]
            buf = wrow[(wi * 16 + dc) % 2]
            ld_at = sLD.inc(16)

            def ld(sync, src=src, dc=dc, buf=buf, pr=sGD.n):
                sync.wait_ge(sGD.h, pr)
                sync.dma_start(buf, src[dc * 128:(dc + 1) * 128, :])\
                    .then_inc(sLD.h, 16)
            p.add("sync", ld)

            st_at = sGD.inc(16)

            def st(g, dst=dst, dc=dc, buf=buf, w=ld_at):
                g.wait_ge(sLD.h, w)
                g.dma_start(dst[:, :, dc, :].rearrange("jc p j -> p jc j"),
                            buf)\
                    .then_inc(sGD.h, 16)
            p.add("gpsimd", st)

        ascr = arC[:, 5888:5888 + 104]

        def s0qA_zero(v):
            v.memset(stat[:, 0:8], 0.0)
        p.add("vector", s0qA_zero)

        qbuf = [arA[:, 0:5632], arA[:, 5632:11264]]
        ACC_G, ACC_D = 0, 1

        def emit_qA_iter(bi):
            is_gate = bi < 88
            buf = qbuf[bi % 2][:, 0:(2048 if is_gate else 5632)]
            src = (gw_in[bi * 128:(bi + 1) * 128, :] if is_gate
                   else dw_in[(bi - 88) * 128:(bi - 87) * 128, :])
            ld_at = sLD.inc(16)

            def ld(sync, src=src, buf=buf, pr=sVE.n):
                sync.wait_ge(sVE.h, pr)
                sync.dma_start(buf, src).then_inc(sLD.h, 16)
            p.add("sync", ld)

            red_at = sVE.inc(1)

            def red(v, buf=buf, w=ld_at, bi=bi):
                v.wait_ge(sLD.h, w)
                v.tensor_reduce(ascr[:, bi:bi + 1], buf, AX.X, ALU.add,
                                apply_absolute_value=True).then_inc(sVE.h, 1)
            p.add("vector", red)

        for step in range(104):
            if step < 80:
                emit_relayout_iter(step // 16, step % 16)
            emit_qA_iter(step)
        relayout_done_sG = sGD.n
        qA_ve = sVE.n

        ve_at = sVE.inc(2)

        def s0qA_comb(v, w=qA_ve):
            v.wait_ge(sVE.h, w)
            v.tensor_reduce(stat[:, 0:1], ascr[:, 0:88], AX.X, ALU.add)\
                .then_inc(sVE.h, 1)
            v.tensor_reduce(stat[:, 1:2], ascr[:, 88:104], AX.X, ALU.add)\
                .then_inc(sVE.h, 1)
        p.add("vector", s0qA_comb)
        qA_ve = sVE.n

        mm_at = sMM.inc(1)

        def s0qA_fin_mm(t, ve=qA_ve):
            t.wait_ge(sVE.h, ve)
            t.wait_ge(sLD.h, consts_ld)
            t.matmul(pb[0][:1, 0:2], ones1c[:, 0:1], stat[:, 0:2],
                     start=True, stop=True).then_inc(sMM.h, 1)
        p.add("tensor", s0qA_fin_mm)

        fin_base = sVE.n
        sVE.inc(3)

        def s0qA_fin_v(v, w=mm_at, base=fin_base):
            v.wait_ge(sMM.h, w)
            v.tensor_single_scalar(tiny[:1, 4:5], pb[0][0:1, 0:1],
                                   1.0 / (2 * INTER * D), ALU.mult)
            v.tensor_single_scalar(tiny[:1, 5:6], pb[0][0:1, 1:2],
                                   1.0 / (D * INTER), ALU.mult).then_inc(sVE.h, 1)
            v.wait_ge(sVE.h, base + 1)
            v.tensor_single_scalar(tiny[:1, 4:6], tiny[:1, 4:6], 1e-5, ALU.max)\
                .then_inc(sVE.h, 1)
            v.wait_ge(sVE.h, base + 2)
            v.reciprocal(tiny[:1, 6:8], tiny[:1, 4:6]).then_inc(sVE.h, 1)
        p.add("vector", s0qA_fin_v)

        mm_at2 = sMM.inc(1)

        def s0qA_rep_mm(t, ve=sVE.n):
            t.wait_ge(sVE.h, ve)
            t.matmul(pb[1][:, 0:2], ones1r[:1, :], tiny[:1, 6:8],
                     start=True, stop=True).then_inc(sMM.h, 1)
        p.add("tensor", s0qA_rep_mm)

        ve_at = sVE.inc(1)

        def s0qA_rep_v(v, w=mm_at2):
            v.wait_ge(sMM.h, w)
            v.tensor_copy(tiny[:, 0:1], pb[1][:, 0:1])
            v.tensor_copy(tiny[:, 2:3], pb[1][:, 1:2])
            v.tensor_single_scalar(tiny[:, 1:2], pb[1][:, 0:1], -1.0, ALU.mult)
            v.tensor_single_scalar(tiny[:, 3:4], pb[1][:, 1:2], -1.0, ALU.mult)\
                .then_inc(sVE.h, 1)
        p.add("vector", s0qA_rep_v)
        scales_ve = sVE.n

        tstage = arBF[:, 0:5632]

        def emit_qB_iter(bi):
            is_gate = bi < 88
            ncols = 2048 if is_gate else 5632
            buf = qbuf[bi % 2][:, 0:ncols]
            scr = qbuf[(bi + 1) % 2][:, 0:ncols]
            src = (gw_in[bi * 128:(bi + 1) * 128, :] if is_gate
                   else dw_in[(bi - 88) * 128:(bi - 87) * 128, :])
            sc = 0 if is_gate else 2

            ld_at = sLD.inc(16)

            def ld(sync, src=src, buf=buf, pr=sVE.n, pm=sMM.n):
                sync.wait_ge(sVE.h, max(pr, scales_ve))
                sync.wait_ge(sMM.h, pm)
                sync.dma_start(buf, src).then_inc(sLD.h, 16)
            p.add("sync", ld)

            q_at = sVE.inc(1)

            def quant(v, buf=buf, scr=scr, w=ld_at, sc=sc, pr=sGD.n,
                      pm=sMM.n, n=ncols):
                v.wait_ge(sLD.h, w)
                v.wait_ge(sGD.h, pr)
                v.wait_ge(sMM.h, pm)
                v.tensor_scalar(scr, buf,
                                tiny[:, sc:sc + 1], 0.5, ALU.mult, ALU.is_ge)
                v.tensor_scalar(buf, buf, tiny[:, sc + 1:sc + 2], 0.5,
                                ALU.mult, ALU.is_ge)
                v.tensor_tensor(buf, scr, buf,
                                ALU.subtract).then_inc(sVE.h, 1)
            p.add("vector", quant)

            nch = ncols // 128
            for g4 in range((nch + 3) // 4):
                n4 = min(4, nch - g4 * 4)
                bk = 5 + (g4 % 2)
                mm_at = sMM.inc(1)

                def trq(t, buf=buf, g4=g4, n4=n4, bk=bk, w=q_at, pe=sVE.n):
                    t.wait_ge(sVE.h, max(w, pe))
                    for i in range(n4):
                        mi = t.matmul(pb[bk][:, i * 128:(i + 1) * 128],
                                      buf[:, (g4 * 4 + i) * 128:
                                          (g4 * 4 + i + 1) * 128],
                                      identf[:], is_transpose=True,
                                      start=(i == 0), stop=(i == n4 - 1))
                    mi.then_inc(sMM.h, 1)
                p.add("tensor", trq)

                ve_at2 = sVE.inc(1)

                def evq(v, g4=g4, n4=n4, bk=bk, w=mm_at):
                    v.wait_ge(sMM.h, w)
                    v.tensor_copy(tstage[:, g4 * 512:g4 * 512 + n4 * 128],
                                  pb[bk][:, 0:n4 * 128]).then_inc(sVE.h, 1)
                p.add("vector", evq)

            st_at = sGD.inc(16)

            def st(g, bi=bi, is_gate=is_gate, w=sVE.n):
                g.wait_ge(sVE.h, w)
                if is_gate:
                    g.dma_start(gwqT_s[:, bi].rearrange("dc p j -> p dc j"),
                                tstage[:, 0:2048]).then_inc(sGD.h, 16)
                else:
                    g.dma_start(dwqT_s[:, bi - 88].rearrange("jc p d -> p jc d"),
                                tstage[:, 0:5632]).then_inc(sGD.h, 16)
            p.add("gpsimd", st)

        crow = arC[:, 0:2048]
        scrow = arC[:, 2048:4096]
        scT = arC[:, 4096:4112]

        ld_at = sLD.inc(16)

        def s0m_ldc(sync):
            sync.dma_start(crow[:1, :], cvec_in).then_inc(sLD.h, 16)
        p.add("sync", s0m_ldc)

        ac_at = sAC.inc(1)

        def s0m_silu(a, w=ld_at):
            a.wait_ge(sLD.h, w)
            a.activation(scrow[:1, :], crow[:1, :], AF.Silu).then_inc(sAC.h, 1)
        p.add("scalar", s0m_silu)

        mm_at = sMM.inc(1)

        def s0m_sct(t, w=ac_at):
            t.wait_ge(sAC.h, w)
            for dc in range(16):
                mi = t.matmul(pb[2][:, dc:dc + 1],
                              scrow[:1, dc * 128:(dc + 1) * 128],
                              ones1c[:1, 0:1],
                              start=True, stop=True)
            mi.then_inc(sMM.h, 1)
        p.add("tensor", s0m_sct)

        ve_at = sVE.inc(1)

        def s0m_sct_ev(v, w=mm_at):
            v.wait_ge(sMM.h, w)
            v.tensor_copy(scT, pb[2][:, 0:16]).then_inc(sVE.h, 1)
        p.add("vector", s0m_sct_ev)
        sct_ve = sVE.n

        wmbuf = [arB[:, 0:4096], arB[:, 4096:8192]]
        def s0m_zero(v, pr=sct_ve):
            v.wait_ge(sVE.h, pr)
            v.memset(modcol[:], 0.0)
        p.add("vector", s0m_zero)

        mm_prev = {0: 0, 1: 0}

        def emit_mod_iter(dc, half):
            buf = wmbuf[(dc * 3 + half) % 2]
            ld_at = sLD.inc(16)

            def ld(sync, dc=dc, half=half, buf=buf,
                   pr=mm_prev[(dc * 3 + half) % 2]):
                sync.wait_ge(sMM.h, pr)
                sync.dma_start(buf, wmod_in[dc * 128:(dc + 1) * 128,
                                            half * 4096:(half + 1) * 4096])\
                    .then_inc(sLD.h, 16)
            p.add("sync", ld)

            mm_at = sMM.inc(1)
            mm_prev[(dc * 3 + half) % 2] = mm_at

            def mm(t, dc=dc, half=half, buf=buf, w=ld_at, ve=sVE.n):
                t.wait_ge(sLD.h, w)
                t.wait_ge(sVE.h, ve)
                for j in range(32):
                    mi = t.matmul(pb[3][:, j:j + 1],
                                  buf[:, j * 128:(j + 1) * 128],
                                  scT[:, dc:dc + 1],
                                  start=True, stop=True)
                mi.then_inc(sMM.h, 1)
            p.add("tensor", mm)

            ve_at = sVE.inc(1)

            def acc(v, half=half, w=mm_at):
                v.wait_ge(sMM.h, w)
                v.tensor_tensor(modcol[:, half * 32:(half + 1) * 32],
                                modcol[:, half * 32:(half + 1) * 32],
                                pb[3][:, 0:32], ALU.add).then_inc(sVE.h, 1)
            p.add("vector", acc)

        mod_iters = [(dc, half) for dc in range(16) for half in range(3)]
        for step in range(104):
            if step < 48:
                emit_mod_iter(*mod_iters[step])
            emit_qB_iter(step)
        quant_done_sG = sGD.n
        quant_done_ve = sVE.n
        mod_acc_ve = sVE.n

        gfx_base = sVE.n
        sVE.inc(7)

        def s0m_gafix(v, base=gfx_base):
            sls = [modcol[:, 32:48], modcol[:, 80:96]]
            sgns = [arC[:, 4112:4128], arC[:, 4128:4144]]
            mags = [arC[:, 4144:4160], arC[:, 4160:4176]]
            for i in (0, 1):
                v.tensor_scalar(sgns[i], sls[i], 0.0, None, ALU.is_ge)
            v.engine_nop().then_inc(sVE.h, 1)
            v.wait_ge(sVE.h, base + 1)
            for i in (0, 1):
                v.tensor_scalar(sgns[i], sgns[i], 2.0, -1.0, ALU.mult, ALU.add)
            v.engine_nop().then_inc(sVE.h, 1)
            v.wait_ge(sVE.h, base + 2)
            for i in (0, 1):
                v.tensor_tensor(mags[i], sls[i], sgns[i], ALU.mult)
            v.engine_nop().then_inc(sVE.h, 1)
            v.wait_ge(sVE.h, base + 3)
            for i in (0, 1):
                v.tensor_single_scalar(mags[i], mags[i], 0.25, ALU.max)
            v.engine_nop().then_inc(sVE.h, 1)
            v.wait_ge(sVE.h, base + 4)
            for i in (0, 1):
                v.tensor_tensor(sls[i], sgns[i], mags[i], ALU.mult)\
                    .then_inc(sVE.h, 1)
            v.wait_ge(sVE.h, base + 6)
            v.engine_nop().then_inc(sVE.h, 1)
        p.add("vector", s0m_gafix)
        gafix_ve = sVE.n

        mm_at = sMM.inc(1)

        def s0m_tr(t, ve=gafix_ve):
            t.wait_ge(sVE.h, ve)
            t.transpose(pb[4][:96, 0:128], modcol[:, 0:96], identf[:])\
                .then_inc(sMM.h, 1)
        p.add("tensor", s0m_tr)

        ve_at = sVE.inc(1)

        def s0m_tr_ev(v, w=mm_at):
            v.wait_ge(sMM.h, w)
            v.tensor_copy(arC[:96, 4352:4480], pb[4][:96, 0:128]).then_inc(sVE.h, 1)
        p.add("vector", s0m_tr_ev)
        modT_ve = sVE.n

        st_at = sGD.inc(16)

        def s0m_st(g, w=modT_ve):
            g.wait_ge(sVE.h, w)
            g.dma_start(mod_s.rearrange("k (p f) -> (k p) f", p=16),
                        arC[:96, 4352:4480])\
                .then_inc(sGD.h, 16)
        p.add("gpsimd", s0m_st)
        mod_st_sG = sGD.n

        if stop_after != "s0":
            arS1 = sb("arS1", [128, 4608], F32)
            wf1res = arS1[:, 0:2048]
            wf2res = arS1[:, 2048:4096]
            wbres = arS1[:, 4096:4352]
            tails = arS1[:, 4352:4544]

            sy, te, ve, ac, gp = Rec(), Rec(), Rec(), Rec(), Rec()

            s0MM, s0VE, s0AC, s0GD = sMM.n, sVE.n, sAC.n, sGD.n
            sy.wait_ge(sMM.h, s0MM)
            sy.wait_ge(sVE.h, s0VE)
            sy.wait_ge(sAC.h, s0AC)
            sy.wait_ge(sGD.h, s0GD)

            for dc in range(16):
                sy.dma_start(wf1res[:, dc * 128:(dc + 1) * 128],
                             wf1_in[dc * 128:(dc + 1) * 128, :])\
                    .then_inc(sLD.h, 16); sLD.inc(16)
            sy.dma_start(wf2res[:, :], wf2_in[:, :]).then_inc(sLD.h, 16)
            sLD.inc(16)
            wg1res = arC[:, 3584:5632]
            for dc in range(16):
                sy.dma_start(wg1res[:, dc * 128:(dc + 1) * 128],
                             wg1_in[dc * 128:(dc + 1) * 128, :])\
                    .then_inc(sLD.h, 16); sLD.inc(16)
            for dc in range(16):
                sy.dma_start(wbres[:, dc * 16:(dc + 1) * 16],
                             wb_in[dc * 128:(dc + 1) * 128, :])\
                    .then_inc(sLD.h, 16); sLD.inc(16)
            sy.dma_start(arRep[:, 0:2048], bg2_in.broadcast_to([128, D]))\
                .then_inc(sLD.h, 16); sLD.inc(16)
            wg2res = arRep[:, 2048:4096]
            sy.dma_start(wg2res, wg2_in[:, :]).then_inc(sLD.h, 16); sLD.inc(16)
            ccA = arRep[:, 4096:4288]
            for si, cin in enumerate((cq_in, ck_in, cv_in)):
                for h in range(16):
                    sy.dma_start(ccA[:, (si * 16 + h) * 4:(si * 16 + h) * 4 + 4],
                                 cin[h * 128:(h + 1) * 128, :])\
                        .then_inc(sLD.h, 16); sLD.inc(16)
            s1p_ld = sLD.n

            ve.wait_ge(sVE.h, gafix_ve)
            ve.tensor_single_scalar(tiny[:, 8:24], modcol[:, 16:32], 1.0,
                                    ALU.add)
            ve.memset(tails[:, :], 0.0)
            ve.memset(stat[:, 16:64], 0.0).then_inc(sVE.h, 1)
            sVE.inc(1)
            s1p_ve = sVE.n

            xbuf = [arB[:, 0:2048], arB[:, 2048:4096]]
            xnbuf = [arB[:, 4096:6144], arB[:, 6144:8192]]
            qlin = arC[:, 0:1027]
            qconv = arC[:, 1027:2051]
            t12buf = arC[:, 2051:3075]
            bstage = arC[:, 3075:3203]
            qsil = [arB[:, 0:1024], arB[:, 1024:2048]]
            tokstage = arB[:, 2048:4096]
            sgstage = arB[:, 4096:6144]
            wpan = [arW[:, 0:2048], arW[:, 2048:4096]]

            bank_free = {i: 0 for i in range(8)}

            def bwait(i):
                if bank_free[i]:
                    te.wait_ge(sVE.h, bank_free[i])

            WSTREAMS = [("q", wq_t, 0, q_fm_s, None),
                        ("k", wk_t, 1, k_fm_s, k_tm_s),
                        ("v", wv_t, 2, None, v_tm_s),
                        ("w", ww_t, 2, None, w_tm_s)]

            pan_mm = {}
            qsil_free = {0: (0, 0), 1: (0, 0)}
            tok_free = 0
            sg_free = 0

            for cc1 in range(NC1):
                sy.wait_ge(sVE.h, sVE.n)
                sy.wait_ge(sAC.h, sAC.n)
                sy.wait_ge(sGD.h, sGD.n)
                sy.wait_ge(sMM.h, sMM.n)

                ld_t, ve_t, ac_t = [], [], []
                for tt in range(8):
                    if tt >= 2:
                        sy.wait_ge(sVE.h, ve_t[tt - 2])
                        sy.wait_ge(sAC.h, ac_t[tt - 2])
                    sy.dma_start(xbuf[tt % 2],
                                 x_in[cc1 * CT1 + tt * 128:
                                      cc1 * CT1 + (tt + 1) * 128, :])\
                        .then_inc(sLD.h, 16)
                    ld_t.append(sLD.inc(16))

                    ve.wait_ge(sLD.h, ld_t[tt])
                    ve.tensor_reduce(stat[:, 16 + tt:17 + tt], xbuf[tt % 2],
                                     AX.X, ALU.add).then_inc(sVE.h, 1)
                    ve_t.append(sVE.inc(1))

                    ac.wait_ge(sLD.h, ld_t[tt])
                    ac.activation(xnbuf[tt % 2], xbuf[tt % 2], AF.Square,
                                  accum_out=stat[:, 24 + tt:25 + tt])\
                        .then_inc(sAC.h, 1)
                    ac_t.append(sAC.inc(1))

                b = sVE.n
                ve.wait_ge(sVE.h, ve_t[-1])
                ve.wait_ge(sAC.h, ac_t[-1])
                ve.tensor_single_scalar(stat[:, 32:40], stat[:, 16:24],
                                        1.0 / D, ALU.mult)
                ve.tensor_single_scalar(stat[:, 40:48], stat[:, 24:32],
                                        1.0 / D, ALU.mult).then_inc(sVE.h, 1)
                ve.wait_ge(sVE.h, b + 1)
                ve.tensor_tensor(stat[:, 48:56], stat[:, 32:40],
                                 stat[:, 32:40], ALU.mult).then_inc(sVE.h, 1)
                ve.wait_ge(sVE.h, b + 2)
                ve.scalar_tensor_tensor(stat[:, 48:56], stat[:, 40:48],
                                        LN_EPS, stat[:, 48:56],
                                        ALU.add, ALU.subtract).then_inc(sVE.h, 1)
                sVE.inc(3)
                p2_ve = sVE.n

                ab = sAC.n
                ac.wait_ge(sVE.h, p2_ve)
                ac.activation(stat[:, 56:64], stat[:, 48:56], AF.Ln)\
                    .then_inc(sAC.h, 1)
                ac.wait_ge(sAC.h, ab + 1)
                ac.activation(stat[:, 48:56], stat[:, 56:64], AF.Exp,
                              scale=-0.5).then_inc(sAC.h, 1)
                sAC.inc(2)
                p2_ac = sAC.n

                b = sVE.n
                ve.wait_ge(sAC.h, p2_ac)
                ve.tensor_tensor(stat[:, 56:64], stat[:, 32:40],
                                 stat[:, 48:56], ALU.mult).then_inc(sVE.h, 1)
                ve.wait_ge(sVE.h, b + 1)
                ve.tensor_single_scalar(stat[:, 56:64], stat[:, 56:64],
                                        -1.0, ALU.mult).then_inc(sVE.h, 1)
                sVE.inc(2)
                p2_done = sVE.n

                xn_ac = []
                for tt in range(8):
                    sy.wait_ge(sVE.h, sVE.n)
                    sy.dma_start(xbuf[tt % 2],
                                 x_in[cc1 * CT1 + tt * 128:
                                      cc1 * CT1 + (tt + 1) * 128, :])\
                        .then_inc(sLD.h, 16)
                    w = sLD.inc(16)

                    ac.wait_ge(sLD.h, w)
                    ac.wait_ge(sVE.h, p2_done)
                    ac.activation(xnbuf[tt % 2], xbuf[tt % 2], AF.Identity,
                                  bias=stat[:, 56 + tt:57 + tt],
                                  scale=stat[:, 48 + tt:49 + tt])\
                        .then_inc(sAC.h, 1)
                    wa = sAC.inc(1)
                    xn_ac.append(wa)

                    for dc in range(16):
                        bk = dc % 4
                        te.wait_ge(sAC.h, wa)
                        bwait(bk)
                        te.transpose(pb[bk][:, 0:128],
                                     xnbuf[tt % 2][:, dc * 128:(dc + 1) * 128],
                                     identf[:]).then_inc(sMM.h, 1)
                        wm = sMM.inc(1)

                        ve.wait_ge(sMM.h, wm)
                        ve.tensor_scalar(
                            arA[:, dc * 1024 + tt * 128:
                                   dc * 1024 + (tt + 1) * 128],
                            pb[bk][:, 0:128],
                            tiny[:, 8 + dc:9 + dc],
                            modcol[:, dc:dc + 1],
                            ALU.mult, ALU.add).then_inc(sVE.h, 1)
                        bank_free[bk] = sVE.inc(1)
                h_ready = sVE.n

                def conv_silu_spill(snm, si, h, ci, fm_s, tm_s, ev_t):
                    tcol = (si * 16 + h) * 3
                    ve.wait_ge(sVE.h, ev_t)
                    ve.tensor_copy(qlin[:, 0:3], tails[:, tcol:tcol + 3])
                    cbase = (ci * 16 + h) * 4
                    ve.tensor_single_scalar(qconv[:, 0:1024], qlin[:, 0:1024],
                                            ccA[:, cbase:cbase + 1], ALU.mult)
                    for i in (1, 2, 3):
                        ve.scalar_tensor_tensor(qconv[:, 0:1024],
                                                qlin[:, i:i + 1024],
                                                ccA[:, cbase + i:cbase + i + 1],
                                                qconv[:, 0:1024],
                                                ALU.mult, ALU.add)
                    ve.tensor_copy(tails[:, tcol:tcol + 3], qlin[:, 1024:1027])\
                        .then_inc(sVE.h, 1)
                    conv_ve = sVE.inc(1)

                    slot = (si * 16 + h) % 2
                    gfree, mfree = qsil_free[slot]
                    ac.wait_ge(sVE.h, conv_ve)
                    if gfree:
                        ac.wait_ge(sGD.h, gfree)
                    if mfree:
                        ac.wait_ge(sMM.h, mfree)
                    ac.activation(qsil[slot], qconv[:, 0:1024], AF.Silu)\
                        .then_inc(sAC.h, 1)
                    sil_ac = sAC.inc(1)
                    return slot, sil_ac

                def spill_stream(snm, h, slot, dep, fm_s, tm_s, cc1=cc1):
                    nonlocal tok_free
                    dsem, dval = dep
                    last_g = 0
                    last_m = 0
                    if fm_s is not None:
                        gp.wait_ge(dsem.h, dval)
                        gp.dma_start(fm_s[h][:, cc1 * CT1:(cc1 + 1) * CT1],
                                     qsil[slot]).then_inc(sGD.h, 16)
                        last_g = sGD.inc(16)
                    if tm_s is not None:
                        for lc in range(8):
                            bk = lc % 4
                            te.wait_ge(dsem.h, dval)
                            bwait(bk)
                            te.transpose(pb[bk][:, 0:128],
                                         qsil[slot][:, lc * 128:(lc + 1) * 128],
                                         identf[:]).then_inc(sMM.h, 1)
                            wm = sMM.inc(1)
                            last_m = wm
                            ve.wait_ge(sMM.h, wm)
                            if lc == 0 and tok_free:
                                ve.wait_ge(sGD.h, tok_free)
                            ve.tensor_copy(tokstage[:, lc * 128:(lc + 1) * 128],
                                           pb[bk][:, 0:128]).then_inc(sVE.h, 1)
                            bank_free[bk] = sVE.inc(1)
                        gp.wait_ge(sVE.h, sVE.n)
                        gp.dma_start(
                            tm_s[cc1 * 8:(cc1 + 1) * 8, h]
                            .rearrange("cc t m -> t cc m"),
                            tokstage[:, 0:1024]).then_inc(sGD.h, 16)
                        tok_free = sGD.inc(16)
                        last_g = tok_free
                    qsil_free[slot] = (last_g, last_m)

                for si, (snm, wtil, ci, fm_s, tm_s) in enumerate(WSTREAMS):
                    for h in range(16):
                        slot = (si * 16 + h) % 2
                        if pan_mm.get(slot):
                            sy.wait_ge(sMM.h, pan_mm[slot])
                        sy.dma_start(wpan[slot],
                                     wtil[h].rearrange("p dc j -> p (dc j)"))\
                            .then_inc(sLD.h, 16)
                        wld = sLD.inc(16)

                        ev_t = 0
                        for tb in range(2):
                            bk = 4 + tb
                            te.wait_ge(sLD.h, wld)
                            if si == 0 and h == 0 and tb == 0:
                                te.wait_ge(sVE.h, h_ready)
                            bwait(bk)
                            for dc in range(16):
                                mi = te.matmul(
                                    pb[bk][:, 0:512],
                                    wpan[slot][:, dc * 128:(dc + 1) * 128],
                                    arA[:, dc * 1024 + tb * 512:
                                           dc * 1024 + (tb + 1) * 512],
                                    start=(dc == 0), stop=(dc == 15))
                            mi.then_inc(sMM.h, 1)
                            wm = sMM.inc(1)
                            pan_mm[slot] = wm

                            ve.wait_ge(sMM.h, wm)
                            ve.tensor_copy(qlin[:, 3 + tb * 512:3 + (tb + 1) * 512],
                                           pb[bk][:, 0:512]).then_inc(sVE.h, 1)
                            ev_t = sVE.inc(1)
                            bank_free[bk] = ev_t

                        slot_q, sil_ac = conv_silu_spill(snm, si, h, ci,
                                                         fm_s, tm_s, ev_t)
                        spill_stream(snm, h, slot_q, (sAC, sil_ac), fm_s, tm_s)

                def small_proj(res, outbuf):
                    ev = 0
                    for tb in range(2):
                        bk = 4 + tb
                        bwait(bk)
                        for dc in range(16):
                            mi = te.matmul(pb[bk][:, 0:512],
                                           res[:, dc * 128:(dc + 1) * 128],
                                           arA[:, dc * 1024 + tb * 512:
                                                  dc * 1024 + (tb + 1) * 512],
                                           start=(dc == 0), stop=(dc == 15))
                        mi.then_inc(sMM.h, 1)
                        wm = sMM.inc(1)
                        ve.wait_ge(sMM.h, wm)
                        ve.tensor_copy(outbuf[:, tb * 512:(tb + 1) * 512],
                                       pb[bk][:, 0:512]).then_inc(sVE.h, 1)
                        ev = sVE.inc(1)
                        bank_free[bk] = ev
                    return ev

                t1_ve = small_proj(wf1res, t12buf)
                for h2 in range(16):
                    slot = h2 % 2
                    gfree, mfree = qsil_free[slot]
                    ev = 0
                    for tb in range(2):
                        bk = 4 + tb
                        te.wait_ge(sVE.h, t1_ve)
                        bwait(bk)
                        mi = te.matmul(pb[bk][:, 0:512],
                                       wf2res[:, h2 * 128:(h2 + 1) * 128],
                                       t12buf[:, tb * 512:(tb + 1) * 512],
                                       start=True, stop=True)
                        mi.then_inc(sMM.h, 1)
                        wm = sMM.inc(1)
                        ve.wait_ge(sMM.h, wm)
                        if tb == 0 and gfree:
                            ve.wait_ge(sGD.h, gfree)
                        if tb == 0 and mfree:
                            ve.wait_ge(sMM.h, mfree)
                        ve.tensor_copy(qsil[slot][:, tb * 512:(tb + 1) * 512],
                                       pb[bk][:, 0:512]).then_inc(sVE.h, 1)
                        ev = sVE.inc(1)
                        bank_free[bk] = ev
                    spill_stream("z", h2, slot, (sVE, ev), None, z_tm_s)

                for tb in range(2):
                    bk = 6
                    bwait(bk)
                    for dc in range(16):
                        mi = te.matmul(pb[bk][:16, 0:512],
                                       wbres[:, dc * 16:(dc + 1) * 16],
                                       arA[:, dc * 1024 + tb * 512:
                                              dc * 1024 + (tb + 1) * 512],
                                       start=(dc == 0), stop=(dc == 15))
                    mi.then_inc(sMM.h, 1)
                    wm = sMM.inc(1)
                    ve.wait_ge(sMM.h, wm)
                    if tb == 0 and sg_free:
                        ve.wait_ge(sGD.h, sg_free)
                    ve.tensor_copy(sgstage[:16, tb * 512:(tb + 1) * 512],
                                   pb[bk][:16, 0:512]).then_inc(sVE.h, 1)
                    bank_free[bk] = sVE.inc(1)
                bv = sVE.n
                ac.wait_ge(sVE.h, bv)
                ac.activation(sgstage[:16, 0:1024], sgstage[:16, 0:1024],
                              AF.Tanh, scale=0.5).then_inc(sAC.h, 1)
                bt_ac = sAC.inc(1)
                ve.wait_ge(sAC.h, bt_ac)
                ve.tensor_scalar(sgstage[:16, 0:1024], sgstage[:16, 0:1024],
                                 0.5, 0.5, ALU.mult, ALU.add).then_inc(sVE.h, 1)
                bt_ve = sVE.inc(1)
                for lc in range(8):
                    bk = lc % 4
                    te.wait_ge(sVE.h, bt_ve)
                    bwait(bk)
                    te.transpose(pb[bk][:, 0:16],
                                 sgstage[:16, lc * 128:(lc + 1) * 128],
                                 identf[:16, :16]).then_inc(sMM.h, 1)
                    wm = sMM.inc(1)
                    ve.wait_ge(sMM.h, wm)
                    ve.tensor_copy(bstage[:, lc * 16:(lc + 1) * 16],
                                   pb[bk][:, 0:16]).then_inc(sVE.h, 1)
                    bank_free[bk] = sVE.inc(1)
                gp.wait_ge(sVE.h, sVE.n)
                for lc in range(8):
                    gp.dma_start(beta_s[cc1 * 8 + lc],
                                 bstage[:, lc * 16:(lc + 1) * 16])\
                        .then_inc(sGD.h, 16)
                    sGD.inc(16)
                beta_gd = sGD.n

                t2_ve = small_proj(wg1res, t12buf)
                for ttile in range(8):
                    evs = []
                    for jb in range(4):
                        bk = 6 + (jb % 2)
                        te.wait_ge(sVE.h, t2_ve)
                        bwait(bk)
                        mi = te.matmul(pb[bk][:, 0:512],
                                       t12buf[:, ttile * 128:(ttile + 1) * 128],
                                       wg2res[:, jb * 512:(jb + 1) * 512],
                                       start=True, stop=True)
                        mi.then_inc(sMM.h, 1)
                        wm = sMM.inc(1)
                        ve.wait_ge(sMM.h, wm)
                        if ttile == 0 and jb == 0 and beta_gd:
                            ve.wait_ge(sGD.h, beta_gd)
                        if sg_free:
                            ve.wait_ge(sGD.h, sg_free)
                        ve.tensor_tensor(sgstage[:, jb * 512:(jb + 1) * 512],
                                         pb[bk][:, 0:512],
                                         arRep[:, jb * 512:(jb + 1) * 512],
                                         ALU.add).then_inc(sVE.h, 1)
                        evs.append(sVE.inc(1))
                        bank_free[bk] = evs[-1]
                    ac.wait_ge(sVE.h, evs[-1])
                    ac.activation(sgstage[:, 0:2048], sgstage[:, 0:2048],
                                  AF.Tanh, scale=0.5).then_inc(sAC.h, 1)
                    sg_ac = sAC.inc(1)
                    ve.wait_ge(sAC.h, sg_ac)
                    ve.tensor_scalar(sgstage[:, 0:2048], sgstage[:, 0:2048],
                                     0.5, 0.5, ALU.mult, ALU.add)\
                        .then_inc(sVE.h, 1)
                    sg_ve = sVE.inc(1)
                    gp.wait_ge(sVE.h, sg_ve)
                    gp.dma_start(sg_s[cc1 * CT1 + ttile * 128:
                                      cc1 * CT1 + (ttile + 1) * 128, :],
                                 sgstage[:, 0:2048]).then_inc(sGD.h, 16)
                    sg_free = sGD.inc(16)

            s1_end = dict(MM=sMM.n, VE=sVE.n, AC=sAC.n, LD=sLD.n, GD=sGD.n)

            p.add("sync", lambda e, r=sy: r.replay(e))
            p.add("tensor", lambda e, r=te: r.replay(e))
            p.add("vector", lambda e, r=ve: r.replay(e))
            p.add("scalar", lambda e, r=ac: r.replay(e))
            p.add("gpsimd", lambda e, r=gp: r.replay(e))

        if stop_after not in ("s0", "s1"):
            sy2, te2, ve2, ac2, gp2 = Rec(), Rec(), Rec(), Rec(), Rec()
            C44 = 44.0

            SkA = arA[:, 0:2048]
            SvA = arA[:, 2048:4096]
            q_c = arA[:, 4096:6144]
            kfm_c = arA[:, 6144:8192]
            ktm_c = arA[:, 8192:10240]
            v_c = arA[:, 10240:12288]
            w_c = arA[:, 12288:14336]
            z_c = arA[:, 14336:16384]
            lam = arB[:, 0:2048]
            e1c = arB[:, 2048:4096]
            e2c = arB[:, 4096:6144]
            u2 = arB[:, 6144:8192]
            araw = arC[:, 0:2048]
            s_t = arC[:, 2048:4096]
            o_t = arC[:, 4096:6144]
            sgld = arS1[:, 0:2048]
            nsg = arS1[:, 2048:4096]
            bcol = arS1[:, 4096:4112]
            lfc = arS1[:, 4112:4128]
            lf44 = arS1[:, 4128:4144]
            nmx = arS1[:, 4144:4160]
            den = arS1[:, 4160:4176]
            rden = arS1[:, 4176:4192]
            ssq = arS1[:, 4192:4208]
            rst = arS1[:, 4208:4224]
            nwRep = arW[:, 0:2048]
            sktb = arW[:, 2048:2176]
            a1b = arW[:, 2176:2304]
            a2b = arW[:, 2304:2432]
            a1tb = arW[:, 2432:2560]
            a2tb = arW[:, 2560:2688]
            u2tb = arW[:, 2688:2816]
            atb = arW[:, 2816:2944]
            btb = arW[:, 2944:3072]
            skb = arW[:, 3072:3200]
            svb = arW[:, 3200:3328]

            sy2.wait_ge(sMM.h, sMM.n)
            sy2.wait_ge(sVE.h, sVE.n)
            sy2.wait_ge(sAC.h, sAC.n)
            sy2.wait_ge(sGD.h, sGD.n)

            sy2.dma_start(nwRep,
                          nw_in.rearrange("o (u m) -> o u m", u=1)
                          .broadcast_to([128, 16, 128])).then_inc(sLD.h, 16)
            sLD.inc(16)

            biasc = arW[:, 3328:3332]
            ve2.wait_ge(sVE.h, sVE.n)
            ve2.memset(SkA[:], 0.0)
            ve2.memset(SvA[:], 0.0)
            ve2.memset(biasc[:, 0:1], C44)
            ve2.memset(biasc[:, 1:2], -C44).then_inc(sVE.h, 1)
            sVE.inc(1)

            bank2_free = {i: None for i in range(8)}

            def bw2(i):
                if bank2_free[i]:
                    s_, n_ = bank2_free[i]
                    te2.wait_ge(s_.h, n_)

            yo_gd = 0

            def v3(buf, inner=128):
                return buf.rearrange("p (a b) -> p a b", b=inner)

            ldn = {}

            def s2_issue(nm, cci):
                if nm == "q":
                    sy2.dma_start(v3(q_c),
                                  q_fm_s[:, :, cci * 128:(cci + 1) * 128]
                                  .rearrange("h p t -> p h t"))\
                        .then_inc(sLD.h, 16)
                elif nm == "kfm":
                    sy2.dma_start(v3(kfm_c),
                                  k_fm_s[:, :, cci * 128:(cci + 1) * 128]
                                  .rearrange("h p t -> p h t"))\
                        .then_inc(sLD.h, 16)
                elif nm == "ktm":
                    sy2.dma_start(v3(ktm_c),
                                  k_tm_s[cci].rearrange("h t m -> t h m"))\
                        .then_inc(sLD.h, 16)
                elif nm == "v":
                    sy2.dma_start(v3(v_c),
                                  v_tm_s[cci].rearrange("h t m -> t h m"))\
                        .then_inc(sLD.h, 16)
                elif nm == "w":
                    sy2.dma_start(v3(w_c),
                                  w_tm_s[cci].rearrange("h t m -> t h m"))\
                        .then_inc(sLD.h, 16)
                elif nm == "z":
                    sy2.dma_start(v3(z_c),
                                  z_tm_s[cci].rearrange("h t m -> t h m"))\
                        .then_inc(sLD.h, 16)
                elif nm == "beta":
                    sy2.dma_start(bcol, beta_s[cci]).then_inc(sLD.h, 16)
                elif nm == "sg":
                    sy2.dma_start(sgld, sg_s[cci * 128:(cci + 1) * 128, :])\
                        .then_inc(sLD.h, 16)
                ldn[nm] = sLD.inc(16)

            ldv = {}
            for cc in range(NCC):
                sy2.wait_ge(sVE.h, sVE.n)
                sy2.wait_ge(sAC.h, sAC.n)
                sy2.wait_ge(sMM.h, sMM.n)
                ac2.wait_ge(sVE.h, sVE.n)

                if cc == 0:
                    for nm in ("q", "kfm", "ktm", "v", "w", "z", "beta", "sg"):
                        s2_issue(nm, 0)
                    ldv = ldn
                    ldn = {}

                ac2.wait_ge(sLD.h, ldv["z"])
                ac2.activation(araw, z_c, AF.Abs).then_inc(sAC.h, 1)
                aw = sAC.inc(1)
                ac2.wait_ge(sAC.h, aw)
                ac2.activation(araw, araw, AF.Exp, scale=-1.0).then_inc(sAC.h, 1)
                aw = sAC.inc(1)
                ac2.wait_ge(sAC.h, aw)
                ac2.activation(araw, araw, AF.Ln, bias=ones1c[:, 0:1])\
                    .then_inc(sAC.h, 1)
                aw = sAC.inc(1)
                ve2.wait_ge(sAC.h, aw)
                ve2.scalar_tensor_tensor(z_c, z_c, 0.0, araw,
                                         ALU.min, ALU.subtract).then_inc(sVE.h, 1)
                g_ve = sVE.inc(1)
                ve2.wait_ge(sLD.h, max(ldv["q"], ldv["sg"]))
                ve2.tensor_single_scalar(q_c, q_c, SCALE, ALU.mult)\
                    .then_inc(sVE.h, 1)
                q_ve = sVE.inc(1)
                ve2.tensor_tensor(nsg, sgld, nwRep, ALU.mult).then_inc(sVE.h, 1)
                nsg_ve = sVE.inc(1)

                exp_ac = []
                for jb in range(4):
                    bk = jb
                    te2.wait_ge(sVE.h, g_ve)
                    bw2(bk)
                    te2.matmul(pb[bk][:, 0:512], ltri[:],
                               z_c[:, jb * 512:(jb + 1) * 512],
                               start=True, stop=True).then_inc(sMM.h, 1)
                    wm = sMM.inc(1)
                    ac2.wait_ge(sMM.h, wm)
                    ac2.activation(lam[:, jb * 512:(jb + 1) * 512],
                                   pb[bk][:, 0:512], AF.Exp)
                    ac2.activation(e1c[:, jb * 512:(jb + 1) * 512],
                                   pb[bk][:, 0:512], AF.Exp, bias=biasc[:, 0:1])
                    ac2.activation(e2c[:, jb * 512:(jb + 1) * 512],
                                   pb[bk][:, 0:512], AF.Exp, scale=-1.0,
                                   bias=biasc[:, 1:2]).then_inc(sAC.h, 1)
                    exp_ac.append(sAC.inc(1))
                    bank2_free[bk] = (sAC, exp_ac[-1])


                bk = 4
                bw2(bk)
                te2.wait_ge(sVE.h, g_ve)
                for h in range(16):
                    mi = te2.matmul(pb[bk][:, h:h + 1],
                                    z_c[:, h * 128:(h + 1) * 128],
                                    ones1c[:, 0:1],
                                    start=True, stop=True)
                mi.then_inc(sMM.h, 1)
                wm = sMM.inc(1)
                lf_wm = wm
                ac2.wait_ge(sMM.h, wm)
                ac2.activation(lfc, pb[bk][:, 0:16], AF.Exp)
                ac2.activation(lf44, pb[bk][:, 0:16], AF.Exp,
                               bias=biasc[:, 0:1]).then_inc(sAC.h, 1)
                lf_ac = sAC.inc(1)
                bank2_free[bk] = (sAC, lf_ac)

                ve2.wait_ge(sLD.h, max(ldv["w"], ldv["beta"]))
                ve2.tensor_tensor(araw, w_c, w_c, ALU.mult).then_inc(sVE.h, 1)
                wsq_ve = sVE.inc(1)
                ve2.wait_ge(sVE.h, wsq_ve)
                ve2.tensor_reduce(ssq, araw.rearrange("p (h m) -> p h m", m=128),
                                  AX.X, ALU.add).then_inc(sVE.h, 1)
                wss_ve = sVE.inc(1)
                ve2.wait_ge(sVE.h, wss_ve)
                ve2.tensor_single_scalar(ssq, ssq, L2_EPS, ALU.add)\
                    .then_inc(sVE.h, 1)
                wse_ve = sVE.inc(1)
                wab = sAC.n
                ac2.wait_ge(sVE.h, wse_ve)
                ac2.activation(rst, ssq, AF.Ln).then_inc(sAC.h, 1)
                ac2.wait_ge(sAC.h, wab + 1)
                ac2.activation(rst, rst, AF.Exp, scale=-0.5).then_inc(sAC.h, 1)
                sAC.inc(2)
                wrst_ac = sAC.n
                ve2.wait_ge(sAC.h, wrst_ac)
                for h in range(16):
                    mi = ve2.tensor_single_scalar(
                        w_c[:, h * 128:(h + 1) * 128],
                        w_c[:, h * 128:(h + 1) * 128],
                        rst[:, h:h + 1], ALU.mult)
                mi.then_inc(sVE.h, 1)
                wnorm_ve = sVE.inc(1)

                ve2.wait_ge(sVE.h, wnorm_ve)
                ve2.wait_ge(sAC.h, exp_ac[-1])
                for h in range(16):
                    ve2.scalar_tensor_tensor(u2[:, h * 128:(h + 1) * 128],
                                             w_c[:, h * 128:(h + 1) * 128],
                                             bcol[:, h:h + 1],
                                             e2c[:, h * 128:(h + 1) * 128],
                                             ALU.mult, ALU.mult)
                ve2.engine_nop().then_inc(sVE.h, 1)
                u2_ve = sVE.inc(1)

                if cc + 1 < NCC:
                    sy2.wait_ge(sVE.h, u2_ve)
                    sy2.wait_ge(sMM.h, lf_wm)
                    for nm in ("z", "w", "beta", "sg"):
                        s2_issue(nm, cc + 1)

                for h in range(16):
                    hs = slice(h * 128, (h + 1) * 128)
                    if h == 0:
                        te2.wait_ge(sLD.h, ldv["kfm"])
                    bk6 = 6
                    bw2(bk6)
                    if cc == 0 and h == 0:
                        te2.wait_ge(sVE.h, u2_ve)
                    te2.transpose(pb[bk6][:, 0:128], SkA[:, hs], identf[:])\
                        .then_inc(sMM.h, 1)
                    wm_skt = sMM.inc(1)
                    ac2.wait_ge(sMM.h, wm_skt)
                    ac2.activation(sktb, pb[bk6][:, 0:128], AF.Identity)\
                        .then_inc(sAC.h, 1)
                    skt_ac = sAC.inc(1)
                    bank2_free[bk6] = (sAC, skt_ac)

                    bk4 = 4
                    bw2(bk4)
                    te2.wait_ge(sVE.h, q_ve)
                    te2.matmul(pb[bk4][:, 0:128], kfm_c[:, hs], q_c[:, hs],
                               start=True, stop=True).then_inc(sMM.h, 1)
                    wm_at = sMM.inc(1)
                    ve2.wait_ge(sMM.h, wm_at)
                    ve2.tensor_tensor(atb, pb[bk4][:, 0:128], ltri[:],
                                      ALU.mult).then_inc(sVE.h, 1)
                    at_ve = sVE.inc(1)
                    bank2_free[bk4] = (sVE, at_ve)

                    bk7 = 7
                    bw2(bk7)
                    te2.wait_ge(sAC.h, skt_ac)
                    te2.matmul(pb[bk7][:, 0:128], q_c[:, hs], sktb,
                               start=True, stop=True).then_inc(sMM.h, 1)
                    wm_qs = sMM.inc(1)

                    bk5 = 5
                    bw2(bk5)
                    te2.wait_ge(sVE.h, max(at_ve, u2_ve))
                    te2.matmul(pb[bk5][:, 0:128], atb, u2[:, hs],
                               start=True, stop=True).then_inc(sMM.h, 1)
                    wm_in = sMM.inc(1)

                    ve2.wait_ge(sMM.h, wm_qs)
                    ve2.tensor_tensor(s_t[:, hs], lam[:, hs], pb[bk7][:, 0:128],
                                      ALU.mult).then_inc(sVE.h, 1)
                    sv1 = sVE.inc(1)
                    bank2_free[bk7] = (sVE, sv1)
                    ve2.wait_ge(sMM.h, wm_in)
                    ve2.tensor_tensor(araw[:, hs], e1c[:, hs], pb[bk5][:, 0:128],
                                      ALU.mult).then_inc(sVE.h, 1)
                    sv2 = sVE.inc(1)
                    bank2_free[bk5] = (sVE, sv2)
                    ve2.tensor_tensor(s_t[:, hs], s_t[:, hs], araw[:, hs],
                                      ALU.add).then_inc(sVE.h, 1)
                    sVE.inc(1)

                if cc + 1 < NCC:
                    sy2.wait_ge(sMM.h, sMM.n)
                    for nm in ("q", "kfm"):
                        s2_issue(nm, cc + 1)

                sm0 = sVE.n
                ve2.tensor_reduce(nmx, s_t.rearrange("p (h m) -> p h m", m=128),
                                  AX.X, ALU.max, negate=True).then_inc(sVE.h, 1)
                nmx_ve = sVE.inc(1)
                ac2.wait_ge(sVE.h, nmx_ve)
                for h in range(16):
                    hs = slice(h * 128, (h + 1) * 128)
                    mi = ac2.activation(araw[:, hs], s_t[:, hs], AF.Exp,
                                        bias=nmx[:, h:h + 1],
                                        accum_out=den[:, h:h + 1])
                mi.then_inc(sAC.h, 1)
                sm_ac = sAC.inc(1)
                ve2.wait_ge(sAC.h, sm_ac)
                ve2.reciprocal(rden, den).then_inc(sVE.h, 1)
                rden_ve = sVE.inc(1)

                for h in range(16):
                    hs = slice(h * 128, (h + 1) * 128)
                    if h == 0:
                        te2.wait_ge(sLD.h, max(ldv["ktm"], ldv["v"]))
                    ve2.wait_ge(sVE.h, rden_ve)
                    ve2.tensor_tensor(a1b, araw[:, hs], lam[:, hs], ALU.mult)
                    ve2.tensor_tensor(a2b, araw[:, hs], e1c[:, hs], ALU.mult)\
                        .then_inc(sVE.h, 1)
                    ab_ve = sVE.inc(1)

                    tps = []
                    for bsrc, bdst, bk in ((a1b, a1tb, 4), (a2b, a2tb, 5),
                                           (u2[:, hs], u2tb, 6)):
                        bw2(bk)
                        te2.wait_ge(sVE.h, ab_ve)
                        te2.transpose(pb[bk][:, 0:128], bsrc, identf[:])\
                            .then_inc(sMM.h, 1)
                        wm = sMM.inc(1)
                        ve2.wait_ge(sMM.h, wm)
                        ve2.tensor_copy(bdst, pb[bk][:, 0:128]).then_inc(sVE.h, 1)
                        bank2_free[bk] = (sVE, sVE.inc(1))
                        tps.append(bank2_free[bk][1])

                    bk4 = 4
                    bw2(bk4)
                    te2.wait_ge(sVE.h, tps[-1])
                    te2.matmul(pb[bk4][0:64, 0:64], u2tb[:, 0:64],
                               a2tb[:, 0:64], start=True, stop=True)
                    te2.matmul(pb[bk4][0:64, 64:128], u2tb[:, 0:64],
                               a2tb[:, 64:128], start=True, stop=True)
                    te2.matmul(pb[bk4][64:128, 64:128], u2tb[:, 64:128],
                               a2tb[:, 64:128], start=True, stop=True)\
                        .then_inc(sMM.h, 1)
                    wm_bt = sMM.inc(1)
                    ve2.wait_ge(sMM.h, wm_bt)
                    ve2.memset(btb[64:128, 0:64], 0.0)
                    ve2.tensor_tensor(btb[0:64, 0:64], pb[bk4][0:64, 0:64],
                                      ltri[0:64, 0:64], ALU.mult)
                    ve2.tensor_copy(btb[0:64, 64:128], pb[bk4][0:64, 64:128])
                    ve2.tensor_tensor(btb[64:128, 64:128],
                                      pb[bk4][64:128, 64:128],
                                      ltri[0:64, 0:64], ALU.mult)\
                        .then_inc(sVE.h, 1)
                    bt_ve = sVE.inc(1)
                    bank2_free[bk4] = (sVE, bt_ve)

                    bk7 = 7
                    bw2(bk7)
                    te2.matmul(pb[bk7][:, 0:128], a1tb, SvA[:, hs],
                               start=True, stop=False)
                    te2.wait_ge(sVE.h, bt_ve)
                    te2.matmul(pb[bk7][:, 0:128], btb, v_c[:, hs],
                               start=False, stop=True).then_inc(sMM.h, 1)
                    wm_o = sMM.inc(1)
                    ve2.wait_ge(sMM.h, wm_o)
                    ve2.tensor_scalar(o_t[:, hs], pb[bk7][:, 0:128],
                                      rden[:, h:h + 1], None, ALU.mult)\
                        .then_inc(sVE.h, 1)
                    bank2_free[bk7] = (sVE, sVE.inc(1))

                    bk5, bk6 = 5, 6
                    bw2(bk5)
                    te2.matmul(pb[bk5][:, 0:128], u2[:, hs], ktm_c[:, hs],
                               start=True, stop=True).then_inc(sMM.h, 1)
                    wm_uk = sMM.inc(1)
                    bw2(bk6)
                    te2.matmul(pb[bk6][:, 0:128], u2[:, hs], v_c[:, hs],
                               start=True, stop=True).then_inc(sMM.h, 1)
                    wm_uv = sMM.inc(1)
                    ac2.wait_ge(sAC.h, lf_ac)
                    ac2.activation(skb, SkA[:, hs], AF.Identity,
                                   scale=lfc[:, h:h + 1])
                    ac2.activation(svb, SvA[:, hs], AF.Identity,
                                   scale=lfc[:, h:h + 1]).then_inc(sAC.h, 1)
                    sc_ac = sAC.inc(1)
                    ve2.wait_ge(sMM.h, wm_uk)
                    ve2.wait_ge(sAC.h, sc_ac)
                    ve2.scalar_tensor_tensor(SkA[:, hs], pb[bk5][:, 0:128],
                                             lf44[:, h:h + 1], skb,
                                             ALU.mult, ALU.add).then_inc(sVE.h, 1)
                    bank2_free[bk5] = (sVE, sVE.inc(1))
                    ve2.wait_ge(sMM.h, wm_uv)
                    ve2.scalar_tensor_tensor(SvA[:, hs], pb[bk6][:, 0:128],
                                             lf44[:, h:h + 1], svb,
                                             ALU.mult, ALU.add).then_inc(sVE.h, 1)
                    bank2_free[bk6] = (sVE, sVE.inc(1))

                ve2.tensor_tensor(araw, o_t, o_t, ALU.mult).then_inc(sVE.h, 1)
                sq_ve = sVE.inc(1)
                ve2.wait_ge(sVE.h, sq_ve)
                ve2.tensor_reduce(ssq, araw.rearrange("p (h m) -> p h m", m=128),
                                  AX.X, ALU.add).then_inc(sVE.h, 1)
                ssq_ve = sVE.inc(1)
                ve2.wait_ge(sVE.h, ssq_ve)
                ve2.tensor_scalar(ssq, ssq, 1.0 / 128.0, LN_EPS, ALU.mult,
                                  ALU.add).then_inc(sVE.h, 1)
                ssq2_ve = sVE.inc(1)
                ab_ = sAC.n
                ac2.wait_ge(sVE.h, ssq2_ve)
                ac2.activation(rst, ssq, AF.Ln).then_inc(sAC.h, 1)
                ac2.wait_ge(sAC.h, ab_ + 1)
                ac2.activation(rst, rst, AF.Exp, scale=-0.5).then_inc(sAC.h, 1)
                sAC.inc(2)
                rst_ac = sAC.n
                for h in range(16):
                    hs = slice(h * 128, (h + 1) * 128)
                    if h == 0:
                        ve2.wait_ge(sAC.h, rst_ac)
                        ve2.wait_ge(sVE.h, nsg_ve)
                    ve2.scalar_tensor_tensor(o_t[:, hs], o_t[:, hs],
                                             rst[:, h:h + 1], nsg[:, hs],
                                             ALU.mult, ALU.mult)
                ve2.engine_nop().then_inc(sVE.h, 1)
                yo_ve = sVE.inc(1)

                gp2.wait_ge(sVE.h, yo_ve)
                gp2.dma_start(yo_s[:, cc * 128:(cc + 1) * 128, :]
                              .rearrange("h t m -> t h m"),
                              o_t.rearrange("p (h m) -> p h m", m=128))\
                    .then_inc(sGD.h, 16)
                yo_gd = sGD.inc(16)

                if cc + 1 < NCC:
                    sy2.wait_ge(sMM.h, sMM.n)
                    for nm in ("ktm", "v"):
                        s2_issue(nm, cc + 1)
                ldv = ldn
                ldn = {}

            p.add("sync", lambda e, r=sy2: r.replay(e))
            p.add("tensor", lambda e, r=te2: r.replay(e))
            p.add("vector", lambda e, r=ve2: r.replay(e))
            p.add("scalar", lambda e, r=ac2: r.replay(e))
            p.add("gpsimd", lambda e, r=gp2: r.replay(e))

        if stop_after not in ("s0", "s1", "s2"):
            sy3, te3, ve3, ac3, gp3 = Rec(), Rec(), Rec(), Rec(), Rec()
            sy3.wait_ge(sMM.h, sMM.n)
            sy3.wait_ge(sVE.h, sVE.n)
            sy3.wait_ge(sAC.h, sAC.n)
            sy3.wait_ge(sGD.h, sGD.n)

            y_fm = arA[:, 0:8192]
            ybuf3 = arA[:, 8192:10240]
            x2acc = arB[:, 0:8192]
            afm = arC[:, 0:512]
            gaRep = arRep[:, 0:2048]
            wpan3 = [arW[:, 0:2048], arW[:, 2048:4096]]

            sy3.dma_start(gaRep, mod_s[2:3, :].broadcast_to([128, D]))\
                .then_inc(sLD.h, 16)
            sLD.inc(16)
            ga_ld = sLD.n

            bank3 = {i: None for i in range(8)}

            def bw3(i):
                if bank3[i]:
                    s_, n_ = bank3[i]
                    te3.wait_ge(s_.h, n_)

            pan3_mm = {}
            x2st_gd = 0
            for gg in range(8):
                sy3.wait_ge(sVE.h, sVE.n)
                sy3.wait_ge(sMM.h, sMM.n)
                if x2st_gd:
                    sy3.wait_ge(sGD.h, x2st_gd)
                for t4 in range(4):
                    sy3.dma_start(x2acc[:, t4 * 2048:(t4 + 1) * 2048],
                                  x_in[gg * 512 + t4 * 128:
                                       gg * 512 + (t4 + 1) * 128, :])\
                        .then_inc(sLD.h, 16)
                    sLD.inc(16)
                    sy3.wait_ge(sMM.h, sMM.n)
                    sy3.dma_start(
                        ybuf3.rearrange("p (h m) -> p h m", m=128),
                        yo_s[:, gg * 512 + t4 * 128:gg * 512 + (t4 + 1) * 128, :]
                        .rearrange("h t m -> t h m")).then_inc(sLD.h, 16)
                    yld = sLD.inc(16)
                    for dc in range(16):
                        bk = dc % 2
                        te3.wait_ge(sLD.h, yld)
                        bw3(bk)
                        te3.matmul(pb[bk][:, 0:128],
                                   ybuf3[:, dc * 128:(dc + 1) * 128],
                                   identf[:], is_transpose=True,
                                   start=True, stop=True).then_inc(sMM.h, 1)
                        wm = sMM.inc(1)
                        ve3.wait_ge(sMM.h, wm)
                        ve3.tensor_copy(y_fm[:, dc * 512 + t4 * 128:
                                             dc * 512 + (t4 + 1) * 128],
                                        pb[bk][:, 0:128]).then_inc(sVE.h, 1)
                        bank3[bk] = (sVE, sVE.inc(1))
                yfm_ve = sVE.n

                for jc in range(16):
                    slot = jc % 2
                    if pan3_mm.get(slot):
                        sy3.wait_ge(sMM.h, pan3_mm[slot])
                    sy3.dma_start(wpan3[slot],
                                  wo_t[jc].rearrange("p dc j -> p (dc j)"))\
                        .then_inc(sLD.h, 16)
                    wld = sLD.inc(16)

                    bk = 4
                    te3.wait_ge(sLD.h, wld)
                    if jc == 0:
                        te3.wait_ge(sVE.h, yfm_ve)
                        te3.wait_ge(sLD.h, ga_ld)
                    bw3(bk)
                    for dc in range(16):
                        mi = te3.matmul(pb[bk][:, 0:512],
                                        wpan3[slot][:, dc * 128:(dc + 1) * 128],
                                        y_fm[:, dc * 512:(dc + 1) * 512],
                                        start=(dc == 0), stop=(dc == 15))
                    mi.then_inc(sMM.h, 1)
                    wm = sMM.inc(1)
                    pan3_mm[slot] = wm
                    ve3.wait_ge(sMM.h, wm)
                    ve3.tensor_copy(afm, pb[bk][:, 0:512]).then_inc(sVE.h, 1)
                    afm_ve = sVE.inc(1)
                    bank3[bk] = (sVE, afm_ve)

                    for t4 in range(4):
                        bk2 = 5 + (t4 % 2)
                        te3.wait_ge(sVE.h, afm_ve)
                        bw3(bk2)
                        te3.matmul(pb[bk2][:, 0:128],
                                   afm[:, t4 * 128:(t4 + 1) * 128],
                                   identf[:], is_transpose=True,
                                   start=True, stop=True).then_inc(sMM.h, 1)
                        wm2 = sMM.inc(1)
                        ve3.wait_ge(sMM.h, wm2)
                        ve3.tensor_tensor(o_t[:, 0:128] if False else
                                          arC[:, 512:640],
                                          pb[bk2][:, 0:128],
                                          gaRep[:, jc * 128:(jc + 1) * 128],
                                          ALU.mult)
                        ve3.tensor_tensor(
                            x2acc[:, t4 * 2048 + jc * 128:
                                  t4 * 2048 + (jc + 1) * 128],
                            x2acc[:, t4 * 2048 + jc * 128:
                                  t4 * 2048 + (jc + 1) * 128],
                            arC[:, 512:640], ALU.add).then_inc(sVE.h, 1)
                        bank3[bk2] = (sVE, sVE.inc(1))

                gp3.wait_ge(sVE.h, sVE.n)
                for t4 in range(4):
                    gp3.dma_start(x2_s[gg * 512 + t4 * 128:
                                       gg * 512 + (t4 + 1) * 128, :],
                                  x2acc[:, t4 * 2048:(t4 + 1) * 2048])\
                        .then_inc(sGD.h, 16)
                    sGD.inc(16)
                x2st_gd = sGD.n

            p.add("sync", lambda e, r=sy3: r.replay(e))
            p.add("tensor", lambda e, r=te3: r.replay(e))
            p.add("vector", lambda e, r=ve3: r.replay(e))
            p.add("scalar", lambda e, r=ac3: r.replay(e))
            p.add("gpsimd", lambda e, r=gp3: r.replay(e))

        if stop_after is None:
            sy4, te4, ve4, ac4, gp4 = Rec(), Rec(), Rec(), Rec(), Rec()
            sy4.wait_ge(sMM.h, sMM.n)
            sy4.wait_ge(sVE.h, sVE.n)
            sy4.wait_ge(sAC.h, sAC.n)
            sy4.wait_ge(sGD.h, sGD.n)

            def _addr(tname):
                for a in nc.m.functions[0].allocations:
                    if not hasattr(a, "memorylocations"):
                        continue
                    ml = a.memorylocations[0]
                    if str(ml.name) == tname:
                        return ml.addr
                raise KeyError(tname)

            baseA = _addr("arA")
            def ov(name, shape, dtype, off):
                return nc.alloc_sbuf_tensor_at(name, shape, dtype,
                                               offset=baseA + off).ap()

            KB = 1024
            xq_fm = ov("s3_xq", [128, 16384], BF16, 0)
            ygbuf = ov("s3_yg", [128, 4096], F32, 32 * KB)
            rX0 = ov("s3_x2a", [128, 2048], F32, 48 * KB)
            rX1 = ov("s3_x2b", [128, 2048], F32, 56 * KB)
            rXn = ov("s3_xn", [128, 2048], F32, 64 * KB)
            tacc = ov("s3_t", [128, 16384], F32, 72 * KB)
            dnwRep = ov("s3_dnw", [128, 5632], F32, 72 * KB)
            wbuf4 = ov("s3_wb", [128, 8192], BF16, 94 * KB)
            h3buf = ov("s3_h3", [128, 5632], F32, 158 * KB)
            h3bufB = ov("s3_h3B", [128, 5632], F32, 94 * KB)
            scrT = ov("s3_scr", [128, 176], F32, 152 * KB)
            sscr = scrT[:, 0:88]
            ascr2 = scrT[:, 88:176]
            xq2buf = ov("s3_xq2", [128, 5632], BF16, 32 * KB)
            xq2I = ov("s3_xq2I", [128, 11264], BF16, 0)
            dwbuf = ov("s3_dw", [128, 11264], BF16, 22 * KB)
            silb = [ov("s3_sil0", [128, 512], F32, 48 * KB),
                    ov("s3_sil1", [128, 512], F32, 50 * KB)]
            h3blk = [ov("s3_h3b0", [128, 512], F32, 52 * KB),
                     ov("s3_h3b1", [128, 512], F32, 54 * KB)]
            maskb = ov("s3_mask", [128, 5632], F32, 48 * KB)
            x2j = ov("s3_x2j", [128, 2048], F32, 48 * KB)
            outj = ov("s3_outj", [128, 2048], F32, 56 * KB)

            cb1R = ov("s3_cb1", [128, 2048], F32, 136 * KB)
            sbR = ov("s3_sb", [128, 2048], F32, 144 * KB)
            gnwR = ov("s3_gnw", [128, 2048], F32, 183 * KB)
            gbR = ov("s3_gb", [128, 2048], F32, 191 * KB)

            sy4.dma_start(cb1R, mod_s[4:5, :].broadcast_to([128, D]))\
                .then_inc(sLD.h, 16)
            sy4.dma_start(sbR, mod_s[3:4, :].broadcast_to([128, D]))\
                .then_inc(sLD.h, 16)
            sy4.dma_start(gnwR, gnw_in.broadcast_to([128, D])).then_inc(sLD.h, 16)
            sy4.dma_start(gbR, mod_s[5:6, :].broadcast_to([128, D]))\
                .then_inc(sLD.h, 16)
            sLD.inc(64)
            reps_ld = sLD.n
            ve4.wait_ge(sVE.h, sVE.n)
            ve4.wait_ge(sLD.h, reps_ld)
            ve4.tensor_single_scalar(cb1R, cb1R, 1.0, ALU.add).then_inc(sVE.h, 1)
            b0 = sVE.inc(1)
            ve4.wait_ge(sVE.h, b0)
            ve4.reciprocal(tiny[:, 40:41], tiny[:, 0:1])
            ve4.reciprocal(tiny[:, 41:42], tiny[:, 2:3]).then_inc(sVE.h, 1)
            sVE.inc(1)
            mh_ve = sVE.n

            bank4f = {i: None for i in range(8)}

            def bw4(i):
                if bank4f[i]:
                    s_, n_ = bank4f[i]
                    te4.wait_ge(s_.h, n_)

            LAD = [128.0, 64.0, 32.0, 16.0, 8.0, 4.0, 2.0, 1.0]

            def ladder(buf, n, scol):
                ve4.tensor_scalar(maskb[:, 0:n], buf, scol, 128.5,
                                  ALU.mult, ALU.add)
                ve4.tensor_copy(rXn[:, 0:n] if n <= 2048 else h3buf[:, 0:0],
                                maskb[:, 0:n]) if False else None
                for k in LAD:
                    ve4.tensor_single_scalar(buf, maskb[:, 0:n], k, ALU.is_ge)
                    ve4.scalar_tensor_tensor(maskb[:, 0:n], buf, -k,
                                             maskb[:, 0:n], ALU.mult, ALU.add)
                raise RuntimeError("ladder needs orig")

            def ladder2(buf, n, scol, ve_sem):
                ve4.tensor_scalar(maskb[:, 0:n], buf, scol, 128.5,
                                  ALU.mult, ALU.add)
                ve4.tensor_copy(buf, maskb[:, 0:n]).then_inc(sVE.h, 1)
                ve_sem[0] = sVE.inc(1)
                for k in LAD:
                    ve4.tensor_single_scalar(maskb[:, 0:n], buf, k, ALU.is_ge)
                    ve4.scalar_tensor_tensor(buf, maskb[:, 0:n], -k, buf,
                                             ALU.mult, ALU.add)
                raise RuntimeError("unused")

            def ladder3(src, dst, n, scol):
                ve4.tensor_scalar(maskb[:, 0:n], src, scol, 128.5,
                                  ALU.mult, ALU.add)
                for k in LAD:
                    ve4.tensor_single_scalar(dst, maskb[:, 0:n], k, ALU.is_ge)
                    ve4.scalar_tensor_tensor(maskb[:, 0:n], dst, -k,
                                             maskb[:, 0:n], ALU.mult, ALU.add)
                ve4.tensor_scalar(dst, src, scol, 0.5, ALU.mult, ALU.add)
                mi = ve4.tensor_tensor(dst, dst, maskb[:, 0:n], ALU.subtract)
                return mi

            out_gd = 0
            for cc3 in range(4):
                sy4.wait_ge(sVE.h, sVE.n)
                sy4.wait_ge(sAC.h, sAC.n)
                sy4.wait_ge(sMM.h, sMM.n)
                sy4.wait_ge(sGD.h, sGD.n)

                ld_t, ve_t, ac_t = [], [], []
                for tt in range(8):
                    if tt >= 2:
                        sy4.wait_ge(sVE.h, ve_t[tt - 2])
                        sy4.wait_ge(sAC.h, ac_t[tt - 2])
                    buf = (rX0, rX1)[tt % 2]
                    sy4.dma_start(buf, x2_s[cc3 * 1024 + tt * 128:
                                            cc3 * 1024 + (tt + 1) * 128, :])\
                        .then_inc(sLD.h, 16)
                    ld_t.append(sLD.inc(16))
                    ve4.wait_ge(sLD.h, ld_t[tt])
                    ve4.tensor_reduce(stat[:, 16 + tt:17 + tt], buf,
                                      AX.X, ALU.add).then_inc(sVE.h, 1)
                    ve_t.append(sVE.inc(1))
                    ac4.wait_ge(sLD.h, ld_t[tt])
                    ac4.activation(rXn, buf, AF.Square,
                                   accum_out=stat[:, 24 + tt:25 + tt])\
                        .then_inc(sAC.h, 1)
                    ac_t.append(sAC.inc(1))

                b = sVE.n
                ve4.wait_ge(sVE.h, ve_t[-1])
                ve4.wait_ge(sAC.h, ac_t[-1])
                ve4.tensor_single_scalar(stat[:, 32:40], stat[:, 16:24],
                                         1.0 / D, ALU.mult)
                ve4.tensor_single_scalar(stat[:, 40:48], stat[:, 24:32],
                                         1.0 / D, ALU.mult).then_inc(sVE.h, 1)
                ve4.wait_ge(sVE.h, b + 1)
                ve4.tensor_tensor(stat[:, 48:56], stat[:, 32:40],
                                  stat[:, 32:40], ALU.mult).then_inc(sVE.h, 1)
                ve4.wait_ge(sVE.h, b + 2)
                ve4.scalar_tensor_tensor(stat[:, 48:56], stat[:, 40:48],
                                         LN_EPS, stat[:, 48:56],
                                         ALU.add, ALU.subtract).then_inc(sVE.h, 1)
                sVE.inc(3)
                p2v = sVE.n
                ab = sAC.n
                ac4.wait_ge(sVE.h, p2v)
                ac4.activation(stat[:, 56:64], stat[:, 48:56], AF.Ln)\
                    .then_inc(sAC.h, 1)
                ac4.wait_ge(sAC.h, ab + 1)
                ac4.activation(stat[:, 48:56], stat[:, 56:64], AF.Exp,
                               scale=-0.5).then_inc(sAC.h, 1)
                sAC.inc(2)
                p2a = sAC.n
                b = sVE.n
                ve4.wait_ge(sAC.h, p2a)
                ve4.tensor_tensor(stat[:, 56:64], stat[:, 32:40],
                                  stat[:, 48:56], ALU.mult).then_inc(sVE.h, 1)
                ve4.wait_ge(sVE.h, b + 1)
                ve4.tensor_single_scalar(stat[:, 56:64], stat[:, 56:64],
                                         -1.0, ALU.mult).then_inc(sVE.h, 1)
                sVE.inc(2)
                rstd1_ve = sVE.n

                for tt in range(8):
                    buf = (rX0, rX1)[tt % 2]
                    sy4.wait_ge(sVE.h, sVE.n)
                    sy4.dma_start(buf, x2_s[cc3 * 1024 + tt * 128:
                                            cc3 * 1024 + (tt + 1) * 128, :])\
                        .then_inc(sLD.h, 16)
                    w = sLD.inc(16)
                    ac4.wait_ge(sLD.h, w)
                    ac4.wait_ge(sVE.h, rstd1_ve)
                    ac4.activation(rXn, buf, AF.Identity,
                                   bias=stat[:, 56 + tt:57 + tt],
                                   scale=stat[:, 48 + tt:49 + tt])\
                        .then_inc(sAC.h, 1)
                    wa = sAC.inc(1)
                    ve4.wait_ge(sAC.h, wa)
                    ve4.tensor_tensor(buf, rXn, cb1R, ALU.mult)
                    ve4.tensor_tensor(buf, buf, sbR, ALU.add).then_inc(sVE.h, 1)
                    h2_ve = sVE.inc(1)
                    ac4.wait_ge(sVE.h, h2_ve)
                    ac4.activation(rXn, buf, AF.Square,
                                   accum_out=stat[:, 24 + tt:25 + tt])\
                        .then_inc(sAC.h, 1)
                    sAC.inc(1)
                    ve4.tensor_tensor(tacc[:, tt * 2048:(tt + 1) * 2048],
                                      buf, gnwR, ALU.mult).then_inc(sVE.h, 1)
                    t_ve = sVE.inc(1)
                    ve4.wait_ge(sVE.h, t_ve)
                    ve4.tensor_reduce(stat[:, 16 + tt:17 + tt],
                                      tacc[:, tt * 2048:(tt + 1) * 2048],
                                      AX.X, ALU.max, apply_absolute_value=True)\
                        .then_inc(sVE.h, 1)
                    sVE.inc(1)
                c_done_ve = sVE.n
                c_done_ac = sAC.n

                b = sVE.n
                ve4.wait_ge(sVE.h, c_done_ve)
                ve4.wait_ge(sAC.h, c_done_ac)
                ve4.tensor_single_scalar(stat[:, 40:48], stat[:, 24:32],
                                         1.0 / D, ALU.mult).then_inc(sVE.h, 1)
                ve4.wait_ge(sVE.h, b + 1)
                ve4.tensor_single_scalar(stat[:, 40:48], stat[:, 40:48],
                                         LN_EPS, ALU.add).then_inc(sVE.h, 1)
                sVE.inc(2)
                dv = sVE.n
                ab = sAC.n
                ac4.wait_ge(sVE.h, dv)
                ac4.activation(stat[:, 48:56], stat[:, 40:48], AF.Ln)\
                    .then_inc(sAC.h, 1)
                ac4.wait_ge(sAC.h, ab + 1)
                ac4.activation(stat[:, 40:48], stat[:, 48:56], AF.Exp,
                               scale=-0.5).then_inc(sAC.h, 1)
                sAC.inc(2)
                da = sAC.n
                b = sVE.n
                ve4.wait_ge(sAC.h, da)
                ve4.tensor_tensor(stat[:, 32:40], stat[:, 16:24],
                                  stat[:, 40:48], ALU.mult).then_inc(sVE.h, 1)
                ve4.wait_ge(sVE.h, b + 1)
                ve4.tensor_single_scalar(stat[:, 32:40], stat[:, 32:40],
                                         1e-5, ALU.max).then_inc(sVE.h, 1)
                ve4.wait_ge(sVE.h, b + 2)
                ve4.reciprocal(stat[:, 48:56], stat[:, 32:40]).then_inc(sVE.h, 1)
                ve4.wait_ge(sVE.h, b + 3)
                ve4.tensor_tensor(stat[:, 56:64], stat[:, 40:48],
                                  stat[:, 48:56], ALU.mult).then_inc(sVE.h, 1)
                ve4.wait_ge(sVE.h, b + 4)
                ve4.tensor_single_scalar(stat[:, 56:64], stat[:, 56:64],
                                         127.0, ALU.mult)
                ve4.wait_ge(sVE.h, mh_ve)
                ve4.tensor_single_scalar(stat[:, 32:40], stat[:, 32:40],
                                         1.0 / 127.0, ALU.mult).then_inc(sVE.h, 1)
                ve4.wait_ge(sVE.h, b + 5)
                for tt in range(8):
                    mi = ve4.tensor_single_scalar(
                        stat[:, 32 + tt:33 + tt], stat[:, 32 + tt:33 + tt],
                        tiny[:, 40:41], ALU.mult)
                mi.then_inc(sVE.h, 1)
                sVE.inc(6)
                d_done = sVE.n

                for tt in range(8):
                    ve4.wait_ge(sVE.h, d_done)
                    ve4.wait_ge(sMM.h, sMM.n)
                    mi = ladder3(tacc[:, tt * 2048:(tt + 1) * 2048],
                                 rXn, 2048, stat[:, 56 + tt:57 + tt])
                    mi.then_inc(sVE.h, 1)
                    lv = sVE.inc(1)
                    for dc in range(16):
                        bk = dc % 2
                        te4.wait_ge(sVE.h, lv)
                        bw4(bk)
                        te4.matmul(pb[bk][:, 0:128],
                                   rXn[:, dc * 128:(dc + 1) * 128],
                                   identf[:], is_transpose=True,
                                   start=True, stop=True).then_inc(sMM.h, 1)
                        wm = sMM.inc(1)
                        ac4.wait_ge(sMM.h, wm)
                        ac4.activation(
                            xq_fm[:, dc * 1024 + tt * 128:
                                  dc * 1024 + (tt + 1) * 128],
                            pb[bk][:, 0:128], AF.Identity).then_inc(sAC.h, 1)
                        bank4f[bk] = (sAC, sAC.inc(1))
                xq_ac = sAC.n

                sy4.wait_ge(sMM.h, sMM.n)
                sy4.dma_start(dnwRep, dnw_in.broadcast_to([128, INTER]))\
                    .then_inc(sLD.h, 16)
                dnw_ld = sLD.inc(16)
                ve4.wait_ge(sLD.h, dnw_ld)
                h3_spill_gd = {0: 0, 1: 0}
                wld_prev = 0
                for jp in range(11):
                    sy4.wait_ge(sMM.h, sMM.n)
                    for b_ in range(4):
                        sy4.dma_start(
                            wbuf4.rearrange("p (dc bj) -> p dc bj", dc=16)
                            [:, :, b_ * 128:(b_ + 1) * 128],
                            gwqT_s[:, jp * 4 + b_]
                            .rearrange("dc p j -> p dc j")).then_inc(sLD.h, 16)
                        sLD.inc(16)
                    wld = sLD.n
                    for tt in range(8):
                        bk = tt % 2
                        te4.wait_ge(sLD.h, wld)
                        if jp == 0 and tt == 0:
                            te4.wait_ge(sAC.h, xq_ac)
                        bw4(bk)
                        for dc in range(16):
                            mi = te4.matmul(
                                pb[bk][:, 0:512],
                                xq_fm[:, dc * 1024 + tt * 128:
                                      dc * 1024 + (tt + 1) * 128],
                                wbuf4[:, dc * 512:(dc + 1) * 512],
                                start=(dc == 0), stop=(dc == 15))
                        mi.then_inc(sMM.h, 1)
                        wm = sMM.inc(1)
                        ac4.wait_ge(sMM.h, wm)
                        ac4.activation(ygbuf[:, tt * 512:(tt + 1) * 512],
                                       pb[bk][:, 0:512], AF.Identity,
                                       scale=stat[:, 32 + tt:33 + tt])\
                            .then_inc(sAC.h, 1)
                        bank4f[bk] = (sAC, sAC.inc(1))
                    sy4.wait_ge(sMM.h, sMM.n)
                    for b_ in range(4):
                        sy4.dma_start(
                            wbuf4.rearrange("p (dc bj) -> p dc bj", dc=16)
                            [:, :, b_ * 128:(b_ + 1) * 128],
                            gwqT_s[:, 44 + jp * 4 + b_]
                            .rearrange("dc p j -> p dc j")).then_inc(sLD.h, 16)
                        sLD.inc(16)
                    wldu = sLD.n
                    for tt in range(8):
                        bk = 2 + tt % 2
                        te4.wait_ge(sLD.h, wldu)
                        bw4(bk)
                        for dc in range(16):
                            mi = te4.matmul(
                                pb[bk][:, 0:512],
                                xq_fm[:, dc * 1024 + tt * 128:
                                      dc * 1024 + (tt + 1) * 128],
                                wbuf4[:, dc * 512:(dc + 1) * 512],
                                start=(dc == 0), stop=(dc == 15))
                        mi.then_inc(sMM.h, 1)
                        wmu = sMM.inc(1)
                        slot = tt % 2
                        ac4.wait_ge(sGD.h, out_gd)
                        ac4.activation(silb[slot], ygbuf[:, tt * 512:(tt + 1) * 512],
                                       AF.Silu).then_inc(sAC.h, 1)
                        sil_ac = sAC.inc(1)
                        ve4.wait_ge(sMM.h, wmu)
                        ve4.wait_ge(sAC.h, sil_ac)
                        if h3_spill_gd[slot]:
                            ve4.wait_ge(sGD.h, h3_spill_gd[slot])
                        ve4.scalar_tensor_tensor(h3blk[slot], pb[bk][:, 0:512],
                                                 stat[:, 32 + tt:33 + tt],
                                                 silb[slot],
                                                 ALU.mult, ALU.mult)\
                            .then_inc(sVE.h, 1)
                        h3_ve = sVE.inc(1)
                        bank4f[bk] = (sVE, h3_ve)
                        ac4.wait_ge(sVE.h, h3_ve)
                        ac4.activation(silb[slot], h3blk[slot], AF.Square,
                                       accum_out=sscr[:, tt * 11 + jp:
                                                      tt * 11 + jp + 1])\
                            .then_inc(sAC.h, 1)
                        sq_ac = sAC.inc(1)
                        ve4.wait_ge(sAC.h, sq_ac)
                        ve4.tensor_tensor(h3blk[slot], h3blk[slot],
                                          dnwRep[:, jp * 512:(jp + 1) * 512],
                                          ALU.mult).then_inc(sVE.h, 1)
                        dnw_ve = sVE.inc(1)
                        ve4.wait_ge(sVE.h, dnw_ve)
                        ve4.tensor_reduce(ascr2[:, tt * 11 + jp:
                                                tt * 11 + jp + 1],
                                          h3blk[slot], AX.X, ALU.max,
                                          apply_absolute_value=True)\
                            .then_inc(sVE.h, 1)
                        sVE.inc(1)
                        gp4.wait_ge(sVE.h, dnw_ve)
                        gp4.dma_start(h3_s[(cc3 % 2) * 8 + tt, jp], h3blk[slot])\
                            .then_inc(sGD.h, 16)
                        out_gd = sGD.inc(16)
                        h3_spill_gd[slot] = out_gd
                f_ve, f_ac, f_gd = sVE.n, sAC.n, sGD.n

                b = sVE.n
                ve4.wait_ge(sVE.h, f_ve)
                ve4.wait_ge(sAC.h, f_ac)
                ve4.tensor_reduce(stat[:, 24:32],
                                  sscr.rearrange("p (t j) -> p t j", j=11),
                                  AX.X, ALU.add).then_inc(sVE.h, 1)
                ve4.wait_ge(sVE.h, b + 1)
                ve4.tensor_reduce(stat[:, 16:24],
                                  ascr2.rearrange("p (t j) -> p t j", j=11),
                                  AX.X, ALU.max).then_inc(sVE.h, 1)
                ve4.wait_ge(sVE.h, b + 2)
                ve4.tensor_scalar(stat[:, 40:48], stat[:, 24:32],
                                  1.0 / INTER, LN_EPS, ALU.mult, ALU.add)\
                    .then_inc(sVE.h, 1)
                sVE.inc(3)
                gv = sVE.n
                ab = sAC.n
                ac4.wait_ge(sVE.h, gv)
                ac4.activation(stat[:, 48:56], stat[:, 40:48], AF.Ln)\
                    .then_inc(sAC.h, 1)
                ac4.wait_ge(sAC.h, ab + 1)
                ac4.activation(stat[:, 40:48], stat[:, 48:56], AF.Exp,
                               scale=-0.5).then_inc(sAC.h, 1)
                sAC.inc(2)
                ga_ = sAC.n
                b = sVE.n
                ve4.wait_ge(sAC.h, ga_)
                ve4.tensor_tensor(stat[:, 32:40], stat[:, 16:24],
                                  stat[:, 40:48], ALU.mult).then_inc(sVE.h, 1)
                ve4.wait_ge(sVE.h, b + 1)
                ve4.tensor_single_scalar(stat[:, 32:40], stat[:, 32:40],
                                         1e-5, ALU.max).then_inc(sVE.h, 1)
                ve4.wait_ge(sVE.h, b + 2)
                ve4.reciprocal(stat[:, 48:56], stat[:, 32:40]).then_inc(sVE.h, 1)
                ve4.wait_ge(sVE.h, b + 3)
                ve4.tensor_tensor(stat[:, 56:64], stat[:, 40:48],
                                  stat[:, 48:56], ALU.mult).then_inc(sVE.h, 1)
                ve4.wait_ge(sVE.h, b + 4)
                ve4.tensor_single_scalar(stat[:, 56:64], stat[:, 56:64],
                                         127.0, ALU.mult)
                ve4.tensor_single_scalar(stat[:, 32:40], stat[:, 32:40],
                                         1.0 / 127.0, ALU.mult).then_inc(sVE.h, 1)
                ve4.wait_ge(sVE.h, b + 5)
                for tt in range(8):
                    mi = ve4.tensor_single_scalar(
                        stat[:, 32 + tt:33 + tt], stat[:, 32 + tt:33 + tt],
                        tiny[:, 41:42], ALU.mult)
                mi.then_inc(sVE.h, 1)
                sVE.inc(6)
                g_done = sVE.n

                xq2_gd = 0
                hslots = [h3buf, h3bufB]
                lad_done = [0, 0]
                sy4.wait_ge(sGD.h, f_gd)
                sy4.wait_ge(sVE.h, sVE.n)
                sy4.wait_ge(sMM.h, sMM.n)
                sy4.dma_start(hslots[0].rearrange("p (j f) -> p j f", f=512),
                              h3_s[(cc3 % 2) * 8]
                              .rearrange("j p f -> p j f")).then_inc(sLD.h, 16)
                h3_ld = [sLD.inc(16), 0]
                for tt in range(8):
                    if tt + 1 < 8:
                        nsl = (tt + 1) % 2
                        if lad_done[nsl]:
                            sy4.wait_ge(sVE.h, lad_done[nsl])
                        sy4.dma_start(
                            hslots[nsl].rearrange("p (j f) -> p j f", f=512),
                            h3_s[(cc3 % 2) * 8 + tt + 1]
                            .rearrange("j p f -> p j f")).then_inc(sLD.h, 16)
                        h3_ld[nsl] = sLD.inc(16)
                    ve4.wait_ge(sLD.h, h3_ld[tt % 2])
                    ve4.wait_ge(sMM.h, sMM.n)
                    mi = ladder3(hslots[tt % 2], tacc[:, 0:5632], 5632,
                                 stat[:, 56 + tt:57 + tt])
                    mi.then_inc(sVE.h, 1)
                    lv = sVE.inc(1)
                    lad_done[tt % 2] = lv
                    for jc in range(44):
                        bk = 4 + jc % 2
                        te4.wait_ge(sVE.h, lv)
                        bw4(bk)
                        te4.matmul(pb[bk][:, 0:128],
                                   tacc[:, jc * 128:(jc + 1) * 128],
                                   identf[:], is_transpose=True,
                                   start=True, stop=True).then_inc(sMM.h, 1)
                        wm = sMM.inc(1)
                        ac4.wait_ge(sMM.h, wm)
                        if jc == 0 and xq2_gd:
                            ac4.wait_ge(sGD.h, xq2_gd)
                        ac4.activation(xq2buf[:, jc * 128:(jc + 1) * 128],
                                       pb[bk][:, 0:128], AF.Identity)\
                            .then_inc(sAC.h, 1)
                        bank4f[bk] = (sAC, sAC.inc(1))
                    gp4.wait_ge(sAC.h, sAC.n)
                    gp4.dma_start(xq2_s[(cc3 % 2) * 8 + tt]
                                  .rearrange("jc p t -> p jc t"),
                                  xq2buf.rearrange("p (jc t) -> p jc t", t=128))\
                        .then_inc(sGD.h, 16)
                    xq2_gd = sGD.inc(16)

                for jcg in range(4):
                    sy4.wait_ge(sGD.h, xq2_gd)
                    if jcg > 0:
                        sy4.wait_ge(sMM.h, sMM.n)
                    for t_ in range(8):
                        sy4.dma_start(
                            xq2I.rearrange("p (t jcf) -> p t jcf", t=8)[:, t_],
                            xq2_s[(cc3 % 2) * 8 + t_, jcg * 11:(jcg + 1) * 11]
                            .rearrange("jc p f -> p jc f")).then_inc(sLD.h, 16)
                        sLD.inc(16)
                    xld = sLD.n
                    for dbh in range(2):
                        sy4.wait_ge(sMM.h, sMM.n)
                        for j_ in range(11):
                            sy4.dma_start(
                                dwbuf.rearrange("p (jc bf) -> p jc bf",
                                                jc=11)[:, j_],
                                dwqT_s[jcg * 11 + j_, dbh * 8:(dbh + 1) * 8]
                                .rearrange("b p f -> p b f")).then_inc(sLD.h, 16)
                            sLD.inc(16)
                        dld = sLD.n
                        for tt in range(8):
                            for db in range(2):
                                bk = (tt * 2 + db) % 8
                                te4.wait_ge(sLD.h, max(xld, dld))
                                bw4(bk)
                                for j_ in range(11):
                                    mi = te4.matmul(
                                        pb[bk][:, 0:512],
                                        xq2I[:, (tt * 11 + j_) * 128:
                                             (tt * 11 + j_ + 1) * 128],
                                        dwbuf[:, j_ * 1024 + db * 512:
                                              j_ * 1024 + (db + 1) * 512],
                                        start=(j_ == 0), stop=(j_ == 10))
                                mi.then_inc(sMM.h, 1)
                                wm = sMM.inc(1)
                                dsl = tacc[:, tt * 2048 + dbh * 1024 +
                                           db * 512:
                                           tt * 2048 + dbh * 1024 +
                                           (db + 1) * 512]
                                ve4.wait_ge(sMM.h, wm)
                                if jcg == 0:
                                    ve4.tensor_copy(dsl, pb[bk][:, 0:512])\
                                        .then_inc(sVE.h, 1)
                                else:
                                    ve4.tensor_tensor(dsl, dsl,
                                                      pb[bk][:, 0:512],
                                                      ALU.add).then_inc(sVE.h, 1)
                                bank4f[bk] = (sVE, sVE.inc(1))
                i_ve = sVE.n

                for tt in range(8):
                    sy4.wait_ge(sVE.h, sVE.n)
                    sy4.dma_start(x2j, x2_s[cc3 * 1024 + tt * 128:
                                            cc3 * 1024 + (tt + 1) * 128, :])\
                        .then_inc(sLD.h, 16)
                    w = sLD.inc(16)
                    ve4.wait_ge(sVE.h, i_ve)
                    ve4.wait_ge(sLD.h, w)
                    ve4.scalar_tensor_tensor(outj,
                                             tacc[:, tt * 2048:(tt + 1) * 2048],
                                             stat[:, 32 + tt:33 + tt],
                                             gbR, ALU.mult, ALU.mult)
                    ve4.tensor_tensor(outj, outj, x2j, ALU.add).then_inc(sVE.h, 1)
                    ov_ = sVE.inc(1)
                    sy4.wait_ge(sVE.h, ov_)
                    sy4.dma_start(out_hbm[cc3 * 1024 + tt * 128:
                                          cc3 * 1024 + (tt + 1) * 128, :],
                                  outj).then_inc(sST.h, 16)
                    sST.inc(16)

            def s3_fin(sync, st=sST.n):
                sync.wait_ge(sST.h, st)
            p.add("sync", lambda e, r=sy4: r.replay(e))
            p.add("tensor", lambda e, r=te4: r.replay(e))
            p.add("vector", lambda e, r=ve4: r.replay(e))
            p.add("scalar", lambda e, r=ac4: r.replay(e))
            p.add("gpsimd", lambda e, r=gp4: r.replay(e))
            p.add("sync", s3_fin)

        if debug:
            dbg_stat = nc.dram_tensor("dbg_stat", [128, 64], F32,
                                      kind="ExternalOutput").ap()
            dbg_tiny = nc.dram_tensor("dbg_tiny", [128, 64], F32,
                                      kind="ExternalOutput").ap()
            dbg_modc = nc.dram_tensor("dbg_modc", [128, 96], F32,
                                      kind="ExternalOutput").ap()
            dbg_modt = nc.dram_tensor("dbg_modt", [96, 128], F32,
                                      kind="ExternalOutput").ap()

            def dbg_dump(g, ve=sVE.n, mm=sMM.n):
                g.wait_ge(sVE.h, ve)
                g.wait_ge(sMM.h, mm)
                g.dma_start(dbg_stat, stat[:]).then_inc(sGD.h, 16)
                g.dma_start(dbg_tiny, tiny[:]).then_inc(sGD.h, 16)
                g.dma_start(dbg_modc, modcol[:]).then_inc(sGD.h, 16)
                g.dma_start(dbg_modt, arC[:96, 4352:4480]).then_inc(sGD.h, 16)
            p.add("gpsimd", dbg_dump)
            sGD.inc(64)

        if stop_after in ("s0", "s1", "s2", "s2b"):
            st_at = sST.inc(16 * NT)

            def fin_copy(sync, gd=sGD.n, ve=sVE.n, ac=sAC.n, mm=sMM.n):
                sync.wait_ge(sGD.h, gd)
                sync.wait_ge(sVE.h, ve)
                sync.wait_ge(sAC.h, ac)
                sync.wait_ge(sMM.h, mm)
                for tt in range(NT):
                    sync.dma_start(out_hbm[tt * 128:(tt + 1) * 128, :],
                                   x_in[tt * 128:(tt + 1) * 128, :])\
                        .then_inc(sST.h, 16)
                sync.wait_ge(sST.h, sST.n)
            p.add("sync", fin_copy)

        with nc.Block() as block:
            @block.sync
            def _(sync):
                for fn in p.ops["sync"]:
                    fn(sync)

            @block.tensor
            def _(t):
                for fn in p.ops["tensor"]:
                    fn(t)

            @block.vector
            def _(v):
                for fn in p.ops["vector"]:
                    fn(v)

            @block.scalar
            def _(a):
                for fn in p.ops["scalar"]:
                    fn(a)

            @block.gpsimd
            def _(g):
                for fn in p.ops["gpsimd"]:
                    fn(g)

    return nc



def kernel(**inputs):
    x = np.ascontiguousarray(inputs["L0_attn_hidden_states"], dtype=np.float32)
    B = x.shape[0]
    nc = build_program(debug=DEBUG, stop_after=STOP_AFTER)

    common = {
        "Wq": inputs["L0_attn_Wq"], "Wk": inputs["L0_attn_Wk"],
        "Wv": inputs["L0_attn_Wv"], "Ww": inputs["L0_attn_Ww"],
        "Wo": inputs["L0_attn_Wo"],
        "cq": inputs["L0_attn_cq"], "ck": inputs["L0_attn_ck"],
        "cv": inputs["L0_attn_cv"],
        "Wf1": inputs["L0_attn_Wf1"], "Wf2": inputs["L0_attn_Wf2"],
        "Wb": inputs["L0_attn_Wb"],
        "Wg1": inputs["L0_attn_Wg1"], "Wg2": inputs["L0_attn_Wg2"],
        "bg2": np.asarray(inputs["L0_attn_bg2"], dtype=np.float32).reshape(1, -1),
        "norm_w": np.asarray(inputs["L0_attn_norm_w"], dtype=np.float32).reshape(1, -1),
        "gate_w": inputs["L0_ffn_gate_w"], "down_w": inputs["L0_ffn_down_w"],
        "gate_norm_w": np.asarray(inputs["L0_ffn_gate_norm_w"],
                                  dtype=np.float32).reshape(1, -1),
        "down_norm_w": np.asarray(inputs["L0_ffn_down_norm_w"],
                                  dtype=np.float32).reshape(1, -1),
        "wmod": inputs["L0_wmod"],
    }
    common = {k: np.ascontiguousarray(v, dtype=np.float32) for k, v in common.items()}

    in_maps = []
    for b in range(B):
        m = dict(common)
        m["x"] = x[b]
        m["cvec"] = np.ascontiguousarray(
            np.asarray(inputs["L0_c"], dtype=np.float32)[b:b + 1])
        in_maps.append(m)

    res = run_bass_kernel_spmd(nc, in_maps, core_ids=list(range(B)))
    out = np.stack([np.asarray(res.results[b]["out"]) for b in range(B)], axis=0)
    return out
